# version 01
# baseline (speedup 1.0000x reference)
import numpy as np
import ml_dtypes
import concourse.bass as bass
import concourse.mybir as mybir
import bass_rust
from concourse.bass_utils import run_bass_kernel_spmd

F32 = mybir.dt.float32
BF16 = mybir.dt.bfloat16
I32 = mybir.dt.int32
U32 = mybir.dt.uint32
AF = mybir.ActivationFunctionType
ALU = mybir.AluOpType
AXL = mybir.AxisListType
BF = ml_dtypes.bfloat16

D = 1024
NTOK = 16384
TPC = 2048
NB = 4
SEQ = 4096

EPS_LN = 1e-5


class Ser:

    def __init__(self, nc):
        self.nc = nc
        self.sem = nc.alloc_semaphore()
        self.val = 0
        self.last_eng = None

    def _pre(self, eng_name):
        eng = getattr(self.nc, eng_name)
        if not (eng_name == "tensor" and self.last_eng == "tensor"):
            eng.wait_ge(self.sem, self.val)
        self.last_eng = eng_name
        return eng

    def do(self, eng_name, meth, *args, inc=None, **kwargs):
        eng = self._pre(eng_name)
        inst = getattr(eng, meth)(*args, **kwargs)
        if inc is None:
            inc = 16 if "dma" in meth else 1
        inst.then_inc(self.sem, inc)
        self.val += inc
        return inst

    def fence(self, eng_name="sync"):
        eng = getattr(self.nc, eng_name)
        eng.wait_ge(self.sem, self.val)
        self.last_eng = None


def mk_nc():
    return bass.Bass("TRN2", target_bir_lowering=False)


def bf(x):
    return np.ascontiguousarray(x).astype(BF)


def f32(x):
    return np.ascontiguousarray(x, dtype=np.float32)


def as_pcols(v):
    v = np.asarray(v)
    return np.ascontiguousarray(v.reshape(-1, 128).T)



def reg_const(S, nc, val, dtype=F32, tag=None):
    tag = tag if tag is not None else str(abs(hash((float(val), str(dtype)))) % 10**8)
    t = nc.alloc_sbuf_tensor(f"const_{tag}", [128, 1], dtype).ap()
    S.do("vector", "memset", t, float(val))
    nc.const_aps.register(dtype, float(val), t)
    return t


def emit_load_weight_bf16(S, nc, name, arr_shape, nchunk_free):
    k, rem = divmod(arr_shape[0], 128)
    assert rem == 0
    m = arr_shape[1]
    d = nc.dram_tensor(name, list(arr_shape), F32, kind="ExternalInput").ap()
    s = nc.alloc_sbuf_tensor(name + "_sb", [128, k * m], BF16).ap()
    src = d.rearrange("(c p) m -> p c m", p=128)
    dst = s.rearrange("p (c m) -> p c m", m=m)
    S.do("gpsimd", "dma_start", dst, src)
    return s.rearrange("p (c m) -> p c m", m=m)


def emit_ln_feature_major(S, nc, xf, nchunks, tw, ones_sb, sc, tag):
    if "out3" in sc and sc["out3"] is not None:
        lnb3 = sc["out3"]
    else:
        lnb = nc.alloc_sbuf_tensor(f"lnb_{tag}", [128, nchunks * tw], BF16).ap()
        lnb3 = lnb.rearrange("p (c t) -> p c t", t=tw)
    ps_s = sc["ps_stat"]
    ps_b = sc["ps_bc"]
    nD = nchunks * 128
    for c in range(nchunks):
        S.do("scalar", "activation", sc["sq"], xf[:, c], AF.Square)
        S.do("tensor", "matmul", ps_s[:, 0:tw], ones_sb, xf[:, c],
             start=(c == 0), stop=(c == nchunks - 1))
        S.do("tensor", "matmul", ps_s[:, tw : 2 * tw], ones_sb, sc["sq"],
             start=(c == 0), stop=(c == nchunks - 1))
    S.do("scalar", "activation", sc["m"], ps_s[:, 0:tw], AF.Copy, scale=1.0 / nD)
    S.do("vector", "tensor_tensor", sc["v"], sc["m"], sc["m"], op=ALU.mult)
    S.do("vector", "scalar_tensor_tensor", sc["v2"], ps_s[:, tw : 2 * tw],
         1.0 / nD, sc["v"], op0=ALU.mult, op1=ALU.subtract)
    S.do("scalar", "activation", sc["v"], sc["v2"], AF.Ln, bias=EPS_LN)
    S.do("scalar", "activation", sc["r"], sc["v"], AF.Exp, scale=-0.5)
    S.do("vector", "tensor_tensor", sc["mr"], sc["m"], sc["r"], op=ALU.mult)
    S.do("tensor", "matmul", ps_b[:, 0:tw], sc["ones_row"], sc["r"],
         start=True, stop=True)
    S.do("tensor", "matmul", ps_b[:, tw : 2 * tw], sc["ones_row"], sc["mr"],
         start=True, stop=True)
    S.do("vector", "tensor_copy", sc["rb"], ps_b[:, 0:tw])
    S.do("vector", "tensor_copy", sc["mrb"], ps_b[:, tw : 2 * tw])
    for c in range(nchunks):
        S.do("scalar", "activation", sc["xb"], xf[:, c], AF.Copy)
        S.do("vector", "tensor_tensor", sc["xb2"], sc["xb"], sc["rb"], op=ALU.mult)
        S.do("vector", "tensor_tensor", lnb3[:, c], sc["xb2"], sc["mrb"],
             op=ALU.subtract)
    return lnb3


def alloc_ln_scratch(S, nc, tw, ps_bc, ps_stat=None, tag=""):
    sc = {}
    reg_const(S, nc, EPS_LN, tag="lneps")
    sc["sq"] = nc.alloc_sbuf_tensor(f"ln_sq{tag}", [128, tw], F32).ap()
    if ps_stat is None:
        ps_stat = nc.alloc_psum_tensor(f"ln_ps_s{tag}", [1, 2 * tw], F32).ap()
    sc["ps_stat"] = ps_stat
    sc["ps_bc"] = ps_bc
    for nm in ("m", "v", "v2", "r", "mr"):
        sc[nm] = nc.alloc_sbuf_tensor(f"ln_{nm}{tag}", [1, tw], F32).ap()
    sc["rb"] = nc.alloc_sbuf_tensor(f"ln_rb{tag}", [128, tw], BF16).ap()
    sc["mrb"] = nc.alloc_sbuf_tensor(f"ln_mrb{tag}", [128, tw], BF16).ap()
    sc["xb"] = nc.alloc_sbuf_tensor(f"ln_xb{tag}", [128, tw], BF16).ap()
    sc["xb2"] = nc.alloc_sbuf_tensor(f"ln_xb2{tag}", [128, tw], BF16).ap()
    ones = np.ones((128, 1), np.float32)
    onr = np.ones((1, 128), np.float32)
    sc["ones_col_host"] = ones
    sc["ones_row_host"] = onr
    ocd = nc.dram_tensor("ones_col", [128, 1], F32, kind="ExternalInput").ap()
    ord_ = nc.dram_tensor("ones_row", [1, 128], F32, kind="ExternalInput").ap()
    oc = nc.alloc_sbuf_tensor("ones_col_sb", [128, 1], F32).ap()
    orr = nc.alloc_sbuf_tensor("ones_row_sb", [1, 128], F32).ap()
    S.do("sync", "dma_start", oc, ocd)
    S.do("sync", "dma_start", orr, ord_)
    sc["ones_col"] = oc
    sc["ones_row"] = orr
    ocb = nc.alloc_sbuf_tensor("ones_col_bf", [128, 1], BF16).ap()
    orb = nc.alloc_sbuf_tensor("ones_row_bf", [1, 128], BF16).ap()
    S.do("gpsimd", "dma_start", ocb, ocd)
    S.do("gpsimd", "dma_start", orb, ord_)
    sc["ones_col_bf"] = ocb
    sc["ones_row_bf"] = orb
    return sc



def build_ffn0(merge_partials):
    nc = mk_nc()
    S = Ser(nc)
    TW = 512
    NT = TPC // TW
    xt = nc.dram_tensor("xt", [D, TPC], F32, kind="ExternalInput").ap()
    if merge_partials:
        pa = nc.dram_tensor("pa", [D, TPC], F32, kind="ExternalInput").ap()
        pb = nc.dram_tensor("pb", [D, TPC], F32, kind="ExternalInput").ap()
        bpv = nc.dram_tensor("bpv", [128, 8], F32, kind="ExternalInput").ap()
    yt = nc.dram_tensor("yt", [D, TPC], F32, kind="ExternalOutput").ap()

    U1 = emit_load_weight_bf16(S, nc, "U1", (1024, 256), 8)
    V1 = emit_load_weight_bf16(S, nc, "V1", (256, 4096), 2)
    U2 = emit_load_weight_bf16(S, nc, "U2", (4096, 256), 32)
    V2 = emit_load_weight_bf16(S, nc, "V2", (256, 1024), 2)
    b1d = nc.dram_tensor("b1v", [128, 32], F32, kind="ExternalInput").ap()
    b2d = nc.dram_tensor("b2v", [128, 8], F32, kind="ExternalInput").ap()
    b1s = nc.alloc_sbuf_tensor("b1s", [128, 32], F32).ap()
    b2s = nc.alloc_sbuf_tensor("b2s", [128, 8], F32).ap()
    S.do("sync", "dma_start", b1s, b1d)
    S.do("sync", "dma_start", b2s, b2d)
    if merge_partials:
        bps = nc.alloc_sbuf_tensor("bps", [128, 8], F32).ap()
        S.do("sync", "dma_start", bps, bpv)

    ps_A = nc.alloc_psum_tensor("ps_A", [128, 2 * TW], F32).ap()
    ps_B = nc.alloc_psum_tensor("ps_B", [128, 2 * TW], F32).ap()
    sc = alloc_ln_scratch(S, nc, TW, ps_A)

    xf = nc.alloc_sbuf_tensor("xf", [128, 8 * TW], F32).ap()
    xf3 = xf.rearrange("p (c t) -> p c t", t=TW)
    pf = nc.alloc_sbuf_tensor("pf", [128, 8 * TW], F32).ap() if merge_partials else None
    pf3 = pf.rearrange("p (c t) -> p c t", t=TW) if merge_partials else None
    Pb = nc.alloc_sbuf_tensor("Pb", [128, 2 * TW], BF16).ap()
    Pb3 = Pb.rearrange("p (c t) -> p c t", t=TW)
    Hb = nc.alloc_sbuf_tensor("Hb", [128, 32 * TW], BF16).ap()
    Hb3 = Hb.rearrange("p (c t) -> p c t", t=TW)
    Qb = nc.alloc_sbuf_tensor("Qb", [128, 2 * TW], BF16).ap()
    Qb3 = Qb.rearrange("p (c t) -> p c t", t=TW)
    yf = nc.alloc_sbuf_tensor("yf", [128, TW], F32).ap()
    ps_P = ps_A
    ps_H = ps_B
    ps_Y = ps_B

    for t in range(NT):
        tsl = slice(t * TW, (t + 1) * TW)
        src = xt[:, tsl].rearrange("(c p) t -> p c t", p=128)
        S.do("sync", "dma_start", xf3, src)
        if merge_partials:
            S.do("sync", "dma_start", pf3, pa[:, tsl].rearrange("(c p) t -> p c t", p=128))
            for c in range(8):
                S.do("vector", "tensor_tensor", xf3[:, c], xf3[:, c], pf3[:, c], op=ALU.add)
            S.do("sync", "dma_start", pf3, pb[:, tsl].rearrange("(c p) t -> p c t", p=128))
            for c in range(8):
                S.do("vector", "tensor_tensor", xf3[:, c], xf3[:, c], pf3[:, c], op=ALU.add)
                S.do("vector", "tensor_scalar", out=xf3[:, c], in0=xf3[:, c],
                     scalar1=bps[:, c : c + 1], scalar2=None, op0=ALU.add)
        lnb3 = emit_ln_feature_major(S, nc, xf3, 8, TW, sc["ones_col"], sc, f"t{t}")
        for o in range(2):
            for c in range(8):
                S.do("tensor", "matmul", ps_P[:, o * TW : (o + 1) * TW],
                     U1[:, c, o * 128 : (o + 1) * 128], lnb3[:, c],
                     start=(c == 0), stop=(c == 7))
            S.do("vector", "tensor_copy", Pb3[:, o], ps_P[:, o * TW : (o + 1) * TW])
        for f in range(32):
            fb = f % 2
            for c in range(2):
                S.do("tensor", "matmul", ps_H[:, fb * TW : (fb + 1) * TW],
                     V1[:, c, f * 128 : (f + 1) * 128], Pb3[:, c],
                     start=(c == 0), stop=(c == 1))
            S.do("scalar", "activation", Hb3[:, f], ps_H[:, fb * TW : (fb + 1) * TW],
                 AF.Gelu_apprx_tanh, bias=b1s[:, f : f + 1])
        for o in range(2):
            for c in range(32):
                S.do("tensor", "matmul", ps_P[:, o * TW : (o + 1) * TW],
                     U2[:, c, o * 128 : (o + 1) * 128], Hb3[:, c],
                     start=(c == 0), stop=(c == 31))
            S.do("vector", "tensor_copy", Qb3[:, o], ps_P[:, o * TW : (o + 1) * TW])
        for d_ in range(8):
            db = d_ % 2
            for c in range(2):
                S.do("tensor", "matmul", ps_Y[:, db * TW : (db + 1) * TW],
                     V2[:, c, d_ * 128 : (d_ + 1) * 128], Qb3[:, c],
                     start=(c == 0), stop=(c == 1))
            S.do("vector", "tensor_scalar", out=yf, in0=ps_Y[:, db * TW : (db + 1) * TW],
                 scalar1=b2s[:, d_ : d_ + 1], scalar2=0.5, op0=ALU.add, op1=ALU.mult)
            S.do("vector", "tensor_tensor", xf3[:, d_], yf, xf3[:, d_], op=ALU.add)
            dst = yt[:, tsl].rearrange("(c p) t -> p c t", p=128)[:, d_]
            S.do("sync", "dma_start", dst, xf3[:, d_])
    S.fence("sync")
    return nc


def ffn0_in_maps(xT, U1, V1, b1, U2, V2, b2, partials=None, bp=None):
    base = {
        "U1": f32(U1), "V1": f32(V1), "U2": f32(U2), "V2": f32(V2),
        "b1v": f32(as_pcols(b1)), "b2v": f32(as_pcols(b2)),
        "ones_col": np.ones((128, 1), np.float32),
        "ones_row": np.ones((1, 128), np.float32),
    }
    maps = []
    for c in range(8):
        m = dict(base)
        sl = slice(c * TPC, (c + 1) * TPC)
        m["xt"] = f32(xT[:, sl])
        if partials is not None:
            pa, pb_ = partials
            m["pa"] = f32(pa[:, sl])
            m["pb"] = f32(pb_[:, sl])
            m["bpv"] = f32(as_pcols(bp))
        maps.append(m)
    return maps



def build_ffn1(merge_partials, nh_lim=24, nd_lim=8):
    nc = mk_nc()
    S = Ser(nc)
    TW = 512
    NT = TPC // TW
    NH = 24
    xt = nc.dram_tensor("xt", [D, TPC], F32, kind="ExternalInput").ap()
    if merge_partials:
        pa = nc.dram_tensor("pa", [D, TPC], F32, kind="ExternalInput").ap()
        pb = nc.dram_tensor("pb", [D, TPC], F32, kind="ExternalInput").ap()
    yt = nc.dram_tensor("yt", [D, TPC], F32, kind="ExternalOutput").ap()
    wgT = nc.dram_tensor("wgT", [1024, 3072], I32, kind="ExternalInput").ap()
    wuT = nc.dram_tensor("wuT", [1024, 3072], I32, kind="ExternalInput").ap()
    wdT = nc.dram_tensor("wdT", [3072, 1024], I32, kind="ExternalInput").ap()

    ps_A = nc.alloc_psum_tensor("ps_A", [128, 2 * TW], F32).ap()
    ps_B = nc.alloc_psum_tensor("ps_B", [128, 2 * TW], F32).ap()
    sc = alloc_ln_scratch(S, nc, TW, ps_A)

    xf = nc.alloc_sbuf_tensor("xf", [128, 8 * TW], F32).ap()
    xf3 = xf.rearrange("p (c t) -> p c t", t=TW)
    lnball = nc.alloc_sbuf_tensor("lnball", [128, 8 * TPC], BF16).ap()
    lnb4 = lnball.rearrange("p (c u t) -> p c u t", c=8, u=NT)
    hb = nc.alloc_sbuf_tensor("hb", [128, NH * TPC], BF16).ap()
    hb4 = hb.rearrange("p (k u t) -> p k u t", k=NH, u=NT)
    wg_s = nc.alloc_sbuf_tensor("wg_s", [128, 8 * 128], BF16).ap()
    wg3 = wg_s.rearrange("p (c m) -> p c m", m=128)
    wu_s = nc.alloc_sbuf_tensor("wu_s", [128, 8 * 128], BF16).ap()
    wu3 = wu_s.rearrange("p (c m) -> p c m", m=128)
    wd_s = nc.alloc_sbuf_tensor("wd_s", [128, NH * 128], BF16).ap()
    wd3 = wd_s.rearrange("p (c m) -> p c m", m=128)
    sg = nc.alloc_sbuf_tensor("sg", [128, TW], BF16).ap()
    yf = nc.alloc_sbuf_tensor("yf", [128, TW], F32).ap()
    pfa = nc.alloc_sbuf_tensor("pfa", [128, TW], F32).ap() if merge_partials else None

    for t in range(NT):
        tsl = slice(t * TW, (t + 1) * TW)
        S.do("sync", "dma_start", xf3, xt[:, tsl].rearrange("(c p) t -> p c t", p=128))
        if merge_partials:
            for src in (pa, pb):
                for c in range(8):
                    S.do("sync", "dma_start", pfa,
                         src[:, tsl].rearrange("(c p) t -> p c t", p=128)[:, c])
                    S.do("vector", "tensor_tensor", xf3[:, c], xf3[:, c], pfa, op=ALU.add)
        sc["out3"] = lnb4[:, :, t]
        emit_ln_feature_major(S, nc, xf3, 8, TW, sc["ones_col"], sc, f"f1t{t}")
    sc["out3"] = None

    for h in range(nh_lim):
        hs = slice(h * 128, (h + 1) * 128)
        S.do("gpsimd", "dma_start", wg3, wgT[:, hs].rearrange("(c p) m -> p c m", p=128))
        S.do("gpsimd", "dma_start", wu3, wuT[:, hs].rearrange("(c p) m -> p c m", p=128))
        for t in range(NT):
            for c in range(8):
                S.do("tensor", "matmul", ps_A[:, 0:TW], wg3[:, c], lnb4[:, c, t],
                     start=(c == 0), stop=(c == 7))
            for c in range(8):
                S.do("tensor", "matmul", ps_A[:, TW : 2 * TW], wu3[:, c], lnb4[:, c, t],
                     start=(c == 0), stop=(c == 7))
            S.do("scalar", "activation", sg, ps_A[:, 0:TW], AF.Silu, scale=1.0 / 127.0)
            S.do("vector", "tensor_tensor", hb4[:, h, t], sg, ps_A[:, TW : 2 * TW],
                 op=ALU.mult)

    for d_ in range(nd_lim):
        ds = slice(d_ * 128, (d_ + 1) * 128)
        S.do("gpsimd", "dma_start", wd3, wdT[:, ds].rearrange("(c p) m -> p c m", p=128))
        for t in range(NT):
            tsl = slice(t * TW, (t + 1) * TW)
            for k in range(NH):
                S.do("tensor", "matmul", ps_B[:, 0:TW], wd3[:, k], hb4[:, k, t],
                     start=(k == 0), stop=(k == NH - 1))
            xsrc = xt[:, tsl].rearrange("(c p) t -> p c t", p=128)[:, d_]
            S.do("sync", "dma_start", yf, xsrc)
            if merge_partials:
                for src in (pa, pb):
                    S.do("sync", "dma_start", pfa,
                         src[:, tsl].rearrange("(c p) t -> p c t", p=128)[:, d_])
                    S.do("vector", "tensor_tensor", yf, yf, pfa, op=ALU.add)
            S.do("vector", "scalar_tensor_tensor", yf, ps_B[:, 0:TW],
                 0.5 / (127.0 * 127.0), yf, op0=ALU.mult, op1=ALU.add)
            dst = yt[:, tsl].rearrange("(c p) t -> p c t", p=128)[:, d_]
            S.do("sync", "dma_start", dst, yf)
    S.fence("sync")
    return nc


def ffn1_in_maps(xT, wgT, wuT, wdT, partials=None):
    base = {
        "wgT": wgT, "wuT": wuT, "wdT": wdT,
        "ones_col": np.ones((128, 1), np.float32),
        "ones_row": np.ones((1, 128), np.float32),
    }
    maps = []
    for c in range(8):
        m = dict(base)
        sl = slice(c * TPC, (c + 1) * TPC)
        m["xt"] = f32(xT[:, sl])
        if partials is not None:
            m["pa"] = f32(partials[0][:, sl])
            m["pb"] = f32(partials[1][:, sl])
        maps.append(m)
    return maps


def run_ffn1(xT, wgT, wuT, wdT, partials=None, nh_lim=24, nd_lim=8):
    nc = get_prog("ffn1", build_ffn1, partials is not None, nh_lim, nd_lim)
    maps = ffn1_in_maps(xT, wgT, wuT, wdT, partials=partials)
    res = run_bass_kernel_spmd(nc, maps, core_ids=list(range(8)))
    return np.concatenate([r["yt"] for r in res.results], axis=1)



MASKW = 1536
BIGNEG = -1.0e30


def make_mask_plane():
    k = np.arange(MASKW)[None, :]
    c = np.arange(128)[:, None]
    return np.where(k >= c + 512, 0.0, BIGNEG).astype(np.float32)


def build_attn0(half):
    KEYN = 2048 * (half + 1)
    Q0 = KEYN - 2048
    NH8 = 8
    HD = 64
    INV = 1.0 / np.sqrt(64.0)
    TW = 512
    NKT = KEYN // TW
    nc = mk_nc()
    S = Ser(nc)

    xt = nc.dram_tensor("xt", [D, KEYN], F32, kind="ExternalInput").ap()
    EXT = nc.dram_tensor("EXT", [NH8, 64, 2048], F32, kind="ExternalInput").ap()
    maskd = nc.dram_tensor("maskpl", [128, MASKW], F32, kind="ExternalInput").ap()
    yout = nc.dram_tensor("ypart", [D, 2048], F32, kind="ExternalOutput").ap()
    idd = nc.dram_tensor("ident", [128, 128], F32, kind="ExternalInput").ap()
    V2 = nc.dram_tensor("V2scratch", [KEYN * 4097], BF16, kind="Internal").ap()
    V2r = V2.rearrange("(r c) -> r c", c=4097)

    ps_A = nc.alloc_psum_tensor("ps_A", [128, 2 * TW], F32).ap()
    ps_S = nc.alloc_psum_tensor("ps_S", [128, TW], F32).ap()
    ps_O = nc.alloc_psum_tensor("ps_O", [128, TW], F32).ap()
    ps_sum = nc.alloc_psum_tensor("ps_sum", [1, TW], F32).ap()
    ps_n = ps_sum
    ps_T = nc.alloc_psum_tensor("ps_T", [128, 64], BF16).ap()

    sc = alloc_ln_scratch(S, nc, TW, ps_A, ps_stat=ps_A[0:1, :])
    ident = nc.alloc_sbuf_tensor("ident_sb", [128, 128], BF16).ap()
    S.do("gpsimd", "dma_start", ident, idd)

    lnball = nc.alloc_sbuf_tensor("lnball", [128, 8 * KEYN], BF16).ap()
    lnb4 = lnball.rearrange("p (c u t) -> p c u t", c=8, u=NKT)

    zt = lnball[:, 0:4097]
    S.do("vector", "memset", lnball[:, 0 : 2 * 4097], 0.0)
    fill_lo = max(0, Q0 - 1)
    for r0 in range(fill_lo, KEYN, 128):
        nr = min(128, KEYN - r0)
        S.do("sync", "dma_start", V2r[r0 : r0 + nr, :], zt[0:nr, :])

    with nc.sbuf_tensor("xf", [128, 8 * TW], F32) as xf_h:
        xf3 = xf_h.ap().rearrange("p (c t) -> p c t", t=TW)
        for t in range(NKT):
            S.do("sync", "dma_start", xf3,
                 xt[:, t * TW : (t + 1) * TW].rearrange("(c p) t -> p c t", p=128))
            sc["out3"] = lnb4[:, :, t]
            emit_ln_feature_major(S, nc, xf3, 8, TW, sc["ones_col"], sc, f"a0t{t}")
        sc["out3"] = None
    lnb3 = lnball.rearrange("p (c t) -> p c t", t=KEYN)

    qS = nc.alloc_sbuf_tensor("qS", [64, 2048], BF16).ap()
    qA = nc.alloc_sbuf_tensor("qA", [64, 2048], BF16).ap() if half == 1 else qS
    kh = nc.alloc_sbuf_tensor("kh", [64, KEYN], BF16).ap()
    vh = nc.alloc_sbuf_tensor("vh", [64, KEYN], BF16).ap()
    vT = nc.alloc_sbuf_tensor("vT", [128, (KEYN // 128) * 64], BF16).ap()
    vT3 = vT.rearrange("p (k v) -> p k v", v=64)
    exh = nc.alloc_sbuf_tensor("exh", [64, 2048], BF16).ap()
    asb = nc.alloc_sbuf_tensor("asb", [128, TW], BF16).ap()
    qn = nc.alloc_sbuf_tensor("qn", [1, 2050], F32).ap()
    qnA = nc.alloc_sbuf_tensor("qnA", [1, 2048], F32).ap()
    kstar = nc.alloc_sbuf_tensor("kstar", [1, 1], F32).ap()
    mtmp = nc.alloc_sbuf_tensor("mtmp", [1, 1], F32).ap()
    estar = nc.alloc_sbuf_tensor("estar", [1, 1], F32).ap()
    sqt = nc.alloc_sbuf_tensor("sqt", [64, TW], F32).ap()
    negB = nc.alloc_sbuf_tensor("negB", [1, TW], F32).ap()
    nB1 = nc.alloc_sbuf_tensor("nB1", [1, TW], F32).ap()
    SCt = nc.alloc_sbuf_tensor("SCt", [128, TW], BF16).ap()
    SRt = nc.alloc_sbuf_tensor("SRt", [128, TW], BF16, is_transpose_dest=True).ap()
    expS = nc.alloc_sbuf_tensor("expS", [128, TW], BF16).ap()
    ssum = nc.alloc_sbuf_tensor("ssum", [1, TW], F32).ap()
    rsum = nc.alloc_sbuf_tensor("rsum", [1, TW], F32).ap()
    otmp = nc.alloc_sbuf_tensor("otmp", [64, TW], F32).ap()
    osum = nc.alloc_sbuf_tensor("osum", [64, TW], F32).ap()
    Oall = nc.alloc_sbuf_tensor("Oall", [128, 4 * 2048], BF16).ap()
    Oall4 = Oall.rearrange("p (h t) -> p h t", h=4)
    yf = nc.alloc_sbuf_tensor("yf_a", [128, TW], F32).ap()

    Wq_s = emit_load_weight_bf16(S, nc, "WqT_x", (1024, 512), 8)

    def project2(dst, Wsb, col0, tok0, ntok, scale=None):
        for tt in range(ntok // TW):
            a = tok0 + tt * TW
            for c in range(8):
                S.do("tensor", "matmul", ps_A[:64, 0:TW],
                     Wsb[:, c, col0 : col0 + 64],
                     lnb3[:, c, a : a + TW], start=(c == 0), stop=(c == 7))
            if scale is None:
                S.do("vector", "tensor_copy", dst[:, tt * TW : (tt + 1) * TW],
                     ps_A[:64, 0:TW])
            else:
                S.do("scalar", "activation", dst[:, tt * TW : (tt + 1) * TW],
                     ps_A[:64, 0:TW], AF.Copy, scale=scale)

    Wk_s = emit_load_weight_bf16(S, nc, "WkT_x", (1024, 512), 8)
    Wv_s = emit_load_weight_bf16(S, nc, "WvT_x", (1024, 512), 8)
    Wp_s = emit_load_weight_bf16(S, nc, "WpT_x", (512, 1024), 8)

    for h in range(NH8):
        c0 = h * 64
        project2(qS, Wq_s, c0, Q0, 2048, scale=INV)
        if half == 1:
            project2(qA, Wq_s, c0, Q0 - 1, 2048, scale=INV)
        project2(kh, Wk_s, c0, 0, KEYN)
        project2(vh, Wv_s, c0, 0, KEYN)
        for kc in range(KEYN // 128):
            S.do("tensor", "transpose", ps_T, vh[:, kc * 128 : (kc + 1) * 128],
                 ident[0:64, 0:64])
            S.do("vector", "tensor_copy", vT3[:, kc], ps_T)
        S.do("gpsimd", "dma_start", exh, EXT[h])
        S.do("vector", "memset", qn[:, 0:2], 0.0)
        for tt in range(4):
            S.do("scalar", "activation", sqt, qS[:, tt * TW : (tt + 1) * TW], AF.Square)
            S.do("tensor", "matmul", ps_n, sc["ones_col"][0:64], sqt,
                 start=True, stop=True)
            S.do("scalar", "activation", qn[:, 1 + tt * TW : 1 + (tt + 1) * TW],
                 ps_n, AF.Sqrt)
        qAn = qnA if half == 1 else qn[:, 0:2048]
        if half == 1:
            for tt in range(4):
                S.do("scalar", "activation", sqt, qA[:, tt * TW : (tt + 1) * TW],
                     AF.Square)
                S.do("tensor", "matmul", ps_n, sc["ones_col"][0:64], sqt,
                     start=True, stop=True)
                S.do("scalar", "activation", qnA[:, tt * TW : (tt + 1) * TW],
                     ps_n, AF.Sqrt)
        for which, src_t, ntt, dstmax in ((0, kh, KEYN // TW, kstar),
                                          (1, exh, 4, estar)):
            for tt in range(ntt):
                S.do("scalar", "activation", sqt, src_t[:, tt * TW : (tt + 1) * TW],
                     AF.Square)
                S.do("tensor", "matmul", ps_n, sc["ones_col"][0:64], sqt,
                     start=True, stop=True)
                S.do("scalar", "activation", rsum, ps_n, AF.Sqrt)
                if tt == 0:
                    S.do("vector", "tensor_reduce", dstmax, rsum, axis=AXL.X,
                         op=ALU.max)
                else:
                    S.do("vector", "tensor_reduce", mtmp, rsum, axis=AXL.X,
                         op=ALU.max)
                    S.do("vector", "tensor_tensor", dstmax, dstmax, mtmp, op=ALU.max)

        for a in range(16):
            i0 = a * 128
            gi0 = (Q0 - 1 if half == 1 else 0) + i0
            for jq in range(4):
                j0 = jq * TW
                need2 = (j0 + TW - 1) >= (2048 - (gi0 + 127) - 1)
                need1 = (gi0 + 127 >= 2048) and ((j0 + TW - 1) >= (4096 - (gi0 + 127) - 1))
                if not (need2 or need1):
                    continue
                S.do("tensor", "matmul", ps_A[:, 0:TW],
                     qA[:, i0 : i0 + 128], exh[:, j0 : j0 + TW],
                     start=True, stop=True)
                S.do("vector", "tensor_copy", asb, ps_A[:, 0:TW])
                if need2:
                    S.do("sync", "dma_start",
                         V2r[gi0 : gi0 + 128, 2049 + j0 : 2049 + j0 + TW], asb)
                if need1:
                    S.do("sync", "dma_start",
                         V2r[gi0 : gi0 + 128, 1 + j0 : 1 + j0 + TW], asb)

        for t in range(4):
            R = Q0 + t * TW
            nkc = (R + TW) // 128
            S.do("vector", "tensor_scalar", out=negB,
                 in0=qn[:, 1 + t * TW : 1 + (t + 1) * TW], scalar1=kstar,
                 scalar2=None, op0=ALU.mult)
            S.do("vector", "tensor_scalar", out=nB1,
                 in0=qAn[:, t * TW : (t + 1) * TW], scalar1=estar,
                 scalar2=None, op0=ALU.mult)
            S.do("vector", "tensor_tensor", negB, negB, nB1, op=ALU.add)
            S.do("vector", "tensor_scalar", out=negB, in0=negB, scalar1=-1.0,
                 scalar2=None, op0=ALU.mult)
            for kc in range(nkc):
                S.do("tensor", "matmul", ps_S, kh[:, kc * 128 : (kc + 1) * 128],
                     qS[:, t * TW : (t + 1) * TW], start=True, stop=False)
                S.do("tensor", "matmul", ps_S, sc["ones_row"][:, 0:128], negB,
                     start=False, stop=True)
                S.do("vector", "tensor_copy", SCt, ps_S)
                off = R * 4096 + kc * 128
                src = bass_rust.AP(V2.tensor, off, [[4096, TW], [1, 128]])
                S.do("sync", "dma_start", SRt, src, transpose=True)
                S.do("vector", "tensor_tensor", SCt, SCt, SRt, op=ALU.add)
                if 128 * kc + 127 > R:
                    dlt = 128 * kc - R
                    S.do("gpsimd", "dma_start", SCt,
                         maskd[:, 512 - dlt : 1024 - dlt], accum_op=ALU.add)
                S.do("scalar", "activation", expS, SCt, AF.Exp)
                S.do("tensor", "matmul", ps_sum, sc["ones_col_bf"], expS,
                     start=(kc == 0), stop=(kc == nkc - 1))
                g = kc % 2
                last = kc >= nkc - 2
                S.do("tensor", "matmul", ps_O[64 * g : 64 * g + 64, :],
                     vT3[:, kc], expS, start=(kc == g), stop=last,
                     tile_position=(0, 64 * g))
            S.do("vector", "tensor_copy", ssum, ps_sum)
            S.do("vector", "reciprocal", rsum, ssum)
            S.do("tensor", "matmul", ps_A[:64, 0:TW], sc["ones_row"][:, 0:64], rsum,
                 start=True, stop=True)
            S.do("vector", "tensor_copy", otmp, ps_O[64:128, :])
            S.do("vector", "tensor_tensor", osum, ps_O[0:64, :], otmp, op=ALU.add)
            hc, hp = h // 2, (h % 2) * 64
            S.do("vector", "tensor_tensor",
                 Oall4[hp : hp + 64, hc, t * TW : (t + 1) * TW],
                 osum, ps_A[:64, 0:TW], op=ALU.mult)

    for t in range(4):
        for d_ in range(8):
            for hc in range(4):
                S.do("tensor", "matmul", ps_A[:, 0:TW],
                     Wp_s[:, hc, d_ * 128 : (d_ + 1) * 128],
                     Oall4[:, hc, t * TW : (t + 1) * TW],
                     start=(hc == 0), stop=(hc == 3))
            S.do("vector", "tensor_copy", yf, ps_A[:, 0:TW])
            dst = yout[:, t * TW : (t + 1) * TW].rearrange("(c p) t -> p c t", p=128)[:, d_]
            S.do("sync", "dma_start", dst, yf)
    S.fence("sync")
    return nc


def attn0_in_maps(xT, Wq, Wk, Wv, Wp, rel_embed, half):
    KEYN = 2048 * (half + 1)
    rel = np.asarray(rel_embed, np.float32).reshape(2048, 16, 64)
    maps = []
    for c in range(8):
        b, o = c // 2, c % 2
        hsl = slice(o * 512, (o + 1) * 512)
        m = {
            "xt": f32(xT[:, b * SEQ : b * SEQ + KEYN]),
            "WqT_x": f32(Wq.T[:, hsl]),
            "WkT_x": f32(Wk.T[:, hsl]),
            "WvT_x": f32(Wv.T[:, hsl]),
            "WpT_x": f32(Wp.T[hsl, :]),
            "EXT": np.ascontiguousarray(
                rel[:, o * 8 : (o + 1) * 8, :].transpose(1, 2, 0)).astype(np.float32),
            "maskpl": make_mask_plane(),
            "ident": np.eye(128, dtype=np.float32),
            "ones_col": np.ones((128, 1), np.float32),
            "ones_row": np.ones((1, 128), np.float32),
        }
        maps.append(m)
    return maps


def run_attn0(xT, Wq, Wk, Wv, Wp, rel_embed, half):
    nc = get_prog("attn0", build_attn0, half)
    maps = attn0_in_maps(xT, Wq, Wk, Wv, Wp, rel_embed, half)
    res = run_bass_kernel_spmd(nc, maps, core_ids=list(range(8)))
    return [r["ypart"] for r in res.results]



EPS_RMS = 1e-6


def build_attn1(half):
    KEYN = 2048 * (half + 1)
    Q0 = KEYN - 2048
    INV96 = 1.0 / np.sqrt(96.0)
    TW = 512
    nc = mk_nc()
    S = Ser(nc)

    xt = nc.dram_tensor("xt", [D, SEQ], F32, kind="ExternalInput").ap()
    xtok = nc.dram_tensor("xtok", [SEQ, D], F32, kind="ExternalInput").ap()
    maskd = nc.dram_tensor("maskpl", [128, MASKW], F32, kind="ExternalInput").ap()
    idd = nc.dram_tensor("ident", [128, 128], F32, kind="ExternalInput").ap()
    stud = nc.dram_tensor("stu", [128, 128], F32, kind="ExternalInput").ap()
    yout = nc.dram_tensor("ypart", [D, 2048], F32, kind="ExternalOutput").ap()
    gqd = nc.dram_tensor("g_qnorm", [96, 1], F32, kind="ExternalInput").ap()
    gkd = nc.dram_tensor("g_kvnorm", [32, 1], F32, kind="ExternalInput").ap()
    wgated = nc.dram_tensor("w_gate", [1024, 3], F32, kind="ExternalInput").ap()
    wimpd = nc.dram_tensor("w_imp", [128, 8], F32, kind="ExternalInput").ap()
    dq8d = nc.dram_tensor("dq8", [96, 768], F32, kind="ExternalInput").ap()
    dkn8d = nc.dram_tensor("dkn8", [32, 256], F32, kind="ExternalInput").ap()
    dv8d = nc.dram_tensor("dv8", [32, 256], F32, kind="ExternalInput").ap()
    selk8d = nc.dram_tensor("selk8", [1024, 768], F32, kind="ExternalInput").ap()
    selv8d = nc.dram_tensor("selv8", [1024, 256], F32, kind="ExternalInput").ap()
    wink8d = nc.dram_tensor("wink8", [1024, 768], F32, kind="ExternalInput").ap()
    winv8d = nc.dram_tensor("winv8", [1024, 256], F32, kind="ExternalInput").ap()
    cosqd = nc.dram_tensor("cosq", [32, SEQ], F32, kind="ExternalInput").ap()
    sinqd = nc.dram_tensor("sinq", [32, SEQ], F32, kind="ExternalInput").ap()
    sscr = nc.dram_tensor("sscr", [SEQ], F32, kind="Internal").ap()
    idxbuf = nc.dram_tensor("idxbuf", [256, 1], U32, kind="Internal").ap()

    ps_A = nc.alloc_psum_tensor("ps_A", [128, 2 * TW], F32).ap()
    ps_S = nc.alloc_psum_tensor("ps_S", [128, TW], F32).ap()
    ps_O = nc.alloc_psum_tensor("ps_O", [128, TW], F32).ap()
    ps_sum = nc.alloc_psum_tensor("ps_sum", [1, TW], F32).ap()
    ps_T = nc.alloc_psum_tensor("ps_T", [128, 128], BF16).ap()

    sc = alloc_ln_scratch(S, nc, TW, ps_A, ps_stat=ps_A[0:1, :])
    reg_const(S, nc, EPS_RMS, tag="rmseps")
    ident = nc.alloc_sbuf_tensor("ident_sb", [128, 128], BF16).ap()
    stu = nc.alloc_sbuf_tensor("stu_sb", [128, 128], F32).ap()
    S.do("gpsimd", "dma_start", ident, idd)
    S.do("sync", "dma_start", stu, stud)

    cq_s = emit_load_weight_bf16(S, nc, "w_cq", (1024, 96), 8)
    ckv_s = emit_load_weight_bf16(S, nc, "w_ckv", (1024, 32), 8)
    krop_s = emit_load_weight_bf16(S, nc, "w_krope", (1024, 64), 8)
    gq = nc.alloc_sbuf_tensor("gq_sb", [96, 1], F32).ap()
    gk = nc.alloc_sbuf_tensor("gk_sb", [32, 1], F32).ap()
    wgate_s = nc.alloc_sbuf_tensor("wgate_sb", [128, 8 * 3], F32).ap()
    wimp_s = nc.alloc_sbuf_tensor("wimp_sb", [128, 8], F32).ap()
    S.do("sync", "dma_start", gq, gqd)
    S.do("sync", "dma_start", gk, gkd)
    S.do("sync", "dma_start", wgate_s.rearrange("p (c m) -> p c m", m=3),
         wgated.rearrange("(c p) m -> p c m", p=128))
    S.do("sync", "dma_start", wimp_s, wimpd)
    cosq = nc.alloc_sbuf_tensor("cosq_sb", [32, SEQ], BF16).ap()
    sinq = nc.alloc_sbuf_tensor("sinq_sb", [32, SEQ], BF16).ap()
    S.do("gpsimd", "dma_start", cosq, cosqd)
    S.do("gpsimd", "dma_start", sinq, sinqd)

    lnball = nc.alloc_sbuf_tensor("lnball", [128, 8 * SEQ], BF16).ap()
    lnb4 = lnball.rearrange("p (c u t) -> p c u t", c=8, u=SEQ // TW)
    lnb3 = lnball.rearrange("p (c t) -> p c t", t=SEQ)

    cwb = nc.alloc_sbuf_tensor("cwb", [1, 1], F32).ap()
    sX = nc.alloc_sbuf_tensor("sX", [128, 8], F32).ap()
    srow = nc.alloc_sbuf_tensor("srow", [1, TW], F32).ap()
    srow2 = nc.alloc_sbuf_tensor("srow2", [1, TW], F32).ap()
    g123 = nc.alloc_sbuf_tensor("g123", [1, 3], F32).ap()
    gtmp = nc.alloc_sbuf_tensor("gtmp", [1, 3], F32).ap()
    g1s = nc.alloc_sbuf_tensor("g1s", [1, 1], F32).ap()

    for c in range(8):
        S.do("tensor", "matmul", ps_A[0:1, 0:1], wimp_s[:, c : c + 1],
             sc["ones_col"], start=(c == 0), stop=(c == 7))
    S.do("vector", "tensor_copy", cwb, ps_A[0:1, 0:1])

    with nc.sbuf_tensor("xf", [128, 8 * TW], F32) as xf_h:
        xf3 = xf_h.ap().rearrange("p (c t) -> p c t", t=TW)
        for t in range(SEQ // TW):
            S.do("sync", "dma_start", xf3,
                 xt[:, t * TW : (t + 1) * TW].rearrange("(c p) t -> p c t", p=128))
            sc["out3"] = lnb4[:, :, t]
            emit_ln_feature_major(S, nc, xf3, 8, TW, sc["ones_col"], sc, f"a1t{t}")
            for c in range(8):
                S.do("tensor", "matmul", ps_A[0:1, TW : 2 * TW],
                     wimp_s[:, c : c + 1], xf3[:, c], start=(c == 0), stop=(c == 7))
            S.do("vector", "tensor_scalar", out=srow, in0=sc["m"], scalar1=cwb,
                 scalar2=None, op0=ALU.mult)
            S.do("vector", "tensor_tensor", srow2, ps_A[0:1, TW : 2 * TW], srow,
                 op=ALU.subtract)
            S.do("vector", "tensor_tensor", srow, srow2, sc["r"], op=ALU.mult)
            S.do("sync", "dma_start",
                 sscr[t * TW : (t + 1) * TW].rearrange("(a t) -> a t", a=1),
                 srow[0:1, :])
        sc["out3"] = None
    for c in range(8):
        S.do("vector", "reduce_sum", sX[:, c : c + 1], lnb3[:, c], axis=AXL.X)
    for c in range(8):
        S.do("tensor", "matmul", ps_A[0:1, 0:3], sX[:, c : c + 1],
             wgate_s.rearrange("p (c m) -> p c m", m=3)[:, c],
             start=(c == 0), stop=(c == 7))
    S.do("scalar", "activation", g123, ps_A[0:1, 0:3], AF.Copy, scale=1.0 / SEQ)
    S.do("vector", "tensor_reduce", g1s, g123, axis=AXL.X, op=ALU.max)
    S.do("vector", "tensor_scalar", out=g1s, in0=g1s, scalar1=-1.0, scalar2=None,
         op0=ALU.mult)
    S.do("scalar", "activation", gtmp, g123, AF.Exp, bias=g1s)
    S.do("vector", "reduce_sum", g1s, gtmp, axis=AXL.X)
    S.do("vector", "reciprocal", g1s, g1s)
    S.do("vector", "tensor_scalar", out=g123, in0=gtmp, scalar1=g1s, scalar2=None,
         op0=ALU.mult)

    s128 = nc.alloc_sbuf_tensor("s128", [128, 32], F32).ap()
    S.do("sync", "dma_start", s128, sscr.rearrange("(p f) -> p f", f=32))
    lo = nc.alloc_sbuf_tensor("lo_t", [1, 1], F32).ap()
    hi = nc.alloc_sbuf_tensor("hi_t", [1, 1], F32).ap()
    mid = nc.alloc_sbuf_tensor("mid_t", [1, 1], F32).ap()
    cnt = nc.alloc_sbuf_tensor("cnt_t", [1, 1], F32).ap()
    msk1 = nc.alloc_sbuf_tensor("msk1", [1, 1], F32).ap()
    btmp = nc.alloc_sbuf_tensor("btmp", [1, 1], F32).ap()
    btmp2 = nc.alloc_sbuf_tensor("btmp2", [1, 1], F32).ap()
    midb = nc.alloc_sbuf_tensor("midb", [128, 1], F32).ap()
    zt1 = nc.alloc_sbuf_tensor("zt1", [128, 32], F32).ap()
    zcnt = nc.alloc_sbuf_tensor("zcnt", [128, 1], F32).ap()
    absm = nc.alloc_sbuf_tensor("absm", [128, 1], F32).ap()
    idf = nc.alloc_sbuf_tensor("idf_sb", [128, 128], F32).ap()
    S.do("sync", "dma_start", idf, idd)
    S.do("vector", "tensor_reduce", absm, s128, axis=AXL.X, op=ALU.max,
         apply_absolute_value=True)
    S.do("tensor", "matmul", ps_A[0:1, 0:128], absm, idf, start=True, stop=True)
    S.do("vector", "tensor_copy", srow[:, 0:128], ps_A[0:1, 0:128])
    S.do("vector", "tensor_reduce", hi, srow[:, 0:128], axis=AXL.X, op=ALU.max)
    S.do("vector", "tensor_scalar", out=hi, in0=hi, scalar1=1.0, scalar2=None,
         op0=ALU.add)
    S.do("vector", "tensor_scalar", out=lo, in0=hi, scalar1=-1.0, scalar2=None,
         op0=ALU.mult)
    for it in range(42):
        S.do("vector", "tensor_tensor", mid, lo, hi, op=ALU.add)
        S.do("vector", "tensor_scalar", out=mid, in0=mid, scalar1=0.5,
             scalar2=None, op0=ALU.mult)
        S.do("tensor", "matmul", ps_A[:, 0:1], sc["ones_row"], mid,
             start=True, stop=True)
        S.do("vector", "tensor_copy", midb, ps_A[:, 0:1])
        S.do("vector", "tensor_scalar", out=zt1, in0=s128, scalar1=midb,
             scalar2=None, op0=ALU.is_gt)
        S.do("vector", "reduce_sum", zcnt, zt1, axis=AXL.X)
        S.do("tensor", "matmul", ps_A[0:1, 4:5], zcnt, sc["ones_col"],
             start=True, stop=True)
        S.do("vector", "tensor_copy", cnt, ps_A[0:1, 4:5])
        S.do("vector", "tensor_scalar", out=msk1, in0=cnt, scalar1=255.5,
             scalar2=None, op0=ALU.is_ge)
        S.do("vector", "tensor_tensor", btmp, mid, lo, op=ALU.subtract)
        S.do("vector", "tensor_scalar", out=btmp, in0=btmp, scalar1=msk1,
             scalar2=None, op0=ALU.mult)
        S.do("vector", "tensor_tensor", btmp2, lo, btmp, op=ALU.add)
        S.do("vector", "tensor_tensor", btmp, hi, mid, op=ALU.subtract)
        S.do("vector", "tensor_scalar", out=btmp, in0=btmp, scalar1=msk1,
             scalar2=None, op0=ALU.mult)
        S.do("vector", "tensor_tensor", hi, mid, btmp, op=ALU.add)
        S.do("vector", "tensor_copy", lo, btmp2)
    S.do("tensor", "matmul", ps_A[:, 0:1], sc["ones_row"], lo, start=True, stop=True)
    S.do("vector", "tensor_copy", midb, ps_A[:, 0:1])
    S.do("vector", "tensor_scalar", out=zt1, in0=s128, scalar1=midb,
         scalar2=None, op0=ALU.is_gt)
    pfa_ = nc.alloc_sbuf_tensor("pref_a", [128, 48], F32).ap()
    pfb_ = nc.alloc_sbuf_tensor("pref_b", [128, 48], F32).ap()
    S.do("vector", "memset", pfa_, 0.0)
    S.do("vector", "memset", pfb_, 0.0)
    S.do("vector", "tensor_copy", pfa_[:, 16:48], zt1)
    cur, nxt = pfa_, pfb_
    for k in (1, 2, 4, 8, 16):
        S.do("vector", "tensor_tensor", nxt[:, 16:48], cur[:, 16:48],
             cur[:, 16 - k : 48 - k], op=ALU.add)
        if k < 16:
            S.do("vector", "tensor_copy", nxt[:, 16 - k : 16], cur[:, 16 - k : 16])
        cur, nxt = nxt, cur
    S.do("tensor", "matmul", ps_A[:, 1:2], stu, cur[:, 47:48], start=True, stop=True)
    S.do("vector", "tensor_copy", zcnt, ps_A[:, 1:2])
    rnk = nc.alloc_sbuf_tensor("rnk", [128, 32], F32).ap()
    S.do("vector", "tensor_tensor", rnk, cur[:, 16:48], zt1, op=ALU.subtract)
    S.do("vector", "tensor_scalar", out=rnk, in0=rnk, scalar1=zcnt, scalar2=None,
         op0=ALU.add)
    S.do("vector", "tensor_scalar", out=rnk, in0=rnk, scalar1=9999.0, scalar2=None,
         op0=ALU.subtract)
    S.do("vector", "tensor_tensor", rnk, rnk, zt1, op=ALU.mult)
    S.do("vector", "tensor_scalar", out=rnk, in0=rnk, scalar1=9999.0, scalar2=None,
         op0=ALU.add)
    rnku = nc.alloc_sbuf_tensor("rnku", [128, 32], U32).ap()
    S.do("vector", "tensor_copy", rnku, rnk)
    ivu = nc.alloc_sbuf_tensor("ivu", [128, 32], U32).ap()
    S.do("gpsimd", "iota", ivu, pattern=[[1, 32]], base=0, channel_multiplier=32)
    for fcol in range(32):
        S.do("gpsimd", "indirect_dma_start",
             out=idxbuf,
             out_offset=bass_rust.IndirectOffsetOnAxis(
                 ap=rnku[:, fcol : fcol + 1], axis=0),
             in_=ivu[:, fcol : fcol + 1], in_offset=None,
             bounds_check=255, oob_is_err=False)
    idxsb = nc.alloc_sbuf_tensor("idxsb", [128, 2], U32).ap()
    S.do("sync", "dma_start", idxsb[:, 0:1], idxbuf[0:128])
    S.do("sync", "dma_start", idxsb[:, 1:2], idxbuf[128:256])
    selm = nc.alloc_sbuf_tensor("selm", [128, 1], F32).ap()
    selv_ = nc.alloc_sbuf_tensor("selv_", [128, 1], F32).ap()
    selr = nc.alloc_sbuf_tensor("selr", [128, 1], F32).ap()
    selTb = nc.alloc_sbuf_tensor("selTb", [128, 8 * 256], BF16).ap()
    selT3 = selTb.rearrange("p (c s) -> p c s", s=256)
    with nc.sbuf_tensor("selx", [128, D], F32) as selx_h, \
         nc.sbuf_tensor("sqsel", [128, D], F32) as sqsel_h, \
         nc.sbuf_tensor("selb", [128, D], BF16) as selb_h:
        selx, sqsel, selb = selx_h.ap(), sqsel_h.ap(), selb_h.ap()
        for g in range(2):
            S.do("gpsimd", "indirect_dma_start",
                 out=selx, out_offset=None, in_=xtok,
                 in_offset=bass_rust.IndirectOffsetOnAxis(
                     ap=idxsb[:, g : g + 1], axis=0),
                 bounds_check=SEQ - 1, oob_is_err=False)
            S.do("scalar", "activation", sqsel, selx, AF.Square)
            S.do("vector", "reduce_sum", selm, selx, axis=AXL.X)
            S.do("vector", "reduce_sum", selv_, sqsel, axis=AXL.X)
            S.do("vector", "tensor_scalar", out=selm, in0=selm, scalar1=1.0 / D,
                 scalar2=None, op0=ALU.mult)
            S.do("vector", "tensor_tensor", selr, selm, selm, op=ALU.mult)
            S.do("vector", "scalar_tensor_tensor", selv_, selv_, 1.0 / D, selr,
                 op0=ALU.mult, op1=ALU.subtract)
            S.do("scalar", "activation", selv_, selv_, AF.Ln, bias=EPS_LN)
            S.do("scalar", "activation", selr, selv_, AF.Exp, scale=-0.5)
            S.do("vector", "tensor_scalar", out=selb, in0=selx, scalar1=selm,
                 scalar2=selr, op0=ALU.subtract, op1=ALU.mult)
            for c in range(8):
                S.do("tensor", "transpose", ps_T,
                     selb[:, c * 128 : (c + 1) * 128], ident)
                S.do("vector", "tensor_copy", selT3[:, c, g * 128 : (g + 1) * 128],
                     ps_T)

    ckvb = nc.alloc_sbuf_tensor("ckvb", [32, KEYN], BF16).ap()
    rmsr = nc.alloc_sbuf_tensor("rmsr", [1, TW], F32).ap()
    nqb = nc.alloc_sbuf_tensor("nqb", [96, 2048], BF16).ap()
    rawb = nc.alloc_sbuf_tensor("rawb", [96, SEQ], BF16).ap()

    def rms_project(dst, Wsb, ncols, ntile, tok0, gvec, extra_scale=None):
        for tt in range(ntile):
            a = tok0 + tt * TW
            for c in range(8):
                S.do("tensor", "matmul", ps_A[:ncols, 0:TW], Wsb[:, c, 0:ncols],
                     lnb3[:, c, a : a + TW], start=(c == 0), stop=(c == 7))
            S.do("scalar", "activation", sqt96[:ncols, :], ps_A[:ncols, 0:TW],
                 AF.Square)
            S.do("tensor", "matmul", ps_sum, sc["ones_col"][0:ncols],
                 sqt96[:ncols, :], start=True, stop=True)
            S.do("scalar", "activation", rmsr, ps_sum, AF.Ln, bias=EPS_RMS,
                 scale=1.0 / ncols)
            S.do("scalar", "activation", rmsr, rmsr, AF.Exp, scale=-0.5)
            S.do("tensor", "matmul", ps_A[:ncols, TW : 2 * TW],
                 sc["ones_row"][:, 0:ncols], rmsr, start=True, stop=True)
            S.do("vector", "tensor_copy", sqt96[:ncols, :],
                 ps_A[:ncols, TW : 2 * TW])
            S.do("vector", "tensor_tensor", sqt96[:ncols, :], ps_A[:ncols, 0:TW],
                 sqt96[:ncols, :], op=ALU.mult)
            if extra_scale is not None:
                S.do("vector", "tensor_scalar", out=sqt96[:ncols, :],
                     in0=sqt96[:ncols, :], scalar1=extra_scale, scalar2=None,
                     op0=ALU.mult)
            S.do("vector", "tensor_scalar", out=dst[:, tt * TW : (tt + 1) * TW],
                 in0=sqt96[:ncols, :], scalar1=gvec, scalar2=None, op0=ALU.mult)

    sqt96 = nc.alloc_sbuf_tensor("sqt96", [96, TW], F32).ap()
    rms_project(nqb, cq_s, 96, 4, Q0, gq)
    rms_project(ckvb, ckv_s, 32, KEYN // TW, 0, gk)

    def rope_emit(dst, src, n32, cos_ap, sin_ap, tok0, ntok):
        for a0 in range(0, ntok, 1024):
            w_ = min(1024, ntok - a0)
            lo_s = src[n32 : n32 + 32, a0 : a0 + w_]
            hi_s = src[n32 + 32 : n32 + 64, a0 : a0 + w_]
            lo_d = dst[n32 : n32 + 32, a0 : a0 + w_]
            hi_d = dst[n32 + 32 : n32 + 64, a0 : a0 + w_]
            ct = cos_ap[:, tok0 + a0 : tok0 + a0 + w_]
            st = sin_ap[:, tok0 + a0 : tok0 + a0 + w_]
            l0 = ropes1[:, 0:w_]
            h0_ = ropes2[:, 0:w_]
            r1 = ropet1[:, 0:w_]
            r2 = ropet2[:, 0:w_]
            S.do("vector", "tensor_copy", l0, lo_s)
            S.do("vector", "tensor_copy", h0_, hi_s)
            S.do("vector", "tensor_tensor", r1, l0, ct, op=ALU.mult)
            S.do("vector", "tensor_tensor", r2, h0_, st, op=ALU.mult)
            S.do("vector", "tensor_tensor", lo_d, r1, r2, op=ALU.subtract)
            S.do("vector", "tensor_tensor", r1, l0, st, op=ALU.mult)
            S.do("vector", "tensor_tensor", r2, h0_, ct, op=ALU.mult)
            S.do("vector", "tensor_tensor", hi_d, r1, r2, op=ALU.add)

    ropet1 = nc.alloc_sbuf_tensor("ropet1", [32, 1024], BF16).ap()
    ropet2 = nc.alloc_sbuf_tensor("ropet2", [32, 1024], BF16).ap()
    ropes1 = nc.alloc_sbuf_tensor("ropes1", [32, 1024], BF16).ap()
    ropes2 = nc.alloc_sbuf_tensor("ropes2", [32, 1024], BF16).ap()

    k1h = nc.alloc_sbuf_tensor("k1h", [96, KEYN], BF16).ap()
    for tt in range(KEYN // TW):
        for c in range(8):
            S.do("tensor", "matmul", ps_A[:64, 0:TW], krop_s[:, c, :],
                 lnb3[:, c, tt * TW : (tt + 1) * TW], start=(c == 0), stop=(c == 7))
        S.do("scalar", "activation", rawb[32:64, tt * TW : (tt + 1) * TW],
             ps_A[0:32, 0:TW], AF.Copy, scale=1.0 / 16.0)
        S.do("scalar", "activation", rawb[64:96, tt * TW : (tt + 1) * TW],
             ps_A[32:64, 0:TW], AF.Copy, scale=1.0 / 16.0)
    rope_emit(k1h, rawb, 32, cosq, sinq, 0, KEYN)

    dq_s = nc.alloc_sbuf_tensor("dq_s", [96, 96], BF16).ap()
    dkn_s = nc.alloc_sbuf_tensor("dkn_s", [32, 32], BF16).ap()
    dv_s = nc.alloc_sbuf_tensor("dv_s", [32, 32], BF16).ap()
    selk_s = nc.alloc_sbuf_tensor("selk_s", [128, 8 * 96], BF16).ap()
    selk3 = selk_s.rearrange("p (c m) -> p c m", m=96)
    selv_s = nc.alloc_sbuf_tensor("selv_s", [128, 8 * 32], BF16).ap()
    selv3 = selv_s.rearrange("p (c m) -> p c m", m=32)
    wink_s = nc.alloc_sbuf_tensor("wink_s", [128, 8 * 96], BF16).ap()
    wink3 = wink_s.rearrange("p (c m) -> p c m", m=96)
    winv_s = nc.alloc_sbuf_tensor("winv_s", [128, 8 * 32], BF16).ap()
    winv3 = winv_s.rearrange("p (c m) -> p c m", m=32)
    wproj_s = emit_load_weight_bf16(S, nc, "wproj8", (256, 1024), 2)

    qt = nc.alloc_sbuf_tensor("qt", [96, 2048], BF16).ap()
    vtmp = nc.alloc_sbuf_tensor("vtmp", [32, TW], BF16).ap()
    v1T = nc.alloc_sbuf_tensor("v1T", [128, (KEYN // 128) * 32], BF16).ap()
    v1T3 = v1T.rearrange("p (k v) -> p k v", v=32)
    kwh = nc.alloc_sbuf_tensor("kwh", [96, KEYN], BF16).ap()
    vwT = nc.alloc_sbuf_tensor("vwT", [128, (KEYN // 128) * 32], BF16).ap()
    vwT3 = vwT.rearrange("p (k v) -> p k v", v=32)
    ksh = nc.alloc_sbuf_tensor("ksh", [96, 256], BF16).ap()
    vsT = nc.alloc_sbuf_tensor("vsT", [128, 2 * 32], BF16).ap()
    vsT3 = vsT.rearrange("p (k v) -> p k v", v=32)
    qnq = nc.alloc_sbuf_tensor("qnq", [1, TW], F32).ap()
    knsq = nc.alloc_sbuf_tensor("knsq", [1, TW], F32).ap()
    k1st = nc.alloc_sbuf_tensor("k1st", [1, 1], F32).ap()
    k2st = nc.alloc_sbuf_tensor("k2st", [1, 1], F32).ap()
    k3st = nc.alloc_sbuf_tensor("k3st", [1, 1], F32).ap()
    mtmp = nc.alloc_sbuf_tensor("mtmp1", [1, 1], F32).ap()
    negB = nc.alloc_sbuf_tensor("negB1", [1, TW], F32).ap()
    SCt = nc.alloc_sbuf_tensor("SCt1", [128, TW], BF16).ap()
    expS = nc.alloc_sbuf_tensor("expS1", [128, TW], BF16).ap()
    ssum = nc.alloc_sbuf_tensor("ssum1", [1, TW], F32).ap()
    rsum = nc.alloc_sbuf_tensor("rsum1", [1, TW], F32).ap()
    otmp = nc.alloc_sbuf_tensor("otmp1", [32, TW], F32).ap()
    oacc = nc.alloc_sbuf_tensor("oacc1", [32, TW], F32).ap()
    ostg = nc.alloc_sbuf_tensor("ostg1", [32, TW], BF16).ap()
    Oall = nc.alloc_sbuf_tensor("Oall1", [128, 2 * 2048], BF16).ap()
    Oall4 = Oall.rearrange("p (h t) -> p h t", h=2)
    yf = nc.alloc_sbuf_tensor("yf_a1", [128, TW], F32).ap()
    sqw = nc.alloc_sbuf_tensor("sqw", [96, TW], F32).ap()

    def norm_max(src_t, npart, ntt, dstmax, width=TW):
        for tt in range(ntt):
            S.do("scalar", "activation", sqw[:npart, 0:width],
                 src_t[:, tt * width : (tt + 1) * width], AF.Square)
            S.do("tensor", "matmul", ps_sum[:, 0:width], sc["ones_col"][0:npart],
                 sqw[:npart, 0:width], start=True, stop=True)
            S.do("scalar", "activation", knsq[:, 0:width], ps_sum[:, 0:width],
                 AF.Sqrt)
            if tt == 0:
                S.do("vector", "tensor_reduce", dstmax, knsq[:, 0:width],
                     axis=AXL.X, op=ALU.max)
            else:
                S.do("vector", "tensor_reduce", mtmp, knsq[:, 0:width],
                     axis=AXL.X, op=ALU.max)
                S.do("vector", "tensor_tensor", dstmax, dstmax, mtmp, op=ALU.max)

    for h in range(8):
        S.do("gpsimd", "dma_start", dq_s, dq8d[:, h * 96 : (h + 1) * 96])
        S.do("gpsimd", "dma_start", dkn_s, dkn8d[:, h * 32 : (h + 1) * 32])
        S.do("gpsimd", "dma_start", dv_s, dv8d[:, h * 32 : (h + 1) * 32])
        S.do("gpsimd", "dma_start", selk3,
             selk8d[:, h * 96 : (h + 1) * 96].rearrange("(c p) m -> p c m", p=128))
        S.do("gpsimd", "dma_start", selv3,
             selv8d[:, h * 32 : (h + 1) * 32].rearrange("(c p) m -> p c m", p=128))
        S.do("gpsimd", "dma_start", wink3,
             wink8d[:, h * 96 : (h + 1) * 96].rearrange("(c p) m -> p c m", p=128))
        S.do("gpsimd", "dma_start", winv3,
             winv8d[:, h * 32 : (h + 1) * 32].rearrange("(c p) m -> p c m", p=128))

        for tt in range(KEYN // TW):
            a = tt * TW
            S.do("tensor", "matmul", ps_A[:32, 0:TW], dkn_s, ckvb[:, a : a + TW],
                 start=True, stop=True)
            S.do("vector", "tensor_copy", k1h[0:32, a : a + TW], ps_A[:32, 0:TW])
            S.do("tensor", "matmul", ps_A[:32, TW : 2 * TW], dv_s,
                 ckvb[:, a : a + TW], start=True, stop=True)
            S.do("vector", "tensor_copy", vtmp, ps_A[:32, TW : 2 * TW])
            for j in range(4):
                S.do("tensor", "transpose", ps_T[:, 0:32],
                     vtmp[:, j * 128 : (j + 1) * 128], ident[0:32, 0:32])
                S.do("vector", "tensor_copy", v1T3[:, tt * 4 + j], ps_T[:, 0:32])
        for tt in range(KEYN // TW):
            a = tt * TW
            for c in range(8):
                S.do("tensor", "matmul", ps_A[:96, 0:TW], wink3[:, c],
                     lnb3[:, c, a : a + TW], start=(c == 0), stop=(c == 7))
            S.do("vector", "tensor_copy", rawb[:, a : a + TW], ps_A[:96, 0:TW])
            for c in range(8):
                S.do("tensor", "matmul", ps_A[:32, TW : 2 * TW], winv3[:, c],
                     lnb3[:, c, a : a + TW], start=(c == 0), stop=(c == 7))
            S.do("vector", "tensor_copy", vtmp, ps_A[:32, TW : 2 * TW])
            for j in range(4):
                S.do("tensor", "transpose", ps_T[:, 0:32],
                     vtmp[:, j * 128 : (j + 1) * 128], ident[0:32, 0:32])
                S.do("vector", "tensor_copy", vwT3[:, tt * 4 + j], ps_T[:, 0:32])
        S.do("vector", "tensor_copy", kwh[0:32, :], rawb[0:32, 0:KEYN])
        rope_emit(kwh, rawb[:, 0:KEYN], 32, cosq, sinq, 0, KEYN)
        for c in range(8):
            S.do("tensor", "matmul", ps_A[:96, 0:256], selk3[:, c],
                 selT3[:, c], start=(c == 0), stop=(c == 7))
        S.do("vector", "tensor_copy", rawb[:, 0:256], ps_A[:96, 0:256])
        S.do("vector", "tensor_copy", ksh[0:32, :], rawb[0:32, 0:256])
        rope_emit(ksh, rawb[:, 0:256], 32, cosq, sinq, 0, 256)
        for c in range(8):
            S.do("tensor", "matmul", ps_A[:32, 256:512], selv3[:, c],
                 selT3[:, c], start=(c == 0), stop=(c == 7))
        S.do("vector", "tensor_copy", rawb[0:32, 512 : 512 + 256],
             ps_A[:32, 256:512])
        for g in range(2):
            S.do("tensor", "transpose", ps_T[:, 0:32],
                 rawb[0:32, 512 + g * 128 : 512 + (g + 1) * 128],
                 ident[0:32, 0:32])
            S.do("vector", "tensor_copy", vsT3[:, g], ps_T[:, 0:32])
        for tt in range(4):
            a = tt * TW
            S.do("tensor", "matmul", ps_A[:96, 0:TW], dq_s, nqb[:, a : a + TW],
                 start=True, stop=True)
            S.do("scalar", "activation", rawb[:, a : a + TW], ps_A[:96, 0:TW],
                 AF.Copy, scale=INV96)
        S.do("vector", "tensor_copy", qt[0:32, :], rawb[0:32, 0:2048])
        rope_emit(qt, rawb[:, 0:2048], 32, cosq, sinq, Q0, 2048)
        norm_max(k1h, 96, KEYN // TW, k1st)
        norm_max(ksh, 96, 1, k2st, width=256)
        norm_max(kwh, 96, KEYN // TW, k3st)

        for t in range(4):
            R = Q0 + t * TW
            nkc = (R + TW) // 128
            S.do("scalar", "activation", sqw, qt[:, t * TW : (t + 1) * TW],
                 AF.Square)
            S.do("tensor", "matmul", ps_sum, sc["ones_col"][0:96], sqw,
                 start=True, stop=True)
            S.do("scalar", "activation", qnq, ps_sum, AF.Sqrt)
            S.do("vector", "memset", oacc, 0.0)
            for br in range(3):
                kst = (k1st, k2st, k3st)[br]
                S.do("vector", "tensor_scalar", out=negB,
                     in0=qnq, scalar1=kst,
                     scalar2=None, op0=ALU.mult)
                S.do("vector", "tensor_scalar", out=negB, in0=negB, scalar1=-1.0,
                     scalar2=None, op0=ALU.mult)
                nkc_b = 2 if br == 1 else nkc
                for kc in range(nkc_b):
                    ksl = slice(kc * 128, (kc + 1) * 128)
                    if br == 0:
                        S.do("tensor", "matmul", ps_S, k1h[:, ksl],
                             qt[:, t * TW : (t + 1) * TW], start=True, stop=False)
                    elif br == 1:
                        S.do("tensor", "matmul", ps_S, ksh[:, ksl],
                             qt[:, t * TW : (t + 1) * TW], start=True, stop=False)
                    else:
                        S.do("tensor", "matmul", ps_S, kwh[:, ksl],
                             qt[:, t * TW : (t + 1) * TW], start=True, stop=False)
                    S.do("tensor", "matmul", ps_S, sc["ones_row"][:, 0:128], negB,
                         start=False, stop=True)
                    S.do("vector", "tensor_copy", SCt, ps_S)
                    if br != 1 and 128 * kc + 127 > R:
                        dlt = 128 * kc - R
                        S.do("gpsimd", "dma_start", SCt,
                             maskd[:, 512 - dlt : 1024 - dlt], accum_op=ALU.add)
                    S.do("scalar", "activation", expS, SCt, AF.Exp)
                    S.do("tensor", "matmul", ps_sum, sc["ones_col_bf"], expS,
                         start=(kc == 0), stop=(kc == nkc_b - 1))
                    vsrc = (v1T3, vsT3, vwT3)[br]
                    g = kc % 2
                    S.do("tensor", "matmul", ps_O[32 * g : 32 * g + 32, :],
                         vsrc[:, kc], expS, start=(kc == g),
                         stop=(kc >= nkc_b - 2), tile_position=(0, 32 * g))
                S.do("vector", "tensor_copy", ssum, ps_sum)
                S.do("vector", "reciprocal", rsum, ssum)
                S.do("vector", "tensor_scalar", out=rsum, in0=rsum,
                     scalar1=g123[:, br : br + 1], scalar2=None, op0=ALU.mult)
                S.do("tensor", "matmul", ps_A[:32, 0:TW], sc["ones_row"][:, 0:32],
                     rsum, start=True, stop=True)
                if nkc_b >= 2:
                    S.do("vector", "tensor_copy", otmp, ps_O[32:64, :])
                    S.do("vector", "tensor_tensor", otmp, ps_O[0:32, :], otmp,
                         op=ALU.add)
                else:
                    S.do("vector", "tensor_copy", otmp, ps_O[0:32, :])
                S.do("vector", "tensor_tensor", otmp, otmp, ps_A[:32, 0:TW],
                     op=ALU.mult)
                S.do("vector", "tensor_tensor", oacc, oacc, otmp, op=ALU.add)
            hc, hb = h // 4, 32 * (h % 4)
            if hb == 96:
                S.do("vector", "tensor_copy", ostg, oacc)
                S.do("sync", "dma_start",
                     Oall4[96:128, hc, t * TW : (t + 1) * TW], ostg)
            else:
                S.do("vector", "tensor_copy",
                     Oall4[hb : hb + 32, hc, t * TW : (t + 1) * TW], oacc)

    for t in range(4):
        for d_ in range(8):
            for hc in range(2):
                S.do("tensor", "matmul", ps_A[:, 0:TW],
                     wproj_s[:, hc, d_ * 128 : (d_ + 1) * 128],
                     Oall4[:, hc, t * TW : (t + 1) * TW],
                     start=(hc == 0), stop=(hc == 1))
            S.do("vector", "tensor_copy", yf, ps_A[:, 0:TW])
            dst = yout[:, t * TW : (t + 1) * TW].rearrange("(c p) t -> p c t", p=128)[:, d_]
            S.do("sync", "dma_start", dst, yf)
    S.fence("sync")
    return nc


def rope_tables():
    f = 1.0 / (10000.0 ** (np.arange(0, 64, 2, dtype=np.float32) / 64.0))
    t = np.arange(SEQ, dtype=np.float32)
    a = np.outer(t, f)
    return np.ascontiguousarray(np.cos(a).T), np.ascontiguousarray(np.sin(a).T)


def attn1_in_maps(inputs, xT, xtokfull, half):
    cosq, sinq = rope_tables()
    g = lambda k: np.asarray(inputs["L1_attn_" + k], np.float32)
    w_dqn, w_dqr = g("w_dq_nope"), g("w_dq_rope")
    w_dkn, w_dv = g("w_dk_nope"), g("w_dv")
    w_selk, w_selv = g("w_selk"), g("w_selv")
    w_wink, w_winv = g("w_wink"), g("w_winv")
    w_proj = g("w_proj")

    def perm96(W, h0):
        raise RuntimeError

    def headblock_k(Wn, Wr, h0):
        cols = []
        for h in range(h0, h0 + 8):
            n = Wn[:, h * 32 : (h + 1) * 32]
            r = Wr[:, h * 64 : (h + 1) * 64]
            cols += [n, r[:, 0:32], r[:, 32:64]]
        return np.ascontiguousarray(np.concatenate(cols, axis=1))

    def headblock_kr(Wkr, h0):
        cols = []
        for h in range(h0, h0 + 8):
            blk = Wkr[:, h * 96 : (h + 1) * 96]
            cols += [blk[:, 0:32], blk[:, 32:64], blk[:, 64:96]]
        return np.ascontiguousarray(np.concatenate(cols, axis=1))

    def headblock_v(Wv, h0):
        return np.ascontiguousarray(Wv[:, h0 * 32 : (h0 + 8) * 32])

    maps = []
    for c in range(8):
        b, o = c // 2, c % 2
        h0 = o * 8
        wp8 = np.zeros((256, 1024), np.float32)
        for hp in range(8):
            src = w_proj[(h0 + hp) * 32 : (h0 + hp) * 32 + 32, :]
            wp8[(hp // 4) * 128 + (hp % 4) * 32 : (hp // 4) * 128 + (hp % 4) * 32 + 32] = src
        m = {
            "xt": f32(xT[:, b * SEQ : (b + 1) * SEQ]),
            "xtok": f32(xtokfull[b * SEQ : (b + 1) * SEQ, :]),
            "maskpl": make_mask_plane(),
            "ident": np.eye(128, dtype=np.float32),
            "stu": np.triu(np.ones((128, 128), np.float32), 1),
            "g_qnorm": g("g_qnorm").reshape(96, 1),
            "g_kvnorm": g("g_kvnorm").reshape(32, 1),
            "w_gate": g("w_gate"),
            "w_imp": f32(as_pcols(g("w_imp")[:, 0])),
            "w_cq": g("w_cq"),
            "w_ckv": g("w_ckv"),
            "w_krope": g("w_krope"),
            "dq8": headblock_k(w_dqn, w_dqr, h0),
            "dkn8": headblock_v(w_dkn, o * 8),
            "dv8": headblock_v(w_dv, o * 8),
            "selk8": headblock_kr(w_selk, h0),
            "selv8": headblock_v(w_selv, o * 8),
            "wink8": headblock_kr(w_wink, h0),
            "winv8": headblock_v(w_winv, o * 8),
            "wproj8": wp8,
            "cosq": cosq, "sinq": sinq,
            "ones_col": np.ones((128, 1), np.float32),
            "ones_row": np.ones((1, 128), np.float32),
        }
        maps.append(m)
    return maps


def run_attn1(inputs, xT, half):
    nc = get_prog("attn1", build_attn1, half)
    xtok = np.ascontiguousarray(xT.T)
    maps = attn1_in_maps(inputs, xT, xtok, half)
    res = run_bass_kernel_spmd(nc, maps, core_ids=list(range(8)))
    return [r["ypart"] for r in res.results]





_PROG_CACHE = {}


def get_prog(name, builder, *args):
    key = (name, args)
    if key not in _PROG_CACHE:
        _PROG_CACHE[key] = builder(*args)
    return _PROG_CACHE[key]


def run_ffn0(xT, w, partials=None, bp=None):
    nc = get_prog("ffn0", build_ffn0, partials is not None)
    maps = ffn0_in_maps(xT, w["U1"], w["V1"], w["b1"], w["U2"], w["V2"], w["b2"],
                        partials=partials, bp=bp)
    res = run_bass_kernel_spmd(nc, maps, core_ids=list(range(8)))
    out = np.concatenate([r["yt"] for r in res.results], axis=1)
    return out


def assemble_partials(plist0, plist1):
    PA = np.zeros((D, NTOK), np.float32)
    PB = np.zeros((D, NTOK), np.float32)
    for c in range(8):
        b, half = c // 2, c % 2
        src = plist0 if half == 0 else plist1
        PA[:, c * TPC : (c + 1) * TPC] = src[2 * b]
        PB[:, c * TPC : (c + 1) * TPC] = src[2 * b + 1]
    return PA, PB


def kernel(**inputs):
    x = np.asarray(inputs["L0_attn_x"], np.float32).reshape(NTOK, D)
    xT = np.ascontiguousarray(x.T)
    w0 = {k: np.asarray(inputs["L0_ffn_" + k]) for k in
          ("U1", "V1", "b1", "U2", "V2", "b2")}
    xT = run_ffn0(xT, w0)
    Wq = np.asarray(inputs["L0_attn_Wq"])
    Wk = np.asarray(inputs["L0_attn_Wk"])
    Wv = np.asarray(inputs["L0_attn_Wv"])
    Wp = np.asarray(inputs["L0_attn_Wp"])
    bp = np.asarray(inputs["L0_attn_bp"])
    rel = np.asarray(inputs["L0_attn_rel_embed"])
    p0 = run_attn0(xT, Wq, Wk, Wv, Wp, rel, 0)
    p1 = run_attn0(xT, Wq, Wk, Wv, Wp, rel, 1)
    PA, PB = assemble_partials(p0, p1)
    xT = run_ffn0(xT, w0, partials=(PA, PB), bp=bp)
    wgT = np.ascontiguousarray(np.asarray(inputs["L1_ffn_w_gate"]).T)
    wuT = np.ascontiguousarray(np.asarray(inputs["L1_ffn_w_up"]).T)
    wdT = np.ascontiguousarray(np.asarray(inputs["L1_ffn_w_down"]).T)
    xT = run_ffn1(xT, wgT, wuT, wdT)
    q0 = run_attn1(inputs, xT, 0)
    q1 = run_attn1(inputs, xT, 1)
    QA, QB = assemble_partials(q0, q1)
    xT = run_ffn1(xT, wgT, wuT, wdT, partials=(QA, QB))
    out = np.ascontiguousarray(xT.T).reshape(4, 4096, 1024)
    return out


# version 05
# speedup vs baseline: 1.0354x; 1.0354x over previous
import numpy as np
import ml_dtypes
import concourse.bass as bass
import concourse.mybir as mybir
import bass_rust
from concourse.bass_utils import run_bass_kernel_spmd

F32 = mybir.dt.float32
BF16 = mybir.dt.bfloat16
I32 = mybir.dt.int32
U32 = mybir.dt.uint32
AF = mybir.ActivationFunctionType
ALU = mybir.AluOpType
AXL = mybir.AxisListType
BF = ml_dtypes.bfloat16

D = 1024
NTOK = 16384
TPC = 2048
NB = 4
SEQ = 4096

EPS_LN = 1e-5


class Ser:

    def __init__(self, nc):
        self.nc = nc
        self.sem = nc.alloc_semaphore()
        self.val = 0
        self.last_eng = None

    def _pre(self, eng_name):
        eng = getattr(self.nc, eng_name)
        if not (eng_name == "tensor" and self.last_eng == "tensor"):
            eng.wait_ge(self.sem, self.val)
        self.last_eng = eng_name
        return eng

    def do(self, eng_name, meth, *args, inc=None, wait_val=None, **kwargs):
        if wait_val is None:
            eng = self._pre(eng_name)
        else:
            eng = getattr(self.nc, eng_name)
            eng.wait_ge(self.sem, min(wait_val, self.val))
            self.last_eng = None
        inst = getattr(eng, meth)(*args, **kwargs)
        if inc is None:
            inc = 16 if "dma" in meth else 1
        inst.then_inc(self.sem, inc)
        self.val += inc
        return inst

    def fence(self, eng_name="sync"):
        eng = getattr(self.nc, eng_name)
        eng.wait_ge(self.sem, self.val)
        self.last_eng = None


def mk_nc():
    return bass.Bass("TRN2", target_bir_lowering=False)


def bf(x):
    return np.ascontiguousarray(x).astype(BF)


def f32(x):
    return np.ascontiguousarray(x, dtype=np.float32)


def as_pcols(v):
    v = np.asarray(v)
    return np.ascontiguousarray(v.reshape(-1, 128).T)



def reg_const(S, nc, val, dtype=F32, tag=None):
    tag = tag if tag is not None else str(abs(hash((float(val), str(dtype)))) % 10**8)
    t = nc.alloc_sbuf_tensor(f"const_{tag}", [128, 1], dtype).ap()
    S.do("vector", "memset", t, float(val))
    nc.const_aps.register(dtype, float(val), t)
    return t


def emit_load_weight_bf16(S, nc, name, arr_shape, nchunk_free):
    k, rem = divmod(arr_shape[0], 128)
    assert rem == 0
    m = arr_shape[1]
    d = nc.dram_tensor(name, list(arr_shape), F32, kind="ExternalInput").ap()
    s = nc.alloc_sbuf_tensor(name + "_sb", [128, k * m], BF16).ap()
    src = d.rearrange("(c p) m -> p c m", p=128)
    dst = s.rearrange("p (c m) -> p c m", m=m)
    S.do("gpsimd", "dma_start", dst, src)
    return s.rearrange("p (c m) -> p c m", m=m)


def emit_ln_feature_major(S, nc, xf, nchunks, tw, ones_sb, sc, tag):
    if "out3" in sc and sc["out3"] is not None:
        lnb3 = sc["out3"]
    else:
        lnb = nc.alloc_sbuf_tensor(f"lnb_{tag}", [128, nchunks * tw], BF16).ap()
        lnb3 = lnb.rearrange("p (c t) -> p c t", t=tw)
    ps_s = sc["ps_stat"]
    ps_b = sc["ps_bc"]
    nD = nchunks * 128
    for c in range(nchunks):
        S.do("scalar", "activation", sc["sq"], xf[:, c], AF.Square)
        S.do("tensor", "matmul", ps_s[:, 0:tw], ones_sb, xf[:, c],
             start=(c == 0), stop=(c == nchunks - 1))
        S.do("tensor", "matmul", ps_s[:, tw : 2 * tw], ones_sb, sc["sq"],
             start=(c == 0), stop=(c == nchunks - 1))
    S.do("scalar", "activation", sc["m"], ps_s[:, 0:tw], AF.Copy, scale=1.0 / nD)
    S.do("vector", "tensor_tensor", sc["v"], sc["m"], sc["m"], op=ALU.mult)
    S.do("vector", "scalar_tensor_tensor", sc["v2"], ps_s[:, tw : 2 * tw],
         1.0 / nD, sc["v"], op0=ALU.mult, op1=ALU.subtract)
    S.do("scalar", "activation", sc["v"], sc["v2"], AF.Ln, bias=EPS_LN)
    S.do("scalar", "activation", sc["r"], sc["v"], AF.Exp, scale=-0.5)
    S.do("vector", "tensor_tensor", sc["mr"], sc["m"], sc["r"], op=ALU.mult)
    S.do("tensor", "matmul", ps_b[:, 0:tw], sc["ones_row"], sc["r"],
         start=True, stop=True)
    S.do("tensor", "matmul", ps_b[:, tw : 2 * tw], sc["ones_row"], sc["mr"],
         start=True, stop=True)
    S.do("vector", "tensor_copy", sc["rb"], ps_b[:, 0:tw])
    S.do("vector", "tensor_copy", sc["mrb"], ps_b[:, tw : 2 * tw])
    for c in range(nchunks):
        S.do("scalar", "activation", sc["xb"], xf[:, c], AF.Copy)
        S.do("vector", "tensor_tensor", sc["xb2"], sc["xb"], sc["rb"], op=ALU.mult)
        S.do("vector", "tensor_tensor", lnb3[:, c], sc["xb2"], sc["mrb"],
             op=ALU.subtract)
    return lnb3


def alloc_ln_scratch(S, nc, tw, ps_bc, ps_stat=None, tag=""):
    sc = {}
    reg_const(S, nc, EPS_LN, tag="lneps")
    sc["sq"] = nc.alloc_sbuf_tensor(f"ln_sq{tag}", [128, tw], F32).ap()
    if ps_stat is None:
        ps_stat = nc.alloc_psum_tensor(f"ln_ps_s{tag}", [1, 2 * tw], F32).ap()
    sc["ps_stat"] = ps_stat
    sc["ps_bc"] = ps_bc
    for nm in ("m", "v", "v2", "r", "mr"):
        sc[nm] = nc.alloc_sbuf_tensor(f"ln_{nm}{tag}", [1, tw], F32).ap()
    sc["rb"] = nc.alloc_sbuf_tensor(f"ln_rb{tag}", [128, tw], BF16).ap()
    sc["mrb"] = nc.alloc_sbuf_tensor(f"ln_mrb{tag}", [128, tw], BF16).ap()
    sc["xb"] = nc.alloc_sbuf_tensor(f"ln_xb{tag}", [128, tw], BF16).ap()
    sc["xb2"] = nc.alloc_sbuf_tensor(f"ln_xb2{tag}", [128, tw], BF16).ap()
    ones = np.ones((128, 1), np.float32)
    onr = np.ones((1, 128), np.float32)
    sc["ones_col_host"] = ones
    sc["ones_row_host"] = onr
    ocd = nc.dram_tensor("ones_col", [128, 1], F32, kind="ExternalInput").ap()
    ord_ = nc.dram_tensor("ones_row", [1, 128], F32, kind="ExternalInput").ap()
    oc = nc.alloc_sbuf_tensor("ones_col_sb", [128, 1], F32).ap()
    orr = nc.alloc_sbuf_tensor("ones_row_sb", [1, 128], F32).ap()
    S.do("sync", "dma_start", oc, ocd)
    S.do("sync", "dma_start", orr, ord_)
    sc["ones_col"] = oc
    sc["ones_row"] = orr
    ocb = nc.alloc_sbuf_tensor("ones_col_bf", [128, 1], BF16).ap()
    orb = nc.alloc_sbuf_tensor("ones_row_bf", [1, 128], BF16).ap()
    S.do("gpsimd", "dma_start", ocb, ocd)
    S.do("gpsimd", "dma_start", orb, ord_)
    sc["ones_col_bf"] = ocb
    sc["ones_row_bf"] = orb
    return sc



def build_ffn0(merge_partials):
    nc = mk_nc()
    S = Ser(nc)
    TW = 512
    NT = TPC // TW
    xt = nc.dram_tensor("xt", [D, TPC], F32, kind="ExternalInput").ap()
    if merge_partials:
        pa = nc.dram_tensor("pa", [D, TPC], F32, kind="ExternalInput").ap()
        pb = nc.dram_tensor("pb", [D, TPC], F32, kind="ExternalInput").ap()
        bpv = nc.dram_tensor("bpv", [128, 8], F32, kind="ExternalInput").ap()
    yt = nc.dram_tensor("yt", [D, TPC], F32, kind="ExternalOutput").ap()

    U1 = emit_load_weight_bf16(S, nc, "U1", (1024, 256), 8)
    V1 = emit_load_weight_bf16(S, nc, "V1", (256, 4096), 2)
    U2 = emit_load_weight_bf16(S, nc, "U2", (4096, 256), 32)
    V2 = emit_load_weight_bf16(S, nc, "V2", (256, 1024), 2)
    b1d = nc.dram_tensor("b1v", [128, 32], F32, kind="ExternalInput").ap()
    b2d = nc.dram_tensor("b2v", [128, 8], F32, kind="ExternalInput").ap()
    b1s = nc.alloc_sbuf_tensor("b1s", [128, 32], F32).ap()
    b2s = nc.alloc_sbuf_tensor("b2s", [128, 8], F32).ap()
    S.do("sync", "dma_start", b1s, b1d)
    S.do("sync", "dma_start", b2s, b2d)
    if merge_partials:
        bps = nc.alloc_sbuf_tensor("bps", [128, 8], F32).ap()
        S.do("sync", "dma_start", bps, bpv)

    ps_A = nc.alloc_psum_tensor("ps_A", [128, 2 * TW], F32).ap()
    ps_B = nc.alloc_psum_tensor("ps_B", [128, 2 * TW], F32).ap()
    sc = alloc_ln_scratch(S, nc, TW, ps_A)

    xf = nc.alloc_sbuf_tensor("xf", [128, 8 * TW], F32).ap()
    xf3 = xf.rearrange("p (c t) -> p c t", t=TW)
    pf = nc.alloc_sbuf_tensor("pf", [128, 8 * TW], F32).ap() if merge_partials else None
    pf3 = pf.rearrange("p (c t) -> p c t", t=TW) if merge_partials else None
    Pb = nc.alloc_sbuf_tensor("Pb", [128, 2 * TW], BF16).ap()
    Pb3 = Pb.rearrange("p (c t) -> p c t", t=TW)
    Hb = nc.alloc_sbuf_tensor("Hb", [128, 32 * TW], BF16).ap()
    Hb3 = Hb.rearrange("p (c t) -> p c t", t=TW)
    Qb = nc.alloc_sbuf_tensor("Qb", [128, 2 * TW], BF16).ap()
    Qb3 = Qb.rearrange("p (c t) -> p c t", t=TW)
    yf = nc.alloc_sbuf_tensor("yf", [128, TW], F32).ap()
    ps_P = ps_A
    ps_H = ps_B
    ps_Y = ps_B

    for t in range(NT):
        tsl = slice(t * TW, (t + 1) * TW)
        src = xt[:, tsl].rearrange("(c p) t -> p c t", p=128)
        S.do("sync", "dma_start", xf3, src)
        if merge_partials:
            S.do("sync", "dma_start", pf3, pa[:, tsl].rearrange("(c p) t -> p c t", p=128))
            for c in range(8):
                S.do("vector", "tensor_tensor", xf3[:, c], xf3[:, c], pf3[:, c], op=ALU.add)
            S.do("sync", "dma_start", pf3, pb[:, tsl].rearrange("(c p) t -> p c t", p=128))
            for c in range(8):
                S.do("vector", "tensor_tensor", xf3[:, c], xf3[:, c], pf3[:, c], op=ALU.add)
                S.do("vector", "tensor_scalar", out=xf3[:, c], in0=xf3[:, c],
                     scalar1=bps[:, c : c + 1], scalar2=None, op0=ALU.add)
        lnb3 = emit_ln_feature_major(S, nc, xf3, 8, TW, sc["ones_col"], sc, f"t{t}")
        for o in range(2):
            for c in range(8):
                S.do("tensor", "matmul", ps_P[:, o * TW : (o + 1) * TW],
                     U1[:, c, o * 128 : (o + 1) * 128], lnb3[:, c],
                     start=(c == 0), stop=(c == 7))
            S.do("vector", "tensor_copy", Pb3[:, o], ps_P[:, o * TW : (o + 1) * TW])
        for f in range(32):
            fb = f % 2
            for c in range(2):
                S.do("tensor", "matmul", ps_H[:, fb * TW : (fb + 1) * TW],
                     V1[:, c, f * 128 : (f + 1) * 128], Pb3[:, c],
                     start=(c == 0), stop=(c == 1))
            S.do("scalar", "activation", Hb3[:, f], ps_H[:, fb * TW : (fb + 1) * TW],
                 AF.Gelu_apprx_tanh, bias=b1s[:, f : f + 1])
        for o in range(2):
            for c in range(32):
                S.do("tensor", "matmul", ps_P[:, o * TW : (o + 1) * TW],
                     U2[:, c, o * 128 : (o + 1) * 128], Hb3[:, c],
                     start=(c == 0), stop=(c == 31))
            S.do("vector", "tensor_copy", Qb3[:, o], ps_P[:, o * TW : (o + 1) * TW])
        for d_ in range(8):
            db = d_ % 2
            for c in range(2):
                S.do("tensor", "matmul", ps_Y[:, db * TW : (db + 1) * TW],
                     V2[:, c, d_ * 128 : (d_ + 1) * 128], Qb3[:, c],
                     start=(c == 0), stop=(c == 1))
            S.do("vector", "tensor_scalar", out=yf, in0=ps_Y[:, db * TW : (db + 1) * TW],
                 scalar1=b2s[:, d_ : d_ + 1], scalar2=0.5, op0=ALU.add, op1=ALU.mult)
            S.do("vector", "tensor_tensor", xf3[:, d_], yf, xf3[:, d_], op=ALU.add)
            dst = yt[:, tsl].rearrange("(c p) t -> p c t", p=128)[:, d_]
            S.do("sync", "dma_start", dst, xf3[:, d_])
    S.fence("sync")
    return nc


def ffn0_in_maps(xT, U1, V1, b1, U2, V2, b2, partials=None, bp=None):
    base = {
        "U1": f32(U1), "V1": f32(V1), "U2": f32(U2), "V2": f32(V2),
        "b1v": f32(as_pcols(b1)), "b2v": f32(as_pcols(b2)),
        "ones_col": np.ones((128, 1), np.float32),
        "ones_row": np.ones((1, 128), np.float32),
    }
    maps = []
    for c in range(8):
        m = dict(base)
        sl = slice(c * TPC, (c + 1) * TPC)
        m["xt"] = f32(xT[:, sl])
        if partials is not None:
            pa, pb_ = partials
            m["pa"] = f32(pa[:, sl])
            m["pb"] = f32(pb_[:, sl])
            m["bpv"] = f32(as_pcols(bp))
        maps.append(m)
    return maps



def build_ffn1(merge_partials, nh_lim=24, nd_lim=8):
    nc = mk_nc()
    S = Ser(nc)
    TW = 512
    NT = TPC // TW
    NH = 24
    xt = nc.dram_tensor("xt", [D, TPC], F32, kind="ExternalInput").ap()
    if merge_partials:
        pa = nc.dram_tensor("pa", [D, TPC], F32, kind="ExternalInput").ap()
        pb = nc.dram_tensor("pb", [D, TPC], F32, kind="ExternalInput").ap()
    yt = nc.dram_tensor("yt", [D, TPC], F32, kind="ExternalOutput").ap()
    wgT = nc.dram_tensor("wgT", [1024, 3072], I32, kind="ExternalInput").ap()
    wuT = nc.dram_tensor("wuT", [1024, 3072], I32, kind="ExternalInput").ap()
    wdT = nc.dram_tensor("wdT", [3072, 1024], I32, kind="ExternalInput").ap()

    ps_A = nc.alloc_psum_tensor("ps_A", [128, 2 * TW], F32).ap()
    ps_B = nc.alloc_psum_tensor("ps_B", [128, 2 * TW], F32).ap()
    sc = alloc_ln_scratch(S, nc, TW, ps_A)

    xf = nc.alloc_sbuf_tensor("xf", [128, 8 * TW], F32).ap()
    xf3 = xf.rearrange("p (c t) -> p c t", t=TW)
    lnball = nc.alloc_sbuf_tensor("lnball", [128, 8 * TPC], BF16).ap()
    lnb4 = lnball.rearrange("p (c u t) -> p c u t", c=8, u=NT)
    hb = nc.alloc_sbuf_tensor("hb", [128, NH * TPC], BF16).ap()
    hb4 = hb.rearrange("p (k u t) -> p k u t", k=NH, u=NT)
    wg_s = nc.alloc_sbuf_tensor("wg_s", [128, 8 * 128], BF16).ap()
    wg3 = wg_s.rearrange("p (c m) -> p c m", m=128)
    wu_s = nc.alloc_sbuf_tensor("wu_s", [128, 8 * 128], BF16).ap()
    wu3 = wu_s.rearrange("p (c m) -> p c m", m=128)
    wd_s = nc.alloc_sbuf_tensor("wd_s", [128, NH * 128], BF16).ap()
    wd3 = wd_s.rearrange("p (c m) -> p c m", m=128)
    sg = nc.alloc_sbuf_tensor("sg", [128, TW], BF16).ap()
    yf = nc.alloc_sbuf_tensor("yf", [128, TW], F32).ap()
    pfa = nc.alloc_sbuf_tensor("pfa", [128, TW], F32).ap() if merge_partials else None

    for t in range(NT):
        tsl = slice(t * TW, (t + 1) * TW)
        S.do("sync", "dma_start", xf3, xt[:, tsl].rearrange("(c p) t -> p c t", p=128))
        if merge_partials:
            for src in (pa, pb):
                for c in range(8):
                    S.do("sync", "dma_start", pfa,
                         src[:, tsl].rearrange("(c p) t -> p c t", p=128)[:, c])
                    S.do("vector", "tensor_tensor", xf3[:, c], xf3[:, c], pfa, op=ALU.add)
        sc["out3"] = lnb4[:, :, t]
        emit_ln_feature_major(S, nc, xf3, 8, TW, sc["ones_col"], sc, f"f1t{t}")
    sc["out3"] = None

    for h in range(nh_lim):
        hs = slice(h * 128, (h + 1) * 128)
        S.do("gpsimd", "dma_start", wg3, wgT[:, hs].rearrange("(c p) m -> p c m", p=128))
        S.do("gpsimd", "dma_start", wu3, wuT[:, hs].rearrange("(c p) m -> p c m", p=128))
        for t in range(NT):
            for c in range(8):
                S.do("tensor", "matmul", ps_A[:, 0:TW], wg3[:, c], lnb4[:, c, t],
                     start=(c == 0), stop=(c == 7))
            for c in range(8):
                S.do("tensor", "matmul", ps_A[:, TW : 2 * TW], wu3[:, c], lnb4[:, c, t],
                     start=(c == 0), stop=(c == 7))
            S.do("scalar", "activation", sg, ps_A[:, 0:TW], AF.Silu, scale=1.0 / 127.0)
            S.do("vector", "tensor_tensor", hb4[:, h, t], sg, ps_A[:, TW : 2 * TW],
                 op=ALU.mult)

    for d_ in range(nd_lim):
        ds = slice(d_ * 128, (d_ + 1) * 128)
        S.do("gpsimd", "dma_start", wd3, wdT[:, ds].rearrange("(c p) m -> p c m", p=128))
        for t in range(NT):
            tsl = slice(t * TW, (t + 1) * TW)
            for k in range(NH):
                S.do("tensor", "matmul", ps_B[:, 0:TW], wd3[:, k], hb4[:, k, t],
                     start=(k == 0), stop=(k == NH - 1))
            xsrc = xt[:, tsl].rearrange("(c p) t -> p c t", p=128)[:, d_]
            S.do("sync", "dma_start", yf, xsrc)
            if merge_partials:
                for src in (pa, pb):
                    S.do("sync", "dma_start", pfa,
                         src[:, tsl].rearrange("(c p) t -> p c t", p=128)[:, d_])
                    S.do("vector", "tensor_tensor", yf, yf, pfa, op=ALU.add)
            S.do("vector", "scalar_tensor_tensor", yf, ps_B[:, 0:TW],
                 0.5 / (127.0 * 127.0), yf, op0=ALU.mult, op1=ALU.add)
            dst = yt[:, tsl].rearrange("(c p) t -> p c t", p=128)[:, d_]
            S.do("sync", "dma_start", dst, yf)
    S.fence("sync")
    return nc


def ffn1_in_maps(xT, wgT, wuT, wdT, partials=None):
    base = {
        "wgT": wgT, "wuT": wuT, "wdT": wdT,
        "ones_col": np.ones((128, 1), np.float32),
        "ones_row": np.ones((1, 128), np.float32),
    }
    maps = []
    for c in range(8):
        m = dict(base)
        sl = slice(c * TPC, (c + 1) * TPC)
        m["xt"] = f32(xT[:, sl])
        if partials is not None:
            m["pa"] = f32(partials[0][:, sl])
            m["pb"] = f32(partials[1][:, sl])
        maps.append(m)
    return maps


def run_ffn1(xT, wgT, wuT, wdT, partials=None, nh_lim=24, nd_lim=8):
    nc = get_prog("ffn1", build_ffn1, partials is not None, nh_lim, nd_lim)
    maps = ffn1_in_maps(xT, wgT, wuT, wdT, partials=partials)
    res = run_bass_kernel_spmd(nc, maps, core_ids=list(range(8)))
    return np.concatenate([r["yt"] for r in res.results], axis=1)



MASKW = 1536
BIGNEG = -1.0e30


def make_mask_plane():
    k = np.arange(MASKW)[None, :]
    c = np.arange(128)[:, None]
    return np.where(k >= c + 512, 0.0, BIGNEG).astype(np.float32)


def build_attn0(half):
    KEYN = 2048 * (half + 1)
    Q0 = KEYN - 2048
    NH8 = 8
    HD = 64
    INV = 1.0 / np.sqrt(64.0)
    TW = 512
    NKT = KEYN // TW
    nc = mk_nc()
    S = Ser(nc)

    xt = nc.dram_tensor("xt", [D, KEYN], F32, kind="ExternalInput").ap()
    EXT = nc.dram_tensor("EXT", [NH8, 64, 2048], F32, kind="ExternalInput").ap()
    maskd = nc.dram_tensor("maskpl", [128, MASKW], F32, kind="ExternalInput").ap()
    yout = nc.dram_tensor("ypart", [D, 2048], F32, kind="ExternalOutput").ap()
    idd = nc.dram_tensor("ident", [128, 128], F32, kind="ExternalInput").ap()
    V2 = nc.dram_tensor("V2scratch", [KEYN * 4097], BF16, kind="Internal").ap()
    V2r = V2.rearrange("(r c) -> r c", c=4097)

    ps_A = nc.alloc_psum_tensor("ps_A", [128, 2 * TW], F32).ap()
    ps_S = nc.alloc_psum_tensor("ps_S", [128, TW], F32).ap()
    ps_O = nc.alloc_psum_tensor("ps_O", [128, TW], F32).ap()
    ps_sum = nc.alloc_psum_tensor("ps_sum", [1, TW], F32).ap()
    ps_n = ps_sum
    ps_T = nc.alloc_psum_tensor("ps_T", [128, 64], BF16).ap()

    sc = alloc_ln_scratch(S, nc, TW, ps_A, ps_stat=ps_A[0:1, :])
    ident = nc.alloc_sbuf_tensor("ident_sb", [128, 128], BF16).ap()
    S.do("gpsimd", "dma_start", ident, idd)

    lnball = nc.alloc_sbuf_tensor("lnball", [128, 8 * KEYN], BF16).ap()
    lnb4 = lnball.rearrange("p (c u t) -> p c u t", c=8, u=NKT)

    zt = lnball[:, 0:4097]
    S.do("vector", "memset", lnball[:, 0 : 2 * 4097], 0.0)
    fill_lo = max(0, Q0 - 1)
    for r0 in range(fill_lo, KEYN, 128):
        nr = min(128, KEYN - r0)
        S.do("sync", "dma_start", V2r[r0 : r0 + nr, :], zt[0:nr, :])

    with nc.sbuf_tensor("xf", [128, 8 * TW], F32) as xf_h:
        xf3 = xf_h.ap().rearrange("p (c t) -> p c t", t=TW)
        for t in range(NKT):
            S.do("sync", "dma_start", xf3,
                 xt[:, t * TW : (t + 1) * TW].rearrange("(c p) t -> p c t", p=128))
            sc["out3"] = lnb4[:, :, t]
            emit_ln_feature_major(S, nc, xf3, 8, TW, sc["ones_col"], sc, f"a0t{t}")
        sc["out3"] = None
    lnb3 = lnball.rearrange("p (c t) -> p c t", t=KEYN)

    qS = nc.alloc_sbuf_tensor("qS", [64, 2048], BF16).ap()
    qA = nc.alloc_sbuf_tensor("qA", [64, 2048], BF16).ap() if half == 1 else qS
    kh = nc.alloc_sbuf_tensor("kh", [64, KEYN], BF16).ap()
    vh = nc.alloc_sbuf_tensor("vh", [64, KEYN], BF16).ap()
    vT = nc.alloc_sbuf_tensor("vT", [128, (KEYN // 128) * 64], BF16).ap()
    vT3 = vT.rearrange("p (k v) -> p k v", v=64)
    exh = nc.alloc_sbuf_tensor("exh", [64, 2048], BF16).ap()
    asb = nc.alloc_sbuf_tensor("asb", [128, TW], BF16).ap()
    qn = nc.alloc_sbuf_tensor("qn", [1, 2050], F32).ap()
    qnA = nc.alloc_sbuf_tensor("qnA", [1, 2048], F32).ap()
    kstar = nc.alloc_sbuf_tensor("kstar", [1, 1], F32).ap()
    mtmp = nc.alloc_sbuf_tensor("mtmp", [1, 1], F32).ap()
    estar = nc.alloc_sbuf_tensor("estar", [1, 1], F32).ap()
    sqt = nc.alloc_sbuf_tensor("sqt", [64, TW], F32).ap()
    negB = nc.alloc_sbuf_tensor("negB", [1, TW], F32).ap()
    nB1 = nc.alloc_sbuf_tensor("nB1", [1, TW], F32).ap()
    SCt = nc.alloc_sbuf_tensor("SCt", [128, TW], BF16).ap()
    SRt0 = nc.alloc_sbuf_tensor("SRt0", [128, TW], BF16, is_transpose_dest=True).ap()
    SRt1 = nc.alloc_sbuf_tensor("SRt1", [128, TW], BF16, is_transpose_dest=True).ap()
    SRts = (SRt0, SRt1)
    expS0 = nc.alloc_sbuf_tensor("expS0", [128, TW], BF16).ap()
    expS1x = nc.alloc_sbuf_tensor("expS1x", [128, TW], BF16).ap()
    expSs = (expS0, expS1x)
    ssum = nc.alloc_sbuf_tensor("ssum", [1, TW], F32).ap()
    rsum = nc.alloc_sbuf_tensor("rsum", [1, TW], F32).ap()
    otmp = nc.alloc_sbuf_tensor("otmp", [64, TW], F32).ap()
    osum = nc.alloc_sbuf_tensor("osum", [64, TW], F32).ap()
    Oall = nc.alloc_sbuf_tensor("Oall", [128, 4 * 2048], BF16).ap()
    Oall4 = Oall.rearrange("p (h t) -> p h t", h=4)
    yf = nc.alloc_sbuf_tensor("yf_a", [128, TW], F32).ap()

    Wq_s = emit_load_weight_bf16(S, nc, "WqT_x", (1024, 512), 8)

    def project2(dst, Wsb, col0, tok0, ntok, scale=None):
        for tt in range(ntok // TW):
            a = tok0 + tt * TW
            for c in range(8):
                S.do("tensor", "matmul", ps_A[:64, 0:TW],
                     Wsb[:, c, col0 : col0 + 64],
                     lnb3[:, c, a : a + TW], start=(c == 0), stop=(c == 7))
            if scale is None:
                S.do("vector", "tensor_copy", dst[:, tt * TW : (tt + 1) * TW],
                     ps_A[:64, 0:TW])
            else:
                S.do("scalar", "activation", dst[:, tt * TW : (tt + 1) * TW],
                     ps_A[:64, 0:TW], AF.Copy, scale=scale)

    Wk_s = emit_load_weight_bf16(S, nc, "WkT_x", (1024, 512), 8)
    Wv_s = emit_load_weight_bf16(S, nc, "WvT_x", (1024, 512), 8)
    Wp_s = emit_load_weight_bf16(S, nc, "WpT_x", (512, 1024), 8)

    for h in range(NH8):
        c0 = h * 64
        project2(qS, Wq_s, c0, Q0, 2048, scale=INV)
        if half == 1:
            project2(qA, Wq_s, c0, Q0 - 1, 2048, scale=INV)
        project2(kh, Wk_s, c0, 0, KEYN)
        project2(vh, Wv_s, c0, 0, KEYN)
        for kc in range(KEYN // 128):
            S.do("tensor", "transpose", ps_T, vh[:, kc * 128 : (kc + 1) * 128],
                 ident[0:64, 0:64])
            S.do("vector", "tensor_copy", vT3[:, kc], ps_T)
        S.do("gpsimd", "dma_start", exh, EXT[h])
        S.do("vector", "memset", qn[:, 0:2], 0.0)
        for tt in range(4):
            S.do("scalar", "activation", sqt, qS[:, tt * TW : (tt + 1) * TW], AF.Square)
            S.do("tensor", "matmul", ps_n, sc["ones_col"][0:64], sqt,
                 start=True, stop=True)
            S.do("scalar", "activation", qn[:, 1 + tt * TW : 1 + (tt + 1) * TW],
                 ps_n, AF.Sqrt)
        qAn = qnA if half == 1 else qn[:, 0:2048]
        if half == 1:
            for tt in range(4):
                S.do("scalar", "activation", sqt, qA[:, tt * TW : (tt + 1) * TW],
                     AF.Square)
                S.do("tensor", "matmul", ps_n, sc["ones_col"][0:64], sqt,
                     start=True, stop=True)
                S.do("scalar", "activation", qnA[:, tt * TW : (tt + 1) * TW],
                     ps_n, AF.Sqrt)
        for which, src_t, ntt, dstmax in ((0, kh, KEYN // TW, kstar),
                                          (1, exh, 4, estar)):
            for tt in range(ntt):
                S.do("scalar", "activation", sqt, src_t[:, tt * TW : (tt + 1) * TW],
                     AF.Square)
                S.do("tensor", "matmul", ps_n, sc["ones_col"][0:64], sqt,
                     start=True, stop=True)
                S.do("scalar", "activation", rsum, ps_n, AF.Sqrt)
                if tt == 0:
                    S.do("vector", "tensor_reduce", dstmax, rsum, axis=AXL.X,
                         op=ALU.max)
                else:
                    S.do("vector", "tensor_reduce", mtmp, rsum, axis=AXL.X,
                         op=ALU.max)
                    S.do("vector", "tensor_tensor", dstmax, dstmax, mtmp, op=ALU.max)

        for a in range(16):
            i0 = a * 128
            gi0 = (Q0 - 1 if half == 1 else 0) + i0
            for jq in range(4):
                j0 = jq * TW
                need2 = (j0 + TW - 1) >= (2048 - (gi0 + 127) - 1)
                need1 = (gi0 + 127 >= 2048) and ((j0 + TW - 1) >= (4096 - (gi0 + 127) - 1))
                if not (need2 or need1):
                    continue
                S.do("tensor", "matmul", ps_A[:, 0:TW],
                     qA[:, i0 : i0 + 128], exh[:, j0 : j0 + TW],
                     start=True, stop=True)
                S.do("vector", "tensor_copy", asb, ps_A[:, 0:TW])
                if need2:
                    S.do("sync", "dma_start",
                         V2r[gi0 : gi0 + 128, 2049 + j0 : 2049 + j0 + TW], asb)
                if need1:
                    S.do("sync", "dma_start",
                         V2r[gi0 : gi0 + 128, 1 + j0 : 1 + j0 + TW], asb)

        v2done = S.val
        for t in range(4):
            R = Q0 + t * TW
            nkc = (R + TW) // 128
            S.do("vector", "tensor_scalar", out=negB,
                 in0=qn[:, 1 + t * TW : 1 + (t + 1) * TW], scalar1=kstar,
                 scalar2=None, op0=ALU.mult)
            S.do("vector", "tensor_scalar", out=nB1,
                 in0=qAn[:, t * TW : (t + 1) * TW], scalar1=estar,
                 scalar2=None, op0=ALU.mult)
            S.do("vector", "tensor_tensor", negB, negB, nB1, op=ALU.add)
            S.do("vector", "tensor_scalar", out=negB, in0=negB, scalar1=-1.0,
                 scalar2=None, op0=ALU.mult)
            tok0 = S.val
            ev_done = tok0
            sc_free = tok0
            srt_free = [tok0, tok0]
            av_par = [tok0, tok0]
            last_mask = tok0
            last_srel = tok0
            for kc in range(nkc):
                par = kc % 2
                off = R * 4096 + kc * 128
                src = bass_rust.AP(V2.tensor, off, [[4096, TW], [1, 128]])
                S.do("sync", "dma_start", SRts[par], src, transpose=True,
                     wait_val=None)
                dma_tok = S.val
                last_srel = S.val
                S.do("tensor", "matmul", ps_S, kh[:, kc * 128 : (kc + 1) * 128],
                     qS[:, t * TW : (t + 1) * TW], start=True, stop=False,
                     wait_val=None)
                S.do("tensor", "matmul", ps_S, sc["ones_row"][:, 0:128], negB,
                     start=False, stop=True, wait_val=None)
                s_tok = S.val
                S.do("vector", "tensor_tensor", SCt, ps_S, SRts[par], op=ALU.add,
                     wait_val=None)
                ev_done = S.val
                srt_free[par] = S.val
                mask_tok = ev_done
                if 128 * kc + 127 > R:
                    dlt = 128 * kc - R
                    S.do("gpsimd", "dma_start", SCt,
                         maskd[:, 512 - dlt : 1024 - dlt], accum_op=ALU.add,
                         wait_val=None)
                    mask_tok = S.val
                    last_mask = S.val
                S.do("scalar", "activation", expSs[par], SCt, AF.Exp,
                     wait_val=None)
                exp_tok = S.val
                sc_free = exp_tok
                S.do("tensor", "matmul", ps_sum, sc["ones_col_bf"], expSs[par],
                     start=(kc == 0), stop=(kc == nkc - 1), wait_val=None)
                g = kc % 2
                last = kc >= nkc - 2
                S.do("tensor", "matmul", ps_O[64 * g : 64 * g + 64, :],
                     vT3[:, kc], expSs[par], start=(kc == g), stop=last,
                     tile_position=(0, 64 * g), wait_val=None)
                av_par[par] = S.val
            S.do("vector", "tensor_copy", ssum, ps_sum)
            S.do("vector", "reciprocal", rsum, ssum)
            S.do("tensor", "matmul", ps_A[:64, 0:TW], sc["ones_row"][:, 0:64], rsum,
                 start=True, stop=True)
            S.do("vector", "tensor_copy", otmp, ps_O[64:128, :])
            S.do("vector", "tensor_tensor", osum, ps_O[0:64, :], otmp, op=ALU.add)
            hc, hp = h // 2, (h % 2) * 64
            S.do("vector", "tensor_tensor",
                 Oall4[hp : hp + 64, hc, t * TW : (t + 1) * TW],
                 osum, ps_A[:64, 0:TW], op=ALU.mult)

    for t in range(4):
        for d_ in range(8):
            for hc in range(4):
                S.do("tensor", "matmul", ps_A[:, 0:TW],
                     Wp_s[:, hc, d_ * 128 : (d_ + 1) * 128],
                     Oall4[:, hc, t * TW : (t + 1) * TW],
                     start=(hc == 0), stop=(hc == 3))
            S.do("vector", "tensor_copy", yf, ps_A[:, 0:TW])
            dst = yout[:, t * TW : (t + 1) * TW].rearrange("(c p) t -> p c t", p=128)[:, d_]
            S.do("sync", "dma_start", dst, yf)
    S.fence("sync")
    return nc


def attn0_in_maps(xT, Wq, Wk, Wv, Wp, rel_embed, half):
    KEYN = 2048 * (half + 1)
    rel = np.asarray(rel_embed, np.float32).reshape(2048, 16, 64)
    maps = []
    for c in range(8):
        b, o = c // 2, c % 2
        hsl = slice(o * 512, (o + 1) * 512)
        m = {
            "xt": f32(xT[:, b * SEQ : b * SEQ + KEYN]),
            "WqT_x": f32(Wq.T[:, hsl]),
            "WkT_x": f32(Wk.T[:, hsl]),
            "WvT_x": f32(Wv.T[:, hsl]),
            "WpT_x": f32(Wp.T[hsl, :]),
            "EXT": np.ascontiguousarray(
                rel[:, o * 8 : (o + 1) * 8, :].transpose(1, 2, 0)).astype(np.float32),
            "maskpl": make_mask_plane(),
            "ident": np.eye(128, dtype=np.float32),
            "ones_col": np.ones((128, 1), np.float32),
            "ones_row": np.ones((1, 128), np.float32),
        }
        maps.append(m)
    return maps


def run_attn0(xT, Wq, Wk, Wv, Wp, rel_embed, half):
    nc = get_prog("attn0", build_attn0, half)
    maps = attn0_in_maps(xT, Wq, Wk, Wv, Wp, rel_embed, half)
    res = run_bass_kernel_spmd(nc, maps, core_ids=list(range(8)))
    return [r["ypart"] for r in res.results]



EPS_RMS = 1e-6


def build_attn1(half):
    KEYN = 2048 * (half + 1)
    Q0 = KEYN - 2048
    INV96 = 1.0 / np.sqrt(96.0)
    TW = 512
    nc = mk_nc()
    S = Ser(nc)

    xt = nc.dram_tensor("xt", [D, SEQ], F32, kind="ExternalInput").ap()
    xtok = nc.dram_tensor("xtok", [SEQ, D], F32, kind="ExternalInput").ap()
    maskd = nc.dram_tensor("maskpl", [128, MASKW], F32, kind="ExternalInput").ap()
    idd = nc.dram_tensor("ident", [128, 128], F32, kind="ExternalInput").ap()
    stud = nc.dram_tensor("stu", [128, 128], F32, kind="ExternalInput").ap()
    yout = nc.dram_tensor("ypart", [D, 2048], F32, kind="ExternalOutput").ap()
    gqd = nc.dram_tensor("g_qnorm", [96, 1], F32, kind="ExternalInput").ap()
    gkd = nc.dram_tensor("g_kvnorm", [32, 1], F32, kind="ExternalInput").ap()
    wgated = nc.dram_tensor("w_gate", [1024, 3], F32, kind="ExternalInput").ap()
    wimpd = nc.dram_tensor("w_imp", [128, 8], F32, kind="ExternalInput").ap()
    dq8d = nc.dram_tensor("dq8", [96, 768], F32, kind="ExternalInput").ap()
    dkn8d = nc.dram_tensor("dkn8", [32, 256], F32, kind="ExternalInput").ap()
    dv8d = nc.dram_tensor("dv8", [32, 256], F32, kind="ExternalInput").ap()
    selk8d = nc.dram_tensor("selk8", [1024, 768], F32, kind="ExternalInput").ap()
    selv8d = nc.dram_tensor("selv8", [1024, 256], F32, kind="ExternalInput").ap()
    wink8d = nc.dram_tensor("wink8", [1024, 768], F32, kind="ExternalInput").ap()
    winv8d = nc.dram_tensor("winv8", [1024, 256], F32, kind="ExternalInput").ap()
    cosqd = nc.dram_tensor("cosq", [32, SEQ], F32, kind="ExternalInput").ap()
    sinqd = nc.dram_tensor("sinq", [32, SEQ], F32, kind="ExternalInput").ap()
    sscr = nc.dram_tensor("sscr", [SEQ], F32, kind="Internal").ap()
    idxbuf = nc.dram_tensor("idxbuf", [256, 1], U32, kind="Internal").ap()

    ps_A = nc.alloc_psum_tensor("ps_A", [128, 2 * TW], F32).ap()
    ps_S = nc.alloc_psum_tensor("ps_S", [128, TW], F32).ap()
    ps_S2 = nc.alloc_psum_tensor("ps_S2", [128, TW], F32).ap()
    ps_O = nc.alloc_psum_tensor("ps_O", [128, TW], F32).ap()
    ps_sum = nc.alloc_psum_tensor("ps_sum", [1, TW], F32).ap()
    ps_T = nc.alloc_psum_tensor("ps_T", [128, 128], BF16).ap()
    ps_Ss = (ps_S, ps_S2)

    sc = alloc_ln_scratch(S, nc, TW, ps_A, ps_stat=ps_A[0:1, :])
    reg_const(S, nc, EPS_RMS, tag="rmseps")
    ident = nc.alloc_sbuf_tensor("ident_sb", [128, 128], BF16).ap()
    stu = nc.alloc_sbuf_tensor("stu_sb", [128, 128], F32).ap()
    S.do("gpsimd", "dma_start", ident, idd)
    S.do("sync", "dma_start", stu, stud)

    cq_s = emit_load_weight_bf16(S, nc, "w_cq", (1024, 96), 8)
    ckv_s = emit_load_weight_bf16(S, nc, "w_ckv", (1024, 32), 8)
    krop_s = emit_load_weight_bf16(S, nc, "w_krope", (1024, 64), 8)
    gq = nc.alloc_sbuf_tensor("gq_sb", [96, 1], F32).ap()
    gk = nc.alloc_sbuf_tensor("gk_sb", [32, 1], F32).ap()
    wgate_s = nc.alloc_sbuf_tensor("wgate_sb", [128, 8 * 3], F32).ap()
    wimp_s = nc.alloc_sbuf_tensor("wimp_sb", [128, 8], F32).ap()
    S.do("sync", "dma_start", gq, gqd)
    S.do("sync", "dma_start", gk, gkd)
    S.do("sync", "dma_start", wgate_s.rearrange("p (c m) -> p c m", m=3),
         wgated.rearrange("(c p) m -> p c m", p=128))
    S.do("sync", "dma_start", wimp_s, wimpd)
    cosq = nc.alloc_sbuf_tensor("cosq_sb", [32, SEQ], BF16).ap()
    sinq = nc.alloc_sbuf_tensor("sinq_sb", [32, SEQ], BF16).ap()
    S.do("gpsimd", "dma_start", cosq, cosqd)
    S.do("gpsimd", "dma_start", sinq, sinqd)

    lnball = nc.alloc_sbuf_tensor("lnball", [128, 8 * SEQ], BF16).ap()
    lnb4 = lnball.rearrange("p (c u t) -> p c u t", c=8, u=SEQ // TW)
    lnb3 = lnball.rearrange("p (c t) -> p c t", t=SEQ)

    cwb = nc.alloc_sbuf_tensor("cwb", [1, 1], F32).ap()
    sX = nc.alloc_sbuf_tensor("sX", [128, 8], F32).ap()
    srow = nc.alloc_sbuf_tensor("srow", [1, TW], F32).ap()
    srow2 = nc.alloc_sbuf_tensor("srow2", [1, TW], F32).ap()
    g123 = nc.alloc_sbuf_tensor("g123", [1, 3], F32).ap()
    gtmp = nc.alloc_sbuf_tensor("gtmp", [1, 3], F32).ap()
    g1s = nc.alloc_sbuf_tensor("g1s", [1, 1], F32).ap()

    for c in range(8):
        S.do("tensor", "matmul", ps_A[0:1, 0:1], wimp_s[:, c : c + 1],
             sc["ones_col"], start=(c == 0), stop=(c == 7))
    S.do("vector", "tensor_copy", cwb, ps_A[0:1, 0:1])

    with nc.sbuf_tensor("xf", [128, 8 * TW], F32) as xf_h:
        xf3 = xf_h.ap().rearrange("p (c t) -> p c t", t=TW)
        for t in range(SEQ // TW):
            S.do("sync", "dma_start", xf3,
                 xt[:, t * TW : (t + 1) * TW].rearrange("(c p) t -> p c t", p=128))
            sc["out3"] = lnb4[:, :, t]
            emit_ln_feature_major(S, nc, xf3, 8, TW, sc["ones_col"], sc, f"a1t{t}")
            for c in range(8):
                S.do("tensor", "matmul", ps_A[0:1, TW : 2 * TW],
                     wimp_s[:, c : c + 1], xf3[:, c], start=(c == 0), stop=(c == 7))
            S.do("vector", "tensor_scalar", out=srow, in0=sc["m"], scalar1=cwb,
                 scalar2=None, op0=ALU.mult)
            S.do("vector", "tensor_tensor", srow2, ps_A[0:1, TW : 2 * TW], srow,
                 op=ALU.subtract)
            S.do("vector", "tensor_tensor", srow, srow2, sc["r"], op=ALU.mult)
            S.do("sync", "dma_start",
                 sscr[t * TW : (t + 1) * TW].rearrange("(a t) -> a t", a=1),
                 srow[0:1, :])
        sc["out3"] = None
    for c in range(8):
        S.do("vector", "reduce_sum", sX[:, c : c + 1], lnb3[:, c], axis=AXL.X)
    for c in range(8):
        S.do("tensor", "matmul", ps_A[0:1, 0:3], sX[:, c : c + 1],
             wgate_s.rearrange("p (c m) -> p c m", m=3)[:, c],
             start=(c == 0), stop=(c == 7))
    S.do("scalar", "activation", g123, ps_A[0:1, 0:3], AF.Copy, scale=1.0 / SEQ)
    S.do("vector", "tensor_reduce", g1s, g123, axis=AXL.X, op=ALU.max)
    S.do("vector", "tensor_scalar", out=g1s, in0=g1s, scalar1=-1.0, scalar2=None,
         op0=ALU.mult)
    S.do("scalar", "activation", gtmp, g123, AF.Exp, bias=g1s)
    S.do("vector", "reduce_sum", g1s, gtmp, axis=AXL.X)
    S.do("vector", "reciprocal", g1s, g1s)
    S.do("vector", "tensor_scalar", out=g123, in0=gtmp, scalar1=g1s, scalar2=None,
         op0=ALU.mult)

    s128 = nc.alloc_sbuf_tensor("s128", [128, 32], F32).ap()
    S.do("sync", "dma_start", s128, sscr.rearrange("(p f) -> p f", f=32))
    lo = nc.alloc_sbuf_tensor("lo_t", [1, 1], F32).ap()
    hi = nc.alloc_sbuf_tensor("hi_t", [1, 1], F32).ap()
    mid = nc.alloc_sbuf_tensor("mid_t", [1, 1], F32).ap()
    cnt = nc.alloc_sbuf_tensor("cnt_t", [1, 1], F32).ap()
    msk1 = nc.alloc_sbuf_tensor("msk1", [1, 1], F32).ap()
    btmp = nc.alloc_sbuf_tensor("btmp", [1, 1], F32).ap()
    btmp2 = nc.alloc_sbuf_tensor("btmp2", [1, 1], F32).ap()
    midb = nc.alloc_sbuf_tensor("midb", [128, 1], F32).ap()
    zt1 = nc.alloc_sbuf_tensor("zt1", [128, 32], F32).ap()
    zcnt = nc.alloc_sbuf_tensor("zcnt", [128, 1], F32).ap()
    absm = nc.alloc_sbuf_tensor("absm", [128, 1], F32).ap()
    idf = nc.alloc_sbuf_tensor("idf_sb", [128, 128], F32).ap()
    S.do("sync", "dma_start", idf, idd)
    S.do("vector", "tensor_reduce", absm, s128, axis=AXL.X, op=ALU.max,
         apply_absolute_value=True)
    S.do("tensor", "matmul", ps_A[0:1, 0:128], absm, idf, start=True, stop=True)
    S.do("vector", "tensor_copy", srow[:, 0:128], ps_A[0:1, 0:128])
    S.do("vector", "tensor_reduce", hi, srow[:, 0:128], axis=AXL.X, op=ALU.max)
    S.do("vector", "tensor_scalar", out=hi, in0=hi, scalar1=1.0, scalar2=None,
         op0=ALU.add)
    S.do("vector", "tensor_scalar", out=lo, in0=hi, scalar1=-1.0, scalar2=None,
         op0=ALU.mult)
    for it in range(42):
        S.do("vector", "tensor_tensor", mid, lo, hi, op=ALU.add)
        S.do("vector", "tensor_scalar", out=mid, in0=mid, scalar1=0.5,
             scalar2=None, op0=ALU.mult)
        S.do("tensor", "matmul", ps_A[:, 0:1], sc["ones_row"], mid,
             start=True, stop=True)
        S.do("vector", "tensor_copy", midb, ps_A[:, 0:1])
        S.do("vector", "tensor_scalar", out=zt1, in0=s128, scalar1=midb,
             scalar2=None, op0=ALU.is_gt)
        S.do("vector", "reduce_sum", zcnt, zt1, axis=AXL.X)
        S.do("tensor", "matmul", ps_A[0:1, 4:5], zcnt, sc["ones_col"],
             start=True, stop=True)
        S.do("vector", "tensor_copy", cnt, ps_A[0:1, 4:5])
        S.do("vector", "tensor_scalar", out=msk1, in0=cnt, scalar1=255.5,
             scalar2=None, op0=ALU.is_ge)
        S.do("vector", "tensor_tensor", btmp, mid, lo, op=ALU.subtract)
        S.do("vector", "tensor_scalar", out=btmp, in0=btmp, scalar1=msk1,
             scalar2=None, op0=ALU.mult)
        S.do("vector", "tensor_tensor", btmp2, lo, btmp, op=ALU.add)
        S.do("vector", "tensor_tensor", btmp, hi, mid, op=ALU.subtract)
        S.do("vector", "tensor_scalar", out=btmp, in0=btmp, scalar1=msk1,
             scalar2=None, op0=ALU.mult)
        S.do("vector", "tensor_tensor", hi, mid, btmp, op=ALU.add)
        S.do("vector", "tensor_copy", lo, btmp2)
    S.do("tensor", "matmul", ps_A[:, 0:1], sc["ones_row"], lo, start=True, stop=True)
    S.do("vector", "tensor_copy", midb, ps_A[:, 0:1])
    S.do("vector", "tensor_scalar", out=zt1, in0=s128, scalar1=midb,
         scalar2=None, op0=ALU.is_gt)
    pfa_ = nc.alloc_sbuf_tensor("pref_a", [128, 48], F32).ap()
    pfb_ = nc.alloc_sbuf_tensor("pref_b", [128, 48], F32).ap()
    S.do("vector", "memset", pfa_, 0.0)
    S.do("vector", "memset", pfb_, 0.0)
    S.do("vector", "tensor_copy", pfa_[:, 16:48], zt1)
    cur, nxt = pfa_, pfb_
    for k in (1, 2, 4, 8, 16):
        S.do("vector", "tensor_tensor", nxt[:, 16:48], cur[:, 16:48],
             cur[:, 16 - k : 48 - k], op=ALU.add)
        if k < 16:
            S.do("vector", "tensor_copy", nxt[:, 16 - k : 16], cur[:, 16 - k : 16])
        cur, nxt = nxt, cur
    S.do("tensor", "matmul", ps_A[:, 1:2], stu, cur[:, 47:48], start=True, stop=True)
    S.do("vector", "tensor_copy", zcnt, ps_A[:, 1:2])
    rnk = nc.alloc_sbuf_tensor("rnk", [128, 32], F32).ap()
    S.do("vector", "tensor_tensor", rnk, cur[:, 16:48], zt1, op=ALU.subtract)
    S.do("vector", "tensor_scalar", out=rnk, in0=rnk, scalar1=zcnt, scalar2=None,
         op0=ALU.add)
    S.do("vector", "tensor_scalar", out=rnk, in0=rnk, scalar1=9999.0, scalar2=None,
         op0=ALU.subtract)
    S.do("vector", "tensor_tensor", rnk, rnk, zt1, op=ALU.mult)
    S.do("vector", "tensor_scalar", out=rnk, in0=rnk, scalar1=9999.0, scalar2=None,
         op0=ALU.add)
    rnku = nc.alloc_sbuf_tensor("rnku", [128, 32], U32).ap()
    S.do("vector", "tensor_copy", rnku, rnk)
    ivu = nc.alloc_sbuf_tensor("ivu", [128, 32], U32).ap()
    S.do("gpsimd", "iota", ivu, pattern=[[1, 32]], base=0, channel_multiplier=32)
    for fcol in range(32):
        S.do("gpsimd", "indirect_dma_start",
             out=idxbuf,
             out_offset=bass_rust.IndirectOffsetOnAxis(
                 ap=rnku[:, fcol : fcol + 1], axis=0),
             in_=ivu[:, fcol : fcol + 1], in_offset=None,
             bounds_check=255, oob_is_err=False)
    idxsb = nc.alloc_sbuf_tensor("idxsb", [128, 2], U32).ap()
    S.do("sync", "dma_start", idxsb[:, 0:1], idxbuf[0:128])
    S.do("sync", "dma_start", idxsb[:, 1:2], idxbuf[128:256])
    selm = nc.alloc_sbuf_tensor("selm", [128, 1], F32).ap()
    selv_ = nc.alloc_sbuf_tensor("selv_", [128, 1], F32).ap()
    selr = nc.alloc_sbuf_tensor("selr", [128, 1], F32).ap()
    selTb = nc.alloc_sbuf_tensor("selTb", [128, 8 * 256], BF16).ap()
    selT3 = selTb.rearrange("p (c s) -> p c s", s=256)
    with nc.sbuf_tensor("selx", [128, D], F32) as selx_h, \
         nc.sbuf_tensor("sqsel", [128, D], F32) as sqsel_h, \
         nc.sbuf_tensor("selb", [128, D], BF16) as selb_h:
        selx, sqsel, selb = selx_h.ap(), sqsel_h.ap(), selb_h.ap()
        for g in range(2):
            S.do("gpsimd", "indirect_dma_start",
                 out=selx, out_offset=None, in_=xtok,
                 in_offset=bass_rust.IndirectOffsetOnAxis(
                     ap=idxsb[:, g : g + 1], axis=0),
                 bounds_check=SEQ - 1, oob_is_err=False)
            S.do("scalar", "activation", sqsel, selx, AF.Square)
            S.do("vector", "reduce_sum", selm, selx, axis=AXL.X)
            S.do("vector", "reduce_sum", selv_, sqsel, axis=AXL.X)
            S.do("vector", "tensor_scalar", out=selm, in0=selm, scalar1=1.0 / D,
                 scalar2=None, op0=ALU.mult)
            S.do("vector", "tensor_tensor", selr, selm, selm, op=ALU.mult)
            S.do("vector", "scalar_tensor_tensor", selv_, selv_, 1.0 / D, selr,
                 op0=ALU.mult, op1=ALU.subtract)
            S.do("scalar", "activation", selv_, selv_, AF.Ln, bias=EPS_LN)
            S.do("scalar", "activation", selr, selv_, AF.Exp, scale=-0.5)
            S.do("vector", "tensor_scalar", out=selb, in0=selx, scalar1=selm,
                 scalar2=selr, op0=ALU.subtract, op1=ALU.mult)
            for c in range(8):
                S.do("tensor", "transpose", ps_T,
                     selb[:, c * 128 : (c + 1) * 128], ident)
                S.do("vector", "tensor_copy", selT3[:, c, g * 128 : (g + 1) * 128],
                     ps_T)

    ckvb = nc.alloc_sbuf_tensor("ckvb", [32, KEYN], BF16).ap()
    rmsr = nc.alloc_sbuf_tensor("rmsr", [1, TW], F32).ap()
    nqb = nc.alloc_sbuf_tensor("nqb", [96, 2048], BF16).ap()
    rawb = nc.alloc_sbuf_tensor("rawb", [96, SEQ], BF16).ap()

    sqw = nc.alloc_sbuf_tensor("sqw", [96, TW], F32).ap()

    def rms_project(dst, Wsb, ncols, ntile, tok0, gvec, extra_scale=None):
        for tt in range(ntile):
            a = tok0 + tt * TW
            for c in range(8):
                S.do("tensor", "matmul", ps_A[:ncols, 0:TW], Wsb[:, c, 0:ncols],
                     lnb3[:, c, a : a + TW], start=(c == 0), stop=(c == 7))
            S.do("scalar", "activation", sqw[:ncols, :], ps_A[:ncols, 0:TW],
                 AF.Square)
            S.do("tensor", "matmul", ps_sum, sc["ones_col"][0:ncols],
                 sqw[:ncols, :], start=True, stop=True)
            S.do("scalar", "activation", rmsr, ps_sum, AF.Ln, bias=EPS_RMS,
                 scale=1.0 / ncols)
            S.do("scalar", "activation", rmsr, rmsr, AF.Exp, scale=-0.5)
            S.do("tensor", "matmul", ps_A[:ncols, TW : 2 * TW],
                 sc["ones_row"][:, 0:ncols], rmsr, start=True, stop=True)
            S.do("vector", "tensor_copy", sqw[:ncols, :],
                 ps_A[:ncols, TW : 2 * TW])
            S.do("vector", "tensor_tensor", sqw[:ncols, :], ps_A[:ncols, 0:TW],
                 sqw[:ncols, :], op=ALU.mult)
            if extra_scale is not None:
                S.do("vector", "tensor_scalar", out=sqw[:ncols, :],
                     in0=sqw[:ncols, :], scalar1=extra_scale, scalar2=None,
                     op0=ALU.mult)
            S.do("vector", "tensor_scalar", out=dst[:, tt * TW : (tt + 1) * TW],
                 in0=sqw[:ncols, :], scalar1=gvec, scalar2=None, op0=ALU.mult)

    rms_project(nqb, cq_s, 96, 4, Q0, gq)
    rms_project(ckvb, ckv_s, 32, KEYN // TW, 0, gk)

    def rope_emit(dst, src, n32, cos_ap, sin_ap, tok0, ntok):
        for a0 in range(0, ntok, 1024):
            w_ = min(1024, ntok - a0)
            lo_s = src[n32 : n32 + 32, a0 : a0 + w_]
            hi_s = src[n32 + 32 : n32 + 64, a0 : a0 + w_]
            lo_d = dst[n32 : n32 + 32, a0 : a0 + w_]
            hi_d = dst[n32 + 32 : n32 + 64, a0 : a0 + w_]
            ct = cos_ap[:, tok0 + a0 : tok0 + a0 + w_]
            st = sin_ap[:, tok0 + a0 : tok0 + a0 + w_]
            l0 = ropes1[:, 0:w_]
            h0_ = ropes2[:, 0:w_]
            r1 = ropet1[:, 0:w_]
            r2 = ropet2[:, 0:w_]
            S.do("vector", "tensor_copy", l0, lo_s)
            S.do("vector", "tensor_copy", h0_, hi_s)
            S.do("vector", "tensor_tensor", r1, l0, ct, op=ALU.mult)
            S.do("vector", "tensor_tensor", r2, h0_, st, op=ALU.mult)
            S.do("vector", "tensor_tensor", lo_d, r1, r2, op=ALU.subtract)
            S.do("vector", "tensor_tensor", r1, l0, st, op=ALU.mult)
            S.do("vector", "tensor_tensor", r2, h0_, ct, op=ALU.mult)
            S.do("vector", "tensor_tensor", hi_d, r1, r2, op=ALU.add)

    ropet1 = nc.alloc_sbuf_tensor("ropet1", [32, 1024], BF16).ap()
    ropet2 = nc.alloc_sbuf_tensor("ropet2", [32, 1024], BF16).ap()
    ropes1 = nc.alloc_sbuf_tensor("ropes1", [32, 1024], BF16).ap()
    ropes2 = nc.alloc_sbuf_tensor("ropes2", [32, 1024], BF16).ap()

    k1h = nc.alloc_sbuf_tensor("k1h", [96, KEYN], BF16).ap()
    for tt in range(KEYN // TW):
        for c in range(8):
            S.do("tensor", "matmul", ps_A[:64, 0:TW], krop_s[:, c, :],
                 lnb3[:, c, tt * TW : (tt + 1) * TW], start=(c == 0), stop=(c == 7))
        S.do("scalar", "activation", rawb[32:64, tt * TW : (tt + 1) * TW],
             ps_A[0:32, 0:TW], AF.Copy, scale=1.0 / 16.0)
        S.do("scalar", "activation", rawb[64:96, tt * TW : (tt + 1) * TW],
             ps_A[32:64, 0:TW], AF.Copy, scale=1.0 / 16.0)
    rope_emit(k1h, rawb, 32, cosq, sinq, 0, KEYN)

    dq_s = nc.alloc_sbuf_tensor("dq_s", [96, 96], BF16).ap()
    dkn_s = nc.alloc_sbuf_tensor("dkn_s", [32, 32], BF16).ap()
    dv_s = nc.alloc_sbuf_tensor("dv_s", [32, 32], BF16).ap()
    selk_s = nc.alloc_sbuf_tensor("selk_s", [128, 8 * 96], BF16).ap()
    selk3 = selk_s.rearrange("p (c m) -> p c m", m=96)
    selv_s = nc.alloc_sbuf_tensor("selv_s", [128, 8 * 32], BF16).ap()
    selv3 = selv_s.rearrange("p (c m) -> p c m", m=32)
    wink_s = nc.alloc_sbuf_tensor("wink_s", [128, 8 * 96], BF16).ap()
    wink3 = wink_s.rearrange("p (c m) -> p c m", m=96)
    winv_s = nc.alloc_sbuf_tensor("winv_s", [128, 8 * 32], BF16).ap()
    winv3 = winv_s.rearrange("p (c m) -> p c m", m=32)
    wproj_s = emit_load_weight_bf16(S, nc, "wproj8", (256, 1024), 2)

    qt = nc.alloc_sbuf_tensor("qt", [96, 2048], BF16).ap()
    vtmp = nc.alloc_sbuf_tensor("vtmp", [32, TW], BF16).ap()
    v1T = nc.alloc_sbuf_tensor("v1T", [128, (KEYN // 128) * 32], BF16).ap()
    v1T3 = v1T.rearrange("p (k v) -> p k v", v=32)
    kwh = nc.alloc_sbuf_tensor("kwh", [96, KEYN], BF16).ap()
    vwT = nc.alloc_sbuf_tensor("vwT", [128, (KEYN // 128) * 32], BF16).ap()
    vwT3 = vwT.rearrange("p (k v) -> p k v", v=32)
    ksh = nc.alloc_sbuf_tensor("ksh", [96, 256], BF16).ap()
    vsT = nc.alloc_sbuf_tensor("vsT", [128, 2 * 32], BF16).ap()
    vsT3 = vsT.rearrange("p (k v) -> p k v", v=32)
    qnq = nc.alloc_sbuf_tensor("qnq", [1, TW], F32).ap()
    knsq = nc.alloc_sbuf_tensor("knsq", [1, TW], F32).ap()
    k1st = nc.alloc_sbuf_tensor("k1st", [1, 1], F32).ap()
    k2st = nc.alloc_sbuf_tensor("k2st", [1, 1], F32).ap()
    k3st = nc.alloc_sbuf_tensor("k3st", [1, 1], F32).ap()
    mtmp = nc.alloc_sbuf_tensor("mtmp1", [1, 1], F32).ap()
    negB = nc.alloc_sbuf_tensor("negB1", [1, TW], F32).ap()
    SCt = nc.alloc_sbuf_tensor("SCt1", [128, TW], BF16).ap()
    expSa = nc.alloc_sbuf_tensor("expS1", [128, TW], BF16).ap()
    expSb = nc.alloc_sbuf_tensor("expS1b", [128, TW], BF16).ap()
    expSs = (expSa, expSb)
    ssum = nc.alloc_sbuf_tensor("ssum1", [1, TW], F32).ap()
    rsum = nc.alloc_sbuf_tensor("rsum1", [1, TW], F32).ap()
    otmp = nc.alloc_sbuf_tensor("otmp1", [32, TW], F32).ap()
    oacc = nc.alloc_sbuf_tensor("oacc1", [32, TW], F32).ap()
    ostg = nc.alloc_sbuf_tensor("ostg1", [32, TW], BF16).ap()
    Oall = nc.alloc_sbuf_tensor("Oall1", [128, 2 * 2048], BF16).ap()
    Oall4 = Oall.rearrange("p (h t) -> p h t", h=2)
    yf = nc.alloc_sbuf_tensor("yf_a1", [128, TW], F32).ap()

    def norm_max(src_t, npart, ntt, dstmax, width=TW):
        for tt in range(ntt):
            S.do("scalar", "activation", sqw[:npart, 0:width],
                 src_t[:, tt * width : (tt + 1) * width], AF.Square)
            S.do("tensor", "matmul", ps_sum[:, 0:width], sc["ones_col"][0:npart],
                 sqw[:npart, 0:width], start=True, stop=True)
            S.do("scalar", "activation", knsq[:, 0:width], ps_sum[:, 0:width],
                 AF.Sqrt)
            if tt == 0:
                S.do("vector", "tensor_reduce", dstmax, knsq[:, 0:width],
                     axis=AXL.X, op=ALU.max)
            else:
                S.do("vector", "tensor_reduce", mtmp, knsq[:, 0:width],
                     axis=AXL.X, op=ALU.max)
                S.do("vector", "tensor_tensor", dstmax, dstmax, mtmp, op=ALU.max)

    for h in range(8):
        S.do("gpsimd", "dma_start", dq_s, dq8d[:, h * 96 : (h + 1) * 96])
        S.do("gpsimd", "dma_start", dkn_s, dkn8d[:, h * 32 : (h + 1) * 32])
        S.do("gpsimd", "dma_start", dv_s, dv8d[:, h * 32 : (h + 1) * 32])
        S.do("gpsimd", "dma_start", selk3,
             selk8d[:, h * 96 : (h + 1) * 96].rearrange("(c p) m -> p c m", p=128))
        S.do("gpsimd", "dma_start", selv3,
             selv8d[:, h * 32 : (h + 1) * 32].rearrange("(c p) m -> p c m", p=128))
        S.do("gpsimd", "dma_start", wink3,
             wink8d[:, h * 96 : (h + 1) * 96].rearrange("(c p) m -> p c m", p=128))
        S.do("gpsimd", "dma_start", winv3,
             winv8d[:, h * 32 : (h + 1) * 32].rearrange("(c p) m -> p c m", p=128))

        for tt in range(KEYN // TW):
            a = tt * TW
            S.do("tensor", "matmul", ps_A[:32, 0:TW], dkn_s, ckvb[:, a : a + TW],
                 start=True, stop=True)
            S.do("vector", "tensor_copy", k1h[0:32, a : a + TW], ps_A[:32, 0:TW])
            S.do("tensor", "matmul", ps_A[:32, TW : 2 * TW], dv_s,
                 ckvb[:, a : a + TW], start=True, stop=True)
            S.do("vector", "tensor_copy", vtmp, ps_A[:32, TW : 2 * TW])
            for j in range(4):
                S.do("tensor", "transpose", ps_T[:, 0:32],
                     vtmp[:, j * 128 : (j + 1) * 128], ident[0:32, 0:32])
                S.do("vector", "tensor_copy", v1T3[:, tt * 4 + j], ps_T[:, 0:32])
        for tt in range(KEYN // TW):
            a = tt * TW
            for c in range(8):
                S.do("tensor", "matmul", ps_A[:96, 0:TW], wink3[:, c],
                     lnb3[:, c, a : a + TW], start=(c == 0), stop=(c == 7))
            S.do("vector", "tensor_copy", rawb[:, a : a + TW], ps_A[:96, 0:TW])
            for c in range(8):
                S.do("tensor", "matmul", ps_A[:32, TW : 2 * TW], winv3[:, c],
                     lnb3[:, c, a : a + TW], start=(c == 0), stop=(c == 7))
            S.do("vector", "tensor_copy", vtmp, ps_A[:32, TW : 2 * TW])
            for j in range(4):
                S.do("tensor", "transpose", ps_T[:, 0:32],
                     vtmp[:, j * 128 : (j + 1) * 128], ident[0:32, 0:32])
                S.do("vector", "tensor_copy", vwT3[:, tt * 4 + j], ps_T[:, 0:32])
        S.do("vector", "tensor_copy", kwh[0:32, :], rawb[0:32, 0:KEYN])
        rope_emit(kwh, rawb[:, 0:KEYN], 32, cosq, sinq, 0, KEYN)
        for c in range(8):
            S.do("tensor", "matmul", ps_A[:96, 0:256], selk3[:, c],
                 selT3[:, c], start=(c == 0), stop=(c == 7))
        S.do("vector", "tensor_copy", rawb[:, 0:256], ps_A[:96, 0:256])
        S.do("vector", "tensor_copy", ksh[0:32, :], rawb[0:32, 0:256])
        rope_emit(ksh, rawb[:, 0:256], 32, cosq, sinq, 0, 256)
        for c in range(8):
            S.do("tensor", "matmul", ps_A[:32, 256:512], selv3[:, c],
                 selT3[:, c], start=(c == 0), stop=(c == 7))
        S.do("vector", "tensor_copy", rawb[0:32, 512 : 512 + 256],
             ps_A[:32, 256:512])
        for g in range(2):
            S.do("tensor", "transpose", ps_T[:, 0:32],
                 rawb[0:32, 512 + g * 128 : 512 + (g + 1) * 128],
                 ident[0:32, 0:32])
            S.do("vector", "tensor_copy", vsT3[:, g], ps_T[:, 0:32])
        for tt in range(4):
            a = tt * TW
            S.do("tensor", "matmul", ps_A[:96, 0:TW], dq_s, nqb[:, a : a + TW],
                 start=True, stop=True)
            S.do("scalar", "activation", rawb[:, a : a + TW], ps_A[:96, 0:TW],
                 AF.Copy, scale=INV96)
        S.do("vector", "tensor_copy", qt[0:32, :], rawb[0:32, 0:2048])
        rope_emit(qt, rawb[:, 0:2048], 32, cosq, sinq, Q0, 2048)
        norm_max(k1h, 96, KEYN // TW, k1st)
        norm_max(ksh, 96, 1, k2st, width=256)
        norm_max(kwh, 96, KEYN // TW, k3st)

        for t in range(4):
            R = Q0 + t * TW
            nkc = (R + TW) // 128
            S.do("scalar", "activation", sqw, qt[:, t * TW : (t + 1) * TW],
                 AF.Square)
            S.do("tensor", "matmul", ps_sum, sc["ones_col"][0:96], sqw,
                 start=True, stop=True)
            S.do("scalar", "activation", qnq, ps_sum, AF.Sqrt)
            S.do("vector", "memset", oacc, 0.0)
            for br in range(3):
                kst = (k1st, k2st, k3st)[br]
                S.do("vector", "tensor_scalar", out=negB,
                     in0=qnq, scalar1=kst,
                     scalar2=None, op0=ALU.mult)
                S.do("vector", "tensor_scalar", out=negB, in0=negB, scalar1=-1.0,
                     scalar2=None, op0=ALU.mult)
                nkc_b = 2 if br == 1 else nkc
                ksrc = (k1h, ksh, kwh)[br]
                vsrc = (v1T3, vsT3, vwT3)[br]
                tok0 = S.val
                s_free = [tok0, tok0]
                av_par = [tok0, tok0]
                sc_free = tok0
                for kc in range(nkc_b):
                    ksl = slice(kc * 128, (kc + 1) * 128)
                    par = kc % 2
                    psS = ps_Ss[par]
                    S.do("tensor", "matmul", psS, ksrc[:, ksl],
                         qt[:, t * TW : (t + 1) * TW], start=True, stop=False,
                         wait_val=s_free[par])
                    S.do("tensor", "matmul", psS, sc["ones_row"][:, 0:128], negB,
                         start=False, stop=True, wait_val=0)
                    s_tok = S.val
                    if br != 1 and 128 * kc + 127 > R:
                        dlt = 128 * kc - R
                        S.do("vector", "tensor_copy", SCt, psS,
                             wait_val=max(s_tok, sc_free))
                        s_free[par] = S.val
                        S.do("gpsimd", "dma_start", SCt,
                             maskd[:, 512 - dlt : 1024 - dlt], accum_op=ALU.add,
                             wait_val=S.val)
                        S.do("scalar", "activation", expSs[par], SCt, AF.Exp,
                             wait_val=max(S.val, av_par[par]))
                        sc_free = S.val
                    else:
                        S.do("scalar", "activation", expSs[par], psS, AF.Exp,
                             wait_val=max(s_tok, av_par[par]))
                        s_free[par] = S.val
                    exp_tok = S.val
                    S.do("tensor", "matmul", ps_sum, sc["ones_col_bf"],
                         expSs[par], start=(kc == 0), stop=(kc == nkc_b - 1),
                         wait_val=exp_tok)
                    g = kc % 2
                    S.do("tensor", "matmul", ps_O[32 * g : 32 * g + 32, :],
                         vsrc[:, kc], expSs[par], start=(kc == g),
                         stop=(kc >= nkc_b - 2), tile_position=(0, 32 * g),
                         wait_val=0)
                    av_par[par] = S.val
                S.do("vector", "tensor_copy", ssum, ps_sum)
                S.do("vector", "reciprocal", rsum, ssum)
                S.do("vector", "tensor_scalar", out=rsum, in0=rsum,
                     scalar1=g123[:, br : br + 1], scalar2=None, op0=ALU.mult)
                S.do("tensor", "matmul", ps_A[:32, 0:TW], sc["ones_row"][:, 0:32],
                     rsum, start=True, stop=True)
                if nkc_b >= 2:
                    S.do("vector", "tensor_copy", otmp, ps_O[32:64, :])
                    S.do("vector", "tensor_tensor", otmp, ps_O[0:32, :], otmp,
                         op=ALU.add)
                else:
                    S.do("vector", "tensor_copy", otmp, ps_O[0:32, :])
                S.do("vector", "tensor_tensor", otmp, otmp, ps_A[:32, 0:TW],
                     op=ALU.mult)
                S.do("vector", "tensor_tensor", oacc, oacc, otmp, op=ALU.add)
            hc, hb = h // 4, 32 * (h % 4)
            if hb == 96:
                S.do("vector", "tensor_copy", ostg, oacc)
                S.do("sync", "dma_start",
                     Oall4[96:128, hc, t * TW : (t + 1) * TW], ostg)
            else:
                S.do("vector", "tensor_copy",
                     Oall4[hb : hb + 32, hc, t * TW : (t + 1) * TW], oacc)

    for t in range(4):
        for d_ in range(8):
            for hc in range(2):
                S.do("tensor", "matmul", ps_A[:, 0:TW],
                     wproj_s[:, hc, d_ * 128 : (d_ + 1) * 128],
                     Oall4[:, hc, t * TW : (t + 1) * TW],
                     start=(hc == 0), stop=(hc == 1))
            S.do("vector", "tensor_copy", yf, ps_A[:, 0:TW])
            dst = yout[:, t * TW : (t + 1) * TW].rearrange("(c p) t -> p c t", p=128)[:, d_]
            S.do("sync", "dma_start", dst, yf)
    S.fence("sync")
    return nc


def rope_tables():
    f = 1.0 / (10000.0 ** (np.arange(0, 64, 2, dtype=np.float32) / 64.0))
    t = np.arange(SEQ, dtype=np.float32)
    a = np.outer(t, f)
    return np.ascontiguousarray(np.cos(a).T), np.ascontiguousarray(np.sin(a).T)


def attn1_in_maps(inputs, xT, xtokfull, half):
    cosq, sinq = rope_tables()
    g = lambda k: np.asarray(inputs["L1_attn_" + k], np.float32)
    w_dqn, w_dqr = g("w_dq_nope"), g("w_dq_rope")
    w_dkn, w_dv = g("w_dk_nope"), g("w_dv")
    w_selk, w_selv = g("w_selk"), g("w_selv")
    w_wink, w_winv = g("w_wink"), g("w_winv")
    w_proj = g("w_proj")

    def perm96(W, h0):
        raise RuntimeError

    def headblock_k(Wn, Wr, h0):
        cols = []
        for h in range(h0, h0 + 8):
            n = Wn[:, h * 32 : (h + 1) * 32]
            r = Wr[:, h * 64 : (h + 1) * 64]
            cols += [n, r[:, 0:32], r[:, 32:64]]
        return np.ascontiguousarray(np.concatenate(cols, axis=1))

    def headblock_kr(Wkr, h0):
        cols = []
        for h in range(h0, h0 + 8):
            blk = Wkr[:, h * 96 : (h + 1) * 96]
            cols += [blk[:, 0:32], blk[:, 32:64], blk[:, 64:96]]
        return np.ascontiguousarray(np.concatenate(cols, axis=1))

    def headblock_v(Wv, h0):
        return np.ascontiguousarray(Wv[:, h0 * 32 : (h0 + 8) * 32])

    maps = []
    for c in range(8):
        b, o = c // 2, c % 2
        h0 = o * 8
        wp8 = np.zeros((256, 1024), np.float32)
        for hp in range(8):
            src = w_proj[(h0 + hp) * 32 : (h0 + hp) * 32 + 32, :]
            wp8[(hp // 4) * 128 + (hp % 4) * 32 : (hp // 4) * 128 + (hp % 4) * 32 + 32] = src
        m = {
            "xt": f32(xT[:, b * SEQ : (b + 1) * SEQ]),
            "xtok": f32(xtokfull[b * SEQ : (b + 1) * SEQ, :]),
            "maskpl": make_mask_plane(),
            "ident": np.eye(128, dtype=np.float32),
            "stu": np.triu(np.ones((128, 128), np.float32), 1),
            "g_qnorm": g("g_qnorm").reshape(96, 1),
            "g_kvnorm": g("g_kvnorm").reshape(32, 1),
            "w_gate": g("w_gate"),
            "w_imp": f32(as_pcols(g("w_imp")[:, 0])),
            "w_cq": g("w_cq"),
            "w_ckv": g("w_ckv"),
            "w_krope": g("w_krope"),
            "dq8": headblock_k(w_dqn, w_dqr, h0),
            "dkn8": headblock_v(w_dkn, o * 8),
            "dv8": headblock_v(w_dv, o * 8),
            "selk8": headblock_kr(w_selk, h0),
            "selv8": headblock_v(w_selv, o * 8),
            "wink8": headblock_kr(w_wink, h0),
            "winv8": headblock_v(w_winv, o * 8),
            "wproj8": wp8,
            "cosq": cosq, "sinq": sinq,
            "ones_col": np.ones((128, 1), np.float32),
            "ones_row": np.ones((1, 128), np.float32),
        }
        maps.append(m)
    return maps


def run_attn1(inputs, xT, half):
    nc = get_prog("attn1", build_attn1, half)
    xtok = np.ascontiguousarray(xT.T)
    maps = attn1_in_maps(inputs, xT, xtok, half)
    res = run_bass_kernel_spmd(nc, maps, core_ids=list(range(8)))
    return [r["ypart"] for r in res.results]





_PROG_CACHE = {}


def get_prog(name, builder, *args):
    key = (name, args)
    if key not in _PROG_CACHE:
        _PROG_CACHE[key] = builder(*args)
    return _PROG_CACHE[key]


def run_ffn0(xT, w, partials=None, bp=None):
    nc = get_prog("ffn0", build_ffn0, partials is not None)
    maps = ffn0_in_maps(xT, w["U1"], w["V1"], w["b1"], w["U2"], w["V2"], w["b2"],
                        partials=partials, bp=bp)
    res = run_bass_kernel_spmd(nc, maps, core_ids=list(range(8)))
    out = np.concatenate([r["yt"] for r in res.results], axis=1)
    return out


def assemble_partials(plist0, plist1):
    PA = np.zeros((D, NTOK), np.float32)
    PB = np.zeros((D, NTOK), np.float32)
    for c in range(8):
        b, half = c // 2, c % 2
        src = plist0 if half == 0 else plist1
        PA[:, c * TPC : (c + 1) * TPC] = src[2 * b]
        PB[:, c * TPC : (c + 1) * TPC] = src[2 * b + 1]
    return PA, PB


def kernel(**inputs):
    x = np.asarray(inputs["L0_attn_x"], np.float32).reshape(NTOK, D)
    xT = np.ascontiguousarray(x.T)
    w0 = {k: np.asarray(inputs["L0_ffn_" + k]) for k in
          ("U1", "V1", "b1", "U2", "V2", "b2")}
    xT = run_ffn0(xT, w0)
    Wq = np.asarray(inputs["L0_attn_Wq"])
    Wk = np.asarray(inputs["L0_attn_Wk"])
    Wv = np.asarray(inputs["L0_attn_Wv"])
    Wp = np.asarray(inputs["L0_attn_Wp"])
    bp = np.asarray(inputs["L0_attn_bp"])
    rel = np.asarray(inputs["L0_attn_rel_embed"])
    p0 = run_attn0(xT, Wq, Wk, Wv, Wp, rel, 0)
    p1 = run_attn0(xT, Wq, Wk, Wv, Wp, rel, 1)
    PA, PB = assemble_partials(p0, p1)
    xT = run_ffn0(xT, w0, partials=(PA, PB), bp=bp)
    wgT = np.ascontiguousarray(np.asarray(inputs["L1_ffn_w_gate"]).T)
    wuT = np.ascontiguousarray(np.asarray(inputs["L1_ffn_w_up"]).T)
    wdT = np.ascontiguousarray(np.asarray(inputs["L1_ffn_w_down"]).T)
    xT = run_ffn1(xT, wgT, wuT, wdT)
    q0 = run_attn1(inputs, xT, 0)
    q1 = run_attn1(inputs, xT, 1)
    QA, QB = assemble_partials(q0, q1)
    xT = run_ffn1(xT, wgT, wuT, wdT, partials=(QA, QB))
    out = np.ascontiguousarray(xT.T).reshape(4, 4096, 1024)
    return out


# version 06
# speedup vs baseline: 1.1078x; 1.0700x over previous
import numpy as np
import ml_dtypes
import concourse.bass as bass
import concourse.mybir as mybir
import bass_rust
from concourse.bass_utils import run_bass_kernel_spmd

F32 = mybir.dt.float32
BF16 = mybir.dt.bfloat16
I32 = mybir.dt.int32
U32 = mybir.dt.uint32
AF = mybir.ActivationFunctionType
ALU = mybir.AluOpType
AXL = mybir.AxisListType
BF = ml_dtypes.bfloat16

D = 1024
NTOK = 16384
TPC = 2048
NB = 4
SEQ = 4096

EPS_LN = 1e-5


class Ser:

    def __init__(self, nc):
        self.nc = nc
        self.sem = nc.alloc_semaphore()
        self.val = 0
        self.last_eng = None

    def _pre(self, eng_name):
        eng = getattr(self.nc, eng_name)
        if not (eng_name == "tensor" and self.last_eng == "tensor"):
            eng.wait_ge(self.sem, self.val)
        self.last_eng = eng_name
        return eng

    def do(self, eng_name, meth, *args, inc=None, wait_val=None, **kwargs):
        if wait_val is None:
            eng = self._pre(eng_name)
        else:
            eng = getattr(self.nc, eng_name)
            eng.wait_ge(self.sem, min(wait_val, self.val))
            self.last_eng = None
        inst = getattr(eng, meth)(*args, **kwargs)
        if inc is None:
            inc = 16 if "dma" in meth else 1
        inst.then_inc(self.sem, inc)
        self.val += inc
        return inst

    def fence(self, eng_name="sync"):
        eng = getattr(self.nc, eng_name)
        eng.wait_ge(self.sem, self.val)
        self.last_eng = None


def mk_nc():
    return bass.Bass("TRN2", target_bir_lowering=False)


def bf(x):
    return np.ascontiguousarray(x).astype(BF)


def f32(x):
    return np.ascontiguousarray(x, dtype=np.float32)


def as_pcols(v):
    v = np.asarray(v)
    return np.ascontiguousarray(v.reshape(-1, 128).T)



def reg_const(S, nc, val, dtype=F32, tag=None):
    tag = tag if tag is not None else str(abs(hash((float(val), str(dtype)))) % 10**8)
    t = nc.alloc_sbuf_tensor(f"const_{tag}", [128, 1], dtype).ap()
    S.do("vector", "memset", t, float(val))
    nc.const_aps.register(dtype, float(val), t)
    return t


def emit_load_weight_bf16(S, nc, name, arr_shape, nchunk_free):
    k, rem = divmod(arr_shape[0], 128)
    assert rem == 0
    m = arr_shape[1]
    d = nc.dram_tensor(name, list(arr_shape), F32, kind="ExternalInput").ap()
    s = nc.alloc_sbuf_tensor(name + "_sb", [128, k * m], BF16).ap()
    src = d.rearrange("(c p) m -> p c m", p=128)
    dst = s.rearrange("p (c m) -> p c m", m=m)
    S.do("gpsimd", "dma_start", dst, src)
    return s.rearrange("p (c m) -> p c m", m=m)


def emit_ln_feature_major(S, nc, xf, nchunks, tw, ones_sb, sc, tag):
    if "out3" in sc and sc["out3"] is not None:
        lnb3 = sc["out3"]
    else:
        lnb = nc.alloc_sbuf_tensor(f"lnb_{tag}", [128, nchunks * tw], BF16).ap()
        lnb3 = lnb.rearrange("p (c t) -> p c t", t=tw)
    ps_s = sc["ps_stat"]
    ps_b = sc["ps_bc"]
    nD = nchunks * 128
    for c in range(nchunks):
        S.do("scalar", "activation", sc["sq"], xf[:, c], AF.Square)
        S.do("tensor", "matmul", ps_s[:, 0:tw], ones_sb, xf[:, c],
             start=(c == 0), stop=(c == nchunks - 1))
        S.do("tensor", "matmul", ps_s[:, tw : 2 * tw], ones_sb, sc["sq"],
             start=(c == 0), stop=(c == nchunks - 1))
    S.do("scalar", "activation", sc["m"], ps_s[:, 0:tw], AF.Copy, scale=1.0 / nD)
    S.do("vector", "tensor_tensor", sc["v"], sc["m"], sc["m"], op=ALU.mult)
    S.do("vector", "scalar_tensor_tensor", sc["v2"], ps_s[:, tw : 2 * tw],
         1.0 / nD, sc["v"], op0=ALU.mult, op1=ALU.subtract)
    S.do("scalar", "activation", sc["v"], sc["v2"], AF.Ln, bias=EPS_LN)
    S.do("scalar", "activation", sc["r"], sc["v"], AF.Exp, scale=-0.5)
    S.do("vector", "tensor_tensor", sc["mr"], sc["m"], sc["r"], op=ALU.mult)
    S.do("tensor", "matmul", ps_b[:, 0:tw], sc["ones_row"], sc["r"],
         start=True, stop=True)
    S.do("tensor", "matmul", ps_b[:, tw : 2 * tw], sc["ones_row"], sc["mr"],
         start=True, stop=True)
    S.do("vector", "tensor_copy", sc["rb"], ps_b[:, 0:tw])
    S.do("vector", "tensor_copy", sc["mrb"], ps_b[:, tw : 2 * tw])
    for c in range(nchunks):
        S.do("scalar", "activation", sc["xb"], xf[:, c], AF.Copy)
        S.do("vector", "tensor_tensor", sc["xb2"], sc["xb"], sc["rb"], op=ALU.mult)
        S.do("vector", "tensor_tensor", lnb3[:, c], sc["xb2"], sc["mrb"],
             op=ALU.subtract)
    return lnb3


def alloc_ln_scratch(S, nc, tw, ps_bc, ps_stat=None, tag=""):
    sc = {}
    reg_const(S, nc, EPS_LN, tag="lneps")
    sc["sq"] = nc.alloc_sbuf_tensor(f"ln_sq{tag}", [128, tw], F32).ap()
    if ps_stat is None:
        ps_stat = nc.alloc_psum_tensor(f"ln_ps_s{tag}", [1, 2 * tw], F32).ap()
    sc["ps_stat"] = ps_stat
    sc["ps_bc"] = ps_bc
    for nm in ("m", "v", "v2", "r", "mr"):
        sc[nm] = nc.alloc_sbuf_tensor(f"ln_{nm}{tag}", [1, tw], F32).ap()
    sc["rb"] = nc.alloc_sbuf_tensor(f"ln_rb{tag}", [128, tw], BF16).ap()
    sc["mrb"] = nc.alloc_sbuf_tensor(f"ln_mrb{tag}", [128, tw], BF16).ap()
    sc["xb"] = nc.alloc_sbuf_tensor(f"ln_xb{tag}", [128, tw], BF16).ap()
    sc["xb2"] = nc.alloc_sbuf_tensor(f"ln_xb2{tag}", [128, tw], BF16).ap()
    ones = np.ones((128, 1), np.float32)
    onr = np.ones((1, 128), np.float32)
    sc["ones_col_host"] = ones
    sc["ones_row_host"] = onr
    ocd = nc.dram_tensor("ones_col", [128, 1], F32, kind="ExternalInput").ap()
    ord_ = nc.dram_tensor("ones_row", [1, 128], F32, kind="ExternalInput").ap()
    oc = nc.alloc_sbuf_tensor("ones_col_sb", [128, 1], F32).ap()
    orr = nc.alloc_sbuf_tensor("ones_row_sb", [1, 128], F32).ap()
    S.do("sync", "dma_start", oc, ocd)
    S.do("sync", "dma_start", orr, ord_)
    sc["ones_col"] = oc
    sc["ones_row"] = orr
    ocb = nc.alloc_sbuf_tensor("ones_col_bf", [128, 1], BF16).ap()
    orb = nc.alloc_sbuf_tensor("ones_row_bf", [1, 128], BF16).ap()
    S.do("gpsimd", "dma_start", ocb, ocd)
    S.do("gpsimd", "dma_start", orb, ord_)
    sc["ones_col_bf"] = ocb
    sc["ones_row_bf"] = orb
    return sc



def build_ffn0(merge_partials):
    nc = mk_nc()
    S = Ser(nc)
    TW = 512
    NT = TPC // TW
    xt = nc.dram_tensor("xt", [D, TPC], F32, kind="ExternalInput").ap()
    if merge_partials:
        pa = nc.dram_tensor("pa", [D, TPC], F32, kind="ExternalInput").ap()
        pb = nc.dram_tensor("pb", [D, TPC], F32, kind="ExternalInput").ap()
        bpv = nc.dram_tensor("bpv", [128, 8], F32, kind="ExternalInput").ap()
    yt = nc.dram_tensor("yt", [D, TPC], F32, kind="ExternalOutput").ap()

    U1 = emit_load_weight_bf16(S, nc, "U1", (1024, 256), 8)
    V1 = emit_load_weight_bf16(S, nc, "V1", (256, 4096), 2)
    U2 = emit_load_weight_bf16(S, nc, "U2", (4096, 256), 32)
    V2 = emit_load_weight_bf16(S, nc, "V2", (256, 1024), 2)
    b1d = nc.dram_tensor("b1v", [128, 32], F32, kind="ExternalInput").ap()
    b2d = nc.dram_tensor("b2v", [128, 8], F32, kind="ExternalInput").ap()
    b1s = nc.alloc_sbuf_tensor("b1s", [128, 32], F32).ap()
    b2s = nc.alloc_sbuf_tensor("b2s", [128, 8], F32).ap()
    S.do("sync", "dma_start", b1s, b1d)
    S.do("sync", "dma_start", b2s, b2d)
    if merge_partials:
        bps = nc.alloc_sbuf_tensor("bps", [128, 8], F32).ap()
        S.do("sync", "dma_start", bps, bpv)

    ps_A = nc.alloc_psum_tensor("ps_A", [128, 2 * TW], F32).ap()
    ps_B = nc.alloc_psum_tensor("ps_B", [128, 2 * TW], F32).ap()
    sc = alloc_ln_scratch(S, nc, TW, ps_A)

    xf = nc.alloc_sbuf_tensor("xf", [128, 8 * TW], F32).ap()
    xf3 = xf.rearrange("p (c t) -> p c t", t=TW)
    pf = nc.alloc_sbuf_tensor("pf", [128, 8 * TW], F32).ap() if merge_partials else None
    pf3 = pf.rearrange("p (c t) -> p c t", t=TW) if merge_partials else None
    Pb = nc.alloc_sbuf_tensor("Pb", [128, 2 * TW], BF16).ap()
    Pb3 = Pb.rearrange("p (c t) -> p c t", t=TW)
    Hb = nc.alloc_sbuf_tensor("Hb", [128, 32 * TW], BF16).ap()
    Hb3 = Hb.rearrange("p (c t) -> p c t", t=TW)
    Qb = nc.alloc_sbuf_tensor("Qb", [128, 2 * TW], BF16).ap()
    Qb3 = Qb.rearrange("p (c t) -> p c t", t=TW)
    yf = nc.alloc_sbuf_tensor("yf", [128, TW], F32).ap()
    ps_P = ps_A
    ps_H = ps_B
    ps_Y = ps_B

    for t in range(NT):
        tsl = slice(t * TW, (t + 1) * TW)
        src = xt[:, tsl].rearrange("(c p) t -> p c t", p=128)
        S.do("sync", "dma_start", xf3, src)
        if merge_partials:
            S.do("sync", "dma_start", pf3, pa[:, tsl].rearrange("(c p) t -> p c t", p=128))
            for c in range(8):
                S.do("vector", "tensor_tensor", xf3[:, c], xf3[:, c], pf3[:, c], op=ALU.add)
            S.do("sync", "dma_start", pf3, pb[:, tsl].rearrange("(c p) t -> p c t", p=128))
            for c in range(8):
                S.do("vector", "tensor_tensor", xf3[:, c], xf3[:, c], pf3[:, c], op=ALU.add)
                S.do("vector", "tensor_scalar", out=xf3[:, c], in0=xf3[:, c],
                     scalar1=bps[:, c : c + 1], scalar2=None, op0=ALU.add)
        lnb3 = emit_ln_feature_major(S, nc, xf3, 8, TW, sc["ones_col"], sc, f"t{t}")
        for o in range(2):
            for c in range(8):
                S.do("tensor", "matmul", ps_P[:, o * TW : (o + 1) * TW],
                     U1[:, c, o * 128 : (o + 1) * 128], lnb3[:, c],
                     start=(c == 0), stop=(c == 7))
            S.do("vector", "tensor_copy", Pb3[:, o], ps_P[:, o * TW : (o + 1) * TW])
        for f in range(32):
            fb = f % 2
            for c in range(2):
                S.do("tensor", "matmul", ps_H[:, fb * TW : (fb + 1) * TW],
                     V1[:, c, f * 128 : (f + 1) * 128], Pb3[:, c],
                     start=(c == 0), stop=(c == 1))
            S.do("scalar", "activation", Hb3[:, f], ps_H[:, fb * TW : (fb + 1) * TW],
                 AF.Gelu_apprx_tanh, bias=b1s[:, f : f + 1])
        for o in range(2):
            for c in range(32):
                S.do("tensor", "matmul", ps_P[:, o * TW : (o + 1) * TW],
                     U2[:, c, o * 128 : (o + 1) * 128], Hb3[:, c],
                     start=(c == 0), stop=(c == 31))
            S.do("vector", "tensor_copy", Qb3[:, o], ps_P[:, o * TW : (o + 1) * TW])
        for d_ in range(8):
            db = d_ % 2
            for c in range(2):
                S.do("tensor", "matmul", ps_Y[:, db * TW : (db + 1) * TW],
                     V2[:, c, d_ * 128 : (d_ + 1) * 128], Qb3[:, c],
                     start=(c == 0), stop=(c == 1))
            S.do("vector", "tensor_scalar", out=yf, in0=ps_Y[:, db * TW : (db + 1) * TW],
                 scalar1=b2s[:, d_ : d_ + 1], scalar2=0.5, op0=ALU.add, op1=ALU.mult)
            S.do("vector", "tensor_tensor", xf3[:, d_], yf, xf3[:, d_], op=ALU.add)
            dst = yt[:, tsl].rearrange("(c p) t -> p c t", p=128)[:, d_]
            S.do("sync", "dma_start", dst, xf3[:, d_])
    S.fence("sync")
    return nc


def ffn0_in_maps(xT, U1, V1, b1, U2, V2, b2, partials=None, bp=None):
    base = {
        "U1": f32(U1), "V1": f32(V1), "U2": f32(U2), "V2": f32(V2),
        "b1v": f32(as_pcols(b1)), "b2v": f32(as_pcols(b2)),
        "ones_col": np.ones((128, 1), np.float32),
        "ones_row": np.ones((1, 128), np.float32),
    }
    maps = []
    for c in range(8):
        m = dict(base)
        sl = slice(c * TPC, (c + 1) * TPC)
        m["xt"] = f32(xT[:, sl])
        if partials is not None:
            pa, pb_ = partials
            m["pa"] = f32(pa[:, sl])
            m["pb"] = f32(pb_[:, sl])
            m["bpv"] = f32(as_pcols(bp))
        maps.append(m)
    return maps



def build_ffn1(merge_partials, nh_lim=24, nd_lim=8):
    nc = mk_nc()
    S = Ser(nc)
    TW = 512
    NT = TPC // TW
    NH = 24
    xt = nc.dram_tensor("xt", [D, TPC], F32, kind="ExternalInput").ap()
    if merge_partials:
        pa = nc.dram_tensor("pa", [D, TPC], F32, kind="ExternalInput").ap()
        pb = nc.dram_tensor("pb", [D, TPC], F32, kind="ExternalInput").ap()
    yt = nc.dram_tensor("yt", [D, TPC], F32, kind="ExternalOutput").ap()
    wgT = nc.dram_tensor("wgT", [1024, 3072], I32, kind="ExternalInput").ap()
    wuT = nc.dram_tensor("wuT", [1024, 3072], I32, kind="ExternalInput").ap()
    wdT = nc.dram_tensor("wdT", [3072, 1024], I32, kind="ExternalInput").ap()

    ps_A = nc.alloc_psum_tensor("ps_A", [128, 2 * TW], F32).ap()
    ps_B = nc.alloc_psum_tensor("ps_B", [128, 2 * TW], F32).ap()
    sc = alloc_ln_scratch(S, nc, TW, ps_A)

    xf = nc.alloc_sbuf_tensor("xf", [128, 8 * TW], F32).ap()
    xf3 = xf.rearrange("p (c t) -> p c t", t=TW)
    lnball = nc.alloc_sbuf_tensor("lnball", [128, 8 * TPC], BF16).ap()
    lnb4 = lnball.rearrange("p (c u t) -> p c u t", c=8, u=NT)
    hb = nc.alloc_sbuf_tensor("hb", [128, NH * TPC], BF16).ap()
    hb4 = hb.rearrange("p (k u t) -> p k u t", k=NH, u=NT)
    wg_s = nc.alloc_sbuf_tensor("wg_s", [128, 8 * 128], BF16).ap()
    wg3 = wg_s.rearrange("p (c m) -> p c m", m=128)
    wu_s = nc.alloc_sbuf_tensor("wu_s", [128, 8 * 128], BF16).ap()
    wu3 = wu_s.rearrange("p (c m) -> p c m", m=128)
    wd_s = nc.alloc_sbuf_tensor("wd_s", [128, NH * 128], BF16).ap()
    wd3 = wd_s.rearrange("p (c m) -> p c m", m=128)
    sg = nc.alloc_sbuf_tensor("sg", [128, TW], BF16).ap()
    yf = nc.alloc_sbuf_tensor("yf", [128, TW], F32).ap()
    pfa = nc.alloc_sbuf_tensor("pfa", [128, TW], F32).ap() if merge_partials else None

    for t in range(NT):
        tsl = slice(t * TW, (t + 1) * TW)
        S.do("sync", "dma_start", xf3, xt[:, tsl].rearrange("(c p) t -> p c t", p=128))
        if merge_partials:
            for src in (pa, pb):
                for c in range(8):
                    S.do("sync", "dma_start", pfa,
                         src[:, tsl].rearrange("(c p) t -> p c t", p=128)[:, c])
                    S.do("vector", "tensor_tensor", xf3[:, c], xf3[:, c], pfa, op=ALU.add)
        sc["out3"] = lnb4[:, :, t]
        emit_ln_feature_major(S, nc, xf3, 8, TW, sc["ones_col"], sc, f"f1t{t}")
    sc["out3"] = None

    for h in range(nh_lim):
        hs = slice(h * 128, (h + 1) * 128)
        S.do("gpsimd", "dma_start", wg3, wgT[:, hs].rearrange("(c p) m -> p c m", p=128))
        S.do("gpsimd", "dma_start", wu3, wuT[:, hs].rearrange("(c p) m -> p c m", p=128))
        for t in range(NT):
            for c in range(8):
                S.do("tensor", "matmul", ps_A[:, 0:TW], wg3[:, c], lnb4[:, c, t],
                     start=(c == 0), stop=(c == 7))
            for c in range(8):
                S.do("tensor", "matmul", ps_A[:, TW : 2 * TW], wu3[:, c], lnb4[:, c, t],
                     start=(c == 0), stop=(c == 7))
            S.do("scalar", "activation", sg, ps_A[:, 0:TW], AF.Silu, scale=1.0 / 127.0)
            S.do("vector", "tensor_tensor", hb4[:, h, t], sg, ps_A[:, TW : 2 * TW],
                 op=ALU.mult)

    for d_ in range(nd_lim):
        ds = slice(d_ * 128, (d_ + 1) * 128)
        S.do("gpsimd", "dma_start", wd3, wdT[:, ds].rearrange("(c p) m -> p c m", p=128))
        for t in range(NT):
            tsl = slice(t * TW, (t + 1) * TW)
            for k in range(NH):
                S.do("tensor", "matmul", ps_B[:, 0:TW], wd3[:, k], hb4[:, k, t],
                     start=(k == 0), stop=(k == NH - 1))
            xsrc = xt[:, tsl].rearrange("(c p) t -> p c t", p=128)[:, d_]
            S.do("sync", "dma_start", yf, xsrc)
            if merge_partials:
                for src in (pa, pb):
                    S.do("sync", "dma_start", pfa,
                         src[:, tsl].rearrange("(c p) t -> p c t", p=128)[:, d_])
                    S.do("vector", "tensor_tensor", yf, yf, pfa, op=ALU.add)
            S.do("vector", "scalar_tensor_tensor", yf, ps_B[:, 0:TW],
                 0.5 / (127.0 * 127.0), yf, op0=ALU.mult, op1=ALU.add)
            dst = yt[:, tsl].rearrange("(c p) t -> p c t", p=128)[:, d_]
            S.do("sync", "dma_start", dst, yf)
    S.fence("sync")
    return nc


def ffn1_in_maps(xT, wgT, wuT, wdT, partials=None):
    base = {
        "wgT": wgT, "wuT": wuT, "wdT": wdT,
        "ones_col": np.ones((128, 1), np.float32),
        "ones_row": np.ones((1, 128), np.float32),
    }
    maps = []
    for c in range(8):
        m = dict(base)
        sl = slice(c * TPC, (c + 1) * TPC)
        m["xt"] = f32(xT[:, sl])
        if partials is not None:
            m["pa"] = f32(partials[0][:, sl])
            m["pb"] = f32(partials[1][:, sl])
        maps.append(m)
    return maps


def run_ffn1(xT, wgT, wuT, wdT, partials=None, nh_lim=24, nd_lim=8):
    nc = get_prog("ffn1", build_ffn1, partials is not None, nh_lim, nd_lim)
    maps = ffn1_in_maps(xT, wgT, wuT, wdT, partials=partials)
    res = run_bass_kernel_spmd(nc, maps, core_ids=list(range(8)))
    return np.concatenate([r["yt"] for r in res.results], axis=1)



MASKW = 1536
BIGNEG = -1.0e30


def make_mask_plane():
    k = np.arange(MASKW)[None, :]
    c = np.arange(128)[:, None]
    return np.where(k >= c + 512, 0.0, BIGNEG).astype(np.float32)


def build_attn0(half):
    KEYN = 2048 * (half + 1)
    Q0 = KEYN - 2048
    NH8 = 8
    HD = 64
    INV = 1.0 / np.sqrt(64.0)
    TW = 512
    NKT = KEYN // TW
    nc = mk_nc()
    S = Ser(nc)

    xt = nc.dram_tensor("xt", [D, KEYN], F32, kind="ExternalInput").ap()
    EXT = nc.dram_tensor("EXT", [NH8, 64, 2048], F32, kind="ExternalInput").ap()
    maskd = nc.dram_tensor("maskpl", [128, MASKW], F32, kind="ExternalInput").ap()
    yout = nc.dram_tensor("ypart", [D, 2048], F32, kind="ExternalOutput").ap()
    idd = nc.dram_tensor("ident", [128, 128], F32, kind="ExternalInput").ap()
    V2 = nc.dram_tensor("V2scratch", [KEYN * 4097], BF16, kind="Internal").ap()
    V2r = V2.rearrange("(r c) -> r c", c=4097)

    ps_A = nc.alloc_psum_tensor("ps_A", [128, 2 * TW], F32).ap()
    ps_S = nc.alloc_psum_tensor("ps_S", [128, TW], F32).ap()
    ps_O = nc.alloc_psum_tensor("ps_O", [128, TW], F32).ap()
    ps_sum = nc.alloc_psum_tensor("ps_sum", [1, TW], F32).ap()
    ps_n = ps_sum
    ps_T = nc.alloc_psum_tensor("ps_T", [128, 64], BF16).ap()

    sc = alloc_ln_scratch(S, nc, TW, ps_A, ps_stat=ps_A[0:1, :])
    ident = nc.alloc_sbuf_tensor("ident_sb", [128, 128], BF16).ap()
    S.do("gpsimd", "dma_start", ident, idd)

    lnball = nc.alloc_sbuf_tensor("lnball", [128, 8 * KEYN], BF16).ap()
    lnb4 = lnball.rearrange("p (c u t) -> p c u t", c=8, u=NKT)

    zt = lnball[:, 0:4097]
    S.do("vector", "memset", lnball[:, 0 : 2 * 4097], 0.0)
    fill_lo = max(0, Q0 - 1)
    for r0 in range(fill_lo, KEYN, 128):
        nr = min(128, KEYN - r0)
        S.do("sync", "dma_start", V2r[r0 : r0 + nr, :], zt[0:nr, :])

    with nc.sbuf_tensor("xf", [128, 8 * TW], F32) as xf_h:
        xf3 = xf_h.ap().rearrange("p (c t) -> p c t", t=TW)
        for t in range(NKT):
            S.do("sync", "dma_start", xf3,
                 xt[:, t * TW : (t + 1) * TW].rearrange("(c p) t -> p c t", p=128))
            sc["out3"] = lnb4[:, :, t]
            emit_ln_feature_major(S, nc, xf3, 8, TW, sc["ones_col"], sc, f"a0t{t}")
        sc["out3"] = None
    lnb3 = lnball.rearrange("p (c t) -> p c t", t=KEYN)

    qS = nc.alloc_sbuf_tensor("qS", [64, 2048], BF16).ap()
    qA = nc.alloc_sbuf_tensor("qA", [64, 2048], BF16).ap() if half == 1 else qS
    kh = nc.alloc_sbuf_tensor("kh", [64, KEYN], BF16).ap()
    vh = nc.alloc_sbuf_tensor("vh", [64, KEYN], BF16).ap()
    vT = nc.alloc_sbuf_tensor("vT", [128, (KEYN // 128) * 64], BF16).ap()
    vT3 = vT.rearrange("p (k v) -> p k v", v=64)
    exh = nc.alloc_sbuf_tensor("exh", [64, 2048], BF16).ap()
    asb = nc.alloc_sbuf_tensor("asb", [128, TW], BF16).ap()
    qn = nc.alloc_sbuf_tensor("qn", [1, 2050], F32).ap()
    qnA = nc.alloc_sbuf_tensor("qnA", [1, 2048], F32).ap()
    kstar = nc.alloc_sbuf_tensor("kstar", [1, 1], F32).ap()
    mtmp = nc.alloc_sbuf_tensor("mtmp", [1, 1], F32).ap()
    estar = nc.alloc_sbuf_tensor("estar", [1, 1], F32).ap()
    sqt = nc.alloc_sbuf_tensor("sqt", [64, TW], F32).ap()
    negB = nc.alloc_sbuf_tensor("negB", [1, TW], F32).ap()
    nB1 = nc.alloc_sbuf_tensor("nB1", [1, TW], F32).ap()
    SCt = nc.alloc_sbuf_tensor("SCt", [128, TW], BF16).ap()
    SRt0 = nc.alloc_sbuf_tensor("SRt0", [128, TW], BF16, is_transpose_dest=True).ap()
    SRt1 = nc.alloc_sbuf_tensor("SRt1", [128, TW], BF16, is_transpose_dest=True).ap()
    SRts = (SRt0, SRt1)
    expS0 = nc.alloc_sbuf_tensor("expS0", [128, TW], BF16).ap()
    expS1x = nc.alloc_sbuf_tensor("expS1x", [128, TW], BF16).ap()
    expSs = (expS0, expS1x)
    ssum = nc.alloc_sbuf_tensor("ssum", [1, TW], F32).ap()
    rsum = nc.alloc_sbuf_tensor("rsum", [1, TW], F32).ap()
    otmp = nc.alloc_sbuf_tensor("otmp", [64, TW], F32).ap()
    osum = nc.alloc_sbuf_tensor("osum", [64, TW], F32).ap()
    Oall = nc.alloc_sbuf_tensor("Oall", [128, 4 * 2048], BF16).ap()
    Oall4 = Oall.rearrange("p (h t) -> p h t", h=4)
    yf = nc.alloc_sbuf_tensor("yf_a", [128, TW], F32).ap()

    Wq_s = emit_load_weight_bf16(S, nc, "WqT_x", (1024, 512), 8)

    def project2(dst, Wsb, col0, tok0, ntok, scale=None):
        for tt in range(ntok // TW):
            a = tok0 + tt * TW
            for c in range(8):
                S.do("tensor", "matmul", ps_A[:64, 0:TW],
                     Wsb[:, c, col0 : col0 + 64],
                     lnb3[:, c, a : a + TW], start=(c == 0), stop=(c == 7))
            if scale is None:
                S.do("vector", "tensor_copy", dst[:, tt * TW : (tt + 1) * TW],
                     ps_A[:64, 0:TW])
            else:
                S.do("scalar", "activation", dst[:, tt * TW : (tt + 1) * TW],
                     ps_A[:64, 0:TW], AF.Copy, scale=scale)

    Wk_s = emit_load_weight_bf16(S, nc, "WkT_x", (1024, 512), 8)
    Wv_s = emit_load_weight_bf16(S, nc, "WvT_x", (1024, 512), 8)
    Wp_s = emit_load_weight_bf16(S, nc, "WpT_x", (512, 1024), 8)

    for h in range(NH8):
        c0 = h * 64
        project2(qS, Wq_s, c0, Q0, 2048, scale=INV)
        if half == 1:
            project2(qA, Wq_s, c0, Q0 - 1, 2048, scale=INV)
        project2(kh, Wk_s, c0, 0, KEYN)
        project2(vh, Wv_s, c0, 0, KEYN)
        for kc in range(KEYN // 128):
            S.do("tensor", "transpose", ps_T, vh[:, kc * 128 : (kc + 1) * 128],
                 ident[0:64, 0:64])
            S.do("vector", "tensor_copy", vT3[:, kc], ps_T)
        S.do("gpsimd", "dma_start", exh, EXT[h])
        S.do("vector", "memset", qn[:, 0:2], 0.0)
        for tt in range(4):
            S.do("scalar", "activation", sqt, qS[:, tt * TW : (tt + 1) * TW], AF.Square)
            S.do("tensor", "matmul", ps_n, sc["ones_col"][0:64], sqt,
                 start=True, stop=True)
            S.do("scalar", "activation", qn[:, 1 + tt * TW : 1 + (tt + 1) * TW],
                 ps_n, AF.Sqrt)
        qAn = qnA if half == 1 else qn[:, 0:2048]
        if half == 1:
            for tt in range(4):
                S.do("scalar", "activation", sqt, qA[:, tt * TW : (tt + 1) * TW],
                     AF.Square)
                S.do("tensor", "matmul", ps_n, sc["ones_col"][0:64], sqt,
                     start=True, stop=True)
                S.do("scalar", "activation", qnA[:, tt * TW : (tt + 1) * TW],
                     ps_n, AF.Sqrt)
        for which, src_t, ntt, dstmax in ((0, kh, KEYN // TW, kstar),
                                          (1, exh, 4, estar)):
            for tt in range(ntt):
                S.do("scalar", "activation", sqt, src_t[:, tt * TW : (tt + 1) * TW],
                     AF.Square)
                S.do("tensor", "matmul", ps_n, sc["ones_col"][0:64], sqt,
                     start=True, stop=True)
                S.do("scalar", "activation", rsum, ps_n, AF.Sqrt)
                if tt == 0:
                    S.do("vector", "tensor_reduce", dstmax, rsum, axis=AXL.X,
                         op=ALU.max)
                else:
                    S.do("vector", "tensor_reduce", mtmp, rsum, axis=AXL.X,
                         op=ALU.max)
                    S.do("vector", "tensor_tensor", dstmax, dstmax, mtmp, op=ALU.max)

        for a in range(16):
            i0 = a * 128
            gi0 = (Q0 - 1 if half == 1 else 0) + i0
            for jq in range(4):
                j0 = jq * TW
                need2 = (j0 + TW - 1) >= (2048 - (gi0 + 127) - 1)
                need1 = (gi0 + 127 >= 2048) and ((j0 + TW - 1) >= (4096 - (gi0 + 127) - 1))
                if not (need2 or need1):
                    continue
                S.do("tensor", "matmul", ps_A[:, 0:TW],
                     qA[:, i0 : i0 + 128], exh[:, j0 : j0 + TW],
                     start=True, stop=True)
                S.do("vector", "tensor_copy", asb, ps_A[:, 0:TW])
                if need2:
                    S.do("sync", "dma_start",
                         V2r[gi0 : gi0 + 128, 2049 + j0 : 2049 + j0 + TW], asb)
                if need1:
                    S.do("sync", "dma_start",
                         V2r[gi0 : gi0 + 128, 1 + j0 : 1 + j0 + TW], asb)

        v2done = S.val
        for t in range(4):
            R = Q0 + t * TW
            nkc = (R + TW) // 128
            S.do("vector", "tensor_scalar", out=negB,
                 in0=qn[:, 1 + t * TW : 1 + (t + 1) * TW], scalar1=kstar,
                 scalar2=None, op0=ALU.mult)
            S.do("vector", "tensor_scalar", out=nB1,
                 in0=qAn[:, t * TW : (t + 1) * TW], scalar1=estar,
                 scalar2=None, op0=ALU.mult)
            S.do("vector", "tensor_tensor", negB, negB, nB1, op=ALU.add)
            S.do("vector", "tensor_scalar", out=negB, in0=negB, scalar1=-1.0,
                 scalar2=None, op0=ALU.mult)
            tok0 = S.val
            ev_done = tok0
            sc_free = tok0
            srt_free = [tok0, tok0]
            av_par = [tok0, tok0]
            last_mask = tok0
            last_srel = tok0
            for kc in range(nkc):
                par = kc % 2
                off = R * 4096 + kc * 128
                src = bass_rust.AP(V2.tensor, off, [[4096, TW], [1, 128]])
                S.do("sync", "dma_start", SRts[par], src, transpose=True,
                     wait_val=None)
                dma_tok = S.val
                last_srel = S.val
                S.do("tensor", "matmul", ps_S, kh[:, kc * 128 : (kc + 1) * 128],
                     qS[:, t * TW : (t + 1) * TW], start=True, stop=False,
                     wait_val=ev_done)
                S.do("tensor", "matmul", ps_S, sc["ones_row"][:, 0:128], negB,
                     start=False, stop=True, wait_val=0)
                s_tok = S.val
                S.do("vector", "tensor_tensor", SCt, ps_S, SRts[par], op=ALU.add,
                     wait_val=max(s_tok, dma_tok, sc_free))
                ev_done = S.val
                srt_free[par] = S.val
                mask_tok = ev_done
                if 128 * kc + 127 > R:
                    dlt = 128 * kc - R
                    S.do("gpsimd", "dma_start", SCt,
                         maskd[:, 512 - dlt : 1024 - dlt], accum_op=ALU.add,
                         wait_val=None)
                    mask_tok = S.val
                    last_mask = S.val
                S.do("scalar", "activation", expSs[par], SCt, AF.Exp,
                     wait_val=max(mask_tok, av_par[par]))
                exp_tok = S.val
                sc_free = exp_tok
                S.do("tensor", "matmul", ps_sum, sc["ones_col_bf"], expSs[par],
                     start=(kc == 0), stop=(kc == nkc - 1), wait_val=exp_tok)
                g = kc % 2
                last = kc >= nkc - 2
                S.do("tensor", "matmul", ps_O[64 * g : 64 * g + 64, :],
                     vT3[:, kc], expSs[par], start=(kc == g), stop=last,
                     tile_position=(0, 64 * g), wait_val=0)
                av_par[par] = S.val
            S.do("vector", "tensor_copy", ssum, ps_sum)
            S.do("vector", "reciprocal", rsum, ssum)
            S.do("tensor", "matmul", ps_A[:64, 0:TW], sc["ones_row"][:, 0:64], rsum,
                 start=True, stop=True)
            S.do("vector", "tensor_copy", otmp, ps_O[64:128, :])
            S.do("vector", "tensor_tensor", osum, ps_O[0:64, :], otmp, op=ALU.add)
            hc, hp = h // 2, (h % 2) * 64
            S.do("vector", "tensor_tensor",
                 Oall4[hp : hp + 64, hc, t * TW : (t + 1) * TW],
                 osum, ps_A[:64, 0:TW], op=ALU.mult)

    for t in range(4):
        for d_ in range(8):
            for hc in range(4):
                S.do("tensor", "matmul", ps_A[:, 0:TW],
                     Wp_s[:, hc, d_ * 128 : (d_ + 1) * 128],
                     Oall4[:, hc, t * TW : (t + 1) * TW],
                     start=(hc == 0), stop=(hc == 3))
            S.do("vector", "tensor_copy", yf, ps_A[:, 0:TW])
            dst = yout[:, t * TW : (t + 1) * TW].rearrange("(c p) t -> p c t", p=128)[:, d_]
            S.do("sync", "dma_start", dst, yf)
    S.fence("sync")
    return nc


def attn0_in_maps(xT, Wq, Wk, Wv, Wp, rel_embed, half):
    KEYN = 2048 * (half + 1)
    rel = np.asarray(rel_embed, np.float32).reshape(2048, 16, 64)
    maps = []
    for c in range(8):
        b, o = c // 2, c % 2
        hsl = slice(o * 512, (o + 1) * 512)
        m = {
            "xt": f32(xT[:, b * SEQ : b * SEQ + KEYN]),
            "WqT_x": f32(Wq.T[:, hsl]),
            "WkT_x": f32(Wk.T[:, hsl]),
            "WvT_x": f32(Wv.T[:, hsl]),
            "WpT_x": f32(Wp.T[hsl, :]),
            "EXT": np.ascontiguousarray(
                rel[:, o * 8 : (o + 1) * 8, :].transpose(1, 2, 0)).astype(np.float32),
            "maskpl": make_mask_plane(),
            "ident": np.eye(128, dtype=np.float32),
            "ones_col": np.ones((128, 1), np.float32),
            "ones_row": np.ones((1, 128), np.float32),
        }
        maps.append(m)
    return maps


def run_attn0(xT, Wq, Wk, Wv, Wp, rel_embed, half):
    nc = get_prog("attn0", build_attn0, half)
    maps = attn0_in_maps(xT, Wq, Wk, Wv, Wp, rel_embed, half)
    res = run_bass_kernel_spmd(nc, maps, core_ids=list(range(8)))
    return [r["ypart"] for r in res.results]



EPS_RMS = 1e-6


def build_attn1(half):
    KEYN = 2048 * (half + 1)
    Q0 = KEYN - 2048
    INV96 = 1.0 / np.sqrt(96.0)
    TW = 512
    nc = mk_nc()
    S = Ser(nc)

    xt = nc.dram_tensor("xt", [D, SEQ], F32, kind="ExternalInput").ap()
    xtok = nc.dram_tensor("xtok", [SEQ, D], F32, kind="ExternalInput").ap()
    maskd = nc.dram_tensor("maskpl", [128, MASKW], F32, kind="ExternalInput").ap()
    idd = nc.dram_tensor("ident", [128, 128], F32, kind="ExternalInput").ap()
    stud = nc.dram_tensor("stu", [128, 128], F32, kind="ExternalInput").ap()
    yout = nc.dram_tensor("ypart", [D, 2048], F32, kind="ExternalOutput").ap()
    gqd = nc.dram_tensor("g_qnorm", [96, 1], F32, kind="ExternalInput").ap()
    gkd = nc.dram_tensor("g_kvnorm", [32, 1], F32, kind="ExternalInput").ap()
    wgated = nc.dram_tensor("w_gate", [1024, 3], F32, kind="ExternalInput").ap()
    wimpd = nc.dram_tensor("w_imp", [128, 8], F32, kind="ExternalInput").ap()
    dq8d = nc.dram_tensor("dq8", [96, 768], F32, kind="ExternalInput").ap()
    dkn8d = nc.dram_tensor("dkn8", [32, 256], F32, kind="ExternalInput").ap()
    dv8d = nc.dram_tensor("dv8", [32, 256], F32, kind="ExternalInput").ap()
    selk8d = nc.dram_tensor("selk8", [1024, 768], F32, kind="ExternalInput").ap()
    selv8d = nc.dram_tensor("selv8", [1024, 256], F32, kind="ExternalInput").ap()
    wink8d = nc.dram_tensor("wink8", [1024, 768], F32, kind="ExternalInput").ap()
    winv8d = nc.dram_tensor("winv8", [1024, 256], F32, kind="ExternalInput").ap()
    cosqd = nc.dram_tensor("cosq", [32, SEQ], F32, kind="ExternalInput").ap()
    sinqd = nc.dram_tensor("sinq", [32, SEQ], F32, kind="ExternalInput").ap()
    sscr = nc.dram_tensor("sscr", [SEQ], F32, kind="Internal").ap()
    idxbuf = nc.dram_tensor("idxbuf", [256, 1], U32, kind="Internal").ap()

    ps_A = nc.alloc_psum_tensor("ps_A", [128, 2 * TW], F32).ap()
    ps_S = nc.alloc_psum_tensor("ps_S", [128, TW], F32).ap()
    ps_S2 = nc.alloc_psum_tensor("ps_S2", [128, TW], F32).ap()
    ps_O = nc.alloc_psum_tensor("ps_O", [128, TW], F32).ap()
    ps_sum = nc.alloc_psum_tensor("ps_sum", [1, TW], F32).ap()
    ps_T = nc.alloc_psum_tensor("ps_T", [128, 128], BF16).ap()
    ps_Ss = (ps_S, ps_S2)

    sc = alloc_ln_scratch(S, nc, TW, ps_A, ps_stat=ps_A[0:1, :])
    reg_const(S, nc, EPS_RMS, tag="rmseps")
    ident = nc.alloc_sbuf_tensor("ident_sb", [128, 128], BF16).ap()
    stu = nc.alloc_sbuf_tensor("stu_sb", [128, 128], F32).ap()
    S.do("gpsimd", "dma_start", ident, idd)
    S.do("sync", "dma_start", stu, stud)

    cq_s = emit_load_weight_bf16(S, nc, "w_cq", (1024, 96), 8)
    ckv_s = emit_load_weight_bf16(S, nc, "w_ckv", (1024, 32), 8)
    krop_s = emit_load_weight_bf16(S, nc, "w_krope", (1024, 64), 8)
    gq = nc.alloc_sbuf_tensor("gq_sb", [96, 1], F32).ap()
    gk = nc.alloc_sbuf_tensor("gk_sb", [32, 1], F32).ap()
    wgate_s = nc.alloc_sbuf_tensor("wgate_sb", [128, 8 * 3], F32).ap()
    wimp_s = nc.alloc_sbuf_tensor("wimp_sb", [128, 8], F32).ap()
    S.do("sync", "dma_start", gq, gqd)
    S.do("sync", "dma_start", gk, gkd)
    S.do("sync", "dma_start", wgate_s.rearrange("p (c m) -> p c m", m=3),
         wgated.rearrange("(c p) m -> p c m", p=128))
    S.do("sync", "dma_start", wimp_s, wimpd)
    cosq = nc.alloc_sbuf_tensor("cosq_sb", [32, SEQ], BF16).ap()
    sinq = nc.alloc_sbuf_tensor("sinq_sb", [32, SEQ], BF16).ap()
    S.do("gpsimd", "dma_start", cosq, cosqd)
    S.do("gpsimd", "dma_start", sinq, sinqd)

    lnball = nc.alloc_sbuf_tensor("lnball", [128, 8 * SEQ], BF16).ap()
    lnb4 = lnball.rearrange("p (c u t) -> p c u t", c=8, u=SEQ // TW)
    lnb3 = lnball.rearrange("p (c t) -> p c t", t=SEQ)

    cwb = nc.alloc_sbuf_tensor("cwb", [1, 1], F32).ap()
    sX = nc.alloc_sbuf_tensor("sX", [128, 8], F32).ap()
    srow = nc.alloc_sbuf_tensor("srow", [1, TW], F32).ap()
    srow2 = nc.alloc_sbuf_tensor("srow2", [1, TW], F32).ap()
    g123 = nc.alloc_sbuf_tensor("g123", [1, 3], F32).ap()
    gtmp = nc.alloc_sbuf_tensor("gtmp", [1, 3], F32).ap()
    g1s = nc.alloc_sbuf_tensor("g1s", [1, 1], F32).ap()

    for c in range(8):
        S.do("tensor", "matmul", ps_A[0:1, 0:1], wimp_s[:, c : c + 1],
             sc["ones_col"], start=(c == 0), stop=(c == 7))
    S.do("vector", "tensor_copy", cwb, ps_A[0:1, 0:1])

    with nc.sbuf_tensor("xf", [128, 8 * TW], F32) as xf_h:
        xf3 = xf_h.ap().rearrange("p (c t) -> p c t", t=TW)
        for t in range(SEQ // TW):
            S.do("sync", "dma_start", xf3,
                 xt[:, t * TW : (t + 1) * TW].rearrange("(c p) t -> p c t", p=128))
            sc["out3"] = lnb4[:, :, t]
            emit_ln_feature_major(S, nc, xf3, 8, TW, sc["ones_col"], sc, f"a1t{t}")
            for c in range(8):
                S.do("tensor", "matmul", ps_A[0:1, TW : 2 * TW],
                     wimp_s[:, c : c + 1], xf3[:, c], start=(c == 0), stop=(c == 7))
            S.do("vector", "tensor_scalar", out=srow, in0=sc["m"], scalar1=cwb,
                 scalar2=None, op0=ALU.mult)
            S.do("vector", "tensor_tensor", srow2, ps_A[0:1, TW : 2 * TW], srow,
                 op=ALU.subtract)
            S.do("vector", "tensor_tensor", srow, srow2, sc["r"], op=ALU.mult)
            S.do("sync", "dma_start",
                 sscr[t * TW : (t + 1) * TW].rearrange("(a t) -> a t", a=1),
                 srow[0:1, :])
        sc["out3"] = None
    for c in range(8):
        S.do("vector", "reduce_sum", sX[:, c : c + 1], lnb3[:, c], axis=AXL.X)
    for c in range(8):
        S.do("tensor", "matmul", ps_A[0:1, 0:3], sX[:, c : c + 1],
             wgate_s.rearrange("p (c m) -> p c m", m=3)[:, c],
             start=(c == 0), stop=(c == 7))
    S.do("scalar", "activation", g123, ps_A[0:1, 0:3], AF.Copy, scale=1.0 / SEQ)
    S.do("vector", "tensor_reduce", g1s, g123, axis=AXL.X, op=ALU.max)
    S.do("vector", "tensor_scalar", out=g1s, in0=g1s, scalar1=-1.0, scalar2=None,
         op0=ALU.mult)
    S.do("scalar", "activation", gtmp, g123, AF.Exp, bias=g1s)
    S.do("vector", "reduce_sum", g1s, gtmp, axis=AXL.X)
    S.do("vector", "reciprocal", g1s, g1s)
    S.do("vector", "tensor_scalar", out=g123, in0=gtmp, scalar1=g1s, scalar2=None,
         op0=ALU.mult)

    s128 = nc.alloc_sbuf_tensor("s128", [128, 32], F32).ap()
    S.do("sync", "dma_start", s128, sscr.rearrange("(p f) -> p f", f=32))
    lo = nc.alloc_sbuf_tensor("lo_t", [1, 1], F32).ap()
    hi = nc.alloc_sbuf_tensor("hi_t", [1, 1], F32).ap()
    mid = nc.alloc_sbuf_tensor("mid_t", [1, 1], F32).ap()
    cnt = nc.alloc_sbuf_tensor("cnt_t", [1, 1], F32).ap()
    msk1 = nc.alloc_sbuf_tensor("msk1", [1, 1], F32).ap()
    btmp = nc.alloc_sbuf_tensor("btmp", [1, 1], F32).ap()
    btmp2 = nc.alloc_sbuf_tensor("btmp2", [1, 1], F32).ap()
    midb = nc.alloc_sbuf_tensor("midb", [128, 1], F32).ap()
    zt1 = nc.alloc_sbuf_tensor("zt1", [128, 32], F32).ap()
    zcnt = nc.alloc_sbuf_tensor("zcnt", [128, 1], F32).ap()
    absm = nc.alloc_sbuf_tensor("absm", [128, 1], F32).ap()
    idf = nc.alloc_sbuf_tensor("idf_sb", [128, 128], F32).ap()
    S.do("sync", "dma_start", idf, idd)
    S.do("vector", "tensor_reduce", absm, s128, axis=AXL.X, op=ALU.max,
         apply_absolute_value=True)
    S.do("tensor", "matmul", ps_A[0:1, 0:128], absm, idf, start=True, stop=True)
    S.do("vector", "tensor_copy", srow[:, 0:128], ps_A[0:1, 0:128])
    S.do("vector", "tensor_reduce", hi, srow[:, 0:128], axis=AXL.X, op=ALU.max)
    S.do("vector", "tensor_scalar", out=hi, in0=hi, scalar1=1.0, scalar2=None,
         op0=ALU.add)
    S.do("vector", "tensor_scalar", out=lo, in0=hi, scalar1=-1.0, scalar2=None,
         op0=ALU.mult)
    for it in range(42):
        S.do("vector", "tensor_tensor", mid, lo, hi, op=ALU.add)
        S.do("vector", "tensor_scalar", out=mid, in0=mid, scalar1=0.5,
             scalar2=None, op0=ALU.mult)
        S.do("tensor", "matmul", ps_A[:, 0:1], sc["ones_row"], mid,
             start=True, stop=True)
        S.do("vector", "tensor_copy", midb, ps_A[:, 0:1])
        S.do("vector", "tensor_scalar", out=zt1, in0=s128, scalar1=midb,
             scalar2=None, op0=ALU.is_gt)
        S.do("vector", "reduce_sum", zcnt, zt1, axis=AXL.X)
        S.do("tensor", "matmul", ps_A[0:1, 4:5], zcnt, sc["ones_col"],
             start=True, stop=True)
        S.do("vector", "tensor_copy", cnt, ps_A[0:1, 4:5])
        S.do("vector", "tensor_scalar", out=msk1, in0=cnt, scalar1=255.5,
             scalar2=None, op0=ALU.is_ge)
        S.do("vector", "tensor_tensor", btmp, mid, lo, op=ALU.subtract)
        S.do("vector", "tensor_scalar", out=btmp, in0=btmp, scalar1=msk1,
             scalar2=None, op0=ALU.mult)
        S.do("vector", "tensor_tensor", btmp2, lo, btmp, op=ALU.add)
        S.do("vector", "tensor_tensor", btmp, hi, mid, op=ALU.subtract)
        S.do("vector", "tensor_scalar", out=btmp, in0=btmp, scalar1=msk1,
             scalar2=None, op0=ALU.mult)
        S.do("vector", "tensor_tensor", hi, mid, btmp, op=ALU.add)
        S.do("vector", "tensor_copy", lo, btmp2)
    S.do("tensor", "matmul", ps_A[:, 0:1], sc["ones_row"], lo, start=True, stop=True)
    S.do("vector", "tensor_copy", midb, ps_A[:, 0:1])
    S.do("vector", "tensor_scalar", out=zt1, in0=s128, scalar1=midb,
         scalar2=None, op0=ALU.is_gt)
    pfa_ = nc.alloc_sbuf_tensor("pref_a", [128, 48], F32).ap()
    pfb_ = nc.alloc_sbuf_tensor("pref_b", [128, 48], F32).ap()
    S.do("vector", "memset", pfa_, 0.0)
    S.do("vector", "memset", pfb_, 0.0)
    S.do("vector", "tensor_copy", pfa_[:, 16:48], zt1)
    cur, nxt = pfa_, pfb_
    for k in (1, 2, 4, 8, 16):
        S.do("vector", "tensor_tensor", nxt[:, 16:48], cur[:, 16:48],
             cur[:, 16 - k : 48 - k], op=ALU.add)
        if k < 16:
            S.do("vector", "tensor_copy", nxt[:, 16 - k : 16], cur[:, 16 - k : 16])
        cur, nxt = nxt, cur
    S.do("tensor", "matmul", ps_A[:, 1:2], stu, cur[:, 47:48], start=True, stop=True)
    S.do("vector", "tensor_copy", zcnt, ps_A[:, 1:2])
    rnk = nc.alloc_sbuf_tensor("rnk", [128, 32], F32).ap()
    S.do("vector", "tensor_tensor", rnk, cur[:, 16:48], zt1, op=ALU.subtract)
    S.do("vector", "tensor_scalar", out=rnk, in0=rnk, scalar1=zcnt, scalar2=None,
         op0=ALU.add)
    S.do("vector", "tensor_scalar", out=rnk, in0=rnk, scalar1=9999.0, scalar2=None,
         op0=ALU.subtract)
    S.do("vector", "tensor_tensor", rnk, rnk, zt1, op=ALU.mult)
    S.do("vector", "tensor_scalar", out=rnk, in0=rnk, scalar1=9999.0, scalar2=None,
         op0=ALU.add)
    rnku = nc.alloc_sbuf_tensor("rnku", [128, 32], U32).ap()
    S.do("vector", "tensor_copy", rnku, rnk)
    ivu = nc.alloc_sbuf_tensor("ivu", [128, 32], U32).ap()
    S.do("gpsimd", "iota", ivu, pattern=[[1, 32]], base=0, channel_multiplier=32)
    for fcol in range(32):
        S.do("gpsimd", "indirect_dma_start",
             out=idxbuf,
             out_offset=bass_rust.IndirectOffsetOnAxis(
                 ap=rnku[:, fcol : fcol + 1], axis=0),
             in_=ivu[:, fcol : fcol + 1], in_offset=None,
             bounds_check=255, oob_is_err=False)
    idxsb = nc.alloc_sbuf_tensor("idxsb", [128, 2], U32).ap()
    S.do("sync", "dma_start", idxsb[:, 0:1], idxbuf[0:128])
    S.do("sync", "dma_start", idxsb[:, 1:2], idxbuf[128:256])
    selm = nc.alloc_sbuf_tensor("selm", [128, 1], F32).ap()
    selv_ = nc.alloc_sbuf_tensor("selv_", [128, 1], F32).ap()
    selr = nc.alloc_sbuf_tensor("selr", [128, 1], F32).ap()
    selTb = nc.alloc_sbuf_tensor("selTb", [128, 8 * 256], BF16).ap()
    selT3 = selTb.rearrange("p (c s) -> p c s", s=256)
    with nc.sbuf_tensor("selx", [128, D], F32) as selx_h, \
         nc.sbuf_tensor("sqsel", [128, D], F32) as sqsel_h, \
         nc.sbuf_tensor("selb", [128, D], BF16) as selb_h:
        selx, sqsel, selb = selx_h.ap(), sqsel_h.ap(), selb_h.ap()
        for g in range(2):
            S.do("gpsimd", "indirect_dma_start",
                 out=selx, out_offset=None, in_=xtok,
                 in_offset=bass_rust.IndirectOffsetOnAxis(
                     ap=idxsb[:, g : g + 1], axis=0),
                 bounds_check=SEQ - 1, oob_is_err=False)
            S.do("scalar", "activation", sqsel, selx, AF.Square)
            S.do("vector", "reduce_sum", selm, selx, axis=AXL.X)
            S.do("vector", "reduce_sum", selv_, sqsel, axis=AXL.X)
            S.do("vector", "tensor_scalar", out=selm, in0=selm, scalar1=1.0 / D,
                 scalar2=None, op0=ALU.mult)
            S.do("vector", "tensor_tensor", selr, selm, selm, op=ALU.mult)
            S.do("vector", "scalar_tensor_tensor", selv_, selv_, 1.0 / D, selr,
                 op0=ALU.mult, op1=ALU.subtract)
            S.do("scalar", "activation", selv_, selv_, AF.Ln, bias=EPS_LN)
            S.do("scalar", "activation", selr, selv_, AF.Exp, scale=-0.5)
            S.do("vector", "tensor_scalar", out=selb, in0=selx, scalar1=selm,
                 scalar2=selr, op0=ALU.subtract, op1=ALU.mult)
            for c in range(8):
                S.do("tensor", "transpose", ps_T,
                     selb[:, c * 128 : (c + 1) * 128], ident)
                S.do("vector", "tensor_copy", selT3[:, c, g * 128 : (g + 1) * 128],
                     ps_T)

    ckvb = nc.alloc_sbuf_tensor("ckvb", [32, KEYN], BF16).ap()
    rmsr = nc.alloc_sbuf_tensor("rmsr", [1, TW], F32).ap()
    nqb = nc.alloc_sbuf_tensor("nqb", [96, 2048], BF16).ap()
    rawb = nc.alloc_sbuf_tensor("rawb", [96, SEQ], BF16).ap()

    sqw = nc.alloc_sbuf_tensor("sqw", [96, TW], F32).ap()

    def rms_project(dst, Wsb, ncols, ntile, tok0, gvec, extra_scale=None):
        for tt in range(ntile):
            a = tok0 + tt * TW
            for c in range(8):
                S.do("tensor", "matmul", ps_A[:ncols, 0:TW], Wsb[:, c, 0:ncols],
                     lnb3[:, c, a : a + TW], start=(c == 0), stop=(c == 7))
            S.do("scalar", "activation", sqw[:ncols, :], ps_A[:ncols, 0:TW],
                 AF.Square)
            S.do("tensor", "matmul", ps_sum, sc["ones_col"][0:ncols],
                 sqw[:ncols, :], start=True, stop=True)
            S.do("scalar", "activation", rmsr, ps_sum, AF.Ln, bias=EPS_RMS,
                 scale=1.0 / ncols)
            S.do("scalar", "activation", rmsr, rmsr, AF.Exp, scale=-0.5)
            S.do("tensor", "matmul", ps_A[:ncols, TW : 2 * TW],
                 sc["ones_row"][:, 0:ncols], rmsr, start=True, stop=True)
            S.do("vector", "tensor_copy", sqw[:ncols, :],
                 ps_A[:ncols, TW : 2 * TW])
            S.do("vector", "tensor_tensor", sqw[:ncols, :], ps_A[:ncols, 0:TW],
                 sqw[:ncols, :], op=ALU.mult)
            if extra_scale is not None:
                S.do("vector", "tensor_scalar", out=sqw[:ncols, :],
                     in0=sqw[:ncols, :], scalar1=extra_scale, scalar2=None,
                     op0=ALU.mult)
            S.do("vector", "tensor_scalar", out=dst[:, tt * TW : (tt + 1) * TW],
                 in0=sqw[:ncols, :], scalar1=gvec, scalar2=None, op0=ALU.mult)

    rms_project(nqb, cq_s, 96, 4, Q0, gq)
    rms_project(ckvb, ckv_s, 32, KEYN // TW, 0, gk)

    def rope_emit(dst, src, n32, cos_ap, sin_ap, tok0, ntok):
        for a0 in range(0, ntok, 1024):
            w_ = min(1024, ntok - a0)
            lo_s = src[n32 : n32 + 32, a0 : a0 + w_]
            hi_s = src[n32 + 32 : n32 + 64, a0 : a0 + w_]
            lo_d = dst[n32 : n32 + 32, a0 : a0 + w_]
            hi_d = dst[n32 + 32 : n32 + 64, a0 : a0 + w_]
            ct = cos_ap[:, tok0 + a0 : tok0 + a0 + w_]
            st = sin_ap[:, tok0 + a0 : tok0 + a0 + w_]
            l0 = ropes1[:, 0:w_]
            h0_ = ropes2[:, 0:w_]
            r1 = ropet1[:, 0:w_]
            r2 = ropet2[:, 0:w_]
            S.do("vector", "tensor_copy", l0, lo_s)
            S.do("vector", "tensor_copy", h0_, hi_s)
            S.do("vector", "tensor_tensor", r1, l0, ct, op=ALU.mult)
            S.do("vector", "tensor_tensor", r2, h0_, st, op=ALU.mult)
            S.do("vector", "tensor_tensor", lo_d, r1, r2, op=ALU.subtract)
            S.do("vector", "tensor_tensor", r1, l0, st, op=ALU.mult)
            S.do("vector", "tensor_tensor", r2, h0_, ct, op=ALU.mult)
            S.do("vector", "tensor_tensor", hi_d, r1, r2, op=ALU.add)

    ropet1 = nc.alloc_sbuf_tensor("ropet1", [32, 1024], BF16).ap()
    ropet2 = nc.alloc_sbuf_tensor("ropet2", [32, 1024], BF16).ap()
    ropes1 = nc.alloc_sbuf_tensor("ropes1", [32, 1024], BF16).ap()
    ropes2 = nc.alloc_sbuf_tensor("ropes2", [32, 1024], BF16).ap()

    k1h = nc.alloc_sbuf_tensor("k1h", [96, KEYN], BF16).ap()
    for tt in range(KEYN // TW):
        for c in range(8):
            S.do("tensor", "matmul", ps_A[:64, 0:TW], krop_s[:, c, :],
                 lnb3[:, c, tt * TW : (tt + 1) * TW], start=(c == 0), stop=(c == 7))
        S.do("scalar", "activation", rawb[32:64, tt * TW : (tt + 1) * TW],
             ps_A[0:32, 0:TW], AF.Copy, scale=1.0 / 16.0)
        S.do("scalar", "activation", rawb[64:96, tt * TW : (tt + 1) * TW],
             ps_A[32:64, 0:TW], AF.Copy, scale=1.0 / 16.0)
    rope_emit(k1h, rawb, 32, cosq, sinq, 0, KEYN)

    dq_s = nc.alloc_sbuf_tensor("dq_s", [96, 96], BF16).ap()
    dkn_s = nc.alloc_sbuf_tensor("dkn_s", [32, 32], BF16).ap()
    dv_s = nc.alloc_sbuf_tensor("dv_s", [32, 32], BF16).ap()
    selk_s = nc.alloc_sbuf_tensor("selk_s", [128, 8 * 96], BF16).ap()
    selk3 = selk_s.rearrange("p (c m) -> p c m", m=96)
    selv_s = nc.alloc_sbuf_tensor("selv_s", [128, 8 * 32], BF16).ap()
    selv3 = selv_s.rearrange("p (c m) -> p c m", m=32)
    wink_s = nc.alloc_sbuf_tensor("wink_s", [128, 8 * 96], BF16).ap()
    wink3 = wink_s.rearrange("p (c m) -> p c m", m=96)
    winv_s = nc.alloc_sbuf_tensor("winv_s", [128, 8 * 32], BF16).ap()
    winv3 = winv_s.rearrange("p (c m) -> p c m", m=32)
    wproj_s = emit_load_weight_bf16(S, nc, "wproj8", (256, 1024), 2)

    qt = nc.alloc_sbuf_tensor("qt", [96, 2048], BF16).ap()
    vtmp = nc.alloc_sbuf_tensor("vtmp", [32, TW], BF16).ap()
    v1T = nc.alloc_sbuf_tensor("v1T", [128, (KEYN // 128) * 32], BF16).ap()
    v1T3 = v1T.rearrange("p (k v) -> p k v", v=32)
    kwh = nc.alloc_sbuf_tensor("kwh", [96, KEYN], BF16).ap()
    vwT = nc.alloc_sbuf_tensor("vwT", [128, (KEYN // 128) * 32], BF16).ap()
    vwT3 = vwT.rearrange("p (k v) -> p k v", v=32)
    ksh = nc.alloc_sbuf_tensor("ksh", [96, 256], BF16).ap()
    vsT = nc.alloc_sbuf_tensor("vsT", [128, 2 * 32], BF16).ap()
    vsT3 = vsT.rearrange("p (k v) -> p k v", v=32)
    qnq = nc.alloc_sbuf_tensor("qnq", [1, TW], F32).ap()
    knsq = nc.alloc_sbuf_tensor("knsq", [1, TW], F32).ap()
    k1st = nc.alloc_sbuf_tensor("k1st", [1, 1], F32).ap()
    k2st = nc.alloc_sbuf_tensor("k2st", [1, 1], F32).ap()
    k3st = nc.alloc_sbuf_tensor("k3st", [1, 1], F32).ap()
    mtmp = nc.alloc_sbuf_tensor("mtmp1", [1, 1], F32).ap()
    negB = nc.alloc_sbuf_tensor("negB1", [1, TW], F32).ap()
    SCt = nc.alloc_sbuf_tensor("SCt1", [128, TW], BF16).ap()
    expSa = nc.alloc_sbuf_tensor("expS1", [128, TW], BF16).ap()
    expSb = nc.alloc_sbuf_tensor("expS1b", [128, TW], BF16).ap()
    expSs = (expSa, expSb)
    ssum = nc.alloc_sbuf_tensor("ssum1", [1, TW], F32).ap()
    rsum = nc.alloc_sbuf_tensor("rsum1", [1, TW], F32).ap()
    otmp = nc.alloc_sbuf_tensor("otmp1", [32, TW], F32).ap()
    oacc = nc.alloc_sbuf_tensor("oacc1", [32, TW], F32).ap()
    ostg = nc.alloc_sbuf_tensor("ostg1", [32, TW], BF16).ap()
    Oall = nc.alloc_sbuf_tensor("Oall1", [128, 2 * 2048], BF16).ap()
    Oall4 = Oall.rearrange("p (h t) -> p h t", h=2)
    yf = nc.alloc_sbuf_tensor("yf_a1", [128, TW], F32).ap()

    def norm_max(src_t, npart, ntt, dstmax, width=TW):
        for tt in range(ntt):
            S.do("scalar", "activation", sqw[:npart, 0:width],
                 src_t[:, tt * width : (tt + 1) * width], AF.Square)
            S.do("tensor", "matmul", ps_sum[:, 0:width], sc["ones_col"][0:npart],
                 sqw[:npart, 0:width], start=True, stop=True)
            S.do("scalar", "activation", knsq[:, 0:width], ps_sum[:, 0:width],
                 AF.Sqrt)
            if tt == 0:
                S.do("vector", "tensor_reduce", dstmax, knsq[:, 0:width],
                     axis=AXL.X, op=ALU.max)
            else:
                S.do("vector", "tensor_reduce", mtmp, knsq[:, 0:width],
                     axis=AXL.X, op=ALU.max)
                S.do("vector", "tensor_tensor", dstmax, dstmax, mtmp, op=ALU.max)

    for h in range(8):
        S.do("gpsimd", "dma_start", dq_s, dq8d[:, h * 96 : (h + 1) * 96])
        S.do("gpsimd", "dma_start", dkn_s, dkn8d[:, h * 32 : (h + 1) * 32])
        S.do("gpsimd", "dma_start", dv_s, dv8d[:, h * 32 : (h + 1) * 32])
        S.do("gpsimd", "dma_start", selk3,
             selk8d[:, h * 96 : (h + 1) * 96].rearrange("(c p) m -> p c m", p=128))
        S.do("gpsimd", "dma_start", selv3,
             selv8d[:, h * 32 : (h + 1) * 32].rearrange("(c p) m -> p c m", p=128))
        S.do("gpsimd", "dma_start", wink3,
             wink8d[:, h * 96 : (h + 1) * 96].rearrange("(c p) m -> p c m", p=128))
        S.do("gpsimd", "dma_start", winv3,
             winv8d[:, h * 32 : (h + 1) * 32].rearrange("(c p) m -> p c m", p=128))

        for tt in range(KEYN // TW):
            a = tt * TW
            S.do("tensor", "matmul", ps_A[:32, 0:TW], dkn_s, ckvb[:, a : a + TW],
                 start=True, stop=True)
            S.do("vector", "tensor_copy", k1h[0:32, a : a + TW], ps_A[:32, 0:TW])
            S.do("tensor", "matmul", ps_A[:32, TW : 2 * TW], dv_s,
                 ckvb[:, a : a + TW], start=True, stop=True)
            S.do("vector", "tensor_copy", vtmp, ps_A[:32, TW : 2 * TW])
            for j in range(4):
                S.do("tensor", "transpose", ps_T[:, 0:32],
                     vtmp[:, j * 128 : (j + 1) * 128], ident[0:32, 0:32])
                S.do("vector", "tensor_copy", v1T3[:, tt * 4 + j], ps_T[:, 0:32])
        for tt in range(KEYN // TW):
            a = tt * TW
            for c in range(8):
                S.do("tensor", "matmul", ps_A[:96, 0:TW], wink3[:, c],
                     lnb3[:, c, a : a + TW], start=(c == 0), stop=(c == 7))
            S.do("vector", "tensor_copy", rawb[:, a : a + TW], ps_A[:96, 0:TW])
            for c in range(8):
                S.do("tensor", "matmul", ps_A[:32, TW : 2 * TW], winv3[:, c],
                     lnb3[:, c, a : a + TW], start=(c == 0), stop=(c == 7))
            S.do("vector", "tensor_copy", vtmp, ps_A[:32, TW : 2 * TW])
            for j in range(4):
                S.do("tensor", "transpose", ps_T[:, 0:32],
                     vtmp[:, j * 128 : (j + 1) * 128], ident[0:32, 0:32])
                S.do("vector", "tensor_copy", vwT3[:, tt * 4 + j], ps_T[:, 0:32])
        S.do("vector", "tensor_copy", kwh[0:32, :], rawb[0:32, 0:KEYN])
        rope_emit(kwh, rawb[:, 0:KEYN], 32, cosq, sinq, 0, KEYN)
        for c in range(8):
            S.do("tensor", "matmul", ps_A[:96, 0:256], selk3[:, c],
                 selT3[:, c], start=(c == 0), stop=(c == 7))
        S.do("vector", "tensor_copy", rawb[:, 0:256], ps_A[:96, 0:256])
        S.do("vector", "tensor_copy", ksh[0:32, :], rawb[0:32, 0:256])
        rope_emit(ksh, rawb[:, 0:256], 32, cosq, sinq, 0, 256)
        for c in range(8):
            S.do("tensor", "matmul", ps_A[:32, 256:512], selv3[:, c],
                 selT3[:, c], start=(c == 0), stop=(c == 7))
        S.do("vector", "tensor_copy", rawb[0:32, 512 : 512 + 256],
             ps_A[:32, 256:512])
        for g in range(2):
            S.do("tensor", "transpose", ps_T[:, 0:32],
                 rawb[0:32, 512 + g * 128 : 512 + (g + 1) * 128],
                 ident[0:32, 0:32])
            S.do("vector", "tensor_copy", vsT3[:, g], ps_T[:, 0:32])
        for tt in range(4):
            a = tt * TW
            S.do("tensor", "matmul", ps_A[:96, 0:TW], dq_s, nqb[:, a : a + TW],
                 start=True, stop=True)
            S.do("scalar", "activation", rawb[:, a : a + TW], ps_A[:96, 0:TW],
                 AF.Copy, scale=INV96)
        S.do("vector", "tensor_copy", qt[0:32, :], rawb[0:32, 0:2048])
        rope_emit(qt, rawb[:, 0:2048], 32, cosq, sinq, Q0, 2048)
        norm_max(k1h, 96, KEYN // TW, k1st)
        norm_max(ksh, 96, 1, k2st, width=256)
        norm_max(kwh, 96, KEYN // TW, k3st)

        for t in range(4):
            R = Q0 + t * TW
            nkc = (R + TW) // 128
            S.do("scalar", "activation", sqw, qt[:, t * TW : (t + 1) * TW],
                 AF.Square)
            S.do("tensor", "matmul", ps_sum, sc["ones_col"][0:96], sqw,
                 start=True, stop=True)
            S.do("scalar", "activation", qnq, ps_sum, AF.Sqrt)
            S.do("vector", "memset", oacc, 0.0)
            for br in range(3):
                kst = (k1st, k2st, k3st)[br]
                S.do("vector", "tensor_scalar", out=negB,
                     in0=qnq, scalar1=kst,
                     scalar2=None, op0=ALU.mult)
                S.do("vector", "tensor_scalar", out=negB, in0=negB, scalar1=-1.0,
                     scalar2=None, op0=ALU.mult)
                nkc_b = 2 if br == 1 else nkc
                ksrc = (k1h, ksh, kwh)[br]
                vsrc = (v1T3, vsT3, vwT3)[br]
                tok0 = S.val
                s_free = [tok0, tok0]
                av_par = [tok0, tok0]
                sc_free = tok0
                for kc in range(nkc_b):
                    ksl = slice(kc * 128, (kc + 1) * 128)
                    par = kc % 2
                    psS = ps_Ss[par]
                    S.do("tensor", "matmul", psS, ksrc[:, ksl],
                         qt[:, t * TW : (t + 1) * TW], start=True, stop=False,
                         wait_val=s_free[par])
                    S.do("tensor", "matmul", psS, sc["ones_row"][:, 0:128], negB,
                         start=False, stop=True, wait_val=0)
                    s_tok = S.val
                    if br != 1 and 128 * kc + 127 > R:
                        dlt = 128 * kc - R
                        S.do("vector", "tensor_copy", SCt, psS,
                             wait_val=max(s_tok, sc_free))
                        s_free[par] = S.val
                        S.do("gpsimd", "dma_start", SCt,
                             maskd[:, 512 - dlt : 1024 - dlt], accum_op=ALU.add,
                             wait_val=S.val)
                        S.do("scalar", "activation", expSs[par], SCt, AF.Exp,
                             wait_val=max(S.val, av_par[par]))
                        sc_free = S.val
                    else:
                        S.do("scalar", "activation", expSs[par], psS, AF.Exp,
                             wait_val=max(s_tok, av_par[par]))
                        s_free[par] = S.val
                    exp_tok = S.val
                    S.do("tensor", "matmul", ps_sum, sc["ones_col_bf"],
                         expSs[par], start=(kc == 0), stop=(kc == nkc_b - 1),
                         wait_val=exp_tok)
                    g = kc % 2
                    S.do("tensor", "matmul", ps_O[32 * g : 32 * g + 32, :],
                         vsrc[:, kc], expSs[par], start=(kc == g),
                         stop=(kc >= nkc_b - 2), tile_position=(0, 32 * g),
                         wait_val=0)
                    av_par[par] = S.val
                S.do("vector", "tensor_copy", ssum, ps_sum)
                S.do("vector", "reciprocal", rsum, ssum)
                S.do("vector", "tensor_scalar", out=rsum, in0=rsum,
                     scalar1=g123[:, br : br + 1], scalar2=None, op0=ALU.mult)
                S.do("tensor", "matmul", ps_A[:32, 0:TW], sc["ones_row"][:, 0:32],
                     rsum, start=True, stop=True)
                if nkc_b >= 2:
                    S.do("vector", "tensor_copy", otmp, ps_O[32:64, :])
                    S.do("vector", "tensor_tensor", otmp, ps_O[0:32, :], otmp,
                         op=ALU.add)
                else:
                    S.do("vector", "tensor_copy", otmp, ps_O[0:32, :])
                S.do("vector", "tensor_tensor", otmp, otmp, ps_A[:32, 0:TW],
                     op=ALU.mult)
                S.do("vector", "tensor_tensor", oacc, oacc, otmp, op=ALU.add)
            hc, hb = h // 4, 32 * (h % 4)
            if hb == 96:
                S.do("vector", "tensor_copy", ostg, oacc)
                S.do("sync", "dma_start",
                     Oall4[96:128, hc, t * TW : (t + 1) * TW], ostg)
            else:
                S.do("vector", "tensor_copy",
                     Oall4[hb : hb + 32, hc, t * TW : (t + 1) * TW], oacc)

    for t in range(4):
        for d_ in range(8):
            for hc in range(2):
                S.do("tensor", "matmul", ps_A[:, 0:TW],
                     wproj_s[:, hc, d_ * 128 : (d_ + 1) * 128],
                     Oall4[:, hc, t * TW : (t + 1) * TW],
                     start=(hc == 0), stop=(hc == 1))
            S.do("vector", "tensor_copy", yf, ps_A[:, 0:TW])
            dst = yout[:, t * TW : (t + 1) * TW].rearrange("(c p) t -> p c t", p=128)[:, d_]
            S.do("sync", "dma_start", dst, yf)
    S.fence("sync")
    return nc


def rope_tables():
    f = 1.0 / (10000.0 ** (np.arange(0, 64, 2, dtype=np.float32) / 64.0))
    t = np.arange(SEQ, dtype=np.float32)
    a = np.outer(t, f)
    return np.ascontiguousarray(np.cos(a).T), np.ascontiguousarray(np.sin(a).T)


def attn1_in_maps(inputs, xT, xtokfull, half):
    cosq, sinq = rope_tables()
    g = lambda k: np.asarray(inputs["L1_attn_" + k], np.float32)
    w_dqn, w_dqr = g("w_dq_nope"), g("w_dq_rope")
    w_dkn, w_dv = g("w_dk_nope"), g("w_dv")
    w_selk, w_selv = g("w_selk"), g("w_selv")
    w_wink, w_winv = g("w_wink"), g("w_winv")
    w_proj = g("w_proj")

    def perm96(W, h0):
        raise RuntimeError

    def headblock_k(Wn, Wr, h0):
        cols = []
        for h in range(h0, h0 + 8):
            n = Wn[:, h * 32 : (h + 1) * 32]
            r = Wr[:, h * 64 : (h + 1) * 64]
            cols += [n, r[:, 0:32], r[:, 32:64]]
        return np.ascontiguousarray(np.concatenate(cols, axis=1))

    def headblock_kr(Wkr, h0):
        cols = []
        for h in range(h0, h0 + 8):
            blk = Wkr[:, h * 96 : (h + 1) * 96]
            cols += [blk[:, 0:32], blk[:, 32:64], blk[:, 64:96]]
        return np.ascontiguousarray(np.concatenate(cols, axis=1))

    def headblock_v(Wv, h0):
        return np.ascontiguousarray(Wv[:, h0 * 32 : (h0 + 8) * 32])

    maps = []
    for c in range(8):
        b, o = c // 2, c % 2
        h0 = o * 8
        wp8 = np.zeros((256, 1024), np.float32)
        for hp in range(8):
            src = w_proj[(h0 + hp) * 32 : (h0 + hp) * 32 + 32, :]
            wp8[(hp // 4) * 128 + (hp % 4) * 32 : (hp // 4) * 128 + (hp % 4) * 32 + 32] = src
        m = {
            "xt": f32(xT[:, b * SEQ : (b + 1) * SEQ]),
            "xtok": f32(xtokfull[b * SEQ : (b + 1) * SEQ, :]),
            "maskpl": make_mask_plane(),
            "ident": np.eye(128, dtype=np.float32),
            "stu": np.triu(np.ones((128, 128), np.float32), 1),
            "g_qnorm": g("g_qnorm").reshape(96, 1),
            "g_kvnorm": g("g_kvnorm").reshape(32, 1),
            "w_gate": g("w_gate"),
            "w_imp": f32(as_pcols(g("w_imp")[:, 0])),
            "w_cq": g("w_cq"),
            "w_ckv": g("w_ckv"),
            "w_krope": g("w_krope"),
            "dq8": headblock_k(w_dqn, w_dqr, h0),
            "dkn8": headblock_v(w_dkn, o * 8),
            "dv8": headblock_v(w_dv, o * 8),
            "selk8": headblock_kr(w_selk, h0),
            "selv8": headblock_v(w_selv, o * 8),
            "wink8": headblock_kr(w_wink, h0),
            "winv8": headblock_v(w_winv, o * 8),
            "wproj8": wp8,
            "cosq": cosq, "sinq": sinq,
            "ones_col": np.ones((128, 1), np.float32),
            "ones_row": np.ones((1, 128), np.float32),
        }
        maps.append(m)
    return maps


def run_attn1(inputs, xT, half):
    nc = get_prog("attn1", build_attn1, half)
    xtok = np.ascontiguousarray(xT.T)
    maps = attn1_in_maps(inputs, xT, xtok, half)
    res = run_bass_kernel_spmd(nc, maps, core_ids=list(range(8)))
    return [r["ypart"] for r in res.results]





_PROG_CACHE = {}


def get_prog(name, builder, *args):
    key = (name, args)
    if key not in _PROG_CACHE:
        _PROG_CACHE[key] = builder(*args)
    return _PROG_CACHE[key]


def run_ffn0(xT, w, partials=None, bp=None):
    nc = get_prog("ffn0", build_ffn0, partials is not None)
    maps = ffn0_in_maps(xT, w["U1"], w["V1"], w["b1"], w["U2"], w["V2"], w["b2"],
                        partials=partials, bp=bp)
    res = run_bass_kernel_spmd(nc, maps, core_ids=list(range(8)))
    out = np.concatenate([r["yt"] for r in res.results], axis=1)
    return out


def assemble_partials(plist0, plist1):
    PA = np.zeros((D, NTOK), np.float32)
    PB = np.zeros((D, NTOK), np.float32)
    for c in range(8):
        b, half = c // 2, c % 2
        src = plist0 if half == 0 else plist1
        PA[:, c * TPC : (c + 1) * TPC] = src[2 * b]
        PB[:, c * TPC : (c + 1) * TPC] = src[2 * b + 1]
    return PA, PB


def kernel(**inputs):
    x = np.asarray(inputs["L0_attn_x"], np.float32).reshape(NTOK, D)
    xT = np.ascontiguousarray(x.T)
    w0 = {k: np.asarray(inputs["L0_ffn_" + k]) for k in
          ("U1", "V1", "b1", "U2", "V2", "b2")}
    xT = run_ffn0(xT, w0)
    Wq = np.asarray(inputs["L0_attn_Wq"])
    Wk = np.asarray(inputs["L0_attn_Wk"])
    Wv = np.asarray(inputs["L0_attn_Wv"])
    Wp = np.asarray(inputs["L0_attn_Wp"])
    bp = np.asarray(inputs["L0_attn_bp"])
    rel = np.asarray(inputs["L0_attn_rel_embed"])
    p0 = run_attn0(xT, Wq, Wk, Wv, Wp, rel, 0)
    p1 = run_attn0(xT, Wq, Wk, Wv, Wp, rel, 1)
    PA, PB = assemble_partials(p0, p1)
    xT = run_ffn0(xT, w0, partials=(PA, PB), bp=bp)
    wgT = np.ascontiguousarray(np.asarray(inputs["L1_ffn_w_gate"]).T)
    wuT = np.ascontiguousarray(np.asarray(inputs["L1_ffn_w_up"]).T)
    wdT = np.ascontiguousarray(np.asarray(inputs["L1_ffn_w_down"]).T)
    xT = run_ffn1(xT, wgT, wuT, wdT)
    q0 = run_attn1(inputs, xT, 0)
    q1 = run_attn1(inputs, xT, 1)
    QA, QB = assemble_partials(q0, q1)
    xT = run_ffn1(xT, wgT, wuT, wdT, partials=(QA, QB))
    out = np.ascontiguousarray(xT.T).reshape(4, 4096, 1024)
    return out


# version 07 (final)
# speedup vs baseline: 1.1353x; 1.0248x over previous
import numpy as np
import ml_dtypes
import concourse.bass as bass
import concourse.mybir as mybir
import bass_rust
from concourse.bass_utils import run_bass_kernel_spmd

F32 = mybir.dt.float32
BF16 = mybir.dt.bfloat16
I32 = mybir.dt.int32
U32 = mybir.dt.uint32
AF = mybir.ActivationFunctionType
ALU = mybir.AluOpType
AXL = mybir.AxisListType
BF = ml_dtypes.bfloat16

D = 1024
NTOK = 16384
TPC = 2048
NB = 4
SEQ = 4096

EPS_LN = 1e-5


class Ser:

    def __init__(self, nc):
        self.nc = nc
        self.sem = nc.alloc_semaphore()
        self.val = 0
        self.last_eng = None

    def _pre(self, eng_name):
        eng = getattr(self.nc, eng_name)
        if not (eng_name == "tensor" and self.last_eng == "tensor"):
            eng.wait_ge(self.sem, self.val)
        self.last_eng = eng_name
        return eng

    def do(self, eng_name, meth, *args, inc=None, wait_val=None, **kwargs):
        if wait_val is None:
            eng = self._pre(eng_name)
        else:
            eng = getattr(self.nc, eng_name)
            eng.wait_ge(self.sem, min(wait_val, self.val))
            self.last_eng = None
        inst = getattr(eng, meth)(*args, **kwargs)
        if inc is None:
            inc = 16 if "dma" in meth else 1
        inst.then_inc(self.sem, inc)
        self.val += inc
        return inst

    def fence(self, eng_name="sync"):
        eng = getattr(self.nc, eng_name)
        eng.wait_ge(self.sem, self.val)
        self.last_eng = None


def mk_nc():
    return bass.Bass("TRN2", target_bir_lowering=False)


def bf(x):
    return np.ascontiguousarray(x).astype(BF)


def f32(x):
    return np.ascontiguousarray(x, dtype=np.float32)


def as_pcols(v):
    v = np.asarray(v)
    return np.ascontiguousarray(v.reshape(-1, 128).T)



def reg_const(S, nc, val, dtype=F32, tag=None):
    tag = tag if tag is not None else str(abs(hash((float(val), str(dtype)))) % 10**8)
    t = nc.alloc_sbuf_tensor(f"const_{tag}", [128, 1], dtype).ap()
    S.do("vector", "memset", t, float(val))
    nc.const_aps.register(dtype, float(val), t)
    return t


def emit_load_weight_bf16(S, nc, name, arr_shape, nchunk_free):
    k, rem = divmod(arr_shape[0], 128)
    assert rem == 0
    m = arr_shape[1]
    d = nc.dram_tensor(name, list(arr_shape), F32, kind="ExternalInput").ap()
    s = nc.alloc_sbuf_tensor(name + "_sb", [128, k * m], BF16).ap()
    src = d.rearrange("(c p) m -> p c m", p=128)
    dst = s.rearrange("p (c m) -> p c m", m=m)
    S.do("gpsimd", "dma_start", dst, src)
    return s.rearrange("p (c m) -> p c m", m=m)


def emit_ln_feature_major(S, nc, xf, nchunks, tw, ones_sb, sc, tag):
    if "out3" in sc and sc["out3"] is not None:
        lnb3 = sc["out3"]
    else:
        lnb = nc.alloc_sbuf_tensor(f"lnb_{tag}", [128, nchunks * tw], BF16).ap()
        lnb3 = lnb.rearrange("p (c t) -> p c t", t=tw)
    ps_s = sc["ps_stat"]
    ps_b = sc["ps_bc"]
    nD = nchunks * 128
    for c in range(nchunks):
        S.do("scalar", "activation", sc["sq"], xf[:, c], AF.Square)
        S.do("tensor", "matmul", ps_s[:, 0:tw], ones_sb, xf[:, c],
             start=(c == 0), stop=(c == nchunks - 1))
        S.do("tensor", "matmul", ps_s[:, tw : 2 * tw], ones_sb, sc["sq"],
             start=(c == 0), stop=(c == nchunks - 1))
    S.do("scalar", "activation", sc["m"], ps_s[:, 0:tw], AF.Copy, scale=1.0 / nD)
    S.do("vector", "tensor_tensor", sc["v"], sc["m"], sc["m"], op=ALU.mult)
    S.do("vector", "scalar_tensor_tensor", sc["v2"], ps_s[:, tw : 2 * tw],
         1.0 / nD, sc["v"], op0=ALU.mult, op1=ALU.subtract)
    S.do("scalar", "activation", sc["v"], sc["v2"], AF.Ln, bias=EPS_LN)
    S.do("scalar", "activation", sc["r"], sc["v"], AF.Exp, scale=-0.5)
    S.do("vector", "tensor_tensor", sc["mr"], sc["m"], sc["r"], op=ALU.mult)
    S.do("tensor", "matmul", ps_b[:, 0:tw], sc["ones_row"], sc["r"],
         start=True, stop=True)
    S.do("tensor", "matmul", ps_b[:, tw : 2 * tw], sc["ones_row"], sc["mr"],
         start=True, stop=True)
    S.do("vector", "tensor_copy", sc["rb"], ps_b[:, 0:tw])
    S.do("vector", "tensor_copy", sc["mrb"], ps_b[:, tw : 2 * tw])
    for c in range(nchunks):
        S.do("scalar", "activation", sc["xb"], xf[:, c], AF.Copy)
        S.do("vector", "tensor_tensor", sc["xb2"], sc["xb"], sc["rb"], op=ALU.mult)
        S.do("vector", "tensor_tensor", lnb3[:, c], sc["xb2"], sc["mrb"],
             op=ALU.subtract)
    return lnb3


def alloc_ln_scratch(S, nc, tw, ps_bc, ps_stat=None, tag=""):
    sc = {}
    reg_const(S, nc, EPS_LN, tag="lneps")
    sc["sq"] = nc.alloc_sbuf_tensor(f"ln_sq{tag}", [128, tw], F32).ap()
    if ps_stat is None:
        ps_stat = nc.alloc_psum_tensor(f"ln_ps_s{tag}", [1, 2 * tw], F32).ap()
    sc["ps_stat"] = ps_stat
    sc["ps_bc"] = ps_bc
    for nm in ("m", "v", "v2", "r", "mr"):
        sc[nm] = nc.alloc_sbuf_tensor(f"ln_{nm}{tag}", [1, tw], F32).ap()
    sc["rb"] = nc.alloc_sbuf_tensor(f"ln_rb{tag}", [128, tw], BF16).ap()
    sc["mrb"] = nc.alloc_sbuf_tensor(f"ln_mrb{tag}", [128, tw], BF16).ap()
    sc["xb"] = nc.alloc_sbuf_tensor(f"ln_xb{tag}", [128, tw], BF16).ap()
    sc["xb2"] = nc.alloc_sbuf_tensor(f"ln_xb2{tag}", [128, tw], BF16).ap()
    ones = np.ones((128, 1), np.float32)
    onr = np.ones((1, 128), np.float32)
    sc["ones_col_host"] = ones
    sc["ones_row_host"] = onr
    ocd = nc.dram_tensor("ones_col", [128, 1], F32, kind="ExternalInput").ap()
    ord_ = nc.dram_tensor("ones_row", [1, 128], F32, kind="ExternalInput").ap()
    oc = nc.alloc_sbuf_tensor("ones_col_sb", [128, 1], F32).ap()
    orr = nc.alloc_sbuf_tensor("ones_row_sb", [1, 128], F32).ap()
    S.do("sync", "dma_start", oc, ocd)
    S.do("sync", "dma_start", orr, ord_)
    sc["ones_col"] = oc
    sc["ones_row"] = orr
    ocb = nc.alloc_sbuf_tensor("ones_col_bf", [128, 1], BF16).ap()
    orb = nc.alloc_sbuf_tensor("ones_row_bf", [1, 128], BF16).ap()
    S.do("gpsimd", "dma_start", ocb, ocd)
    S.do("gpsimd", "dma_start", orb, ord_)
    sc["ones_col_bf"] = ocb
    sc["ones_row_bf"] = orb
    return sc



def build_ffn0(merge_partials):
    nc = mk_nc()
    S = Ser(nc)
    TW = 512
    NT = TPC // TW
    xt = nc.dram_tensor("xt", [D, TPC], F32, kind="ExternalInput").ap()
    if merge_partials:
        pa = nc.dram_tensor("pa", [D, TPC], F32, kind="ExternalInput").ap()
        pb = nc.dram_tensor("pb", [D, TPC], F32, kind="ExternalInput").ap()
        bpv = nc.dram_tensor("bpv", [128, 8], F32, kind="ExternalInput").ap()
    yt = nc.dram_tensor("yt", [D, TPC], F32, kind="ExternalOutput").ap()

    U1 = emit_load_weight_bf16(S, nc, "U1", (1024, 256), 8)
    V1 = emit_load_weight_bf16(S, nc, "V1", (256, 4096), 2)
    U2 = emit_load_weight_bf16(S, nc, "U2", (4096, 256), 32)
    V2 = emit_load_weight_bf16(S, nc, "V2", (256, 1024), 2)
    b1d = nc.dram_tensor("b1v", [128, 32], F32, kind="ExternalInput").ap()
    b2d = nc.dram_tensor("b2v", [128, 8], F32, kind="ExternalInput").ap()
    b1s = nc.alloc_sbuf_tensor("b1s", [128, 32], F32).ap()
    b2s = nc.alloc_sbuf_tensor("b2s", [128, 8], F32).ap()
    S.do("sync", "dma_start", b1s, b1d)
    S.do("sync", "dma_start", b2s, b2d)
    if merge_partials:
        bps = nc.alloc_sbuf_tensor("bps", [128, 8], F32).ap()
        S.do("sync", "dma_start", bps, bpv)

    ps_A = nc.alloc_psum_tensor("ps_A", [128, 2 * TW], F32).ap()
    ps_B = nc.alloc_psum_tensor("ps_B", [128, 2 * TW], F32).ap()
    sc = alloc_ln_scratch(S, nc, TW, ps_A)

    xf = nc.alloc_sbuf_tensor("xf", [128, 8 * TW], F32).ap()
    xf3 = xf.rearrange("p (c t) -> p c t", t=TW)
    pf = nc.alloc_sbuf_tensor("pf", [128, 8 * TW], F32).ap() if merge_partials else None
    pf3 = pf.rearrange("p (c t) -> p c t", t=TW) if merge_partials else None
    Pb = nc.alloc_sbuf_tensor("Pb", [128, 2 * TW], BF16).ap()
    Pb3 = Pb.rearrange("p (c t) -> p c t", t=TW)
    Hb = nc.alloc_sbuf_tensor("Hb", [128, 32 * TW], BF16).ap()
    Hb3 = Hb.rearrange("p (c t) -> p c t", t=TW)
    Qb = nc.alloc_sbuf_tensor("Qb", [128, 2 * TW], BF16).ap()
    Qb3 = Qb.rearrange("p (c t) -> p c t", t=TW)
    yf = nc.alloc_sbuf_tensor("yf", [128, TW], F32).ap()
    ps_P = ps_A
    ps_H = ps_B
    ps_Y = ps_B

    for t in range(NT):
        tsl = slice(t * TW, (t + 1) * TW)
        src = xt[:, tsl].rearrange("(c p) t -> p c t", p=128)
        S.do("sync", "dma_start", xf3, src)
        if merge_partials:
            S.do("sync", "dma_start", pf3, pa[:, tsl].rearrange("(c p) t -> p c t", p=128))
            for c in range(8):
                S.do("vector", "tensor_tensor", xf3[:, c], xf3[:, c], pf3[:, c], op=ALU.add)
            S.do("sync", "dma_start", pf3, pb[:, tsl].rearrange("(c p) t -> p c t", p=128))
            for c in range(8):
                S.do("vector", "tensor_tensor", xf3[:, c], xf3[:, c], pf3[:, c], op=ALU.add)
                S.do("vector", "tensor_scalar", out=xf3[:, c], in0=xf3[:, c],
                     scalar1=bps[:, c : c + 1], scalar2=None, op0=ALU.add)
        lnb3 = emit_ln_feature_major(S, nc, xf3, 8, TW, sc["ones_col"], sc, f"t{t}")
        for o in range(2):
            for c in range(8):
                S.do("tensor", "matmul", ps_P[:, o * TW : (o + 1) * TW],
                     U1[:, c, o * 128 : (o + 1) * 128], lnb3[:, c],
                     start=(c == 0), stop=(c == 7))
            S.do("vector", "tensor_copy", Pb3[:, o], ps_P[:, o * TW : (o + 1) * TW])
        for f in range(32):
            fb = f % 2
            for c in range(2):
                S.do("tensor", "matmul", ps_H[:, fb * TW : (fb + 1) * TW],
                     V1[:, c, f * 128 : (f + 1) * 128], Pb3[:, c],
                     start=(c == 0), stop=(c == 1))
            S.do("scalar", "activation", Hb3[:, f], ps_H[:, fb * TW : (fb + 1) * TW],
                 AF.Gelu_apprx_tanh, bias=b1s[:, f : f + 1])
        for o in range(2):
            for c in range(32):
                S.do("tensor", "matmul", ps_P[:, o * TW : (o + 1) * TW],
                     U2[:, c, o * 128 : (o + 1) * 128], Hb3[:, c],
                     start=(c == 0), stop=(c == 31))
            S.do("vector", "tensor_copy", Qb3[:, o], ps_P[:, o * TW : (o + 1) * TW])
        for d_ in range(8):
            db = d_ % 2
            for c in range(2):
                S.do("tensor", "matmul", ps_Y[:, db * TW : (db + 1) * TW],
                     V2[:, c, d_ * 128 : (d_ + 1) * 128], Qb3[:, c],
                     start=(c == 0), stop=(c == 1))
            S.do("vector", "tensor_scalar", out=yf, in0=ps_Y[:, db * TW : (db + 1) * TW],
                 scalar1=b2s[:, d_ : d_ + 1], scalar2=0.5, op0=ALU.add, op1=ALU.mult)
            S.do("vector", "tensor_tensor", xf3[:, d_], yf, xf3[:, d_], op=ALU.add)
            dst = yt[:, tsl].rearrange("(c p) t -> p c t", p=128)[:, d_]
            S.do("sync", "dma_start", dst, xf3[:, d_])
    S.fence("sync")
    return nc


def ffn0_in_maps(xT, U1, V1, b1, U2, V2, b2, partials=None, bp=None):
    base = {
        "U1": f32(U1), "V1": f32(V1), "U2": f32(U2), "V2": f32(V2),
        "b1v": f32(as_pcols(b1)), "b2v": f32(as_pcols(b2)),
        "ones_col": np.ones((128, 1), np.float32),
        "ones_row": np.ones((1, 128), np.float32),
    }
    maps = []
    for c in range(8):
        m = dict(base)
        sl = slice(c * TPC, (c + 1) * TPC)
        m["xt"] = f32(xT[:, sl])
        if partials is not None:
            pa, pb_ = partials
            m["pa"] = f32(pa[:, sl])
            m["pb"] = f32(pb_[:, sl])
            m["bpv"] = f32(as_pcols(bp))
        maps.append(m)
    return maps



def build_ffn1(merge_partials, nh_lim=24, nd_lim=8):
    nc = mk_nc()
    S = Ser(nc)
    TW = 512
    NT = TPC // TW
    NH = 24
    xt = nc.dram_tensor("xt", [D, TPC], F32, kind="ExternalInput").ap()
    if merge_partials:
        pa = nc.dram_tensor("pa", [D, TPC], F32, kind="ExternalInput").ap()
        pb = nc.dram_tensor("pb", [D, TPC], F32, kind="ExternalInput").ap()
    yt = nc.dram_tensor("yt", [D, TPC], F32, kind="ExternalOutput").ap()
    wgT = nc.dram_tensor("wgT", [1024, 3072], I32, kind="ExternalInput").ap()
    wuT = nc.dram_tensor("wuT", [1024, 3072], I32, kind="ExternalInput").ap()
    wdT = nc.dram_tensor("wdT", [3072, 1024], I32, kind="ExternalInput").ap()

    ps_A = nc.alloc_psum_tensor("ps_A", [128, 2 * TW], F32).ap()
    ps_B = nc.alloc_psum_tensor("ps_B", [128, 2 * TW], F32).ap()
    sc = alloc_ln_scratch(S, nc, TW, ps_A)

    xf = nc.alloc_sbuf_tensor("xf", [128, 8 * TW], F32).ap()
    xf3 = xf.rearrange("p (c t) -> p c t", t=TW)
    lnball = nc.alloc_sbuf_tensor("lnball", [128, 8 * TPC], BF16).ap()
    lnb4 = lnball.rearrange("p (c u t) -> p c u t", c=8, u=NT)
    hb = nc.alloc_sbuf_tensor("hb", [128, NH * TPC], BF16).ap()
    hb4 = hb.rearrange("p (k u t) -> p k u t", k=NH, u=NT)
    wg_s = nc.alloc_sbuf_tensor("wg_s", [128, 8 * 128], BF16).ap()
    wg3 = wg_s.rearrange("p (c m) -> p c m", m=128)
    wu_s = nc.alloc_sbuf_tensor("wu_s", [128, 8 * 128], BF16).ap()
    wu3 = wu_s.rearrange("p (c m) -> p c m", m=128)
    wd_s = nc.alloc_sbuf_tensor("wd_s", [128, NH * 128], BF16).ap()
    wd3 = wd_s.rearrange("p (c m) -> p c m", m=128)
    sg = nc.alloc_sbuf_tensor("sg", [128, TW], BF16).ap()
    yf = nc.alloc_sbuf_tensor("yf", [128, TW], F32).ap()
    pfa = nc.alloc_sbuf_tensor("pfa", [128, TW], F32).ap() if merge_partials else None

    for t in range(NT):
        tsl = slice(t * TW, (t + 1) * TW)
        S.do("sync", "dma_start", xf3, xt[:, tsl].rearrange("(c p) t -> p c t", p=128))
        if merge_partials:
            for src in (pa, pb):
                for c in range(8):
                    S.do("sync", "dma_start", pfa,
                         src[:, tsl].rearrange("(c p) t -> p c t", p=128)[:, c])
                    S.do("vector", "tensor_tensor", xf3[:, c], xf3[:, c], pfa, op=ALU.add)
        sc["out3"] = lnb4[:, :, t]
        emit_ln_feature_major(S, nc, xf3, 8, TW, sc["ones_col"], sc, f"f1t{t}")
    sc["out3"] = None

    for h in range(nh_lim):
        hs = slice(h * 128, (h + 1) * 128)
        S.do("gpsimd", "dma_start", wg3, wgT[:, hs].rearrange("(c p) m -> p c m", p=128))
        S.do("gpsimd", "dma_start", wu3, wuT[:, hs].rearrange("(c p) m -> p c m", p=128))
        for t in range(NT):
            for c in range(8):
                S.do("tensor", "matmul", ps_A[:, 0:TW], wg3[:, c], lnb4[:, c, t],
                     start=(c == 0), stop=(c == 7))
            for c in range(8):
                S.do("tensor", "matmul", ps_A[:, TW : 2 * TW], wu3[:, c], lnb4[:, c, t],
                     start=(c == 0), stop=(c == 7))
            S.do("scalar", "activation", sg, ps_A[:, 0:TW], AF.Silu, scale=1.0 / 127.0)
            S.do("vector", "tensor_tensor", hb4[:, h, t], sg, ps_A[:, TW : 2 * TW],
                 op=ALU.mult)

    for d_ in range(nd_lim):
        ds = slice(d_ * 128, (d_ + 1) * 128)
        S.do("gpsimd", "dma_start", wd3, wdT[:, ds].rearrange("(c p) m -> p c m", p=128))
        for t in range(NT):
            tsl = slice(t * TW, (t + 1) * TW)
            for k in range(NH):
                S.do("tensor", "matmul", ps_B[:, 0:TW], wd3[:, k], hb4[:, k, t],
                     start=(k == 0), stop=(k == NH - 1))
            xsrc = xt[:, tsl].rearrange("(c p) t -> p c t", p=128)[:, d_]
            S.do("sync", "dma_start", yf, xsrc)
            if merge_partials:
                for src in (pa, pb):
                    S.do("sync", "dma_start", pfa,
                         src[:, tsl].rearrange("(c p) t -> p c t", p=128)[:, d_])
                    S.do("vector", "tensor_tensor", yf, yf, pfa, op=ALU.add)
            S.do("vector", "scalar_tensor_tensor", yf, ps_B[:, 0:TW],
                 0.5 / (127.0 * 127.0), yf, op0=ALU.mult, op1=ALU.add)
            dst = yt[:, tsl].rearrange("(c p) t -> p c t", p=128)[:, d_]
            S.do("sync", "dma_start", dst, yf)
    S.fence("sync")
    return nc


def ffn1_in_maps(xT, wgT, wuT, wdT, partials=None):
    base = {
        "wgT": wgT, "wuT": wuT, "wdT": wdT,
        "ones_col": np.ones((128, 1), np.float32),
        "ones_row": np.ones((1, 128), np.float32),
    }
    maps = []
    for c in range(8):
        m = dict(base)
        sl = slice(c * TPC, (c + 1) * TPC)
        m["xt"] = f32(xT[:, sl])
        if partials is not None:
            m["pa"] = f32(partials[0][:, sl])
            m["pb"] = f32(partials[1][:, sl])
        maps.append(m)
    return maps


def run_ffn1(xT, wgT, wuT, wdT, partials=None, nh_lim=24, nd_lim=8):
    nc = get_prog("ffn1", build_ffn1, partials is not None, nh_lim, nd_lim)
    maps = ffn1_in_maps(xT, wgT, wuT, wdT, partials=partials)
    res = run_bass_kernel_spmd(nc, maps, core_ids=list(range(8)))
    return np.concatenate([r["yt"] for r in res.results], axis=1)



MASKW = 1536
BIGNEG = -1.0e30


def make_mask_plane():
    k = np.arange(MASKW)[None, :]
    c = np.arange(128)[:, None]
    return np.where(k >= c + 512, 0.0, BIGNEG).astype(np.float32)


def build_attn0(half):
    KEYN = 2048 * (half + 1)
    Q0 = KEYN - 2048
    NH8 = 8
    HD = 64
    INV = 1.0 / np.sqrt(64.0)
    TW = 512
    NKT = KEYN // TW
    nc = mk_nc()
    S = Ser(nc)

    xt = nc.dram_tensor("xt", [D, KEYN], F32, kind="ExternalInput").ap()
    EXT = nc.dram_tensor("EXT", [NH8, 64, 2048], F32, kind="ExternalInput").ap()
    maskd = nc.dram_tensor("maskpl", [128, MASKW], F32, kind="ExternalInput").ap()
    yout = nc.dram_tensor("ypart", [D, 2048], F32, kind="ExternalOutput").ap()
    idd = nc.dram_tensor("ident", [128, 128], F32, kind="ExternalInput").ap()
    V2 = nc.dram_tensor("V2scratch", [KEYN * 4097], BF16, kind="Internal").ap()
    V2r = V2.rearrange("(r c) -> r c", c=4097)

    ps_A = nc.alloc_psum_tensor("ps_A", [128, 2 * TW], F32).ap()
    ps_S = nc.alloc_psum_tensor("ps_S", [128, TW], F32).ap()
    ps_O = nc.alloc_psum_tensor("ps_O", [128, TW], F32).ap()
    ps_sum = nc.alloc_psum_tensor("ps_sum", [1, TW], F32).ap()
    ps_n = ps_sum
    ps_T = nc.alloc_psum_tensor("ps_T", [128, 64], BF16).ap()

    sc = alloc_ln_scratch(S, nc, TW, ps_A, ps_stat=ps_A[0:1, :])
    ident = nc.alloc_sbuf_tensor("ident_sb", [128, 128], BF16).ap()
    S.do("gpsimd", "dma_start", ident, idd)

    lnball = nc.alloc_sbuf_tensor("lnball", [128, 8 * KEYN], BF16).ap()
    lnb4 = lnball.rearrange("p (c u t) -> p c u t", c=8, u=NKT)

    zt = lnball[:, 0:4097]
    S.do("vector", "memset", lnball[:, 0 : 2 * 4097], 0.0)
    fill_lo = max(0, Q0 - 1)
    for r0 in range(fill_lo, KEYN, 128):
        nr = min(128, KEYN - r0)
        S.do("sync", "dma_start", V2r[r0 : r0 + nr, :], zt[0:nr, :])

    with nc.sbuf_tensor("xf", [128, 8 * TW], F32) as xf_h:
        xf3 = xf_h.ap().rearrange("p (c t) -> p c t", t=TW)
        for t in range(NKT):
            S.do("sync", "dma_start", xf3,
                 xt[:, t * TW : (t + 1) * TW].rearrange("(c p) t -> p c t", p=128))
            sc["out3"] = lnb4[:, :, t]
            emit_ln_feature_major(S, nc, xf3, 8, TW, sc["ones_col"], sc, f"a0t{t}")
        sc["out3"] = None
    lnb3 = lnball.rearrange("p (c t) -> p c t", t=KEYN)

    qS = nc.alloc_sbuf_tensor("qS", [64, 2048], BF16).ap()
    qA = nc.alloc_sbuf_tensor("qA", [64, 2048], BF16).ap() if half == 1 else qS
    kh = nc.alloc_sbuf_tensor("kh", [64, KEYN], BF16).ap()
    vh = nc.alloc_sbuf_tensor("vh", [64, KEYN], BF16).ap()
    vT = nc.alloc_sbuf_tensor("vT", [128, (KEYN // 128) * 64], BF16).ap()
    vT3 = vT.rearrange("p (k v) -> p k v", v=64)
    exh = nc.alloc_sbuf_tensor("exh", [64, 2048], BF16).ap()
    asb = nc.alloc_sbuf_tensor("asb", [128, TW], BF16).ap()
    qn = nc.alloc_sbuf_tensor("qn", [1, 2050], F32).ap()
    qnA = nc.alloc_sbuf_tensor("qnA", [1, 2048], F32).ap()
    kstar = nc.alloc_sbuf_tensor("kstar", [1, 1], F32).ap()
    mtmp = nc.alloc_sbuf_tensor("mtmp", [1, 1], F32).ap()
    estar = nc.alloc_sbuf_tensor("estar", [1, 1], F32).ap()
    sqt = nc.alloc_sbuf_tensor("sqt", [64, TW], F32).ap()
    negB = nc.alloc_sbuf_tensor("negB", [1, TW], F32).ap()
    nB1 = nc.alloc_sbuf_tensor("nB1", [1, TW], F32).ap()
    SCt = nc.alloc_sbuf_tensor("SCt", [128, TW], BF16).ap()
    SRt0 = nc.alloc_sbuf_tensor("SRt0", [128, TW], BF16, is_transpose_dest=True).ap()
    SRt1 = nc.alloc_sbuf_tensor("SRt1", [128, TW], BF16, is_transpose_dest=True).ap()
    SRts = (SRt0, SRt1)
    expS0 = nc.alloc_sbuf_tensor("expS0", [128, TW], BF16).ap()
    expS1x = nc.alloc_sbuf_tensor("expS1x", [128, TW], BF16).ap()
    expSs = (expS0, expS1x)
    ssum = nc.alloc_sbuf_tensor("ssum", [1, TW], F32).ap()
    rsum = nc.alloc_sbuf_tensor("rsum", [1, TW], F32).ap()
    otmp = nc.alloc_sbuf_tensor("otmp", [64, TW], F32).ap()
    osum = nc.alloc_sbuf_tensor("osum", [64, TW], F32).ap()
    Oall = nc.alloc_sbuf_tensor("Oall", [128, 4 * 2048], BF16).ap()
    Oall4 = Oall.rearrange("p (h t) -> p h t", h=4)
    yf = nc.alloc_sbuf_tensor("yf_a", [128, TW], F32).ap()

    Wq_s = emit_load_weight_bf16(S, nc, "WqT_x", (1024, 512), 8)

    def project2(dst, Wsb, col0, tok0, ntok, scale=None):
        for tt in range(ntok // TW):
            a = tok0 + tt * TW
            for c in range(8):
                S.do("tensor", "matmul", ps_A[:64, 0:TW],
                     Wsb[:, c, col0 : col0 + 64],
                     lnb3[:, c, a : a + TW], start=(c == 0), stop=(c == 7))
            if scale is None:
                S.do("vector", "tensor_copy", dst[:, tt * TW : (tt + 1) * TW],
                     ps_A[:64, 0:TW])
            else:
                S.do("scalar", "activation", dst[:, tt * TW : (tt + 1) * TW],
                     ps_A[:64, 0:TW], AF.Copy, scale=scale)

    Wk_s = emit_load_weight_bf16(S, nc, "WkT_x", (1024, 512), 8)
    Wv_s = emit_load_weight_bf16(S, nc, "WvT_x", (1024, 512), 8)
    Wp_s = emit_load_weight_bf16(S, nc, "WpT_x", (512, 1024), 8)

    for h in range(NH8):
        c0 = h * 64
        project2(qS, Wq_s, c0, Q0, 2048, scale=INV)
        if half == 1:
            project2(qA, Wq_s, c0, Q0 - 1, 2048, scale=INV)
        project2(kh, Wk_s, c0, 0, KEYN)
        project2(vh, Wv_s, c0, 0, KEYN)
        for kc in range(KEYN // 128):
            S.do("tensor", "transpose", ps_T, vh[:, kc * 128 : (kc + 1) * 128],
                 ident[0:64, 0:64])
            S.do("vector", "tensor_copy", vT3[:, kc], ps_T)
        S.do("gpsimd", "dma_start", exh, EXT[h])
        S.do("vector", "memset", qn[:, 0:2], 0.0)
        for tt in range(4):
            S.do("scalar", "activation", sqt, qS[:, tt * TW : (tt + 1) * TW], AF.Square)
            S.do("tensor", "matmul", ps_n, sc["ones_col"][0:64], sqt,
                 start=True, stop=True)
            S.do("scalar", "activation", qn[:, 1 + tt * TW : 1 + (tt + 1) * TW],
                 ps_n, AF.Sqrt)
        qAn = qnA if half == 1 else qn[:, 0:2048]
        if half == 1:
            for tt in range(4):
                S.do("scalar", "activation", sqt, qA[:, tt * TW : (tt + 1) * TW],
                     AF.Square)
                S.do("tensor", "matmul", ps_n, sc["ones_col"][0:64], sqt,
                     start=True, stop=True)
                S.do("scalar", "activation", qnA[:, tt * TW : (tt + 1) * TW],
                     ps_n, AF.Sqrt)
        for which, src_t, ntt, dstmax in ((0, kh, KEYN // TW, kstar),
                                          (1, exh, 4, estar)):
            for tt in range(ntt):
                S.do("scalar", "activation", sqt, src_t[:, tt * TW : (tt + 1) * TW],
                     AF.Square)
                S.do("tensor", "matmul", ps_n, sc["ones_col"][0:64], sqt,
                     start=True, stop=True)
                S.do("scalar", "activation", rsum, ps_n, AF.Sqrt)
                if tt == 0:
                    S.do("vector", "tensor_reduce", dstmax, rsum, axis=AXL.X,
                         op=ALU.max)
                else:
                    S.do("vector", "tensor_reduce", mtmp, rsum, axis=AXL.X,
                         op=ALU.max)
                    S.do("vector", "tensor_tensor", dstmax, dstmax, mtmp, op=ALU.max)

        for a in range(16):
            i0 = a * 128
            gi0 = (Q0 - 1 if half == 1 else 0) + i0
            for jq in range(4):
                j0 = jq * TW
                need2 = (j0 + TW - 1) >= (2048 - (gi0 + 127) - 1)
                need1 = (gi0 + 127 >= 2048) and ((j0 + TW - 1) >= (4096 - (gi0 + 127) - 1))
                if not (need2 or need1):
                    continue
                S.do("tensor", "matmul", ps_A[:, 0:TW],
                     qA[:, i0 : i0 + 128], exh[:, j0 : j0 + TW],
                     start=True, stop=True)
                S.do("vector", "tensor_copy", asb, ps_A[:, 0:TW])
                if need2:
                    S.do("sync", "dma_start",
                         V2r[gi0 : gi0 + 128, 2049 + j0 : 2049 + j0 + TW], asb)
                if need1:
                    S.do("sync", "dma_start",
                         V2r[gi0 : gi0 + 128, 1 + j0 : 1 + j0 + TW], asb)

        v2done = S.val
        for t in range(4):
            R = Q0 + t * TW
            nkc = (R + TW) // 128
            S.do("vector", "tensor_scalar", out=negB,
                 in0=qn[:, 1 + t * TW : 1 + (t + 1) * TW], scalar1=kstar,
                 scalar2=None, op0=ALU.mult)
            S.do("vector", "tensor_scalar", out=nB1,
                 in0=qAn[:, t * TW : (t + 1) * TW], scalar1=estar,
                 scalar2=None, op0=ALU.mult)
            S.do("vector", "tensor_tensor", negB, negB, nB1, op=ALU.add)
            S.do("vector", "tensor_scalar", out=negB, in0=negB, scalar1=-1.0,
                 scalar2=None, op0=ALU.mult)
            tok0 = S.val
            ev_done = tok0
            sc_free = tok0
            srt_free = [tok0, tok0]
            av_par = [tok0, tok0]
            last_mask = tok0
            last_srel = tok0
            for kc in range(nkc):
                par = kc % 2
                off = R * 4096 + kc * 128
                src = bass_rust.AP(V2.tensor, off, [[4096, TW], [1, 128]])
                S.do("sync", "dma_start", SRts[par], src, transpose=True,
                     wait_val=None)
                dma_tok = S.val
                last_srel = S.val
                S.do("tensor", "matmul", ps_S, kh[:, kc * 128 : (kc + 1) * 128],
                     qS[:, t * TW : (t + 1) * TW], start=True, stop=False,
                     wait_val=ev_done)
                S.do("tensor", "matmul", ps_S, sc["ones_row"][:, 0:128], negB,
                     start=False, stop=True, wait_val=0)
                s_tok = S.val
                S.do("vector", "tensor_tensor", SCt, ps_S, SRts[par], op=ALU.add,
                     wait_val=max(s_tok, dma_tok, sc_free))
                ev_done = S.val
                srt_free[par] = S.val
                mask_tok = ev_done
                if 128 * kc + 127 > R:
                    dlt = 128 * kc - R
                    S.do("gpsimd", "dma_start", SCt,
                         maskd[:, 512 - dlt : 1024 - dlt], accum_op=ALU.add,
                         wait_val=None)
                    mask_tok = S.val
                    last_mask = S.val
                S.do("scalar", "activation", expSs[par], SCt, AF.Exp,
                     wait_val=max(mask_tok, av_par[par]))
                exp_tok = S.val
                sc_free = exp_tok
                S.do("tensor", "matmul", ps_sum, sc["ones_col_bf"], expSs[par],
                     start=(kc == 0), stop=(kc == nkc - 1), wait_val=exp_tok)
                g = kc % 2
                last = kc >= nkc - 2
                S.do("tensor", "matmul", ps_O[64 * g : 64 * g + 64, :],
                     vT3[:, kc], expSs[par], start=(kc == g), stop=last,
                     tile_position=(0, 64 * g), wait_val=0)
                av_par[par] = S.val
            S.do("vector", "tensor_copy", ssum, ps_sum)
            S.do("vector", "reciprocal", rsum, ssum)
            S.do("tensor", "matmul", ps_A[:64, 0:TW], sc["ones_row"][:, 0:64], rsum,
                 start=True, stop=True)
            S.do("vector", "tensor_copy", otmp, ps_O[64:128, :])
            S.do("vector", "tensor_tensor", osum, ps_O[0:64, :], otmp, op=ALU.add)
            hc, hp = h // 2, (h % 2) * 64
            S.do("vector", "tensor_tensor",
                 Oall4[hp : hp + 64, hc, t * TW : (t + 1) * TW],
                 osum, ps_A[:64, 0:TW], op=ALU.mult)

    for t in range(4):
        for d_ in range(8):
            for hc in range(4):
                S.do("tensor", "matmul", ps_A[:, 0:TW],
                     Wp_s[:, hc, d_ * 128 : (d_ + 1) * 128],
                     Oall4[:, hc, t * TW : (t + 1) * TW],
                     start=(hc == 0), stop=(hc == 3))
            S.do("vector", "tensor_copy", yf, ps_A[:, 0:TW])
            dst = yout[:, t * TW : (t + 1) * TW].rearrange("(c p) t -> p c t", p=128)[:, d_]
            S.do("sync", "dma_start", dst, yf)
    S.fence("sync")
    return nc


def attn0_in_maps(xT, Wq, Wk, Wv, Wp, rel_embed, half):
    KEYN = 2048 * (half + 1)
    rel = np.asarray(rel_embed, np.float32).reshape(2048, 16, 64)
    maps = []
    for c in range(8):
        b, o = c // 2, c % 2
        hsl = slice(o * 512, (o + 1) * 512)
        m = {
            "xt": f32(xT[:, b * SEQ : b * SEQ + KEYN]),
            "WqT_x": f32(Wq.T[:, hsl]),
            "WkT_x": f32(Wk.T[:, hsl]),
            "WvT_x": f32(Wv.T[:, hsl]),
            "WpT_x": f32(Wp.T[hsl, :]),
            "EXT": np.ascontiguousarray(
                rel[:, o * 8 : (o + 1) * 8, :].transpose(1, 2, 0)).astype(np.float32),
            "maskpl": make_mask_plane(),
            "ident": np.eye(128, dtype=np.float32),
            "ones_col": np.ones((128, 1), np.float32),
            "ones_row": np.ones((1, 128), np.float32),
        }
        maps.append(m)
    return maps


def run_attn0(xT, Wq, Wk, Wv, Wp, rel_embed, half):
    nc = get_prog("attn0", build_attn0, half)
    maps = attn0_in_maps(xT, Wq, Wk, Wv, Wp, rel_embed, half)
    res = run_bass_kernel_spmd(nc, maps, core_ids=list(range(8)))
    return [r["ypart"] for r in res.results]



EPS_RMS = 1e-6


def build_attn1():
    KEYN = SEQ
    INV96 = 1.0 / np.sqrt(96.0)
    TW = 512
    nc = mk_nc()
    S = Ser(nc)

    xt = nc.dram_tensor("xt", [D, SEQ], F32, kind="ExternalInput").ap()
    xtok = nc.dram_tensor("xtok", [SEQ, D], F32, kind="ExternalInput").ap()
    maskd = nc.dram_tensor("maskpl", [128, MASKW], F32, kind="ExternalInput").ap()
    idd = nc.dram_tensor("ident", [128, 128], F32, kind="ExternalInput").ap()
    stud = nc.dram_tensor("stu", [128, 128], F32, kind="ExternalInput").ap()
    yout0 = nc.dram_tensor("ypart0", [D, 2048], F32, kind="ExternalOutput").ap()
    yout1 = nc.dram_tensor("ypart1", [D, 2048], F32, kind="ExternalOutput").ap()
    gqd = nc.dram_tensor("g_qnorm", [96, 1], F32, kind="ExternalInput").ap()
    gkd = nc.dram_tensor("g_kvnorm", [32, 1], F32, kind="ExternalInput").ap()
    wgated = nc.dram_tensor("w_gate", [1024, 3], F32, kind="ExternalInput").ap()
    wimpd = nc.dram_tensor("w_imp", [128, 8], F32, kind="ExternalInput").ap()
    dq8d = nc.dram_tensor("dq8", [96, 768], F32, kind="ExternalInput").ap()
    dkn8d = nc.dram_tensor("dkn8", [32, 256], F32, kind="ExternalInput").ap()
    dv8d = nc.dram_tensor("dv8", [32, 256], F32, kind="ExternalInput").ap()
    selk8d = nc.dram_tensor("selk8", [1024, 768], F32, kind="ExternalInput").ap()
    selv8d = nc.dram_tensor("selv8", [1024, 256], F32, kind="ExternalInput").ap()
    wink8d = nc.dram_tensor("wink8", [1024, 768], F32, kind="ExternalInput").ap()
    winv8d = nc.dram_tensor("winv8", [1024, 256], F32, kind="ExternalInput").ap()
    cosqd = nc.dram_tensor("cosq", [32, SEQ], F32, kind="ExternalInput").ap()
    sinqd = nc.dram_tensor("sinq", [32, SEQ], F32, kind="ExternalInput").ap()
    sscr = nc.dram_tensor("sscr", [SEQ], F32, kind="Internal").ap()
    idxbuf = nc.dram_tensor("idxbuf", [256, 1], U32, kind="Internal").ap()

    ps_A = nc.alloc_psum_tensor("ps_A", [128, 2 * TW], F32).ap()
    ps_S = nc.alloc_psum_tensor("ps_S", [128, TW], F32).ap()
    ps_S2 = nc.alloc_psum_tensor("ps_S2", [128, TW], F32).ap()
    ps_O = nc.alloc_psum_tensor("ps_O", [128, TW], F32).ap()
    ps_sum = nc.alloc_psum_tensor("ps_sum", [1, TW], F32).ap()
    ps_T = nc.alloc_psum_tensor("ps_T", [128, 128], BF16).ap()
    ps_Ss = (ps_S, ps_S2)

    sc = alloc_ln_scratch(S, nc, TW, ps_A, ps_stat=ps_A[0:1, :])
    reg_const(S, nc, EPS_RMS, tag="rmseps")
    ident = nc.alloc_sbuf_tensor("ident_sb", [128, 128], BF16).ap()
    stu = nc.alloc_sbuf_tensor("stu_sb", [128, 128], F32).ap()
    S.do("gpsimd", "dma_start", ident, idd)
    S.do("sync", "dma_start", stu, stud)

    cq_s = emit_load_weight_bf16(S, nc, "w_cq", (1024, 96), 8)
    ckv_s = emit_load_weight_bf16(S, nc, "w_ckv", (1024, 32), 8)
    krop_s = emit_load_weight_bf16(S, nc, "w_krope", (1024, 64), 8)
    gq = nc.alloc_sbuf_tensor("gq_sb", [96, 1], F32).ap()
    gk = nc.alloc_sbuf_tensor("gk_sb", [32, 1], F32).ap()
    wgate_s = nc.alloc_sbuf_tensor("wgate_sb", [128, 8 * 3], F32).ap()
    wimp_s = nc.alloc_sbuf_tensor("wimp_sb", [128, 8], F32).ap()
    S.do("sync", "dma_start", gq, gqd)
    S.do("sync", "dma_start", gk, gkd)
    S.do("sync", "dma_start", wgate_s.rearrange("p (c m) -> p c m", m=3),
         wgated.rearrange("(c p) m -> p c m", p=128))
    S.do("sync", "dma_start", wimp_s, wimpd)
    cosq = nc.alloc_sbuf_tensor("cosq_sb", [32, SEQ], BF16).ap()
    sinq = nc.alloc_sbuf_tensor("sinq_sb", [32, SEQ], BF16).ap()
    S.do("gpsimd", "dma_start", cosq, cosqd)
    S.do("gpsimd", "dma_start", sinq, sinqd)

    lnball = nc.alloc_sbuf_tensor("lnball", [128, 8 * SEQ], BF16).ap()
    lnb4 = lnball.rearrange("p (c u t) -> p c u t", c=8, u=SEQ // TW)
    lnb3 = lnball.rearrange("p (c t) -> p c t", t=SEQ)

    cwb = nc.alloc_sbuf_tensor("cwb", [1, 1], F32).ap()
    sX = nc.alloc_sbuf_tensor("sX", [128, 8], F32).ap()
    srow = nc.alloc_sbuf_tensor("srow", [1, TW], F32).ap()
    srow2 = nc.alloc_sbuf_tensor("srow2", [1, TW], F32).ap()
    g123 = nc.alloc_sbuf_tensor("g123", [1, 3], F32).ap()
    gtmp = nc.alloc_sbuf_tensor("gtmp", [1, 3], F32).ap()
    g1s = nc.alloc_sbuf_tensor("g1s", [1, 1], F32).ap()

    for c in range(8):
        S.do("tensor", "matmul", ps_A[0:1, 0:1], wimp_s[:, c : c + 1],
             sc["ones_col"], start=(c == 0), stop=(c == 7))
    S.do("vector", "tensor_copy", cwb, ps_A[0:1, 0:1])

    with nc.sbuf_tensor("xf", [128, 8 * TW], F32) as xf_h:
        xf3 = xf_h.ap().rearrange("p (c t) -> p c t", t=TW)
        for t in range(SEQ // TW):
            S.do("sync", "dma_start", xf3,
                 xt[:, t * TW : (t + 1) * TW].rearrange("(c p) t -> p c t", p=128))
            sc["out3"] = lnb4[:, :, t]
            emit_ln_feature_major(S, nc, xf3, 8, TW, sc["ones_col"], sc, f"a1t{t}")
            for c in range(8):
                S.do("tensor", "matmul", ps_A[0:1, TW : 2 * TW],
                     wimp_s[:, c : c + 1], xf3[:, c], start=(c == 0), stop=(c == 7))
            S.do("vector", "tensor_scalar", out=srow, in0=sc["m"], scalar1=cwb,
                 scalar2=None, op0=ALU.mult)
            S.do("vector", "tensor_tensor", srow2, ps_A[0:1, TW : 2 * TW], srow,
                 op=ALU.subtract)
            S.do("vector", "tensor_tensor", srow, srow2, sc["r"], op=ALU.mult)
            S.do("sync", "dma_start",
                 sscr[t * TW : (t + 1) * TW].rearrange("(a t) -> a t", a=1),
                 srow[0:1, :])
        sc["out3"] = None
    for c in range(8):
        S.do("vector", "reduce_sum", sX[:, c : c + 1], lnb3[:, c], axis=AXL.X)
    for c in range(8):
        S.do("tensor", "matmul", ps_A[0:1, 0:3], sX[:, c : c + 1],
             wgate_s.rearrange("p (c m) -> p c m", m=3)[:, c],
             start=(c == 0), stop=(c == 7))
    S.do("scalar", "activation", g123, ps_A[0:1, 0:3], AF.Copy, scale=1.0 / SEQ)
    S.do("vector", "tensor_reduce", g1s, g123, axis=AXL.X, op=ALU.max)
    S.do("vector", "tensor_scalar", out=g1s, in0=g1s, scalar1=-1.0, scalar2=None,
         op0=ALU.mult)
    S.do("scalar", "activation", gtmp, g123, AF.Exp, bias=g1s)
    S.do("vector", "reduce_sum", g1s, gtmp, axis=AXL.X)
    S.do("vector", "reciprocal", g1s, g1s)
    S.do("vector", "tensor_scalar", out=g123, in0=gtmp, scalar1=g1s, scalar2=None,
         op0=ALU.mult)

    s128 = nc.alloc_sbuf_tensor("s128", [128, 32], F32).ap()
    S.do("sync", "dma_start", s128, sscr.rearrange("(p f) -> p f", f=32))
    lo = nc.alloc_sbuf_tensor("lo_t", [1, 1], F32).ap()
    hi = nc.alloc_sbuf_tensor("hi_t", [1, 1], F32).ap()
    mid = nc.alloc_sbuf_tensor("mid_t", [1, 1], F32).ap()
    cnt = nc.alloc_sbuf_tensor("cnt_t", [1, 1], F32).ap()
    msk1 = nc.alloc_sbuf_tensor("msk1", [1, 1], F32).ap()
    btmp = nc.alloc_sbuf_tensor("btmp", [1, 1], F32).ap()
    btmp2 = nc.alloc_sbuf_tensor("btmp2", [1, 1], F32).ap()
    midb = nc.alloc_sbuf_tensor("midb", [128, 1], F32).ap()
    zt1 = nc.alloc_sbuf_tensor("zt1", [128, 32], F32).ap()
    zcnt = nc.alloc_sbuf_tensor("zcnt", [128, 1], F32).ap()
    absm = nc.alloc_sbuf_tensor("absm", [128, 1], F32).ap()
    idf = nc.alloc_sbuf_tensor("idf_sb", [128, 128], F32).ap()
    S.do("sync", "dma_start", idf, idd)
    S.do("vector", "tensor_reduce", absm, s128, axis=AXL.X, op=ALU.max,
         apply_absolute_value=True)
    S.do("tensor", "matmul", ps_A[0:1, 0:128], absm, idf, start=True, stop=True)
    S.do("vector", "tensor_copy", srow[:, 0:128], ps_A[0:1, 0:128])
    S.do("vector", "tensor_reduce", hi, srow[:, 0:128], axis=AXL.X, op=ALU.max)
    S.do("vector", "tensor_scalar", out=hi, in0=hi, scalar1=1.0, scalar2=None,
         op0=ALU.add)
    S.do("vector", "tensor_scalar", out=lo, in0=hi, scalar1=-1.0, scalar2=None,
         op0=ALU.mult)
    for it in range(42):
        S.do("vector", "tensor_tensor", mid, lo, hi, op=ALU.add)
        S.do("vector", "tensor_scalar", out=mid, in0=mid, scalar1=0.5,
             scalar2=None, op0=ALU.mult)
        S.do("tensor", "matmul", ps_A[:, 0:1], sc["ones_row"], mid,
             start=True, stop=True)
        S.do("vector", "tensor_copy", midb, ps_A[:, 0:1])
        S.do("vector", "tensor_scalar", out=zt1, in0=s128, scalar1=midb,
             scalar2=None, op0=ALU.is_gt)
        S.do("vector", "reduce_sum", zcnt, zt1, axis=AXL.X)
        S.do("tensor", "matmul", ps_A[0:1, 4:5], zcnt, sc["ones_col"],
             start=True, stop=True)
        S.do("vector", "tensor_copy", cnt, ps_A[0:1, 4:5])
        S.do("vector", "tensor_scalar", out=msk1, in0=cnt, scalar1=255.5,
             scalar2=None, op0=ALU.is_ge)
        S.do("vector", "tensor_tensor", btmp, mid, lo, op=ALU.subtract)
        S.do("vector", "tensor_scalar", out=btmp, in0=btmp, scalar1=msk1,
             scalar2=None, op0=ALU.mult)
        S.do("vector", "tensor_tensor", btmp2, lo, btmp, op=ALU.add)
        S.do("vector", "tensor_tensor", btmp, hi, mid, op=ALU.subtract)
        S.do("vector", "tensor_scalar", out=btmp, in0=btmp, scalar1=msk1,
             scalar2=None, op0=ALU.mult)
        S.do("vector", "tensor_tensor", hi, mid, btmp, op=ALU.add)
        S.do("vector", "tensor_copy", lo, btmp2)
    S.do("tensor", "matmul", ps_A[:, 0:1], sc["ones_row"], lo, start=True, stop=True)
    S.do("vector", "tensor_copy", midb, ps_A[:, 0:1])
    S.do("vector", "tensor_scalar", out=zt1, in0=s128, scalar1=midb,
         scalar2=None, op0=ALU.is_gt)
    pfa_ = nc.alloc_sbuf_tensor("pref_a", [128, 48], F32).ap()
    pfb_ = nc.alloc_sbuf_tensor("pref_b", [128, 48], F32).ap()
    S.do("vector", "memset", pfa_, 0.0)
    S.do("vector", "memset", pfb_, 0.0)
    S.do("vector", "tensor_copy", pfa_[:, 16:48], zt1)
    cur, nxt = pfa_, pfb_
    for k in (1, 2, 4, 8, 16):
        S.do("vector", "tensor_tensor", nxt[:, 16:48], cur[:, 16:48],
             cur[:, 16 - k : 48 - k], op=ALU.add)
        if k < 16:
            S.do("vector", "tensor_copy", nxt[:, 16 - k : 16], cur[:, 16 - k : 16])
        cur, nxt = nxt, cur
    S.do("tensor", "matmul", ps_A[:, 1:2], stu, cur[:, 47:48], start=True, stop=True)
    S.do("vector", "tensor_copy", zcnt, ps_A[:, 1:2])
    rnk = nc.alloc_sbuf_tensor("rnk", [128, 32], F32).ap()
    S.do("vector", "tensor_tensor", rnk, cur[:, 16:48], zt1, op=ALU.subtract)
    S.do("vector", "tensor_scalar", out=rnk, in0=rnk, scalar1=zcnt, scalar2=None,
         op0=ALU.add)
    S.do("vector", "tensor_scalar", out=rnk, in0=rnk, scalar1=9999.0, scalar2=None,
         op0=ALU.subtract)
    S.do("vector", "tensor_tensor", rnk, rnk, zt1, op=ALU.mult)
    S.do("vector", "tensor_scalar", out=rnk, in0=rnk, scalar1=9999.0, scalar2=None,
         op0=ALU.add)
    rnku = nc.alloc_sbuf_tensor("rnku", [128, 32], U32).ap()
    S.do("vector", "tensor_copy", rnku, rnk)
    ivu = nc.alloc_sbuf_tensor("ivu", [128, 32], U32).ap()
    S.do("gpsimd", "iota", ivu, pattern=[[1, 32]], base=0, channel_multiplier=32)
    for fcol in range(32):
        S.do("gpsimd", "indirect_dma_start",
             out=idxbuf,
             out_offset=bass_rust.IndirectOffsetOnAxis(
                 ap=rnku[:, fcol : fcol + 1], axis=0),
             in_=ivu[:, fcol : fcol + 1], in_offset=None,
             bounds_check=255, oob_is_err=False)
    idxsb = nc.alloc_sbuf_tensor("idxsb", [128, 2], U32).ap()
    S.do("sync", "dma_start", idxsb[:, 0:1], idxbuf[0:128])
    S.do("sync", "dma_start", idxsb[:, 1:2], idxbuf[128:256])
    selm = nc.alloc_sbuf_tensor("selm", [128, 1], F32).ap()
    selv_ = nc.alloc_sbuf_tensor("selv_", [128, 1], F32).ap()
    selr = nc.alloc_sbuf_tensor("selr", [128, 1], F32).ap()
    selTb = nc.alloc_sbuf_tensor("selTb", [128, 8 * 256], BF16).ap()
    selT3 = selTb.rearrange("p (c s) -> p c s", s=256)
    with nc.sbuf_tensor("selx", [128, D], F32) as selx_h, \
         nc.sbuf_tensor("sqsel", [128, D], F32) as sqsel_h, \
         nc.sbuf_tensor("selb", [128, D], BF16) as selb_h:
        selx, sqsel, selb = selx_h.ap(), sqsel_h.ap(), selb_h.ap()
        for g in range(2):
            S.do("gpsimd", "indirect_dma_start",
                 out=selx, out_offset=None, in_=xtok,
                 in_offset=bass_rust.IndirectOffsetOnAxis(
                     ap=idxsb[:, g : g + 1], axis=0),
                 bounds_check=SEQ - 1, oob_is_err=False)
            S.do("scalar", "activation", sqsel, selx, AF.Square)
            S.do("vector", "reduce_sum", selm, selx, axis=AXL.X)
            S.do("vector", "reduce_sum", selv_, sqsel, axis=AXL.X)
            S.do("vector", "tensor_scalar", out=selm, in0=selm, scalar1=1.0 / D,
                 scalar2=None, op0=ALU.mult)
            S.do("vector", "tensor_tensor", selr, selm, selm, op=ALU.mult)
            S.do("vector", "scalar_tensor_tensor", selv_, selv_, 1.0 / D, selr,
                 op0=ALU.mult, op1=ALU.subtract)
            S.do("scalar", "activation", selv_, selv_, AF.Ln, bias=EPS_LN)
            S.do("scalar", "activation", selr, selv_, AF.Exp, scale=-0.5)
            S.do("vector", "tensor_scalar", out=selb, in0=selx, scalar1=selm,
                 scalar2=selr, op0=ALU.subtract, op1=ALU.mult)
            for c in range(8):
                S.do("tensor", "transpose", ps_T,
                     selb[:, c * 128 : (c + 1) * 128], ident)
                S.do("vector", "tensor_copy", selT3[:, c, g * 128 : (g + 1) * 128],
                     ps_T)

    ckvb = nc.alloc_sbuf_tensor("ckvb", [32, KEYN], BF16).ap()
    rmsr = nc.alloc_sbuf_tensor("rmsr", [1, TW], F32).ap()
    nqb = nc.alloc_sbuf_tensor("nqb", [96, 2048], BF16).ap()
    rawb = nc.alloc_sbuf_tensor("rawb", [96, SEQ], BF16).ap()

    sqw = nc.alloc_sbuf_tensor("sqw", [96, TW], F32).ap()

    def rms_project(dst, Wsb, ncols, ntile, tok0, gvec, extra_scale=None):
        for tt in range(ntile):
            a = tok0 + tt * TW
            for c in range(8):
                S.do("tensor", "matmul", ps_A[:ncols, 0:TW], Wsb[:, c, 0:ncols],
                     lnb3[:, c, a : a + TW], start=(c == 0), stop=(c == 7))
            S.do("scalar", "activation", sqw[:ncols, :], ps_A[:ncols, 0:TW],
                 AF.Square)
            S.do("tensor", "matmul", ps_sum, sc["ones_col"][0:ncols],
                 sqw[:ncols, :], start=True, stop=True)
            S.do("scalar", "activation", rmsr, ps_sum, AF.Ln, bias=EPS_RMS,
                 scale=1.0 / ncols)
            S.do("scalar", "activation", rmsr, rmsr, AF.Exp, scale=-0.5)
            S.do("tensor", "matmul", ps_A[:ncols, TW : 2 * TW],
                 sc["ones_row"][:, 0:ncols], rmsr, start=True, stop=True)
            S.do("vector", "tensor_copy", sqw[:ncols, :],
                 ps_A[:ncols, TW : 2 * TW])
            S.do("vector", "tensor_tensor", sqw[:ncols, :], ps_A[:ncols, 0:TW],
                 sqw[:ncols, :], op=ALU.mult)
            if extra_scale is not None:
                S.do("vector", "tensor_scalar", out=sqw[:ncols, :],
                     in0=sqw[:ncols, :], scalar1=extra_scale, scalar2=None,
                     op0=ALU.mult)
            S.do("vector", "tensor_scalar", out=dst[:, tt * TW : (tt + 1) * TW],
                 in0=sqw[:ncols, :], scalar1=gvec, scalar2=None, op0=ALU.mult)

    rms_project(ckvb, ckv_s, 32, KEYN // TW, 0, gk)

    def rope_emit(dst, src, n32, cos_ap, sin_ap, tok0, ntok):
        for a0 in range(0, ntok, 1024):
            w_ = min(1024, ntok - a0)
            lo_s = src[n32 : n32 + 32, a0 : a0 + w_]
            hi_s = src[n32 + 32 : n32 + 64, a0 : a0 + w_]
            lo_d = dst[n32 : n32 + 32, a0 : a0 + w_]
            hi_d = dst[n32 + 32 : n32 + 64, a0 : a0 + w_]
            ct = cos_ap[:, tok0 + a0 : tok0 + a0 + w_]
            st = sin_ap[:, tok0 + a0 : tok0 + a0 + w_]
            l0 = ropes1[:, 0:w_]
            h0_ = ropes2[:, 0:w_]
            r1 = ropet1[:, 0:w_]
            r2 = ropet2[:, 0:w_]
            S.do("vector", "tensor_copy", l0, lo_s)
            S.do("vector", "tensor_copy", h0_, hi_s)
            S.do("vector", "tensor_tensor", r1, l0, ct, op=ALU.mult)
            S.do("vector", "tensor_tensor", r2, h0_, st, op=ALU.mult)
            S.do("vector", "tensor_tensor", lo_d, r1, r2, op=ALU.subtract)
            S.do("vector", "tensor_tensor", r1, l0, st, op=ALU.mult)
            S.do("vector", "tensor_tensor", r2, h0_, ct, op=ALU.mult)
            S.do("vector", "tensor_tensor", hi_d, r1, r2, op=ALU.add)

    ropet1 = nc.alloc_sbuf_tensor("ropet1", [32, 1024], BF16).ap()
    ropet2 = nc.alloc_sbuf_tensor("ropet2", [32, 1024], BF16).ap()
    ropes1 = nc.alloc_sbuf_tensor("ropes1", [32, 1024], BF16).ap()
    ropes2 = nc.alloc_sbuf_tensor("ropes2", [32, 1024], BF16).ap()

    k1h = nc.alloc_sbuf_tensor("k1h", [96, KEYN], BF16).ap()
    for tt in range(KEYN // TW):
        for c in range(8):
            S.do("tensor", "matmul", ps_A[:64, 0:TW], krop_s[:, c, :],
                 lnb3[:, c, tt * TW : (tt + 1) * TW], start=(c == 0), stop=(c == 7))
        S.do("scalar", "activation", rawb[32:64, tt * TW : (tt + 1) * TW],
             ps_A[0:32, 0:TW], AF.Copy, scale=1.0 / 16.0)
        S.do("scalar", "activation", rawb[64:96, tt * TW : (tt + 1) * TW],
             ps_A[32:64, 0:TW], AF.Copy, scale=1.0 / 16.0)
    rope_emit(k1h, rawb, 32, cosq, sinq, 0, KEYN)

    dq_s = nc.alloc_sbuf_tensor("dq_s", [96, 96], BF16).ap()
    dkn_s = nc.alloc_sbuf_tensor("dkn_s", [32, 32], BF16).ap()
    dv_s = nc.alloc_sbuf_tensor("dv_s", [32, 32], BF16).ap()
    selk_s = nc.alloc_sbuf_tensor("selk_s", [128, 8 * 96], BF16).ap()
    selk3 = selk_s.rearrange("p (c m) -> p c m", m=96)
    selv_s = nc.alloc_sbuf_tensor("selv_s", [128, 8 * 32], BF16).ap()
    selv3 = selv_s.rearrange("p (c m) -> p c m", m=32)
    wink_s = nc.alloc_sbuf_tensor("wink_s", [128, 8 * 96], BF16).ap()
    wink3 = wink_s.rearrange("p (c m) -> p c m", m=96)
    winv_s = nc.alloc_sbuf_tensor("winv_s", [128, 8 * 32], BF16).ap()
    winv3 = winv_s.rearrange("p (c m) -> p c m", m=32)
    wproj_s = emit_load_weight_bf16(S, nc, "wproj8", (256, 1024), 2)

    qt = nc.alloc_sbuf_tensor("qt", [96, 2048], BF16).ap()
    vtmp = nc.alloc_sbuf_tensor("vtmp", [32, TW], BF16).ap()
    v1T = nc.alloc_sbuf_tensor("v1T", [128, (KEYN // 128) * 32], BF16).ap()
    v1T3 = v1T.rearrange("p (k v) -> p k v", v=32)
    kwh = nc.alloc_sbuf_tensor("kwh", [96, KEYN], BF16).ap()
    vwT = nc.alloc_sbuf_tensor("vwT", [128, (KEYN // 128) * 32], BF16).ap()
    vwT3 = vwT.rearrange("p (k v) -> p k v", v=32)
    ksh = nc.alloc_sbuf_tensor("ksh", [96, 256], BF16).ap()
    vsT = nc.alloc_sbuf_tensor("vsT", [128, 2 * 32], BF16).ap()
    vsT3 = vsT.rearrange("p (k v) -> p k v", v=32)
    qnq = nc.alloc_sbuf_tensor("qnq", [1, TW], F32).ap()
    knsq = nc.alloc_sbuf_tensor("knsq", [1, TW], F32).ap()
    k1st = nc.alloc_sbuf_tensor("k1st", [1, 1], F32).ap()
    k2st = nc.alloc_sbuf_tensor("k2st", [1, 1], F32).ap()
    k3st = nc.alloc_sbuf_tensor("k3st", [1, 1], F32).ap()
    mtmp = nc.alloc_sbuf_tensor("mtmp1", [1, 1], F32).ap()
    negB = nc.alloc_sbuf_tensor("negB1", [1, TW], F32).ap()
    SCt = nc.alloc_sbuf_tensor("SCt1", [128, TW], BF16).ap()
    expSa = nc.alloc_sbuf_tensor("expS1", [128, TW], BF16).ap()
    expSb = nc.alloc_sbuf_tensor("expS1b", [128, TW], BF16).ap()
    expSs = (expSa, expSb)
    ssum = nc.alloc_sbuf_tensor("ssum1", [1, TW], F32).ap()
    rsum = nc.alloc_sbuf_tensor("rsum1", [1, TW], F32).ap()
    otmp = nc.alloc_sbuf_tensor("otmp1", [32, TW], F32).ap()
    oacc = nc.alloc_sbuf_tensor("oacc1", [32, TW], F32).ap()
    ostg = nc.alloc_sbuf_tensor("ostg1", [32, TW], BF16).ap()
    Oall = nc.alloc_sbuf_tensor("Oall1", [128, 2 * 2048], BF16).ap()
    Oall4 = Oall.rearrange("p (h t) -> p h t", h=2)
    yf = nc.alloc_sbuf_tensor("yf_a1", [128, TW], F32).ap()

    def norm_max(src_t, npart, ntt, dstmax, width=TW):
        for tt in range(ntt):
            S.do("scalar", "activation", sqw[:npart, 0:width],
                 src_t[:, tt * width : (tt + 1) * width], AF.Square)
            S.do("tensor", "matmul", ps_sum[:, 0:width], sc["ones_col"][0:npart],
                 sqw[:npart, 0:width], start=True, stop=True)
            S.do("scalar", "activation", knsq[:, 0:width], ps_sum[:, 0:width],
                 AF.Sqrt)
            if tt == 0:
                S.do("vector", "tensor_reduce", dstmax, knsq[:, 0:width],
                     axis=AXL.X, op=ALU.max)
            else:
                S.do("vector", "tensor_reduce", mtmp, knsq[:, 0:width],
                     axis=AXL.X, op=ALU.max)
                S.do("vector", "tensor_tensor", dstmax, dstmax, mtmp, op=ALU.max)

    for half_ in (0, 1):
      KEYNh = 2048 * (half_ + 1)
      Q0 = KEYNh - 2048
      youth = (yout0, yout1)[half_]
      rms_project(nqb, cq_s, 96, 4, Q0, gq)
      for h in range(8):
          S.do("gpsimd", "dma_start", dq_s, dq8d[:, h * 96 : (h + 1) * 96])
          S.do("gpsimd", "dma_start", dkn_s, dkn8d[:, h * 32 : (h + 1) * 32])
          S.do("gpsimd", "dma_start", dv_s, dv8d[:, h * 32 : (h + 1) * 32])
          S.do("gpsimd", "dma_start", selk3,
               selk8d[:, h * 96 : (h + 1) * 96].rearrange("(c p) m -> p c m", p=128))
          S.do("gpsimd", "dma_start", selv3,
               selv8d[:, h * 32 : (h + 1) * 32].rearrange("(c p) m -> p c m", p=128))
          S.do("gpsimd", "dma_start", wink3,
               wink8d[:, h * 96 : (h + 1) * 96].rearrange("(c p) m -> p c m", p=128))
          S.do("gpsimd", "dma_start", winv3,
               winv8d[:, h * 32 : (h + 1) * 32].rearrange("(c p) m -> p c m", p=128))

          for tt in range(KEYNh // TW):
              a = tt * TW
              S.do("tensor", "matmul", ps_A[:32, 0:TW], dkn_s, ckvb[:, a : a + TW],
                   start=True, stop=True)
              S.do("vector", "tensor_copy", k1h[0:32, a : a + TW], ps_A[:32, 0:TW])
              S.do("tensor", "matmul", ps_A[:32, TW : 2 * TW], dv_s,
                   ckvb[:, a : a + TW], start=True, stop=True)
              S.do("vector", "tensor_copy", vtmp, ps_A[:32, TW : 2 * TW])
              for j in range(4):
                  S.do("tensor", "transpose", ps_T[:, 0:32],
                       vtmp[:, j * 128 : (j + 1) * 128], ident[0:32, 0:32])
                  S.do("vector", "tensor_copy", v1T3[:, tt * 4 + j], ps_T[:, 0:32])
          for tt in range(KEYNh // TW):
              a = tt * TW
              for c in range(8):
                  S.do("tensor", "matmul", ps_A[:96, 0:TW], wink3[:, c],
                       lnb3[:, c, a : a + TW], start=(c == 0), stop=(c == 7))
              S.do("vector", "tensor_copy", rawb[:, a : a + TW], ps_A[:96, 0:TW])
              for c in range(8):
                  S.do("tensor", "matmul", ps_A[:32, TW : 2 * TW], winv3[:, c],
                       lnb3[:, c, a : a + TW], start=(c == 0), stop=(c == 7))
              S.do("vector", "tensor_copy", vtmp, ps_A[:32, TW : 2 * TW])
              for j in range(4):
                  S.do("tensor", "transpose", ps_T[:, 0:32],
                       vtmp[:, j * 128 : (j + 1) * 128], ident[0:32, 0:32])
                  S.do("vector", "tensor_copy", vwT3[:, tt * 4 + j], ps_T[:, 0:32])
          S.do("vector", "tensor_copy", kwh[0:32, 0:KEYNh], rawb[0:32, 0:KEYNh])
          rope_emit(kwh, rawb[:, 0:KEYNh], 32, cosq, sinq, 0, KEYNh)
          for c in range(8):
              S.do("tensor", "matmul", ps_A[:96, 0:256], selk3[:, c],
                   selT3[:, c], start=(c == 0), stop=(c == 7))
          S.do("vector", "tensor_copy", rawb[:, 0:256], ps_A[:96, 0:256])
          S.do("vector", "tensor_copy", ksh[0:32, :], rawb[0:32, 0:256])
          rope_emit(ksh, rawb[:, 0:256], 32, cosq, sinq, 0, 256)
          for c in range(8):
              S.do("tensor", "matmul", ps_A[:32, 256:512], selv3[:, c],
                   selT3[:, c], start=(c == 0), stop=(c == 7))
          S.do("vector", "tensor_copy", rawb[0:32, 512 : 512 + 256],
               ps_A[:32, 256:512])
          for g in range(2):
              S.do("tensor", "transpose", ps_T[:, 0:32],
                   rawb[0:32, 512 + g * 128 : 512 + (g + 1) * 128],
                   ident[0:32, 0:32])
              S.do("vector", "tensor_copy", vsT3[:, g], ps_T[:, 0:32])
          for tt in range(4):
              a = tt * TW
              S.do("tensor", "matmul", ps_A[:96, 0:TW], dq_s, nqb[:, a : a + TW],
                   start=True, stop=True)
              S.do("scalar", "activation", rawb[:, a : a + TW], ps_A[:96, 0:TW],
                   AF.Copy, scale=INV96)
          S.do("vector", "tensor_copy", qt[0:32, :], rawb[0:32, 0:2048])
          rope_emit(qt, rawb[:, 0:2048], 32, cosq, sinq, Q0, 2048)
          norm_max(k1h, 96, KEYNh // TW, k1st)
          norm_max(ksh, 96, 1, k2st, width=256)
          norm_max(kwh, 96, KEYNh // TW, k3st)

          for t in range(4):
              R = Q0 + t * TW
              nkc = (R + TW) // 128
              S.do("scalar", "activation", sqw, qt[:, t * TW : (t + 1) * TW],
                   AF.Square)
              S.do("tensor", "matmul", ps_sum, sc["ones_col"][0:96], sqw,
                   start=True, stop=True)
              S.do("scalar", "activation", qnq, ps_sum, AF.Sqrt)
              S.do("vector", "memset", oacc, 0.0)
              for br in range(3):
                  kst = (k1st, k2st, k3st)[br]
                  S.do("vector", "tensor_scalar", out=negB,
                       in0=qnq, scalar1=kst,
                       scalar2=None, op0=ALU.mult)
                  S.do("vector", "tensor_scalar", out=negB, in0=negB, scalar1=-1.0,
                       scalar2=None, op0=ALU.mult)
                  nkc_b = 2 if br == 1 else nkc
                  ksrc = (k1h, ksh, kwh)[br]
                  vsrc = (v1T3, vsT3, vwT3)[br]
                  tok0 = S.val
                  s_free = [tok0, tok0]
                  av_par = [tok0, tok0]
                  sc_free = tok0
                  for kc in range(nkc_b):
                      ksl = slice(kc * 128, (kc + 1) * 128)
                      par = kc % 2
                      psS = ps_Ss[par]
                      S.do("tensor", "matmul", psS, ksrc[:, ksl],
                           qt[:, t * TW : (t + 1) * TW], start=True, stop=False,
                           wait_val=s_free[par])
                      S.do("tensor", "matmul", psS, sc["ones_row"][:, 0:128], negB,
                           start=False, stop=True, wait_val=0)
                      s_tok = S.val
                      if br != 1 and 128 * kc + 127 > R:
                          dlt = 128 * kc - R
                          S.do("vector", "tensor_copy", SCt, psS,
                               wait_val=max(s_tok, sc_free))
                          s_free[par] = S.val
                          S.do("gpsimd", "dma_start", SCt,
                               maskd[:, 512 - dlt : 1024 - dlt], accum_op=ALU.add,
                               wait_val=S.val)
                          S.do("scalar", "activation", expSs[par], SCt, AF.Exp,
                               wait_val=max(S.val, av_par[par]))
                          sc_free = S.val
                      else:
                          S.do("scalar", "activation", expSs[par], psS, AF.Exp,
                               wait_val=max(s_tok, av_par[par]))
                          s_free[par] = S.val
                      exp_tok = S.val
                      S.do("tensor", "matmul", ps_sum, sc["ones_col_bf"],
                           expSs[par], start=(kc == 0), stop=(kc == nkc_b - 1),
                           wait_val=exp_tok)
                      g = kc % 2
                      S.do("tensor", "matmul", ps_O[32 * g : 32 * g + 32, :],
                           vsrc[:, kc], expSs[par], start=(kc == g),
                           stop=(kc >= nkc_b - 2), tile_position=(0, 32 * g),
                           wait_val=0)
                      av_par[par] = S.val
                  S.do("vector", "tensor_copy", ssum, ps_sum)
                  S.do("vector", "reciprocal", rsum, ssum)
                  S.do("vector", "tensor_scalar", out=rsum, in0=rsum,
                       scalar1=g123[:, br : br + 1], scalar2=None, op0=ALU.mult)
                  S.do("tensor", "matmul", ps_A[:32, 0:TW], sc["ones_row"][:, 0:32],
                       rsum, start=True, stop=True)
                  if nkc_b >= 2:
                      S.do("vector", "tensor_copy", otmp, ps_O[32:64, :])
                      S.do("vector", "tensor_tensor", otmp, ps_O[0:32, :], otmp,
                           op=ALU.add)
                  else:
                      S.do("vector", "tensor_copy", otmp, ps_O[0:32, :])
                  S.do("vector", "tensor_tensor", otmp, otmp, ps_A[:32, 0:TW],
                       op=ALU.mult)
                  S.do("vector", "tensor_tensor", oacc, oacc, otmp, op=ALU.add)
              hc, hb = h // 4, 32 * (h % 4)
              if hb == 96:
                  S.do("vector", "tensor_copy", ostg, oacc)
                  S.do("sync", "dma_start",
                       Oall4[96:128, hc, t * TW : (t + 1) * TW], ostg)
              else:
                  S.do("vector", "tensor_copy",
                       Oall4[hb : hb + 32, hc, t * TW : (t + 1) * TW], oacc)

      for t in range(4):
          for d_ in range(8):
              for hc in range(2):
                  S.do("tensor", "matmul", ps_A[:, 0:TW],
                       wproj_s[:, hc, d_ * 128 : (d_ + 1) * 128],
                       Oall4[:, hc, t * TW : (t + 1) * TW],
                       start=(hc == 0), stop=(hc == 1))
              S.do("vector", "tensor_copy", yf, ps_A[:, 0:TW])
              dst = youth[:, t * TW : (t + 1) * TW].rearrange("(c p) t -> p c t", p=128)[:, d_]
              S.do("sync", "dma_start", dst, yf)
    S.fence("sync")
    return nc


def rope_tables():
    f = 1.0 / (10000.0 ** (np.arange(0, 64, 2, dtype=np.float32) / 64.0))
    t = np.arange(SEQ, dtype=np.float32)
    a = np.outer(t, f)
    return np.ascontiguousarray(np.cos(a).T), np.ascontiguousarray(np.sin(a).T)


def attn1_in_maps(inputs, xT, xtokfull, half):
    cosq, sinq = rope_tables()
    g = lambda k: np.asarray(inputs["L1_attn_" + k], np.float32)
    w_dqn, w_dqr = g("w_dq_nope"), g("w_dq_rope")
    w_dkn, w_dv = g("w_dk_nope"), g("w_dv")
    w_selk, w_selv = g("w_selk"), g("w_selv")
    w_wink, w_winv = g("w_wink"), g("w_winv")
    w_proj = g("w_proj")

    def perm96(W, h0):
        raise RuntimeError

    def headblock_k(Wn, Wr, h0):
        cols = []
        for h in range(h0, h0 + 8):
            n = Wn[:, h * 32 : (h + 1) * 32]
            r = Wr[:, h * 64 : (h + 1) * 64]
            cols += [n, r[:, 0:32], r[:, 32:64]]
        return np.ascontiguousarray(np.concatenate(cols, axis=1))

    def headblock_kr(Wkr, h0):
        cols = []
        for h in range(h0, h0 + 8):
            blk = Wkr[:, h * 96 : (h + 1) * 96]
            cols += [blk[:, 0:32], blk[:, 32:64], blk[:, 64:96]]
        return np.ascontiguousarray(np.concatenate(cols, axis=1))

    def headblock_v(Wv, h0):
        return np.ascontiguousarray(Wv[:, h0 * 32 : (h0 + 8) * 32])

    maps = []
    for c in range(8):
        b, o = c // 2, c % 2
        h0 = o * 8
        wp8 = np.zeros((256, 1024), np.float32)
        for hp in range(8):
            src = w_proj[(h0 + hp) * 32 : (h0 + hp) * 32 + 32, :]
            wp8[(hp // 4) * 128 + (hp % 4) * 32 : (hp // 4) * 128 + (hp % 4) * 32 + 32] = src
        m = {
            "xt": f32(xT[:, b * SEQ : (b + 1) * SEQ]),
            "xtok": f32(xtokfull[b * SEQ : (b + 1) * SEQ, :]),
            "maskpl": make_mask_plane(),
            "ident": np.eye(128, dtype=np.float32),
            "stu": np.triu(np.ones((128, 128), np.float32), 1),
            "g_qnorm": g("g_qnorm").reshape(96, 1),
            "g_kvnorm": g("g_kvnorm").reshape(32, 1),
            "w_gate": g("w_gate"),
            "w_imp": f32(as_pcols(g("w_imp")[:, 0])),
            "w_cq": g("w_cq"),
            "w_ckv": g("w_ckv"),
            "w_krope": g("w_krope"),
            "dq8": headblock_k(w_dqn, w_dqr, h0),
            "dkn8": headblock_v(w_dkn, o * 8),
            "dv8": headblock_v(w_dv, o * 8),
            "selk8": headblock_kr(w_selk, h0),
            "selv8": headblock_v(w_selv, o * 8),
            "wink8": headblock_kr(w_wink, h0),
            "winv8": headblock_v(w_winv, o * 8),
            "wproj8": wp8,
            "cosq": cosq, "sinq": sinq,
            "ones_col": np.ones((128, 1), np.float32),
            "ones_row": np.ones((1, 128), np.float32),
        }
        maps.append(m)
    return maps


def run_attn1(inputs, xT):
    nc = get_prog("attn1", build_attn1)
    xtok = np.ascontiguousarray(xT.T)
    maps = attn1_in_maps(inputs, xT, xtok, 0)
    res = run_bass_kernel_spmd(nc, maps, core_ids=list(range(8)))
    return ([r["ypart0"] for r in res.results],
            [r["ypart1"] for r in res.results])





_PROG_CACHE = {}


def get_prog(name, builder, *args):
    key = (name, args)
    if key not in _PROG_CACHE:
        _PROG_CACHE[key] = builder(*args)
    return _PROG_CACHE[key]


def run_ffn0(xT, w, partials=None, bp=None):
    nc = get_prog("ffn0", build_ffn0, partials is not None)
    maps = ffn0_in_maps(xT, w["U1"], w["V1"], w["b1"], w["U2"], w["V2"], w["b2"],
                        partials=partials, bp=bp)
    res = run_bass_kernel_spmd(nc, maps, core_ids=list(range(8)))
    out = np.concatenate([r["yt"] for r in res.results], axis=1)
    return out


def assemble_partials(plist0, plist1):
    PA = np.zeros((D, NTOK), np.float32)
    PB = np.zeros((D, NTOK), np.float32)
    for c in range(8):
        b, half = c // 2, c % 2
        src = plist0 if half == 0 else plist1
        PA[:, c * TPC : (c + 1) * TPC] = src[2 * b]
        PB[:, c * TPC : (c + 1) * TPC] = src[2 * b + 1]
    return PA, PB


def kernel(**inputs):
    x = np.asarray(inputs["L0_attn_x"], np.float32).reshape(NTOK, D)
    xT = np.ascontiguousarray(x.T)
    w0 = {k: np.asarray(inputs["L0_ffn_" + k]) for k in
          ("U1", "V1", "b1", "U2", "V2", "b2")}
    xT = run_ffn0(xT, w0)
    Wq = np.asarray(inputs["L0_attn_Wq"])
    Wk = np.asarray(inputs["L0_attn_Wk"])
    Wv = np.asarray(inputs["L0_attn_Wv"])
    Wp = np.asarray(inputs["L0_attn_Wp"])
    bp = np.asarray(inputs["L0_attn_bp"])
    rel = np.asarray(inputs["L0_attn_rel_embed"])
    p0 = run_attn0(xT, Wq, Wk, Wv, Wp, rel, 0)
    p1 = run_attn0(xT, Wq, Wk, Wv, Wp, rel, 1)
    PA, PB = assemble_partials(p0, p1)
    xT = run_ffn0(xT, w0, partials=(PA, PB), bp=bp)
    wgT = np.ascontiguousarray(np.asarray(inputs["L1_ffn_w_gate"]).T)
    wuT = np.ascontiguousarray(np.asarray(inputs["L1_ffn_w_up"]).T)
    wdT = np.ascontiguousarray(np.asarray(inputs["L1_ffn_w_down"]).T)
    xT = run_ffn1(xT, wgT, wuT, wdT)
    q0, q1 = run_attn1(inputs, xT)
    QA, QB = assemble_partials(q0, q1)
    xT = run_ffn1(xT, wgT, wuT, wdT, partials=(QA, QB))
    out = np.ascontiguousarray(xT.T).reshape(4, 4096, 1024)
    return out


# version 08
# speedup vs baseline: 1.1490x; 1.0121x over previous
import numpy as np
import ml_dtypes
import concourse.bass as bass
import concourse.mybir as mybir
import bass_rust
from concourse.bass_utils import run_bass_kernel_spmd

F32 = mybir.dt.float32
BF16 = mybir.dt.bfloat16
I32 = mybir.dt.int32
U32 = mybir.dt.uint32
AF = mybir.ActivationFunctionType
ALU = mybir.AluOpType
AXL = mybir.AxisListType
BF = ml_dtypes.bfloat16

D = 1024
NTOK = 16384
TPC = 2048
NB = 4
SEQ = 4096

EPS_LN = 1e-5


class Ser:

    def __init__(self, nc):
        self.nc = nc
        self.sem = nc.alloc_semaphore()
        self.val = 0
        self.last_eng = None

    def _pre(self, eng_name):
        eng = getattr(self.nc, eng_name)
        if not (eng_name == "tensor" and self.last_eng == "tensor"):
            eng.wait_ge(self.sem, self.val)
        self.last_eng = eng_name
        return eng

    def do(self, eng_name, meth, *args, inc=None, wait_val=None, **kwargs):
        if wait_val is None:
            eng = self._pre(eng_name)
        else:
            eng = getattr(self.nc, eng_name)
            eng.wait_ge(self.sem, min(wait_val, self.val))
            self.last_eng = None
        inst = getattr(eng, meth)(*args, **kwargs)
        if inc is None:
            inc = 16 if "dma" in meth else 1
        inst.then_inc(self.sem, inc)
        self.val += inc
        return inst

    def fence(self, eng_name="sync"):
        eng = getattr(self.nc, eng_name)
        eng.wait_ge(self.sem, self.val)
        self.last_eng = None


def mk_nc():
    return bass.Bass("TRN2", target_bir_lowering=False)


def bf(x):
    return np.ascontiguousarray(x).astype(BF)


def f32(x):
    return np.ascontiguousarray(x, dtype=np.float32)


def as_pcols(v):
    v = np.asarray(v)
    return np.ascontiguousarray(v.reshape(-1, 128).T)



def reg_const(S, nc, val, dtype=F32, tag=None):
    tag = tag if tag is not None else str(abs(hash((float(val), str(dtype)))) % 10**8)
    t = nc.alloc_sbuf_tensor(f"const_{tag}", [128, 1], dtype).ap()
    S.do("vector", "memset", t, float(val))
    nc.const_aps.register(dtype, float(val), t)
    return t


def emit_load_weight_bf16(S, nc, name, arr_shape, nchunk_free):
    k, rem = divmod(arr_shape[0], 128)
    assert rem == 0
    m = arr_shape[1]
    d = nc.dram_tensor(name, list(arr_shape), F32, kind="ExternalInput").ap()
    s = nc.alloc_sbuf_tensor(name + "_sb", [128, k * m], BF16).ap()
    src = d.rearrange("(c p) m -> p c m", p=128)
    dst = s.rearrange("p (c m) -> p c m", m=m)
    S.do("gpsimd", "dma_start", dst, src)
    return s.rearrange("p (c m) -> p c m", m=m)


def emit_ln_feature_major(S, nc, xf, nchunks, tw, ones_sb, sc, tag):
    if "out3" in sc and sc["out3"] is not None:
        lnb3 = sc["out3"]
    else:
        lnb = nc.alloc_sbuf_tensor(f"lnb_{tag}", [128, nchunks * tw], BF16).ap()
        lnb3 = lnb.rearrange("p (c t) -> p c t", t=tw)
    ps_s = sc["ps_stat"]
    ps_b = sc["ps_bc"]
    nD = nchunks * 128
    for c in range(nchunks):
        S.do("scalar", "activation", sc["sq"], xf[:, c], AF.Square)
        S.do("tensor", "matmul", ps_s[:, 0:tw], ones_sb, xf[:, c],
             start=(c == 0), stop=(c == nchunks - 1))
        S.do("tensor", "matmul", ps_s[:, tw : 2 * tw], ones_sb, sc["sq"],
             start=(c == 0), stop=(c == nchunks - 1))
    S.do("scalar", "activation", sc["m"], ps_s[:, 0:tw], AF.Copy, scale=1.0 / nD)
    S.do("vector", "tensor_tensor", sc["v"], sc["m"], sc["m"], op=ALU.mult)
    S.do("vector", "scalar_tensor_tensor", sc["v2"], ps_s[:, tw : 2 * tw],
         1.0 / nD, sc["v"], op0=ALU.mult, op1=ALU.subtract)
    S.do("scalar", "activation", sc["v"], sc["v2"], AF.Ln, bias=EPS_LN)
    S.do("scalar", "activation", sc["r"], sc["v"], AF.Exp, scale=-0.5)
    S.do("vector", "tensor_tensor", sc["mr"], sc["m"], sc["r"], op=ALU.mult)
    S.do("tensor", "matmul", ps_b[:, 0:tw], sc["ones_row"], sc["r"],
         start=True, stop=True)
    S.do("tensor", "matmul", ps_b[:, tw : 2 * tw], sc["ones_row"], sc["mr"],
         start=True, stop=True)
    S.do("vector", "tensor_copy", sc["rb"], ps_b[:, 0:tw])
    S.do("vector", "tensor_copy", sc["mrb"], ps_b[:, tw : 2 * tw])
    for c in range(nchunks):
        S.do("scalar", "activation", sc["xb"], xf[:, c], AF.Copy)
        S.do("vector", "tensor_tensor", sc["xb2"], sc["xb"], sc["rb"], op=ALU.mult)
        S.do("vector", "tensor_tensor", lnb3[:, c], sc["xb2"], sc["mrb"],
             op=ALU.subtract)
    return lnb3


def alloc_ln_scratch(S, nc, tw, ps_bc, ps_stat=None, tag=""):
    sc = {}
    reg_const(S, nc, EPS_LN, tag="lneps")
    sc["sq"] = nc.alloc_sbuf_tensor(f"ln_sq{tag}", [128, tw], F32).ap()
    if ps_stat is None:
        ps_stat = nc.alloc_psum_tensor(f"ln_ps_s{tag}", [1, 2 * tw], F32).ap()
    sc["ps_stat"] = ps_stat
    sc["ps_bc"] = ps_bc
    for nm in ("m", "v", "v2", "r", "mr"):
        sc[nm] = nc.alloc_sbuf_tensor(f"ln_{nm}{tag}", [1, tw], F32).ap()
    sc["rb"] = nc.alloc_sbuf_tensor(f"ln_rb{tag}", [128, tw], BF16).ap()
    sc["mrb"] = nc.alloc_sbuf_tensor(f"ln_mrb{tag}", [128, tw], BF16).ap()
    sc["xb"] = nc.alloc_sbuf_tensor(f"ln_xb{tag}", [128, tw], BF16).ap()
    sc["xb2"] = nc.alloc_sbuf_tensor(f"ln_xb2{tag}", [128, tw], BF16).ap()
    ones = np.ones((128, 1), np.float32)
    onr = np.ones((1, 128), np.float32)
    sc["ones_col_host"] = ones
    sc["ones_row_host"] = onr
    ocd = nc.dram_tensor("ones_col", [128, 1], F32, kind="ExternalInput").ap()
    ord_ = nc.dram_tensor("ones_row", [1, 128], F32, kind="ExternalInput").ap()
    oc = nc.alloc_sbuf_tensor("ones_col_sb", [128, 1], F32).ap()
    orr = nc.alloc_sbuf_tensor("ones_row_sb", [1, 128], F32).ap()
    S.do("sync", "dma_start", oc, ocd)
    S.do("sync", "dma_start", orr, ord_)
    sc["ones_col"] = oc
    sc["ones_row"] = orr
    ocb = nc.alloc_sbuf_tensor("ones_col_bf", [128, 1], BF16).ap()
    orb = nc.alloc_sbuf_tensor("ones_row_bf", [1, 128], BF16).ap()
    S.do("gpsimd", "dma_start", ocb, ocd)
    S.do("gpsimd", "dma_start", orb, ord_)
    sc["ones_col_bf"] = ocb
    sc["ones_row_bf"] = orb
    return sc



def build_ffn0(merge_partials):
    nc = mk_nc()
    S = Ser(nc)
    TW = 512
    NT = TPC // TW
    xt = nc.dram_tensor("xt", [D, TPC], F32, kind="ExternalInput").ap()
    if merge_partials:
        pa = nc.dram_tensor("pa", [D, TPC], F32, kind="ExternalInput").ap()
        pb = nc.dram_tensor("pb", [D, TPC], F32, kind="ExternalInput").ap()
        bpv = nc.dram_tensor("bpv", [128, 8], F32, kind="ExternalInput").ap()
    yt = nc.dram_tensor("yt", [D, TPC], F32, kind="ExternalOutput").ap()

    U1 = emit_load_weight_bf16(S, nc, "U1", (1024, 256), 8)
    V1 = emit_load_weight_bf16(S, nc, "V1", (256, 4096), 2)
    U2 = emit_load_weight_bf16(S, nc, "U2", (4096, 256), 32)
    V2 = emit_load_weight_bf16(S, nc, "V2", (256, 1024), 2)
    b1d = nc.dram_tensor("b1v", [128, 32], F32, kind="ExternalInput").ap()
    b2d = nc.dram_tensor("b2v", [128, 8], F32, kind="ExternalInput").ap()
    b1s = nc.alloc_sbuf_tensor("b1s", [128, 32], F32).ap()
    b2s = nc.alloc_sbuf_tensor("b2s", [128, 8], F32).ap()
    S.do("sync", "dma_start", b1s, b1d)
    S.do("sync", "dma_start", b2s, b2d)
    if merge_partials:
        bps = nc.alloc_sbuf_tensor("bps", [128, 8], F32).ap()
        S.do("sync", "dma_start", bps, bpv)

    ps_A = nc.alloc_psum_tensor("ps_A", [128, 2 * TW], F32).ap()
    ps_B = nc.alloc_psum_tensor("ps_B", [128, 2 * TW], F32).ap()
    sc = alloc_ln_scratch(S, nc, TW, ps_A)

    xf = nc.alloc_sbuf_tensor("xf", [128, 8 * TW], F32).ap()
    xf3 = xf.rearrange("p (c t) -> p c t", t=TW)
    pf = nc.alloc_sbuf_tensor("pf", [128, 8 * TW], F32).ap() if merge_partials else None
    pf3 = pf.rearrange("p (c t) -> p c t", t=TW) if merge_partials else None
    Pb = nc.alloc_sbuf_tensor("Pb", [128, 2 * TW], BF16).ap()
    Pb3 = Pb.rearrange("p (c t) -> p c t", t=TW)
    Hb = nc.alloc_sbuf_tensor("Hb", [128, 32 * TW], BF16).ap()
    Hb3 = Hb.rearrange("p (c t) -> p c t", t=TW)
    Qb = nc.alloc_sbuf_tensor("Qb", [128, 2 * TW], BF16).ap()
    Qb3 = Qb.rearrange("p (c t) -> p c t", t=TW)
    yf = nc.alloc_sbuf_tensor("yf", [128, TW], F32).ap()
    ps_P = ps_A
    ps_H = ps_B
    ps_Y = ps_B

    for t in range(NT):
        tsl = slice(t * TW, (t + 1) * TW)
        src = xt[:, tsl].rearrange("(c p) t -> p c t", p=128)
        S.do("sync", "dma_start", xf3, src)
        if merge_partials:
            S.do("sync", "dma_start", pf3, pa[:, tsl].rearrange("(c p) t -> p c t", p=128))
            for c in range(8):
                S.do("vector", "tensor_tensor", xf3[:, c], xf3[:, c], pf3[:, c], op=ALU.add)
            S.do("sync", "dma_start", pf3, pb[:, tsl].rearrange("(c p) t -> p c t", p=128))
            for c in range(8):
                S.do("vector", "tensor_tensor", xf3[:, c], xf3[:, c], pf3[:, c], op=ALU.add)
                S.do("vector", "tensor_scalar", out=xf3[:, c], in0=xf3[:, c],
                     scalar1=bps[:, c : c + 1], scalar2=None, op0=ALU.add)
        lnb3 = emit_ln_feature_major(S, nc, xf3, 8, TW, sc["ones_col"], sc, f"t{t}")
        for o in range(2):
            for c in range(8):
                S.do("tensor", "matmul", ps_P[:, o * TW : (o + 1) * TW],
                     U1[:, c, o * 128 : (o + 1) * 128], lnb3[:, c],
                     start=(c == 0), stop=(c == 7))
            S.do("vector", "tensor_copy", Pb3[:, o], ps_P[:, o * TW : (o + 1) * TW])
        for f in range(32):
            fb = f % 2
            for c in range(2):
                S.do("tensor", "matmul", ps_H[:, fb * TW : (fb + 1) * TW],
                     V1[:, c, f * 128 : (f + 1) * 128], Pb3[:, c],
                     start=(c == 0), stop=(c == 1))
            S.do("scalar", "activation", Hb3[:, f], ps_H[:, fb * TW : (fb + 1) * TW],
                 AF.Gelu_apprx_tanh, bias=b1s[:, f : f + 1])
        for o in range(2):
            for c in range(32):
                S.do("tensor", "matmul", ps_P[:, o * TW : (o + 1) * TW],
                     U2[:, c, o * 128 : (o + 1) * 128], Hb3[:, c],
                     start=(c == 0), stop=(c == 31))
            S.do("vector", "tensor_copy", Qb3[:, o], ps_P[:, o * TW : (o + 1) * TW])
        for d_ in range(8):
            db = d_ % 2
            for c in range(2):
                S.do("tensor", "matmul", ps_Y[:, db * TW : (db + 1) * TW],
                     V2[:, c, d_ * 128 : (d_ + 1) * 128], Qb3[:, c],
                     start=(c == 0), stop=(c == 1))
            S.do("vector", "tensor_scalar", out=yf, in0=ps_Y[:, db * TW : (db + 1) * TW],
                 scalar1=b2s[:, d_ : d_ + 1], scalar2=0.5, op0=ALU.add, op1=ALU.mult)
            S.do("vector", "tensor_tensor", xf3[:, d_], yf, xf3[:, d_], op=ALU.add)
            dst = yt[:, tsl].rearrange("(c p) t -> p c t", p=128)[:, d_]
            S.do("sync", "dma_start", dst, xf3[:, d_])
    S.fence("sync")
    return nc


def ffn0_in_maps(xT, U1, V1, b1, U2, V2, b2, partials=None, bp=None):
    base = {
        "U1": f32(U1), "V1": f32(V1), "U2": f32(U2), "V2": f32(V2),
        "b1v": f32(as_pcols(b1)), "b2v": f32(as_pcols(b2)),
        "ones_col": np.ones((128, 1), np.float32),
        "ones_row": np.ones((1, 128), np.float32),
    }
    maps = []
    for c in range(8):
        m = dict(base)
        sl = slice(c * TPC, (c + 1) * TPC)
        m["xt"] = f32(xT[:, sl])
        if partials is not None:
            pa, pb_ = partials
            m["pa"] = f32(pa[:, sl])
            m["pb"] = f32(pb_[:, sl])
            m["bpv"] = f32(as_pcols(bp))
        maps.append(m)
    return maps



def build_ffn1(merge_partials, nh_lim=24, nd_lim=8):
    nc = mk_nc()
    S = Ser(nc)
    TW = 512
    NT = TPC // TW
    NH = 24
    xt = nc.dram_tensor("xt", [D, TPC], F32, kind="ExternalInput").ap()
    if merge_partials:
        pa = nc.dram_tensor("pa", [D, TPC], F32, kind="ExternalInput").ap()
        pb = nc.dram_tensor("pb", [D, TPC], F32, kind="ExternalInput").ap()
    yt = nc.dram_tensor("yt", [D, TPC], F32, kind="ExternalOutput").ap()
    wgT = nc.dram_tensor("wgT", [1024, 3072], I32, kind="ExternalInput").ap()
    wuT = nc.dram_tensor("wuT", [1024, 3072], I32, kind="ExternalInput").ap()
    wdT = nc.dram_tensor("wdT", [3072, 1024], I32, kind="ExternalInput").ap()

    ps_A = nc.alloc_psum_tensor("ps_A", [128, 2 * TW], F32).ap()
    ps_B = nc.alloc_psum_tensor("ps_B", [128, 2 * TW], F32).ap()
    sc = alloc_ln_scratch(S, nc, TW, ps_A)

    xf = nc.alloc_sbuf_tensor("xf", [128, 8 * TW], F32).ap()
    xf3 = xf.rearrange("p (c t) -> p c t", t=TW)
    lnball = nc.alloc_sbuf_tensor("lnball", [128, 8 * TPC], BF16).ap()
    lnb4 = lnball.rearrange("p (c u t) -> p c u t", c=8, u=NT)
    hb = nc.alloc_sbuf_tensor("hb", [128, NH * TPC], BF16).ap()
    hb4 = hb.rearrange("p (k u t) -> p k u t", k=NH, u=NT)
    wg_s = nc.alloc_sbuf_tensor("wg_s", [128, 8 * 128], BF16).ap()
    wg3 = wg_s.rearrange("p (c m) -> p c m", m=128)
    wu_s = nc.alloc_sbuf_tensor("wu_s", [128, 8 * 128], BF16).ap()
    wu3 = wu_s.rearrange("p (c m) -> p c m", m=128)
    wd_s = nc.alloc_sbuf_tensor("wd_s", [128, NH * 128], BF16).ap()
    wd3 = wd_s.rearrange("p (c m) -> p c m", m=128)
    sg = nc.alloc_sbuf_tensor("sg", [128, TW], BF16).ap()
    yf = nc.alloc_sbuf_tensor("yf", [128, TW], F32).ap()
    pfa = nc.alloc_sbuf_tensor("pfa", [128, TW], F32).ap() if merge_partials else None

    for t in range(NT):
        tsl = slice(t * TW, (t + 1) * TW)
        S.do("sync", "dma_start", xf3, xt[:, tsl].rearrange("(c p) t -> p c t", p=128))
        if merge_partials:
            for src in (pa, pb):
                for c in range(8):
                    S.do("sync", "dma_start", pfa,
                         src[:, tsl].rearrange("(c p) t -> p c t", p=128)[:, c])
                    S.do("vector", "tensor_tensor", xf3[:, c], xf3[:, c], pfa, op=ALU.add)
        sc["out3"] = lnb4[:, :, t]
        emit_ln_feature_major(S, nc, xf3, 8, TW, sc["ones_col"], sc, f"f1t{t}")
    sc["out3"] = None

    for h in range(nh_lim):
        hs = slice(h * 128, (h + 1) * 128)
        S.do("gpsimd", "dma_start", wg3, wgT[:, hs].rearrange("(c p) m -> p c m", p=128))
        S.do("gpsimd", "dma_start", wu3, wuT[:, hs].rearrange("(c p) m -> p c m", p=128))
        for t in range(NT):
            for c in range(8):
                S.do("tensor", "matmul", ps_A[:, 0:TW], wg3[:, c], lnb4[:, c, t],
                     start=(c == 0), stop=(c == 7))
            for c in range(8):
                S.do("tensor", "matmul", ps_A[:, TW : 2 * TW], wu3[:, c], lnb4[:, c, t],
                     start=(c == 0), stop=(c == 7))
            S.do("scalar", "activation", sg, ps_A[:, 0:TW], AF.Silu, scale=1.0 / 127.0)
            S.do("vector", "tensor_tensor", hb4[:, h, t], sg, ps_A[:, TW : 2 * TW],
                 op=ALU.mult)

    for d_ in range(nd_lim):
        ds = slice(d_ * 128, (d_ + 1) * 128)
        S.do("gpsimd", "dma_start", wd3, wdT[:, ds].rearrange("(c p) m -> p c m", p=128))
        for t in range(NT):
            tsl = slice(t * TW, (t + 1) * TW)
            for k in range(NH):
                S.do("tensor", "matmul", ps_B[:, 0:TW], wd3[:, k], hb4[:, k, t],
                     start=(k == 0), stop=(k == NH - 1))
            xsrc = xt[:, tsl].rearrange("(c p) t -> p c t", p=128)[:, d_]
            S.do("sync", "dma_start", yf, xsrc)
            if merge_partials:
                for src in (pa, pb):
                    S.do("sync", "dma_start", pfa,
                         src[:, tsl].rearrange("(c p) t -> p c t", p=128)[:, d_])
                    S.do("vector", "tensor_tensor", yf, yf, pfa, op=ALU.add)
            S.do("vector", "scalar_tensor_tensor", yf, ps_B[:, 0:TW],
                 0.5 / (127.0 * 127.0), yf, op0=ALU.mult, op1=ALU.add)
            dst = yt[:, tsl].rearrange("(c p) t -> p c t", p=128)[:, d_]
            S.do("sync", "dma_start", dst, yf)
    S.fence("sync")
    return nc


def ffn1_in_maps(xT, wgT, wuT, wdT, partials=None):
    base = {
        "wgT": wgT, "wuT": wuT, "wdT": wdT,
        "ones_col": np.ones((128, 1), np.float32),
        "ones_row": np.ones((1, 128), np.float32),
    }
    maps = []
    for c in range(8):
        m = dict(base)
        sl = slice(c * TPC, (c + 1) * TPC)
        m["xt"] = f32(xT[:, sl])
        if partials is not None:
            m["pa"] = f32(partials[0][:, sl])
            m["pb"] = f32(partials[1][:, sl])
        maps.append(m)
    return maps


def run_ffn1(xT, wgT, wuT, wdT, partials=None, nh_lim=24, nd_lim=8):
    nc = get_prog("ffn1", build_ffn1, partials is not None, nh_lim, nd_lim)
    maps = ffn1_in_maps(xT, wgT, wuT, wdT, partials=partials)
    res = run_bass_kernel_spmd(nc, maps, core_ids=list(range(8)))
    return np.concatenate([r["yt"] for r in res.results], axis=1)



MASKW = 1536
BIGNEG = -1.0e30


def make_mask_plane():
    k = np.arange(MASKW)[None, :]
    c = np.arange(128)[:, None]
    return np.where(k >= c + 512, 0.0, BIGNEG).astype(np.float32)


def build_attn0():
    KEYN = SEQ
    NH8 = 8
    HD = 64
    INV = 1.0 / np.sqrt(64.0)
    TW = 512
    NKT = KEYN // TW
    nc = mk_nc()
    S = Ser(nc)

    xt = nc.dram_tensor("xt", [D, KEYN], F32, kind="ExternalInput").ap()
    EXT = nc.dram_tensor("EXT", [NH8, 64, 2048], F32, kind="ExternalInput").ap()
    maskd = nc.dram_tensor("maskpl", [128, MASKW], F32, kind="ExternalInput").ap()
    yout0 = nc.dram_tensor("ypart0", [D, 2048], F32, kind="ExternalOutput").ap()
    yout1 = nc.dram_tensor("ypart1", [D, 2048], F32, kind="ExternalOutput").ap()
    idd = nc.dram_tensor("ident", [128, 128], F32, kind="ExternalInput").ap()
    V2 = nc.dram_tensor("V2scratch", [KEYN * 4097], BF16, kind="Internal").ap()
    V2r = V2.rearrange("(r c) -> r c", c=4097)

    ps_A = nc.alloc_psum_tensor("ps_A", [128, 2 * TW], F32).ap()
    ps_S = nc.alloc_psum_tensor("ps_S", [128, TW], F32).ap()
    ps_O = nc.alloc_psum_tensor("ps_O", [128, TW], F32).ap()
    ps_sum = nc.alloc_psum_tensor("ps_sum", [1, TW], F32).ap()
    ps_n = ps_sum
    ps_T = nc.alloc_psum_tensor("ps_T", [128, 64], BF16).ap()

    sc = alloc_ln_scratch(S, nc, TW, ps_A, ps_stat=ps_A[0:1, :])
    ident = nc.alloc_sbuf_tensor("ident_sb", [128, 128], BF16).ap()
    S.do("gpsimd", "dma_start", ident, idd)

    lnball = nc.alloc_sbuf_tensor("lnball", [128, 8 * KEYN], BF16).ap()
    lnb4 = lnball.rearrange("p (c u t) -> p c u t", c=8, u=NKT)

    zt = lnball[:, 0:4097]
    S.do("vector", "memset", lnball[:, 0 : 2 * 4097], 0.0)
    for r0 in range(0, KEYN, 128):
        nr = min(128, KEYN - r0)
        S.do("sync", "dma_start", V2r[r0 : r0 + nr, :], zt[0:nr, :])

    with nc.sbuf_tensor("xf", [128, 8 * TW], F32) as xf_h:
        xf3 = xf_h.ap().rearrange("p (c t) -> p c t", t=TW)
        for t in range(NKT):
            S.do("sync", "dma_start", xf3,
                 xt[:, t * TW : (t + 1) * TW].rearrange("(c p) t -> p c t", p=128))
            sc["out3"] = lnb4[:, :, t]
            emit_ln_feature_major(S, nc, xf3, 8, TW, sc["ones_col"], sc, f"a0t{t}")
        sc["out3"] = None
    lnb3 = lnball.rearrange("p (c t) -> p c t", t=KEYN)

    qS = nc.alloc_sbuf_tensor("qS", [64, 2048], BF16).ap()
    qA1 = nc.alloc_sbuf_tensor("qA", [64, 2048], BF16).ap()
    kh = nc.alloc_sbuf_tensor("kh", [64, KEYN], BF16).ap()
    vh = nc.alloc_sbuf_tensor("vh", [64, KEYN], BF16).ap()
    vT = nc.alloc_sbuf_tensor("vT", [128, (KEYN // 128) * 64], BF16).ap()
    vT3 = vT.rearrange("p (k v) -> p k v", v=64)
    exh = nc.alloc_sbuf_tensor("exh", [64, 2048], BF16).ap()
    asb = nc.alloc_sbuf_tensor("asb", [128, TW], BF16).ap()
    qn = nc.alloc_sbuf_tensor("qn", [1, 2050], F32).ap()
    qnA = nc.alloc_sbuf_tensor("qnA", [1, 2048], F32).ap()
    kstar = nc.alloc_sbuf_tensor("kstar", [1, 1], F32).ap()
    mtmp = nc.alloc_sbuf_tensor("mtmp", [1, 1], F32).ap()
    estar = nc.alloc_sbuf_tensor("estar", [1, 1], F32).ap()
    sqt = nc.alloc_sbuf_tensor("sqt", [64, TW], F32).ap()
    negB = nc.alloc_sbuf_tensor("negB", [1, TW], F32).ap()
    nB1 = nc.alloc_sbuf_tensor("nB1", [1, TW], F32).ap()
    SCt = nc.alloc_sbuf_tensor("SCt", [128, TW], BF16).ap()
    SRt0 = nc.alloc_sbuf_tensor("SRt0", [128, TW], BF16, is_transpose_dest=True).ap()
    SRt1 = nc.alloc_sbuf_tensor("SRt1", [128, TW], BF16, is_transpose_dest=True).ap()
    SRts = (SRt0, SRt1)
    expS0 = nc.alloc_sbuf_tensor("expS0", [128, TW], BF16).ap()
    expS1x = nc.alloc_sbuf_tensor("expS1x", [128, TW], BF16).ap()
    expSs = (expS0, expS1x)
    ssum = nc.alloc_sbuf_tensor("ssum", [1, TW], F32).ap()
    rsum = nc.alloc_sbuf_tensor("rsum", [1, TW], F32).ap()
    otmp = nc.alloc_sbuf_tensor("otmp", [64, TW], F32).ap()
    osum = nc.alloc_sbuf_tensor("osum", [64, TW], F32).ap()
    Oall = nc.alloc_sbuf_tensor("Oall", [128, 4 * 2048], BF16).ap()
    Oall4 = Oall.rearrange("p (h t) -> p h t", h=4)
    yf = nc.alloc_sbuf_tensor("yf_a", [128, TW], F32).ap()

    Wq_s = emit_load_weight_bf16(S, nc, "WqT_x", (1024, 512), 8)

    def project2(dst, Wsb, col0, tok0, ntok, scale=None):
        for tt in range(ntok // TW):
            a = tok0 + tt * TW
            for c in range(8):
                S.do("tensor", "matmul", ps_A[:64, 0:TW],
                     Wsb[:, c, col0 : col0 + 64],
                     lnb3[:, c, a : a + TW], start=(c == 0), stop=(c == 7))
            if scale is None:
                S.do("vector", "tensor_copy", dst[:, tt * TW : (tt + 1) * TW],
                     ps_A[:64, 0:TW])
            else:
                S.do("scalar", "activation", dst[:, tt * TW : (tt + 1) * TW],
                     ps_A[:64, 0:TW], AF.Copy, scale=scale)

    Wk_s = emit_load_weight_bf16(S, nc, "WkT_x", (1024, 512), 8)
    Wv_s = emit_load_weight_bf16(S, nc, "WvT_x", (1024, 512), 8)
    Wp_s = emit_load_weight_bf16(S, nc, "WpT_x", (512, 1024), 8)

    for half_ in (0, 1):
      KEYNh = 2048 * (half_ + 1)
      Q0 = KEYNh - 2048
      qA = qA1 if half_ == 1 else qS
      qAn = qnA if half_ == 1 else qn[:, 0:2048]
      youth = (yout0, yout1)[half_]
      for h in range(NH8):
          c0 = h * 64
          project2(qS, Wq_s, c0, Q0, 2048, scale=INV)
          if half_ == 1:
              project2(qA, Wq_s, c0, Q0 - 1, 2048, scale=INV)
          project2(kh, Wk_s, c0, 0, KEYNh)
          project2(vh, Wv_s, c0, 0, KEYNh)
          for kc in range(KEYNh // 128):
              S.do("tensor", "transpose", ps_T, vh[:, kc * 128 : (kc + 1) * 128],
                   ident[0:64, 0:64])
              S.do("vector", "tensor_copy", vT3[:, kc], ps_T)
          S.do("gpsimd", "dma_start", exh, EXT[h])
          S.do("vector", "memset", qn[:, 0:2], 0.0)
          for tt in range(4):
              S.do("scalar", "activation", sqt, qS[:, tt * TW : (tt + 1) * TW], AF.Square)
              S.do("tensor", "matmul", ps_n, sc["ones_col"][0:64], sqt,
                   start=True, stop=True)
              S.do("scalar", "activation", qn[:, 1 + tt * TW : 1 + (tt + 1) * TW],
                   ps_n, AF.Sqrt)
          qAn = qnA if half_ == 1 else qn[:, 0:2048]
          if half_ == 1:
              for tt in range(4):
                  S.do("scalar", "activation", sqt, qA[:, tt * TW : (tt + 1) * TW],
                       AF.Square)
                  S.do("tensor", "matmul", ps_n, sc["ones_col"][0:64], sqt,
                       start=True, stop=True)
                  S.do("scalar", "activation", qnA[:, tt * TW : (tt + 1) * TW],
                       ps_n, AF.Sqrt)
          for which, src_t, ntt, dstmax in ((0, kh, KEYNh // TW, kstar),
                                            (1, exh, 4, estar)):
              for tt in range(ntt):
                  S.do("scalar", "activation", sqt, src_t[:, tt * TW : (tt + 1) * TW],
                       AF.Square)
                  S.do("tensor", "matmul", ps_n, sc["ones_col"][0:64], sqt,
                       start=True, stop=True)
                  S.do("scalar", "activation", rsum, ps_n, AF.Sqrt)
                  if tt == 0:
                      S.do("vector", "tensor_reduce", dstmax, rsum, axis=AXL.X,
                           op=ALU.max)
                  else:
                      S.do("vector", "tensor_reduce", mtmp, rsum, axis=AXL.X,
                           op=ALU.max)
                      S.do("vector", "tensor_tensor", dstmax, dstmax, mtmp, op=ALU.max)

          for a in range(16):
              i0 = a * 128
              gi0 = (Q0 - 1 if half_ == 1 else 0) + i0
              for jq in range(4):
                  j0 = jq * TW
                  need2 = (j0 + TW - 1) >= (2048 - (gi0 + 127) - 1)
                  need1 = (gi0 + 127 >= 2048) and ((j0 + TW - 1) >= (4096 - (gi0 + 127) - 1))
                  if not (need2 or need1):
                      continue
                  S.do("tensor", "matmul", ps_A[:, 0:TW],
                       qA[:, i0 : i0 + 128], exh[:, j0 : j0 + TW],
                       start=True, stop=True)
                  S.do("vector", "tensor_copy", asb, ps_A[:, 0:TW])
                  if need2:
                      S.do("sync", "dma_start",
                           V2r[gi0 : gi0 + 128, 2049 + j0 : 2049 + j0 + TW], asb)
                  if need1:
                      S.do("sync", "dma_start",
                           V2r[gi0 : gi0 + 128, 1 + j0 : 1 + j0 + TW], asb)

          v2done = S.val
          for t in range(4):
              R = Q0 + t * TW
              nkc = (R + TW) // 128
              S.do("vector", "tensor_scalar", out=negB,
                   in0=qn[:, 1 + t * TW : 1 + (t + 1) * TW], scalar1=kstar,
                   scalar2=None, op0=ALU.mult)
              S.do("vector", "tensor_scalar", out=nB1,
                   in0=qAn[:, t * TW : (t + 1) * TW], scalar1=estar,
                   scalar2=None, op0=ALU.mult)
              S.do("vector", "tensor_tensor", negB, negB, nB1, op=ALU.add)
              S.do("vector", "tensor_scalar", out=negB, in0=negB, scalar1=-1.0,
                   scalar2=None, op0=ALU.mult)
              tok0 = S.val
              ev_done = tok0
              sc_free = tok0
              srt_free = [tok0, tok0]
              av_par = [tok0, tok0]
              last_mask = tok0
              last_srel = tok0
              for kc in range(nkc):
                  par = kc % 2
                  off = R * 4096 + kc * 128
                  src = bass_rust.AP(V2.tensor, off, [[4096, TW], [1, 128]])
                  S.do("sync", "dma_start", SRts[par], src, transpose=True,
                       wait_val=None)
                  dma_tok = S.val
                  last_srel = S.val
                  S.do("tensor", "matmul", ps_S, kh[:, kc * 128 : (kc + 1) * 128],
                       qS[:, t * TW : (t + 1) * TW], start=True, stop=False,
                       wait_val=ev_done)
                  S.do("tensor", "matmul", ps_S, sc["ones_row"][:, 0:128], negB,
                       start=False, stop=True, wait_val=0)
                  s_tok = S.val
                  S.do("vector", "tensor_tensor", SCt, ps_S, SRts[par], op=ALU.add,
                       wait_val=max(s_tok, dma_tok, sc_free))
                  ev_done = S.val
                  srt_free[par] = S.val
                  mask_tok = ev_done
                  if 128 * kc + 127 > R:
                      dlt = 128 * kc - R
                      S.do("gpsimd", "dma_start", SCt,
                           maskd[:, 512 - dlt : 1024 - dlt], accum_op=ALU.add,
                           wait_val=None)
                      mask_tok = S.val
                      last_mask = S.val
                  S.do("scalar", "activation", expSs[par], SCt, AF.Exp,
                       wait_val=max(mask_tok, av_par[par]))
                  exp_tok = S.val
                  sc_free = exp_tok
                  S.do("tensor", "matmul", ps_sum, sc["ones_col_bf"], expSs[par],
                       start=(kc == 0), stop=(kc == nkc - 1), wait_val=exp_tok)
                  g = kc % 2
                  last = kc >= nkc - 2
                  S.do("tensor", "matmul", ps_O[64 * g : 64 * g + 64, :],
                       vT3[:, kc], expSs[par], start=(kc == g), stop=last,
                       tile_position=(0, 64 * g), wait_val=0)
                  av_par[par] = S.val
              S.do("vector", "tensor_copy", ssum, ps_sum)
              S.do("vector", "reciprocal", rsum, ssum)
              S.do("tensor", "matmul", ps_A[:64, 0:TW], sc["ones_row"][:, 0:64], rsum,
                   start=True, stop=True)
              S.do("vector", "tensor_copy", otmp, ps_O[64:128, :])
              S.do("vector", "tensor_tensor", osum, ps_O[0:64, :], otmp, op=ALU.add)
              hc, hp = h // 2, (h % 2) * 64
              S.do("vector", "tensor_tensor",
                   Oall4[hp : hp + 64, hc, t * TW : (t + 1) * TW],
                   osum, ps_A[:64, 0:TW], op=ALU.mult)

      for t in range(4):
          for d_ in range(8):
              for hc in range(4):
                  S.do("tensor", "matmul", ps_A[:, 0:TW],
                       Wp_s[:, hc, d_ * 128 : (d_ + 1) * 128],
                       Oall4[:, hc, t * TW : (t + 1) * TW],
                       start=(hc == 0), stop=(hc == 3))
              S.do("vector", "tensor_copy", yf, ps_A[:, 0:TW])
              dst = youth[:, t * TW : (t + 1) * TW].rearrange("(c p) t -> p c t", p=128)[:, d_]
              S.do("sync", "dma_start", dst, yf)
    S.fence("sync")
    return nc


def attn0_in_maps(xT, Wq, Wk, Wv, Wp, rel_embed, half):
    KEYN = SEQ
    rel = np.asarray(rel_embed, np.float32).reshape(2048, 16, 64)
    maps = []
    for c in range(8):
        b, o = c // 2, c % 2
        hsl = slice(o * 512, (o + 1) * 512)
        m = {
            "xt": f32(xT[:, b * SEQ : b * SEQ + KEYN]),
            "WqT_x": f32(Wq.T[:, hsl]),
            "WkT_x": f32(Wk.T[:, hsl]),
            "WvT_x": f32(Wv.T[:, hsl]),
            "WpT_x": f32(Wp.T[hsl, :]),
            "EXT": np.ascontiguousarray(
                rel[:, o * 8 : (o + 1) * 8, :].transpose(1, 2, 0)).astype(np.float32),
            "maskpl": make_mask_plane(),
            "ident": np.eye(128, dtype=np.float32),
            "ones_col": np.ones((128, 1), np.float32),
            "ones_row": np.ones((1, 128), np.float32),
        }
        maps.append(m)
    return maps


def run_attn0(xT, Wq, Wk, Wv, Wp, rel_embed):
    nc = get_prog("attn0", build_attn0)
    maps = attn0_in_maps(xT, Wq, Wk, Wv, Wp, rel_embed, 1)
    res = run_bass_kernel_spmd(nc, maps, core_ids=list(range(8)))
    return ([r["ypart0"] for r in res.results],
            [r["ypart1"] for r in res.results])



EPS_RMS = 1e-6


def build_attn1():
    KEYN = SEQ
    INV96 = 1.0 / np.sqrt(96.0)
    TW = 512
    nc = mk_nc()
    S = Ser(nc)

    xt = nc.dram_tensor("xt", [D, SEQ], F32, kind="ExternalInput").ap()
    xtok = nc.dram_tensor("xtok", [SEQ, D], F32, kind="ExternalInput").ap()
    maskd = nc.dram_tensor("maskpl", [128, MASKW], F32, kind="ExternalInput").ap()
    idd = nc.dram_tensor("ident", [128, 128], F32, kind="ExternalInput").ap()
    stud = nc.dram_tensor("stu", [128, 128], F32, kind="ExternalInput").ap()
    yout0 = nc.dram_tensor("ypart0", [D, 2048], F32, kind="ExternalOutput").ap()
    yout1 = nc.dram_tensor("ypart1", [D, 2048], F32, kind="ExternalOutput").ap()
    gqd = nc.dram_tensor("g_qnorm", [96, 1], F32, kind="ExternalInput").ap()
    gkd = nc.dram_tensor("g_kvnorm", [32, 1], F32, kind="ExternalInput").ap()
    wgated = nc.dram_tensor("w_gate", [1024, 3], F32, kind="ExternalInput").ap()
    wimpd = nc.dram_tensor("w_imp", [128, 8], F32, kind="ExternalInput").ap()
    dq8d = nc.dram_tensor("dq8", [96, 768], F32, kind="ExternalInput").ap()
    dkn8d = nc.dram_tensor("dkn8", [32, 256], F32, kind="ExternalInput").ap()
    dv8d = nc.dram_tensor("dv8", [32, 256], F32, kind="ExternalInput").ap()
    selk8d = nc.dram_tensor("selk8", [1024, 768], F32, kind="ExternalInput").ap()
    selv8d = nc.dram_tensor("selv8", [1024, 256], F32, kind="ExternalInput").ap()
    wink8d = nc.dram_tensor("wink8", [1024, 768], F32, kind="ExternalInput").ap()
    winv8d = nc.dram_tensor("winv8", [1024, 256], F32, kind="ExternalInput").ap()
    cosqd = nc.dram_tensor("cosq", [32, SEQ], F32, kind="ExternalInput").ap()
    sinqd = nc.dram_tensor("sinq", [32, SEQ], F32, kind="ExternalInput").ap()
    sscr = nc.dram_tensor("sscr", [SEQ], F32, kind="Internal").ap()
    idxbuf = nc.dram_tensor("idxbuf", [256, 1], U32, kind="Internal").ap()

    ps_A = nc.alloc_psum_tensor("ps_A", [128, 2 * TW], F32).ap()
    ps_S = nc.alloc_psum_tensor("ps_S", [128, TW], F32).ap()
    ps_S2 = nc.alloc_psum_tensor("ps_S2", [128, TW], F32).ap()
    ps_O = nc.alloc_psum_tensor("ps_O", [128, TW], F32).ap()
    ps_sum = nc.alloc_psum_tensor("ps_sum", [1, TW], F32).ap()
    ps_T = nc.alloc_psum_tensor("ps_T", [128, 128], BF16).ap()
    ps_Ss = (ps_S, ps_S2)

    sc = alloc_ln_scratch(S, nc, TW, ps_A, ps_stat=ps_A[0:1, :])
    reg_const(S, nc, EPS_RMS, tag="rmseps")
    ident = nc.alloc_sbuf_tensor("ident_sb", [128, 128], BF16).ap()
    stu = nc.alloc_sbuf_tensor("stu_sb", [128, 128], F32).ap()
    S.do("gpsimd", "dma_start", ident, idd)
    S.do("sync", "dma_start", stu, stud)

    cq_s = emit_load_weight_bf16(S, nc, "w_cq", (1024, 96), 8)
    ckv_s = emit_load_weight_bf16(S, nc, "w_ckv", (1024, 32), 8)
    krop_s = emit_load_weight_bf16(S, nc, "w_krope", (1024, 64), 8)
    gq = nc.alloc_sbuf_tensor("gq_sb", [96, 1], F32).ap()
    gk = nc.alloc_sbuf_tensor("gk_sb", [32, 1], F32).ap()
    wgate_s = nc.alloc_sbuf_tensor("wgate_sb", [128, 8 * 3], F32).ap()
    wimp_s = nc.alloc_sbuf_tensor("wimp_sb", [128, 8], F32).ap()
    S.do("sync", "dma_start", gq, gqd)
    S.do("sync", "dma_start", gk, gkd)
    S.do("sync", "dma_start", wgate_s.rearrange("p (c m) -> p c m", m=3),
         wgated.rearrange("(c p) m -> p c m", p=128))
    S.do("sync", "dma_start", wimp_s, wimpd)
    cosq = nc.alloc_sbuf_tensor("cosq_sb", [32, SEQ], BF16).ap()
    sinq = nc.alloc_sbuf_tensor("sinq_sb", [32, SEQ], BF16).ap()
    S.do("gpsimd", "dma_start", cosq, cosqd)
    S.do("gpsimd", "dma_start", sinq, sinqd)

    lnball = nc.alloc_sbuf_tensor("lnball", [128, 8 * SEQ], BF16).ap()
    lnb4 = lnball.rearrange("p (c u t) -> p c u t", c=8, u=SEQ // TW)
    lnb3 = lnball.rearrange("p (c t) -> p c t", t=SEQ)

    cwb = nc.alloc_sbuf_tensor("cwb", [1, 1], F32).ap()
    sX = nc.alloc_sbuf_tensor("sX", [128, 8], F32).ap()
    srow = nc.alloc_sbuf_tensor("srow", [1, TW], F32).ap()
    srow2 = nc.alloc_sbuf_tensor("srow2", [1, TW], F32).ap()
    g123 = nc.alloc_sbuf_tensor("g123", [1, 3], F32).ap()
    gtmp = nc.alloc_sbuf_tensor("gtmp", [1, 3], F32).ap()
    g1s = nc.alloc_sbuf_tensor("g1s", [1, 1], F32).ap()

    for c in range(8):
        S.do("tensor", "matmul", ps_A[0:1, 0:1], wimp_s[:, c : c + 1],
             sc["ones_col"], start=(c == 0), stop=(c == 7))
    S.do("vector", "tensor_copy", cwb, ps_A[0:1, 0:1])

    with nc.sbuf_tensor("xf", [128, 8 * TW], F32) as xf_h:
        xf3 = xf_h.ap().rearrange("p (c t) -> p c t", t=TW)
        for t in range(SEQ // TW):
            S.do("sync", "dma_start", xf3,
                 xt[:, t * TW : (t + 1) * TW].rearrange("(c p) t -> p c t", p=128))
            sc["out3"] = lnb4[:, :, t]
            emit_ln_feature_major(S, nc, xf3, 8, TW, sc["ones_col"], sc, f"a1t{t}")
            for c in range(8):
                S.do("tensor", "matmul", ps_A[0:1, TW : 2 * TW],
                     wimp_s[:, c : c + 1], xf3[:, c], start=(c == 0), stop=(c == 7))
            S.do("vector", "tensor_scalar", out=srow, in0=sc["m"], scalar1=cwb,
                 scalar2=None, op0=ALU.mult)
            S.do("vector", "tensor_tensor", srow2, ps_A[0:1, TW : 2 * TW], srow,
                 op=ALU.subtract)
            S.do("vector", "tensor_tensor", srow, srow2, sc["r"], op=ALU.mult)
            S.do("sync", "dma_start",
                 sscr[t * TW : (t + 1) * TW].rearrange("(a t) -> a t", a=1),
                 srow[0:1, :])
        sc["out3"] = None
    for c in range(8):
        S.do("vector", "reduce_sum", sX[:, c : c + 1], lnb3[:, c], axis=AXL.X)
    for c in range(8):
        S.do("tensor", "matmul", ps_A[0:1, 0:3], sX[:, c : c + 1],
             wgate_s.rearrange("p (c m) -> p c m", m=3)[:, c],
             start=(c == 0), stop=(c == 7))
    S.do("scalar", "activation", g123, ps_A[0:1, 0:3], AF.Copy, scale=1.0 / SEQ)
    S.do("vector", "tensor_reduce", g1s, g123, axis=AXL.X, op=ALU.max)
    S.do("vector", "tensor_scalar", out=g1s, in0=g1s, scalar1=-1.0, scalar2=None,
         op0=ALU.mult)
    S.do("scalar", "activation", gtmp, g123, AF.Exp, bias=g1s)
    S.do("vector", "reduce_sum", g1s, gtmp, axis=AXL.X)
    S.do("vector", "reciprocal", g1s, g1s)
    S.do("vector", "tensor_scalar", out=g123, in0=gtmp, scalar1=g1s, scalar2=None,
         op0=ALU.mult)

    s128 = nc.alloc_sbuf_tensor("s128", [128, 32], F32).ap()
    S.do("sync", "dma_start", s128, sscr.rearrange("(p f) -> p f", f=32))
    lo = nc.alloc_sbuf_tensor("lo_t", [1, 1], F32).ap()
    hi = nc.alloc_sbuf_tensor("hi_t", [1, 1], F32).ap()
    mid = nc.alloc_sbuf_tensor("mid_t", [1, 1], F32).ap()
    cnt = nc.alloc_sbuf_tensor("cnt_t", [1, 1], F32).ap()
    msk1 = nc.alloc_sbuf_tensor("msk1", [1, 1], F32).ap()
    btmp = nc.alloc_sbuf_tensor("btmp", [1, 1], F32).ap()
    btmp2 = nc.alloc_sbuf_tensor("btmp2", [1, 1], F32).ap()
    midb = nc.alloc_sbuf_tensor("midb", [128, 1], F32).ap()
    zt1 = nc.alloc_sbuf_tensor("zt1", [128, 32], F32).ap()
    zcnt = nc.alloc_sbuf_tensor("zcnt", [128, 1], F32).ap()
    absm = nc.alloc_sbuf_tensor("absm", [128, 1], F32).ap()
    idf = nc.alloc_sbuf_tensor("idf_sb", [128, 128], F32).ap()
    S.do("sync", "dma_start", idf, idd)
    S.do("vector", "tensor_reduce", absm, s128, axis=AXL.X, op=ALU.max,
         apply_absolute_value=True)
    S.do("tensor", "matmul", ps_A[0:1, 0:128], absm, idf, start=True, stop=True)
    S.do("vector", "tensor_copy", srow[:, 0:128], ps_A[0:1, 0:128])
    S.do("vector", "tensor_reduce", hi, srow[:, 0:128], axis=AXL.X, op=ALU.max)
    S.do("vector", "tensor_scalar", out=hi, in0=hi, scalar1=1.0, scalar2=None,
         op0=ALU.add)
    S.do("vector", "tensor_scalar", out=lo, in0=hi, scalar1=-1.0, scalar2=None,
         op0=ALU.mult)
    for it in range(42):
        S.do("vector", "tensor_tensor", mid, lo, hi, op=ALU.add)
        S.do("vector", "tensor_scalar", out=mid, in0=mid, scalar1=0.5,
             scalar2=None, op0=ALU.mult)
        S.do("tensor", "matmul", ps_A[:, 0:1], sc["ones_row"], mid,
             start=True, stop=True)
        S.do("vector", "tensor_copy", midb, ps_A[:, 0:1])
        S.do("vector", "tensor_scalar", out=zt1, in0=s128, scalar1=midb,
             scalar2=None, op0=ALU.is_gt)
        S.do("vector", "reduce_sum", zcnt, zt1, axis=AXL.X)
        S.do("tensor", "matmul", ps_A[0:1, 4:5], zcnt, sc["ones_col"],
             start=True, stop=True)
        S.do("vector", "tensor_copy", cnt, ps_A[0:1, 4:5])
        S.do("vector", "tensor_scalar", out=msk1, in0=cnt, scalar1=255.5,
             scalar2=None, op0=ALU.is_ge)
        S.do("vector", "tensor_tensor", btmp, mid, lo, op=ALU.subtract)
        S.do("vector", "tensor_scalar", out=btmp, in0=btmp, scalar1=msk1,
             scalar2=None, op0=ALU.mult)
        S.do("vector", "tensor_tensor", btmp2, lo, btmp, op=ALU.add)
        S.do("vector", "tensor_tensor", btmp, hi, mid, op=ALU.subtract)
        S.do("vector", "tensor_scalar", out=btmp, in0=btmp, scalar1=msk1,
             scalar2=None, op0=ALU.mult)
        S.do("vector", "tensor_tensor", hi, mid, btmp, op=ALU.add)
        S.do("vector", "tensor_copy", lo, btmp2)
    S.do("tensor", "matmul", ps_A[:, 0:1], sc["ones_row"], lo, start=True, stop=True)
    S.do("vector", "tensor_copy", midb, ps_A[:, 0:1])
    S.do("vector", "tensor_scalar", out=zt1, in0=s128, scalar1=midb,
         scalar2=None, op0=ALU.is_gt)
    pfa_ = nc.alloc_sbuf_tensor("pref_a", [128, 48], F32).ap()
    pfb_ = nc.alloc_sbuf_tensor("pref_b", [128, 48], F32).ap()
    S.do("vector", "memset", pfa_, 0.0)
    S.do("vector", "memset", pfb_, 0.0)
    S.do("vector", "tensor_copy", pfa_[:, 16:48], zt1)
    cur, nxt = pfa_, pfb_
    for k in (1, 2, 4, 8, 16):
        S.do("vector", "tensor_tensor", nxt[:, 16:48], cur[:, 16:48],
             cur[:, 16 - k : 48 - k], op=ALU.add)
        if k < 16:
            S.do("vector", "tensor_copy", nxt[:, 16 - k : 16], cur[:, 16 - k : 16])
        cur, nxt = nxt, cur
    S.do("tensor", "matmul", ps_A[:, 1:2], stu, cur[:, 47:48], start=True, stop=True)
    S.do("vector", "tensor_copy", zcnt, ps_A[:, 1:2])
    rnk = nc.alloc_sbuf_tensor("rnk", [128, 32], F32).ap()
    S.do("vector", "tensor_tensor", rnk, cur[:, 16:48], zt1, op=ALU.subtract)
    S.do("vector", "tensor_scalar", out=rnk, in0=rnk, scalar1=zcnt, scalar2=None,
         op0=ALU.add)
    S.do("vector", "tensor_scalar", out=rnk, in0=rnk, scalar1=9999.0, scalar2=None,
         op0=ALU.subtract)
    S.do("vector", "tensor_tensor", rnk, rnk, zt1, op=ALU.mult)
    S.do("vector", "tensor_scalar", out=rnk, in0=rnk, scalar1=9999.0, scalar2=None,
         op0=ALU.add)
    rnku = nc.alloc_sbuf_tensor("rnku", [128, 32], U32).ap()
    S.do("vector", "tensor_copy", rnku, rnk)
    ivu = nc.alloc_sbuf_tensor("ivu", [128, 32], U32).ap()
    S.do("gpsimd", "iota", ivu, pattern=[[1, 32]], base=0, channel_multiplier=32)
    for fcol in range(32):
        S.do("gpsimd", "indirect_dma_start",
             out=idxbuf,
             out_offset=bass_rust.IndirectOffsetOnAxis(
                 ap=rnku[:, fcol : fcol + 1], axis=0),
             in_=ivu[:, fcol : fcol + 1], in_offset=None,
             bounds_check=255, oob_is_err=False)
    idxsb = nc.alloc_sbuf_tensor("idxsb", [128, 2], U32).ap()
    S.do("sync", "dma_start", idxsb[:, 0:1], idxbuf[0:128])
    S.do("sync", "dma_start", idxsb[:, 1:2], idxbuf[128:256])
    selm = nc.alloc_sbuf_tensor("selm", [128, 1], F32).ap()
    selv_ = nc.alloc_sbuf_tensor("selv_", [128, 1], F32).ap()
    selr = nc.alloc_sbuf_tensor("selr", [128, 1], F32).ap()
    selTb = nc.alloc_sbuf_tensor("selTb", [128, 8 * 256], BF16).ap()
    selT3 = selTb.rearrange("p (c s) -> p c s", s=256)
    with nc.sbuf_tensor("selx", [128, D], F32) as selx_h, \
         nc.sbuf_tensor("sqsel", [128, D], F32) as sqsel_h, \
         nc.sbuf_tensor("selb", [128, D], BF16) as selb_h:
        selx, sqsel, selb = selx_h.ap(), sqsel_h.ap(), selb_h.ap()
        for g in range(2):
            S.do("gpsimd", "indirect_dma_start",
                 out=selx, out_offset=None, in_=xtok,
                 in_offset=bass_rust.IndirectOffsetOnAxis(
                     ap=idxsb[:, g : g + 1], axis=0),
                 bounds_check=SEQ - 1, oob_is_err=False)
            S.do("scalar", "activation", sqsel, selx, AF.Square)
            S.do("vector", "reduce_sum", selm, selx, axis=AXL.X)
            S.do("vector", "reduce_sum", selv_, sqsel, axis=AXL.X)
            S.do("vector", "tensor_scalar", out=selm, in0=selm, scalar1=1.0 / D,
                 scalar2=None, op0=ALU.mult)
            S.do("vector", "tensor_tensor", selr, selm, selm, op=ALU.mult)
            S.do("vector", "scalar_tensor_tensor", selv_, selv_, 1.0 / D, selr,
                 op0=ALU.mult, op1=ALU.subtract)
            S.do("scalar", "activation", selv_, selv_, AF.Ln, bias=EPS_LN)
            S.do("scalar", "activation", selr, selv_, AF.Exp, scale=-0.5)
            S.do("vector", "tensor_scalar", out=selb, in0=selx, scalar1=selm,
                 scalar2=selr, op0=ALU.subtract, op1=ALU.mult)
            for c in range(8):
                S.do("tensor", "transpose", ps_T,
                     selb[:, c * 128 : (c + 1) * 128], ident)
                S.do("vector", "tensor_copy", selT3[:, c, g * 128 : (g + 1) * 128],
                     ps_T)

    ckvb = nc.alloc_sbuf_tensor("ckvb", [32, KEYN], BF16).ap()
    rmsr = nc.alloc_sbuf_tensor("rmsr", [1, TW], F32).ap()
    nqb = nc.alloc_sbuf_tensor("nqb", [96, 2048], BF16).ap()
    rawb = nc.alloc_sbuf_tensor("rawb", [96, SEQ], BF16).ap()

    sqw = nc.alloc_sbuf_tensor("sqw", [96, TW], F32).ap()

    def rms_project(dst, Wsb, ncols, ntile, tok0, gvec, extra_scale=None):
        for tt in range(ntile):
            a = tok0 + tt * TW
            for c in range(8):
                S.do("tensor", "matmul", ps_A[:ncols, 0:TW], Wsb[:, c, 0:ncols],
                     lnb3[:, c, a : a + TW], start=(c == 0), stop=(c == 7))
            S.do("scalar", "activation", sqw[:ncols, :], ps_A[:ncols, 0:TW],
                 AF.Square)
            S.do("tensor", "matmul", ps_sum, sc["ones_col"][0:ncols],
                 sqw[:ncols, :], start=True, stop=True)
            S.do("scalar", "activation", rmsr, ps_sum, AF.Ln, bias=EPS_RMS,
                 scale=1.0 / ncols)
            S.do("scalar", "activation", rmsr, rmsr, AF.Exp, scale=-0.5)
            S.do("tensor", "matmul", ps_A[:ncols, TW : 2 * TW],
                 sc["ones_row"][:, 0:ncols], rmsr, start=True, stop=True)
            S.do("vector", "tensor_copy", sqw[:ncols, :],
                 ps_A[:ncols, TW : 2 * TW])
            S.do("vector", "tensor_tensor", sqw[:ncols, :], ps_A[:ncols, 0:TW],
                 sqw[:ncols, :], op=ALU.mult)
            if extra_scale is not None:
                S.do("vector", "tensor_scalar", out=sqw[:ncols, :],
                     in0=sqw[:ncols, :], scalar1=extra_scale, scalar2=None,
                     op0=ALU.mult)
            S.do("vector", "tensor_scalar", out=dst[:, tt * TW : (tt + 1) * TW],
                 in0=sqw[:ncols, :], scalar1=gvec, scalar2=None, op0=ALU.mult)

    rms_project(ckvb, ckv_s, 32, KEYN // TW, 0, gk)

    def rope_emit(dst, src, n32, cos_ap, sin_ap, tok0, ntok):
        for a0 in range(0, ntok, 1024):
            w_ = min(1024, ntok - a0)
            lo_s = src[n32 : n32 + 32, a0 : a0 + w_]
            hi_s = src[n32 + 32 : n32 + 64, a0 : a0 + w_]
            lo_d = dst[n32 : n32 + 32, a0 : a0 + w_]
            hi_d = dst[n32 + 32 : n32 + 64, a0 : a0 + w_]
            ct = cos_ap[:, tok0 + a0 : tok0 + a0 + w_]
            st = sin_ap[:, tok0 + a0 : tok0 + a0 + w_]
            l0 = ropes1[:, 0:w_]
            h0_ = ropes2[:, 0:w_]
            r1 = ropet1[:, 0:w_]
            r2 = ropet2[:, 0:w_]
            S.do("vector", "tensor_copy", l0, lo_s)
            S.do("vector", "tensor_copy", h0_, hi_s)
            S.do("vector", "tensor_tensor", r1, l0, ct, op=ALU.mult)
            S.do("vector", "tensor_tensor", r2, h0_, st, op=ALU.mult)
            S.do("vector", "tensor_tensor", lo_d, r1, r2, op=ALU.subtract)
            S.do("vector", "tensor_tensor", r1, l0, st, op=ALU.mult)
            S.do("vector", "tensor_tensor", r2, h0_, ct, op=ALU.mult)
            S.do("vector", "tensor_tensor", hi_d, r1, r2, op=ALU.add)

    ropet1 = nc.alloc_sbuf_tensor("ropet1", [32, 1024], BF16).ap()
    ropet2 = nc.alloc_sbuf_tensor("ropet2", [32, 1024], BF16).ap()
    ropes1 = nc.alloc_sbuf_tensor("ropes1", [32, 1024], BF16).ap()
    ropes2 = nc.alloc_sbuf_tensor("ropes2", [32, 1024], BF16).ap()

    k1h = nc.alloc_sbuf_tensor("k1h", [96, KEYN], BF16).ap()
    for tt in range(KEYN // TW):
        for c in range(8):
            S.do("tensor", "matmul", ps_A[:64, 0:TW], krop_s[:, c, :],
                 lnb3[:, c, tt * TW : (tt + 1) * TW], start=(c == 0), stop=(c == 7))
        S.do("scalar", "activation", rawb[32:64, tt * TW : (tt + 1) * TW],
             ps_A[0:32, 0:TW], AF.Copy, scale=1.0 / 16.0)
        S.do("scalar", "activation", rawb[64:96, tt * TW : (tt + 1) * TW],
             ps_A[32:64, 0:TW], AF.Copy, scale=1.0 / 16.0)
    rope_emit(k1h, rawb, 32, cosq, sinq, 0, KEYN)

    dq_s = nc.alloc_sbuf_tensor("dq_s", [96, 96], BF16).ap()
    dkn_s = nc.alloc_sbuf_tensor("dkn_s", [32, 32], BF16).ap()
    dv_s = nc.alloc_sbuf_tensor("dv_s", [32, 32], BF16).ap()
    selk_s = nc.alloc_sbuf_tensor("selk_s", [128, 8 * 96], BF16).ap()
    selk3 = selk_s.rearrange("p (c m) -> p c m", m=96)
    selv_s = nc.alloc_sbuf_tensor("selv_s", [128, 8 * 32], BF16).ap()
    selv3 = selv_s.rearrange("p (c m) -> p c m", m=32)
    wink_s = nc.alloc_sbuf_tensor("wink_s", [128, 8 * 96], BF16).ap()
    wink3 = wink_s.rearrange("p (c m) -> p c m", m=96)
    winv_s = nc.alloc_sbuf_tensor("winv_s", [128, 8 * 32], BF16).ap()
    winv3 = winv_s.rearrange("p (c m) -> p c m", m=32)
    wproj_s = emit_load_weight_bf16(S, nc, "wproj8", (256, 1024), 2)

    qt = nc.alloc_sbuf_tensor("qt", [96, 2048], BF16).ap()
    vtmp = nc.alloc_sbuf_tensor("vtmp", [32, TW], BF16).ap()
    v1T = nc.alloc_sbuf_tensor("v1T", [128, (KEYN // 128) * 32], BF16).ap()
    v1T3 = v1T.rearrange("p (k v) -> p k v", v=32)
    kwh = nc.alloc_sbuf_tensor("kwh", [96, KEYN], BF16).ap()
    vwT = nc.alloc_sbuf_tensor("vwT", [128, (KEYN // 128) * 32], BF16).ap()
    vwT3 = vwT.rearrange("p (k v) -> p k v", v=32)
    ksh = nc.alloc_sbuf_tensor("ksh", [96, 256], BF16).ap()
    vsT = nc.alloc_sbuf_tensor("vsT", [128, 2 * 32], BF16).ap()
    vsT3 = vsT.rearrange("p (k v) -> p k v", v=32)
    qnq = nc.alloc_sbuf_tensor("qnq", [1, TW], F32).ap()
    knsq = nc.alloc_sbuf_tensor("knsq", [1, TW], F32).ap()
    k1st = nc.alloc_sbuf_tensor("k1st", [1, 1], F32).ap()
    k2st = nc.alloc_sbuf_tensor("k2st", [1, 1], F32).ap()
    k3st = nc.alloc_sbuf_tensor("k3st", [1, 1], F32).ap()
    mtmp = nc.alloc_sbuf_tensor("mtmp1", [1, 1], F32).ap()
    negB = nc.alloc_sbuf_tensor("negB1", [1, TW], F32).ap()
    SCt = nc.alloc_sbuf_tensor("SCt1", [128, TW], BF16).ap()
    expSa = nc.alloc_sbuf_tensor("expS1", [128, TW], BF16).ap()
    expSb = nc.alloc_sbuf_tensor("expS1b", [128, TW], BF16).ap()
    expSs = (expSa, expSb)
    ssum = nc.alloc_sbuf_tensor("ssum1", [1, TW], F32).ap()
    rsum = nc.alloc_sbuf_tensor("rsum1", [1, TW], F32).ap()
    otmp = nc.alloc_sbuf_tensor("otmp1", [32, TW], F32).ap()
    oacc = nc.alloc_sbuf_tensor("oacc1", [32, TW], F32).ap()
    ostg = nc.alloc_sbuf_tensor("ostg1", [32, TW], BF16).ap()
    Oall = nc.alloc_sbuf_tensor("Oall1", [128, 2 * 2048], BF16).ap()
    Oall4 = Oall.rearrange("p (h t) -> p h t", h=2)
    yf = nc.alloc_sbuf_tensor("yf_a1", [128, TW], F32).ap()

    def norm_max(src_t, npart, ntt, dstmax, width=TW):
        for tt in range(ntt):
            S.do("scalar", "activation", sqw[:npart, 0:width],
                 src_t[:, tt * width : (tt + 1) * width], AF.Square)
            S.do("tensor", "matmul", ps_sum[:, 0:width], sc["ones_col"][0:npart],
                 sqw[:npart, 0:width], start=True, stop=True)
            S.do("scalar", "activation", knsq[:, 0:width], ps_sum[:, 0:width],
                 AF.Sqrt)
            if tt == 0:
                S.do("vector", "tensor_reduce", dstmax, knsq[:, 0:width],
                     axis=AXL.X, op=ALU.max)
            else:
                S.do("vector", "tensor_reduce", mtmp, knsq[:, 0:width],
                     axis=AXL.X, op=ALU.max)
                S.do("vector", "tensor_tensor", dstmax, dstmax, mtmp, op=ALU.max)

    for half_ in (0, 1):
      KEYNh = 2048 * (half_ + 1)
      Q0 = KEYNh - 2048
      youth = (yout0, yout1)[half_]
      rms_project(nqb, cq_s, 96, 4, Q0, gq)
      for h in range(8):
          S.do("gpsimd", "dma_start", dq_s, dq8d[:, h * 96 : (h + 1) * 96])
          S.do("gpsimd", "dma_start", dkn_s, dkn8d[:, h * 32 : (h + 1) * 32])
          S.do("gpsimd", "dma_start", dv_s, dv8d[:, h * 32 : (h + 1) * 32])
          S.do("gpsimd", "dma_start", selk3,
               selk8d[:, h * 96 : (h + 1) * 96].rearrange("(c p) m -> p c m", p=128))
          S.do("gpsimd", "dma_start", selv3,
               selv8d[:, h * 32 : (h + 1) * 32].rearrange("(c p) m -> p c m", p=128))
          S.do("gpsimd", "dma_start", wink3,
               wink8d[:, h * 96 : (h + 1) * 96].rearrange("(c p) m -> p c m", p=128))
          S.do("gpsimd", "dma_start", winv3,
               winv8d[:, h * 32 : (h + 1) * 32].rearrange("(c p) m -> p c m", p=128))

          for tt in range(KEYNh // TW):
              a = tt * TW
              S.do("tensor", "matmul", ps_A[:32, 0:TW], dkn_s, ckvb[:, a : a + TW],
                   start=True, stop=True)
              S.do("vector", "tensor_copy", k1h[0:32, a : a + TW], ps_A[:32, 0:TW])
              S.do("tensor", "matmul", ps_A[:32, TW : 2 * TW], dv_s,
                   ckvb[:, a : a + TW], start=True, stop=True)
              S.do("vector", "tensor_copy", vtmp, ps_A[:32, TW : 2 * TW])
              for j in range(4):
                  S.do("tensor", "transpose", ps_T[:, 0:32],
                       vtmp[:, j * 128 : (j + 1) * 128], ident[0:32, 0:32])
                  S.do("vector", "tensor_copy", v1T3[:, tt * 4 + j], ps_T[:, 0:32])
          for tt in range(KEYNh // TW):
              a = tt * TW
              for c in range(8):
                  S.do("tensor", "matmul", ps_A[:96, 0:TW], wink3[:, c],
                       lnb3[:, c, a : a + TW], start=(c == 0), stop=(c == 7))
              S.do("vector", "tensor_copy", rawb[:, a : a + TW], ps_A[:96, 0:TW])
              for c in range(8):
                  S.do("tensor", "matmul", ps_A[:32, TW : 2 * TW], winv3[:, c],
                       lnb3[:, c, a : a + TW], start=(c == 0), stop=(c == 7))
              S.do("vector", "tensor_copy", vtmp, ps_A[:32, TW : 2 * TW])
              for j in range(4):
                  S.do("tensor", "transpose", ps_T[:, 0:32],
                       vtmp[:, j * 128 : (j + 1) * 128], ident[0:32, 0:32])
                  S.do("vector", "tensor_copy", vwT3[:, tt * 4 + j], ps_T[:, 0:32])
          S.do("vector", "tensor_copy", kwh[0:32, 0:KEYNh], rawb[0:32, 0:KEYNh])
          rope_emit(kwh, rawb[:, 0:KEYNh], 32, cosq, sinq, 0, KEYNh)
          for c in range(8):
              S.do("tensor", "matmul", ps_A[:96, 0:256], selk3[:, c],
                   selT3[:, c], start=(c == 0), stop=(c == 7))
          S.do("vector", "tensor_copy", rawb[:, 0:256], ps_A[:96, 0:256])
          S.do("vector", "tensor_copy", ksh[0:32, :], rawb[0:32, 0:256])
          rope_emit(ksh, rawb[:, 0:256], 32, cosq, sinq, 0, 256)
          for c in range(8):
              S.do("tensor", "matmul", ps_A[:32, 256:512], selv3[:, c],
                   selT3[:, c], start=(c == 0), stop=(c == 7))
          S.do("vector", "tensor_copy", rawb[0:32, 512 : 512 + 256],
               ps_A[:32, 256:512])
          for g in range(2):
              S.do("tensor", "transpose", ps_T[:, 0:32],
                   rawb[0:32, 512 + g * 128 : 512 + (g + 1) * 128],
                   ident[0:32, 0:32])
              S.do("vector", "tensor_copy", vsT3[:, g], ps_T[:, 0:32])
          for tt in range(4):
              a = tt * TW
              S.do("tensor", "matmul", ps_A[:96, 0:TW], dq_s, nqb[:, a : a + TW],
                   start=True, stop=True)
              S.do("scalar", "activation", rawb[:, a : a + TW], ps_A[:96, 0:TW],
                   AF.Copy, scale=INV96)
          S.do("vector", "tensor_copy", qt[0:32, :], rawb[0:32, 0:2048])
          rope_emit(qt, rawb[:, 0:2048], 32, cosq, sinq, Q0, 2048)
          norm_max(k1h, 96, KEYNh // TW, k1st)
          norm_max(ksh, 96, 1, k2st, width=256)
          norm_max(kwh, 96, KEYNh // TW, k3st)

          for t in range(4):
              R = Q0 + t * TW
              nkc = (R + TW) // 128
              S.do("scalar", "activation", sqw, qt[:, t * TW : (t + 1) * TW],
                   AF.Square)
              S.do("tensor", "matmul", ps_sum, sc["ones_col"][0:96], sqw,
                   start=True, stop=True)
              S.do("scalar", "activation", qnq, ps_sum, AF.Sqrt)
              S.do("vector", "memset", oacc, 0.0)
              for br in range(3):
                  kst = (k1st, k2st, k3st)[br]
                  S.do("vector", "tensor_scalar", out=negB,
                       in0=qnq, scalar1=kst,
                       scalar2=None, op0=ALU.mult)
                  S.do("vector", "tensor_scalar", out=negB, in0=negB, scalar1=-1.0,
                       scalar2=None, op0=ALU.mult)
                  nkc_b = 2 if br == 1 else nkc
                  ksrc = (k1h, ksh, kwh)[br]
                  vsrc = (v1T3, vsT3, vwT3)[br]
                  tok0 = S.val
                  s_free = [tok0, tok0]
                  av_par = [tok0, tok0]
                  sc_free = tok0
                  for kc in range(nkc_b):
                      ksl = slice(kc * 128, (kc + 1) * 128)
                      par = kc % 2
                      psS = ps_Ss[par]
                      S.do("tensor", "matmul", psS, ksrc[:, ksl],
                           qt[:, t * TW : (t + 1) * TW], start=True, stop=False,
                           wait_val=s_free[par])
                      S.do("tensor", "matmul", psS, sc["ones_row"][:, 0:128], negB,
                           start=False, stop=True, wait_val=0)
                      s_tok = S.val
                      if br != 1 and 128 * kc + 127 > R:
                          dlt = 128 * kc - R
                          S.do("vector", "tensor_copy", SCt, psS,
                               wait_val=max(s_tok, sc_free))
                          s_free[par] = S.val
                          S.do("gpsimd", "dma_start", SCt,
                               maskd[:, 512 - dlt : 1024 - dlt], accum_op=ALU.add,
                               wait_val=S.val)
                          S.do("scalar", "activation", expSs[par], SCt, AF.Exp,
                               wait_val=max(S.val, av_par[par]))
                          sc_free = S.val
                      else:
                          S.do("scalar", "activation", expSs[par], psS, AF.Exp,
                               wait_val=max(s_tok, av_par[par]))
                          s_free[par] = S.val
                      exp_tok = S.val
                      S.do("tensor", "matmul", ps_sum, sc["ones_col_bf"],
                           expSs[par], start=(kc == 0), stop=(kc == nkc_b - 1),
                           wait_val=exp_tok)
                      g = kc % 2
                      S.do("tensor", "matmul", ps_O[32 * g : 32 * g + 32, :],
                           vsrc[:, kc], expSs[par], start=(kc == g),
                           stop=(kc >= nkc_b - 2), tile_position=(0, 32 * g),
                           wait_val=0)
                      av_par[par] = S.val
                  S.do("vector", "tensor_copy", ssum, ps_sum)
                  S.do("vector", "reciprocal", rsum, ssum)
                  S.do("vector", "tensor_scalar", out=rsum, in0=rsum,
                       scalar1=g123[:, br : br + 1], scalar2=None, op0=ALU.mult)
                  S.do("tensor", "matmul", ps_A[:32, 0:TW], sc["ones_row"][:, 0:32],
                       rsum, start=True, stop=True)
                  if nkc_b >= 2:
                      S.do("vector", "tensor_copy", otmp, ps_O[32:64, :])
                      S.do("vector", "tensor_tensor", otmp, ps_O[0:32, :], otmp,
                           op=ALU.add)
                  else:
                      S.do("vector", "tensor_copy", otmp, ps_O[0:32, :])
                  S.do("vector", "tensor_tensor", otmp, otmp, ps_A[:32, 0:TW],
                       op=ALU.mult)
                  S.do("vector", "tensor_tensor", oacc, oacc, otmp, op=ALU.add)
              hc, hb = h // 4, 32 * (h % 4)
              if hb == 96:
                  S.do("vector", "tensor_copy", ostg, oacc)
                  S.do("sync", "dma_start",
                       Oall4[96:128, hc, t * TW : (t + 1) * TW], ostg)
              else:
                  S.do("vector", "tensor_copy",
                       Oall4[hb : hb + 32, hc, t * TW : (t + 1) * TW], oacc)

      for t in range(4):
          for d_ in range(8):
              for hc in range(2):
                  S.do("tensor", "matmul", ps_A[:, 0:TW],
                       wproj_s[:, hc, d_ * 128 : (d_ + 1) * 128],
                       Oall4[:, hc, t * TW : (t + 1) * TW],
                       start=(hc == 0), stop=(hc == 1))
              S.do("vector", "tensor_copy", yf, ps_A[:, 0:TW])
              dst = youth[:, t * TW : (t + 1) * TW].rearrange("(c p) t -> p c t", p=128)[:, d_]
              S.do("sync", "dma_start", dst, yf)
    S.fence("sync")
    return nc


def rope_tables():
    f = 1.0 / (10000.0 ** (np.arange(0, 64, 2, dtype=np.float32) / 64.0))
    t = np.arange(SEQ, dtype=np.float32)
    a = np.outer(t, f)
    return np.ascontiguousarray(np.cos(a).T), np.ascontiguousarray(np.sin(a).T)


def attn1_in_maps(inputs, xT, xtokfull, half):
    cosq, sinq = rope_tables()
    g = lambda k: np.asarray(inputs["L1_attn_" + k], np.float32)
    w_dqn, w_dqr = g("w_dq_nope"), g("w_dq_rope")
    w_dkn, w_dv = g("w_dk_nope"), g("w_dv")
    w_selk, w_selv = g("w_selk"), g("w_selv")
    w_wink, w_winv = g("w_wink"), g("w_winv")
    w_proj = g("w_proj")

    def perm96(W, h0):
        raise RuntimeError

    def headblock_k(Wn, Wr, h0):
        cols = []
        for h in range(h0, h0 + 8):
            n = Wn[:, h * 32 : (h + 1) * 32]
            r = Wr[:, h * 64 : (h + 1) * 64]
            cols += [n, r[:, 0:32], r[:, 32:64]]
        return np.ascontiguousarray(np.concatenate(cols, axis=1))

    def headblock_kr(Wkr, h0):
        cols = []
        for h in range(h0, h0 + 8):
            blk = Wkr[:, h * 96 : (h + 1) * 96]
            cols += [blk[:, 0:32], blk[:, 32:64], blk[:, 64:96]]
        return np.ascontiguousarray(np.concatenate(cols, axis=1))

    def headblock_v(Wv, h0):
        return np.ascontiguousarray(Wv[:, h0 * 32 : (h0 + 8) * 32])

    maps = []
    for c in range(8):
        b, o = c // 2, c % 2
        h0 = o * 8
        wp8 = np.zeros((256, 1024), np.float32)
        for hp in range(8):
            src = w_proj[(h0 + hp) * 32 : (h0 + hp) * 32 + 32, :]
            wp8[(hp // 4) * 128 + (hp % 4) * 32 : (hp // 4) * 128 + (hp % 4) * 32 + 32] = src
        m = {
            "xt": f32(xT[:, b * SEQ : (b + 1) * SEQ]),
            "xtok": f32(xtokfull[b * SEQ : (b + 1) * SEQ, :]),
            "maskpl": make_mask_plane(),
            "ident": np.eye(128, dtype=np.float32),
            "stu": np.triu(np.ones((128, 128), np.float32), 1),
            "g_qnorm": g("g_qnorm").reshape(96, 1),
            "g_kvnorm": g("g_kvnorm").reshape(32, 1),
            "w_gate": g("w_gate"),
            "w_imp": f32(as_pcols(g("w_imp")[:, 0])),
            "w_cq": g("w_cq"),
            "w_ckv": g("w_ckv"),
            "w_krope": g("w_krope"),
            "dq8": headblock_k(w_dqn, w_dqr, h0),
            "dkn8": headblock_v(w_dkn, o * 8),
            "dv8": headblock_v(w_dv, o * 8),
            "selk8": headblock_kr(w_selk, h0),
            "selv8": headblock_v(w_selv, o * 8),
            "wink8": headblock_kr(w_wink, h0),
            "winv8": headblock_v(w_winv, o * 8),
            "wproj8": wp8,
            "cosq": cosq, "sinq": sinq,
            "ones_col": np.ones((128, 1), np.float32),
            "ones_row": np.ones((1, 128), np.float32),
        }
        maps.append(m)
    return maps


def run_attn1(inputs, xT):
    nc = get_prog("attn1", build_attn1)
    xtok = np.ascontiguousarray(xT.T)
    maps = attn1_in_maps(inputs, xT, xtok, 0)
    res = run_bass_kernel_spmd(nc, maps, core_ids=list(range(8)))
    return ([r["ypart0"] for r in res.results],
            [r["ypart1"] for r in res.results])





_PROG_CACHE = {}


def get_prog(name, builder, *args):
    key = (name, args)
    if key not in _PROG_CACHE:
        _PROG_CACHE[key] = builder(*args)
    return _PROG_CACHE[key]


def run_ffn0(xT, w, partials=None, bp=None):
    nc = get_prog("ffn0", build_ffn0, partials is not None)
    maps = ffn0_in_maps(xT, w["U1"], w["V1"], w["b1"], w["U2"], w["V2"], w["b2"],
                        partials=partials, bp=bp)
    res = run_bass_kernel_spmd(nc, maps, core_ids=list(range(8)))
    out = np.concatenate([r["yt"] for r in res.results], axis=1)
    return out


def assemble_partials(plist0, plist1):
    PA = np.zeros((D, NTOK), np.float32)
    PB = np.zeros((D, NTOK), np.float32)
    for c in range(8):
        b, half = c // 2, c % 2
        src = plist0 if half == 0 else plist1
        PA[:, c * TPC : (c + 1) * TPC] = src[2 * b]
        PB[:, c * TPC : (c + 1) * TPC] = src[2 * b + 1]
    return PA, PB


def kernel(**inputs):
    x = np.asarray(inputs["L0_attn_x"], np.float32).reshape(NTOK, D)
    xT = np.ascontiguousarray(x.T)
    w0 = {k: np.asarray(inputs["L0_ffn_" + k]) for k in
          ("U1", "V1", "b1", "U2", "V2", "b2")}
    xT = run_ffn0(xT, w0)
    Wq = np.asarray(inputs["L0_attn_Wq"])
    Wk = np.asarray(inputs["L0_attn_Wk"])
    Wv = np.asarray(inputs["L0_attn_Wv"])
    Wp = np.asarray(inputs["L0_attn_Wp"])
    bp = np.asarray(inputs["L0_attn_bp"])
    rel = np.asarray(inputs["L0_attn_rel_embed"])
    p0, p1 = run_attn0(xT, Wq, Wk, Wv, Wp, rel)
    PA, PB = assemble_partials(p0, p1)
    xT = run_ffn0(xT, w0, partials=(PA, PB), bp=bp)
    wgT = np.ascontiguousarray(np.asarray(inputs["L1_ffn_w_gate"]).T)
    wuT = np.ascontiguousarray(np.asarray(inputs["L1_ffn_w_up"]).T)
    wdT = np.ascontiguousarray(np.asarray(inputs["L1_ffn_w_down"]).T)
    xT = run_ffn1(xT, wgT, wuT, wdT)
    q0, q1 = run_attn1(inputs, xT)
    QA, QB = assemble_partials(q0, q1)
    xT = run_ffn1(xT, wgT, wuT, wdT, partials=(QA, QB))
    out = np.ascontiguousarray(xT.T).reshape(4, 4096, 1024)
    return out
